# version 01
# baseline (speedup 1.0000x reference)
import os as _os
import numpy as np
from contextlib import ExitStack, contextmanager

import concourse.bass as bass
import concourse.mybir as mybir
from concourse.bass_utils import run_bass_kernel_spmd

F32 = mybir.dt.float32
BF16 = mybir.dt.bfloat16
AF = mybir.ActivationFunctionType
ALU = mybir.AluOpType
AX = mybir.AxisListType

NCORE = 8
NB = 2
SEQ = 8192
D = 1024
L = SEQ // 4
EPS = 1e-5

ENGINES = ("tensor", "vector", "scalar", "gpsimd", "sync")
DSEM = {"sync": "dsy", "scalar": "dsc", "gpsimd": "dgp"}
ALL_SEMS = ("ts", "vs", "ss", "gs", "sy", "dsy", "dsc", "dgp", "cc")


class Prog:
    def __init__(self, nc, es):
        self.nc = nc
        self.es = es
        self.sem = {}
        self.cnt = {}
        for name in ALL_SEMS:
            self.sem[name] = es.enter_context(nc.semaphore(name))
            self.cnt[name] = 0
        self.streams = {e: [] for e in ENGINES}

    def get_sem(self, name):
        if name not in self.sem:
            self.sem[name] = self.es.enter_context(self.nc.semaphore(name))
            self.cnt[name] = 0
        return self.sem[name]

    def at(self, name):
        if name not in self.cnt:
            self.get_sem(name)
        return self.cnt[name]

    CHAINED = {"vector": "vs", "scalar": "ss", "gpsimd": "gs"}

    def step(self, eng, fn, waits=(), incs=(), chain=True):
        wl = []
        cs = self.CHAINED.get(eng) if chain else None
        if cs is not None:
            wl.append((cs, self.cnt[cs]))
            incs = list(incs) + [cs]
        for w in waits:
            if isinstance(w, str):
                wl.append((w, self.at(w)))
            else:
                assert w[1] <= self.at(w[0]), \
                    f"forward wait {w[0]}>={w[1]} planned={self.at(w[0])}"
                wl.append((w[0], w[1]))
        merged = {}
        for inc in incs:
            if isinstance(inc, str):
                inc = (inc, 1)
            merged[inc[0]] = merged.get(inc[0], 0) + inc[1]
        assert len(merged) <= 1, f"one sem update per instruction: {merged}"
        il = []
        for name, k in merged.items():
            self.cnt[name] += k
            il.append((self.sem[name], k))
        sems = self.sem

        def run(e, fn=fn, wl=wl, il=il):
            for name, val in wl:
                if val > 0:
                    e.wait_ge(sems[name], val)
            ins = fn(e)
            if il:
                if isinstance(ins, (list, tuple)):
                    ins = ins[-1]
                for h, k in il:
                    ins.then_inc(h, k)
        self.streams[eng].append(run)

    def dma(self, queue, out, in_, waits=(), extra_incs=(), sem=None, **kw):
        dname = sem if sem is not None else DSEM[queue]
        if sem is not None:
            self.get_sem(sem)
        self.step(queue, lambda e: e.dma_start(out=out, in_=in_, **kw),
                  waits=waits, incs=[(dname, 16)] + list(extra_incs),
                  chain=False)
        return (dname, self.cnt[dname])

    def build(self):
        with self.nc.Block() as blk:
            for ename in ENGINES:
                fns = self.streams[ename]
                if not fns:
                    continue
                getattr(blk, ename)(lambda e, fns=fns: [f(e) for f in fns])
        self.streams = {e: [] for e in ENGINES}

    def sbuf(self, name, shape, dtype=F32):
        return self.es.enter_context(self.nc.sbuf_tensor(name, shape, dtype))

    def psum(self, name, shape, dtype=F32):
        return self.es.enter_context(self.nc.psum_tensor(name, shape, dtype))

    @contextmanager
    def scope(self):
        old = self.es
        with ExitStack() as es2:
            self.es = es2
            try:
                yield
            finally:
                self.es = old



def setup_consts(p, ident_dram=None):
    c = {}
    c["ones1b"] = p.sbuf("c_ones1b", [1, 128], BF16)
    c["onescb"] = p.sbuf("c_onescb", [128, 1], BF16)
    c["ones1f"] = p.sbuf("c_ones1f", [1, 128], F32)
    c["onescf"] = p.sbuf("c_onescf", [128, 1], F32)
    p.step("gpsimd", lambda g: g.memset(c["ones1b"][:, :], 1.0))
    p.step("gpsimd", lambda g: g.memset(c["onescb"][:, :], 1.0))
    p.step("gpsimd", lambda g: g.memset(c["ones1f"][:, :], 1.0))
    p.step("gpsimd", lambda g: g.memset(c["onescf"][:, :], 1.0), incs=["gs"])
    if ident_dram is not None:
        c["identf"] = p.sbuf("c_identf", [128, 128], F32)
        p.dma("gpsimd", c["identf"][:, :], ident_dram.ap())
    c["ready"] = ("gs", p.at("gs"))
    if ident_dram is not None:
        c["ready2"] = ("dgp", p.at("dgp"))
    return c


def load_stream_fm(p, xs, x_dram):
    for cch in range(8):
        p.dma("sync", xs[:, cch * L:(cch + 1) * L],
              bass.AP(x_dram, cch * 128 * L, [[L, 128], [1, L]]))
    return ("dsy", p.at("dsy"))


def store_stream_fm(p, x_dram, xs, waits):
    for cch in range(8):
        p.dma("sync", bass.AP(x_dram, cch * 128 * L, [[L, 128], [1, L]]),
              xs[:, cch * L:(cch + 1) * L], waits=waits)
    return ("dsy", p.at("dsy"))


def make_ln_scratch(p):
    return {"sq": p.sbuf("ln_sq", [128, 1024]), "tmp": p.sbuf("ln_tmp", [128, 512]),
            "row": p.sbuf("ln_row", [1, 512]), "rowt": p.sbuf("ln_rowt", [1, 512])}


def emit_ln_fm(p, co, xs, xc, tag, xs_ready, psa, lsc):
    sq = lsc["sq"]
    tmp = lsc["tmp"]
    s1 = psa[0:1, 0:512]
    s2 = psa[0:1, 512:1024]
    mub = psa[:, 1024:1536]
    rsb = psa[:, 1536:2048]
    row = lsc["row"]
    rowt = lsc["rowt"]
    NQ = L // 512
    slot_done = [0, 0]
    for lq in range(NQ):
        o = lq * 512
        for c in range(8):
            sl = (c % 2) * 512
            p.step("scalar",
                   lambda s, sl=sl, c=c, o=o: s.activation(
                       sq[:, sl:sl + 512], xs[:, c * L + o: c * L + o + 512],
                       AF.Square),
                   waits=list(xs_ready) + [("ts", slot_done[c % 2])],
                   incs=["ss"])
            p.step("tensor",
                   lambda t, c=c, o=o: t.matmul(
                       s1[:, :], co["onescf"][:, :],
                       xs[:, c * L + o: c * L + o + 512],
                       start=(c == 0), stop=(c == 7)),
                   waits=list(xs_ready) + [co["ready"]] +
                         ([("vs", p.at("vs"))] if c == 0 else []))
            p.step("tensor",
                   lambda t, sl=sl, c=c: t.matmul(
                       s2[:, :], co["onescf"][:, :], sq[:, sl:sl + 512],
                       start=(c == 0), stop=(c == 7)),
                   waits=["ss"], incs=["ts"])
            slot_done[c % 2] = p.at("ts")
        t_stats = p.at("ts")
        p.step("vector", lambda v: v.tensor_scalar_mul(row[:, :], s1[0:1, :],
                                                       1.0 / D),
               waits=[("ts", t_stats)], incs=["vs"])
        p.step("vector", lambda v: v.tensor_mul(rowt[:, :], row[:, :], row[:, :]),
               waits=["vs"], incs=["vs"])
        p.step("vector", lambda v: v.scalar_tensor_tensor(
                   rowt[:, :], s2[0:1, :], 1.0 / D, rowt[:, :],
                   ALU.mult, ALU.subtract),
               waits=["vs"], incs=["vs"])
        p.step("vector", lambda v: v.tensor_scalar_add(rowt[:, :], rowt[:, :],
                                                       EPS),
               waits=["vs"], incs=["vs"])
        v_var = p.at("vs")
        p.step("scalar", lambda s: s.activation(rowt[:, :], rowt[:, :], AF.Sqrt),
               waits=[("vs", v_var)], incs=["ss"])
        p.step("vector", lambda v: v.reciprocal(rowt[:, :], rowt[:, :]),
               waits=["ss"], incs=["vs"])
        v_rs = p.at("vs")
        p.step("tensor", lambda t: t.matmul(mub[:, :], co["ones1f"][:, :],
                                            row[0:1, :], start=True, stop=True),
               waits=[("vs", v_var)])
        p.step("tensor", lambda t: t.matmul(rsb[:, :], co["ones1f"][:, :],
                                            rowt[0:1, :], start=True, stop=True),
               waits=[("vs", v_rs)], incs=["ts"])
        t_b = p.at("ts")
        for c in range(8):
            p.step("vector",
                   lambda v, c=c, o=o: v.tensor_sub(
                       tmp[:, :], xs[:, c * L + o: c * L + o + 512], mub[:, :]),
                   waits=[("ts", t_b)] + ([("vs", p.at("vs"))] if c == 0 else []),
                   incs=[])
            p.step("vector",
                   lambda v, c=c, o=o: v.tensor_mul(
                       xc[:, c * L + o: c * L + o + 512], tmp[:, :], rsb[:, :]),
                   incs=(["vs"] if c == 7 else []))
    return ("vs", p.at("vs"))




def prep_l1_ffn(ins):
    rm = ins["L1_ffn_rail_maker"]
    rup = ins["L1_ffn_rail_up_proj"]
    bd8 = np.zeros((4, 8, 128, 128), np.float32)
    for h in range(4):
        for g in range(8):
            for r in range(8):
                bd8[h, g, 16 * r:16 * r + 16, 16 * r:16 * r + 16] = rm[h, 8 * g + r]
    bd2 = np.zeros((2, 4 * 16, 32, 128), np.float32)
    for h in range(4):
        for pp in range(32):
            t = np.zeros((32, 128), np.float32)
            for rr in range(2):
                t[16 * rr:16 * rr + 16, 64 * rr:64 * rr + 64] = rup[h, 2 * pp + rr].T
            bd2[pp % 2, h * 16 + pp // 2] = t
    return {"l1f_bd8": np.ascontiguousarray(bd8),
            "l1f_bd2": np.ascontiguousarray(bd2),
            "l1f_wcT": np.ascontiguousarray(ins["L1_ffn_w_cproj"].T)}


def emit_l1_ffn(p, co, xs, xc, dr, xc_ready, psa):
    bd8 = p.sbuf("l1f_bd8s", [128, 32 * 128], BF16)
    bd2 = p.sbuf("l1f_bd2s", [64, 64 * 128], BF16)
    ysa = p.sbuf("l1f_ysa", [64, 8 * 1024], BF16)
    ysbb = p.sbuf("l1f_ysb", [64, 8 * 1024], BF16)
    hsb = p.sbuf("l1f_hsb", [128, 16 * 1024], BF16)
    wct = p.sbuf("l1f_wct", [128, 2 * 512], BF16)
    p.dma("gpsimd", bd8[:, :],
          bass.AP(dr["l1f_bd8"], 0, [[128, 128], [128 * 128, 32], [1, 128]]))
    p.dma("gpsimd", bass.AP(bd2, 0, [[64 * 128, 32], [128, 64], [1, 128]]),
          bass.AP(dr["l1f_bd2"], 0, [[128, 32], [32 * 128, 64], [1, 128]]))
    p.dma("gpsimd",
          bass.AP(bd2, 32 * 64 * 128, [[64 * 128, 32], [128, 64], [1, 128]]),
          bass.AP(dr["l1f_bd2"], 64 * 32 * 128,
                  [[128, 32], [32 * 128, 64], [1, 128]]))
    w_bd8 = w_bd2a = w_bd2b = ("dgp", p.at("dgp"))

    yslots = [psa[:, 0:1024], psa[:, 1024:2048]]
    zslots = [psa[:, 2048:3072], psa[:, 3072:4096]]
    wct_free = [0, 0]
    y_done = [0, 0]
    y_done_s = [0, 0]
    z_done = [0, 0]
    c_flush = 0
    for half in range(2):
        lo = half * 1024
        for h in range(4):
            for g in range(8):
                yp = yslots[g % 2]
                p.step("tensor",
                       lambda t, h=h, g=g, lo=lo, yp=yp: [
                           t.matmul(yp[:, q * 512:(q + 1) * 512],
                                    bd8[:, (h * 8 + g) * 128:(h * 8 + g + 1) * 128],
                                    xc[:, g * L + lo + q * 512:
                                       g * L + lo + (q + 1) * 512],
                                    start=True, stop=True)
                           for q in range(2)],
                       waits=[w_bd8] + list(xc_ready) +
                             [("vs", max(y_done[g % 2], c_flush)),
                              ("ss", y_done_s[g % 2])],
                       incs=["ts"])
                t_y = p.at("ts")
                p.step("scalar",
                       lambda s, g=g, yp=yp: s.copy(
                           ysa[:, g * 1024:(g + 1) * 1024], yp[0:64, :]),
                       waits=[("ts", t_y)], incs=["ss"])
                y_done_s[g % 2] = p.at("ss")
                p.step("vector",
                       lambda v, g=g, yp=yp: v.tensor_copy(
                           ysbb[:, g * 1024:(g + 1) * 1024], yp[64:128, :]),
                       waits=[("ts", t_y)], incs=["vs"])
                y_done[g % 2] = p.at("vs")
            v_yall = p.at("vs")
            s_yall = p.at("ss")
            for sub in range(2):
                for j in range(16):
                    pp = 16 * sub + j
                    zp = zslots[j % 2]
                    g = pp // 4
                    s4 = pp % 4
                    ysrc = ysa if s4 < 2 else ysbb
                    rb = 32 * (s4 % 2)
                    p.step("tensor",
                           lambda t, h=h, pp=pp, g=g, rb=rb, zp=zp, ysrc=ysrc: [
                               t.matmul(zp[:, q * 512:(q + 1) * 512],
                                        bd2[32 * (pp % 2):32 * (pp % 2) + 32,
                                            (h * 16 + pp // 2) * 128:
                                            (h * 16 + pp // 2 + 1) * 128],
                                        ysrc[rb:rb + 32,
                                             g * 1024 + q * 512:
                                             g * 1024 + (q + 1) * 512],
                                        start=True, stop=True)
                               for q in range(2)],
                           waits=[w_bd2a, w_bd2b, ("vs", max(v_yall, c_flush)),
                                  ("ss", max(s_yall, z_done[j % 2]))],
                           incs=["ts"])
                    t_z = p.at("ts")
                    p.step("scalar",
                           lambda s, j=j, zp=zp: s.activation(
                               hsb[:, j * 1024:(j + 1) * 1024], zp[:, :], AF.Gelu),
                           waits=[("ts", t_z)], incs=["ss"])
                    z_done[j % 2] = p.at("ss")
                s_hall = p.at("ss")
                for mg in range(2):
                    for kc in range(16):
                        slot = kc % 2
                        kglob = h * 32 + sub * 16 + kc
                        wld = p.dma("gpsimd",
                                    wct[:, slot * 512:(slot + 1) * 512],
                                    bass.AP(dr["l1f_wcT"],
                                            kglob * 128 * 1024 + mg * 512,
                                            [[1024, 128], [1, 512]]),
                                    waits=[("ts", wct_free[slot])],
                                    sem="w%d" % slot)
                        for lq in range(2):
                            p.step("tensor",
                                   lambda t, kc=kc, lq=lq, slot=slot: [
                                       t.matmul(psa[:, (mi * 2 + lq) * 512:
                                                    (mi * 2 + lq) * 512 + 512],
                                                wct[:, slot * 512 + mi * 128:
                                                    slot * 512 + (mi + 1) * 128],
                                                hsb[:, kc * 1024 + lq * 512:
                                                    kc * 1024 + lq * 512 + 512],
                                                start=(kc == 0), stop=(kc == 15))
                                       for mi in range(4)],
                                   waits=[wld, ("ss", s_hall)] +
                                         ([("vs", p.at("vs"))] if kc == 0 else []),
                                   incs=(["ts"] if lq == 1 else []))
                        wct_free[slot] = p.at("ts")
                    t_c = p.at("ts")
                    for mi in range(4):
                        for lq in range(2):
                            m = mg * 4 + mi
                            p.step("vector",
                                   lambda v, m=m, mi=mi, lq=lq, lo=lo: v.tensor_add(
                                       xs[:, m * L + lo + lq * 512:
                                          m * L + lo + lq * 512 + 512],
                                       xs[:, m * L + lo + lq * 512:
                                          m * L + lo + lq * 512 + 512],
                                       psa[:, (mi * 2 + lq) * 512:
                                           (mi * 2 + lq) * 512 + 512]),
                                   waits=[("ts", t_c)],
                                   incs=(["vs"] if (mi == 3 and lq == 1) else []))
                    c_flush = p.at("vs")
    return ("vs", p.at("vs"))



def build_l1_ffn_standalone():
    es = ExitStack()
    nc = bass.Bass("TRN2", target_bir_lowering=False)
    x_in = nc.dram_tensor("x_in", [D, L], F32, kind="ExternalInput")
    bd8 = nc.dram_tensor("l1f_bd8", [4, 8, 128, 128], F32, kind="ExternalInput")
    bd2 = nc.dram_tensor("l1f_bd2", [2, 64, 32, 128], F32, kind="ExternalInput")
    wcT = nc.dram_tensor("l1f_wcT", [16384, 1024], F32, kind="ExternalInput")
    x_out = nc.dram_tensor("x_out", [D, L], F32, kind="ExternalOutput")
    dr = {"l1f_bd8": bd8, "l1f_bd2": bd2, "l1f_wcT": wcT}
    with es:
        p = Prog(nc, es)
        co = setup_consts(p)
        xs = p.sbuf("xs", [128, 8 * L])
        xc = p.sbuf("xc", [128, 8 * L], BF16)
        psa = p.psum("psa", [128, 4096])
        lsc = make_ln_scratch(p)
        ld = load_stream_fm(p, xs, x_in)
        lnr = emit_ln_fm(p, co, xs, xc, "ln3", [ld], psa, lsc)
        fr = emit_l1_ffn(p, co, xs, xc, dr, [lnr], psa)
        st = store_stream_fm(p, x_out, xs, [fr])
        p.step("sync", lambda sy: None, waits=[st], chain=False)
        p.build()
    return nc


def run_l1_ffn_standalone(x3_full, ins):
    w = prep_l1_ffn(ins)
    nc = build_l1_ffn_standalone()
    in_maps = []
    for ci in range(NCORE):
        b, cch = divmod(ci, 4)
        xfm = np.ascontiguousarray(x3_full[b, cch * L:(cch + 1) * L, :].T)
        in_maps.append({"x_in": xfm, **w})
    res = run_bass_kernel_spmd(nc, in_maps, core_ids=list(range(NCORE)))
    out = np.empty((2, SEQ, D), np.float32)
    for ci in range(NCORE):
        b, cch = divmod(ci, 4)
        out[b, cch * L:(cch + 1) * L, :] = np.asarray(res.results[ci]["x_out"]).T
    return out


def kernel(**inputs):
    ins = {k: np.asarray(v) for k, v in inputs.items()}
    x0 = np.asarray(ins["L0_attn_x"], np.float32)
    id_ = np.eye(128, dtype=np.float32)
    cores = list(range(NCORE))

    nc = build_l0_attn_standalone()
    in_maps = []
    for ci in cores:
        b, cch = divmod(ci, 4)
        m = dict(prep_l0_attn(ins, ci))
        m["c_ident"] = id_
        m["x_in"] = np.ascontiguousarray(x0[b, cch * L:(cch + 1) * L, :])
        m["l0a_x0h"] = (np.zeros((128, 1024), np.float32) if cch == 0 else
                        np.ascontiguousarray(x0[b, cch * L - 128:cch * L, :]))
        in_maps.append(m)
    res = run_bass_kernel_spmd(nc, in_maps, core_ids=cores)
    xs_ = [np.ascontiguousarray(np.asarray(res.results[ci]["x_out"],
                                           np.float32)) for ci in cores]

    nc = build_l0_ffn_standalone()
    in_maps = []
    for ci in cores:
        m = dict(prep_l0_ffn(ins, ci))
        m["c_ident"] = id_
        m["x_in"] = xs_[ci]
        in_maps.append(m)
    res = run_bass_kernel_spmd(nc, in_maps, core_ids=cores)
    xs_ = [np.ascontiguousarray(np.asarray(res.results[ci]["x_out"],
                                           np.float32)) for ci in cores]

    w = prep_l1_attn(ins)
    nc = build_l1_attn_standalone()
    in_maps = [{"x_in": xs_[ci], **w} for ci in cores]
    res = run_bass_kernel_spmd(nc, in_maps, core_ids=cores)
    xs_ = [np.ascontiguousarray(np.asarray(res.results[ci]["x_out"],
                                           np.float32)) for ci in cores]

    w = prep_l1_ffn(ins)
    nc = build_l1_ffn_standalone()
    in_maps = [{"x_in": xs_[ci], **w} for ci in cores]
    res = run_bass_kernel_spmd(nc, in_maps, core_ids=cores)

    out = np.empty((NB, SEQ, D), np.float32)
    for ci in cores:
        b, cch = divmod(ci, 4)
        out[b, cch * L:(cch + 1) * L, :] = np.asarray(
            res.results[ci]["x_out"], np.float32).T
    return out


AGR = 2049


def prep_l1_attn(ins):
    e2 = np.zeros((8 * 128, 16), np.float32)
    for c in range(8):
        for hh in range(2):
            h = 2 * c + hh
            e2[c * 128 + hh * 64: c * 128 + hh * 64 + 64, h] = 1.0
    return {"l1a_wqT": np.ascontiguousarray(ins["L1_attn_Wq"].T),
            "l1a_wkT": np.ascontiguousarray(ins["L1_attn_Wk"].T),
            "l1a_wvT": np.ascontiguousarray(ins["L1_attn_Wv"].T),
            "l1a_woT": np.ascontiguousarray(ins["L1_attn_Wout"].T),
            "l1a_bout": np.ascontiguousarray(ins["L1_attn_bout"]),
            "l1a_e2": e2}


def emit_l1_attn(p, co, xs, xc, dr, xc_ready, psa):
    nc = p.nc
    agin = nc.dram_tensor("l1a_agin", [AGR, 2048], BF16)
    agout = nc.dram_tensor("l1a_agout", [4 * AGR, 2048], BF16)
    o_d = nc.dram_tensor("l1a_od", [1024, 2048], BF16)

    wbuf = p.sbuf("l1a_wbuf", [128, 8 * 1024], BF16)
    stg = p.sbuf("l1a_stg", [128, 2 * 1024], BF16)
    q_fm = p.sbuf("l1a_qfm", [128, 8 * 2048], BF16)
    sqf = p.sbuf("l1a_sqf", [128, 2048])
    sqs = sqf[:, 0:1024]
    e2s = p.sbuf("l1a_e2s", [128, 8 * 16])
    qn = p.sbuf("l1a_qn", [16, 2048], BF16)
    km1 = p.sbuf("l1a_km1", [16, 1])
    kmb = p.sbuf("l1a_kmb", [16, 128], BF16)
    kmg = p.sbuf("l1a_kmg", [16, 4])
    bline = p.sbuf("l1a_bline", [16, 1])
    nbrow = sqf[0:1, 0:16]
    negb = p.sbuf("l1a_negb", [128, 16])
    kh2 = p.sbuf("l1a_kh2", [128, 8192], BF16)
    v65 = p.sbuf("l1a_v65", [128, 64 * 65], BF16)
    sh1 = p.sbuf("l1a_sh1", [128, 2048], BF16)
    pt = sh1[:, 0:2048]
    dsg = sqf[0:1, 0:2048]
    dst = p.sbuf("l1a_dst", [16, 2048])
    recrow = sqf[0:1, 0:2048]
    och = sh1[:, 0:2048]
    boutc = e2s[:, 0:8]

    p.dma("gpsimd", e2s[:, :],
          bass.AP(dr["l1a_e2"], 0, [[16, 128], [128 * 16, 8], [1, 16]]))
    w_e2 = ("dgp", p.at("dgp"))

    pslot = [psa[:, 0:1024], psa[:, 1024:2048]]
    sqp = psa[0:16, 2048:4096]
    p._stg_free = {}

    def proj_qk(which):
        wT = "l1a_wqT" if which == "q" else "l1a_wkT"
        ev_done = [p.at("ts"), p.at("ts")]
        sq_done = [0, 0]
        for hd in range(8):
            sw = hd % 2
            wld = p.dma("gpsimd", wbuf[:, sw * 1024:(sw + 1) * 1024],
                        bass.AP(dr[wT], hd * 128,
                                [[1024, 128], [1024 * 128, 8], [1, 128]]),
                        waits=[("ts", ev_done[sw])], sem="w%d" % sw)
            for lh in range(2):
                sp = lh
                p.step("tensor",
                       lambda t, hd=hd, lh=lh, sw=sw, sp=sp: [
                           t.matmul(pslot[sp][:, q * 512:(q + 1) * 512],
                                    wbuf[:, sw * 1024 + cc * 128:
                                         sw * 1024 + (cc + 1) * 128],
                                    xc[:, cc * L + lh * 1024 + q * 512:
                                       cc * L + lh * 1024 + (q + 1) * 512],
                                    start=(cc == 0), stop=(cc == 7))
                           for q in range(2) for cc in range(8)],
                       waits=[wld] + list(xc_ready) +
                             [("vs", p.at("vs")), ("ss", p.at("ss"))],
                       incs=["ts"])
                t_mm = p.at("ts")
                if which == "q":
                    p.step("vector",
                           lambda v, hd=hd, lh=lh, sp=sp: v.tensor_copy(
                               q_fm[:, hd * 2048 + lh * 1024:
                                    hd * 2048 + (lh + 1) * 1024],
                               pslot[sp][:, :]),
                           waits=[("ts", t_mm)])
                else:
                    p.step("vector",
                           lambda v, sp=sp: v.tensor_copy(
                               stg[:, sp * 1024:(sp + 1) * 1024],
                               pslot[sp][:, :]),
                           waits=[("ts", t_mm), ("ss", sq_done[sp]),
                                  ("sa%d" % sp, p._stg_free.get(sp, 0))])
                    p.dma("sync",
                          bass.AP(agin, hd * 128 * 2048 + lh * 1024,
                                  [[2048, 128], [1, 1024]]),
                          stg[:, sp * 1024:(sp + 1) * 1024],
                          waits=["vs"], sem="sa%d" % sp)
                    p._stg_free[sp] = p.at("sa%d" % sp)
                src_ap = (q_fm[:, hd * 2048 + lh * 1024:
                               hd * 2048 + (lh + 1) * 1024]
                          if which == "q" else stg[:, sp * 1024:(sp + 1) * 1024])
                p.step("scalar",
                       lambda s, src_ap=src_ap: s.activation(sqs, src_ap,
                                                             AF.Square),
                       waits=["vs"], incs=["ss"])
                sq_done[sp] = p.at("ss")
                p.step("tensor",
                       lambda t, hd=hd, lh=lh: [
                           t.matmul(sqp[:, lh * 1024 + q * 512:
                                        lh * 1024 + (q + 1) * 512],
                                    e2s[:, hd * 16:(hd + 1) * 16],
                                    sqs[:, q * 512:(q + 1) * 512],
                                    start=(hd == 0), stop=(hd == 7))
                           for q in range(2)],
                       waits=["ss", w_e2], incs=["ts"])
                ev_done[sw] = p.at("ts")
        return p.at("ts")

    proj_qk("k")
    t_ksq = p.at("ts")
    p.step("scalar", lambda s: s.activation(dst[:, :], sqp[:, :], AF.Sqrt),
           waits=[("ts", t_ksq)], incs=["ss"])
    p.step("vector", lambda v: v.reduce_max(km1[:, :], dst[:, :], AX.X),
           waits=["ss"], incs=["vs"])
    p.step("vector", lambda v: v.tensor_scalar_mul(km1[:, :], km1[:, :], 1.01),
           incs=["vs"])
    p.step("gpsimd", lambda g: g.memset(kmb[:, :], 0.0), incs=["gs"])
    p.step("vector", lambda v: v.tensor_copy(kmb[:, 0:1], km1[:, :]),
           waits=["gs"], incs=["vs"])
    p.dma("sync", bass.AP(agin, 2048 * 2048, [[2048, 1], [1, 2048]]),
          bass.AP(kmb, 0, [[128, 16], [1, 128]]), waits=["vs"])
    w_vwt = p.dma("gpsimd", wbuf[:, :],
                  bass.AP(dr["l1a_wvT"], 0, [[1024, 128], [1024 * 128, 8],
                                             [1, 1024]]),
                  waits=[("ts", p.at("ts"))])
    vv_done = [p.at("vs"), p.at("vs")]
    for lt in range(16):
        sp = lt % 2
        p.step("tensor",
               lambda t, lt=lt, sp=sp: [
                   t.matmul(pslot[sp][:, hb * 512:(hb + 1) * 512],
                            xc[:, cc * L + lt * 128: cc * L + (lt + 1) * 128],
                            wbuf[:, cc * 1024 + hb * 512:
                                 cc * 1024 + (hb + 1) * 512],
                            start=(cc == 0), stop=(cc == 7))
                   for hb in range(2) for cc in range(8)],
               waits=[w_vwt] + list(xc_ready) +
                     [("vs", vv_done[sp] if lt >= 2 else p.at("vs"))],
               incs=["ts"])
        t_mm = p.at("ts")
        p.step("vector",
               lambda v, sp=sp: v.tensor_copy(stg[:, sp * 1024:(sp + 1) * 1024],
                                              pslot[sp][:, :]),
               waits=[("ts", t_mm), ("sa%d" % sp, p._stg_free.get(sp, 0))],
               incs=["vs"])
        vv_done[sp] = p.at("vs")
        p.dma("sync",
              bass.AP(agin, 1024 * 2048 + lt * 128 * 1024,
                      [[1024, 128], [1, 1024]]),
              stg[:, sp * 1024:(sp + 1) * 1024], waits=["vs"], sem="sa%d" % sp)
        p._stg_free[sp] = p.at("sa%d" % sp)
    q_end = proj_qk("q")
    p.step("scalar", lambda s: s.activation(qn[:, :], sqp[:, :], AF.Sqrt),
           waits=[("ts", q_end)], incs=["ss"])
    p.step("gpsimd",
           lambda g: g.collective_compute(
               "AllGather", ALU.bypass,
               replica_groups=[[0, 1, 2, 3], [4, 5, 6, 7]],
               ins=[agin.ap().opt()], outs=[agout.ap().opt()]),
           waits=["dsy", "dgp", "dsc", "sa0", "sa1"], incs=["cc"],
           chain=False)
    agv = ("cc", p.at("cc"))
    p.dma("gpsimd", kmg[:, :],
          bass.AP(agout, 2048 * 2048, [[128, 16], [AGR * 2048, 4]]),
          waits=[agv], allow_slow_non_contiguous=True, sem="r0")
    k_gl0 = ("r0", p.at("r0"))
    p.step("vector", lambda v: v.reduce_max(km1[:, :], kmg[:, :], AX.X),
           waits=[k_gl0], incs=["vs"])
    p.step("vector", lambda v: v.reduce_max(bline[:, :], qn[:, :], AX.X),
           waits=["ss"], incs=["vs"])
    p.step("vector", lambda v: v.tensor_mul(bline[:, :], bline[:, :], km1[:, :]),
           incs=["vs"])
    p.step("vector", lambda v: v.tensor_scalar_mul(bline[:, :], bline[:, :],
                                                   -0.125),
           incs=["vs"])
    p.dma("sync", nbrow, bass.AP(bline, 0, [[1, 16], [1, 1]]),
          waits=["vs"], allow_slow_non_contiguous=True)
    nb_ld = ("dsy", p.at("dsy"))
    p.step("tensor", lambda t: t.matmul(psa[:, 0:16], co["ones1f"][:, :],
                                        nbrow[0:1, :], start=True, stop=True),
           waits=[nb_ld, ("ss", p.at("ss")), ("vs", p.at("vs"))], incs=["ts"])
    p.step("vector", lambda v: v.tensor_copy(negb[:, :], psa[:, 0:16]),
           waits=["ts"], incs=["vs"])
    negb_ok = ("vs", p.at("vs"))
    p.step("gpsimd", lambda g: g.memset(v65[:, :], 1.0), incs=["gs"])
    v65_ones = ("gs", p.at("gs"))

    ost = psa[0:65, 2048:4096]
    p._l1a_exp = {}
    slab_free = [0, 0]
    vslab_free = 0
    pt_free = [0, 0]
    sS_done = p.at("ss")
    o_evac = p.at("ss")
    for h in range(16):
        par = h % 2
        vsl = v65
        k_ld = p.dma("gpsimd",
                     bass.AP(kh2, 64 * par * 8192,
                             [[8192, 64], [2048, 4], [1, 2048]]),
                     bass.AP(agout, h * 64 * 2048,
                             [[2048, 64], [AGR * 2048, 4], [1, 2048]]),
                     waits=[agv, ("ts", slab_free[par])], sem="k%d" % par)
        for vr in range(4):
            v_ld = p.dma("gpsimd",
                         bass.AP(vsl, vr * 16 * 65,
                                 [[64 * 65, 128], [65, 16], [1, 64]]),
                         bass.AP(agout, (1024 + vr * AGR) * 2048 + h * 64,
                                 [[1024, 128], [128 * 1024, 16], [1, 64]]),
                         waits=[agv, v65_ones, ("ts", vslab_free)],
                         sem="x0")
        for tt in range(64):
            for half in range(2):
                i = tt * 2 + half
                ps = i % 2
                p.step("tensor",
                       lambda t, h=h, tt=tt, par=par, ps=ps, half=half: [
                           t.matmul(pslot[ps][:, q * 512:(q + 1) * 512],
                                    kh2[64 * par:64 * par + 64,
                                        tt * 128:(tt + 1) * 128],
                                    q_fm[64 * (h % 2):64 * (h % 2) + 64,
                                         (h // 2) * 2048 + half * 1024 + q * 512:
                                         (h // 2) * 2048 + half * 1024 +
                                         (q + 1) * 512],
                                    start=True, stop=True)
                           for q in range(2)],
                       waits=[k_ld, ("ss", p._l1a_exp.get((i - 2) % 128, sS_done)),
                              ("vs", negb_ok[1])],
                       incs=["ts"])
                t_s = p.at("ts")
                p.step("scalar",
                       lambda s, h=h, ps=ps: s.activation(
                           pt[:, ps * 1024:(ps + 1) * 1024],
                           pslot[ps][:, :], AF.Exp,
                           bias=negb[:, h:h + 1], scale=0.125),
                       waits=[("ts", t_s), ("ts", pt_free[ps])], incs=["ss"])
                p._l1a_exp[i % 128] = p.at("ss")
                s_exp = p.at("ss")
                p.step("tensor",
                       lambda t, tt=tt, half=half, ps=ps, vsl=vsl: [
                           t.matmul(ost[:, half * 1024 + q * 512:
                                        half * 1024 + (q + 1) * 512],
                                    vsl[:, tt * 65:tt * 65 + 65],
                                    pt[:, ps * 1024 + q * 512:
                                       ps * 1024 + (q + 1) * 512],
                                    start=(tt == 0), stop=(tt == 63))
                           for q in range(2)],
                       waits=[v_ld, ("ss", s_exp)] +
                             ([("ss", o_evac)] if tt == 0 and half == 0 else []),
                       incs=["ts"])
                pt_free[ps] = p.at("ts")
        slab_free[par] = p.at("ts")
        vslab_free = p.at("ts")
        t_o = p.at("ts")
        p.step("scalar",
               lambda s: s.copy(stg[0:64, :].rearrange("p (a b) -> p a b", b=1024)
                                if False else stg[0:64, 0:2048], ost[0:64, :]),
               waits=[("ts", t_o), ("dsy", p.at("dsy"))], incs=["ss"])
        p.step("scalar",
               lambda s: s.copy(dsg, ost[64:65, :]),
               incs=["ss"])
        o_evac = p.at("ss")
        p.dma("sync", bass.AP(o_d, h * 64 * 2048, [[2048, 64], [1, 2048]]),
              stg[0:64, 0:2048], waits=["ss"])
        p.dma("sync", bass.AP(dst, h * 2048, [[2048, 1], [1, 2048]]),
              dsg, waits=["ss"])
    den_all = ("dsy", p.at("dsy"))
    p.dma("gpsimd", boutc,
          bass.AP(dr["l1a_bout"], 0, [[1, 128], [128, 8]]),
          waits=[("ts", p.at("ts"))], allow_slow_non_contiguous=True, sem="r0")
    w_bout = ("r0", p.at("r0"))
    p.step("vector", lambda v: v.reciprocal(dst[:, :], dst[:, :]),
           waits=[den_all], incs=["vs"])
    rec_all = ("vs", p.at("vs"))
    o_d2 = nc.dram_tensor("l1a_od2", [1024, 2048], BF16)
    och_free = [p.at("dsy"), p.at("dsy")]
    for ccx in range(8):
        so = 0
        o_ld = p.dma("sync", och[:, so * 2048:(so + 1) * 2048],
                     bass.AP(o_d, ccx * 128 * 2048, [[2048, 128], [1, 2048]]),
                     waits=[den_all, ("dsy", och_free[so])], sem="o%d" % so)
        mul_done = 0
        for hh in range(2):
            hglob = 2 * ccx + hh
            r_ld = p.dma("sync", recrow,
                         bass.AP(dst, hglob * 2048, [[2048, 1], [1, 2048]]),
                         waits=[rec_all, ("vs", mul_done)], sem="r1")
            for lq in range(4):
                rbv = psa[:, 2048 + lq * 512:2048 + lq * 512 + 512]
                p.step("tensor",
                       lambda t, lq=lq, rbv=rbv: t.matmul(
                           rbv, co["ones1f"][:, :],
                           sqf[0:1, lq * 512:(lq + 1) * 512],
                           start=True, stop=True),
                       waits=[r_ld, ("vs", p.at("vs"))], incs=["ts"])
                t_rb = p.at("ts")
                p.step("vector",
                       lambda v, hh=hh, so=so, lq=lq, rbv=rbv: v.tensor_mul(
                           och[64 * hh:64 * hh + 64,
                               so * 2048 + lq * 512:so * 2048 + (lq + 1) * 512],
                           och[64 * hh:64 * hh + 64,
                               so * 2048 + lq * 512:so * 2048 + (lq + 1) * 512],
                           rbv[64 * hh:64 * hh + 64, :]),
                       waits=[("ts", t_rb), o_ld], incs=["vs"])
            mul_done = p.at("vs")
        p.dma("sync", bass.AP(o_d2, ccx * 128 * 2048, [[2048, 128], [1, 2048]]),
              och[:, so * 2048:(so + 1) * 2048], waits=[("vs", mul_done)])
        och_free[so] = p.at("dsy")
    on_all = ("dsy", p.at("dsy"))
    dslot = [psa[:, 0:2048], psa[:, 2048:4096]]
    w_free = [p.at("ts"), p.at("ts")]
    on_free = [p.at("ts"), p.at("ts")]
    d_free = [p.at("vs"), p.at("vs")]
    for ccx in range(8):
        sw = ccx % 2
        wld = p.dma("gpsimd", wbuf[:, sw * 1024:(sw + 1) * 1024],
                    bass.AP(dr["l1a_woT"], ccx * 128,
                            [[1024, 128], [1024 * 128, 8], [1, 128]]),
                    waits=[("ts", w_free[sw])], sem="w%d" % sw)
        for hc in range(8):
            so = 0
            o_ld = p.dma("sync", och[:, so * 2048:(so + 1) * 2048],
                         bass.AP(o_d2, hc * 128 * 2048, [[2048, 128], [1, 2048]]),
                         waits=[on_all, ("ts", on_free[so])], sem="o%d" % so)
            p.step("tensor",
                   lambda t, ccx=ccx, hc=hc, sw=sw, so=so: [
                       t.matmul(dslot[ccx % 2][:, q * 512:(q + 1) * 512],
                                wbuf[:, sw * 1024 + hc * 128:
                                     sw * 1024 + (hc + 1) * 128],
                                och[:, so * 2048 + q * 512:
                                    so * 2048 + (q + 1) * 512],
                                start=(hc == 0), stop=(hc == 7))
                       for q in range(4)],
                   waits=[wld, o_ld] +
                         ([("vs", d_free[ccx % 2])] if hc == 0 else []),
                   incs=["ts"])
            on_free[so] = p.at("ts")
        w_free[sw] = p.at("ts")
        t_d = p.at("ts")
        for lh in range(2):
            p.step("vector",
                   lambda v, ccx=ccx, lh=lh: v.scalar_tensor_tensor(
                       xs[:, ccx * L + lh * 1024: ccx * L + (lh + 1) * 1024],
                       dslot[ccx % 2][:, lh * 1024:(lh + 1) * 1024],
                       boutc[:, ccx:ccx + 1],
                       xs[:, ccx * L + lh * 1024: ccx * L + (lh + 1) * 1024],
                       ALU.add, ALU.add),
                   waits=[("ts", t_d), w_bout], incs=["vs"])
        d_free[ccx % 2] = p.at("vs")
    return ("vs", p.at("vs"))


def build_l1_attn_standalone():
    es = ExitStack()
    nc = bass.Bass("TRN2", target_bir_lowering=False)
    x_in = nc.dram_tensor("x_in", [D, L], F32, kind="ExternalInput")
    dr = {}
    for nme, shape in [("l1a_wqT", [1024, 1024]), ("l1a_wkT", [1024, 1024]),
                       ("l1a_wvT", [1024, 1024]), ("l1a_woT", [1024, 1024]),
                       ("l1a_bout", [1024]), ("l1a_e2", [1024, 16])]:
        dr[nme] = nc.dram_tensor(nme, shape, F32, kind="ExternalInput")
    x_out = nc.dram_tensor("x_out", [D, L], F32, kind="ExternalOutput")
    with es:
        p = Prog(nc, es)
        co = setup_consts(p)
        xs = p.sbuf("xs", [128, 8 * L])
        xc = p.sbuf("xc", [128, 8 * L], BF16)
        psa = p.psum("psa", [128, 4096])
        lsc = make_ln_scratch(p)
        ld = load_stream_fm(p, xs, x_in)
        lnr = emit_ln_fm(p, co, xs, xc, "ln2", [ld], psa, lsc)
        fr = emit_l1_attn(p, co, xs, xc, dr, [lnr], psa)
        st = store_stream_fm(p, x_out, xs, [fr])
        p.step("sync", lambda sy: None, waits=[st], chain=False)
        p.build()
    return nc


def run_l1_attn_standalone(x2_full, ins):
    w = prep_l1_attn(ins)
    nc = build_l1_attn_standalone()
    in_maps = []
    for ci in range(NCORE):
        b, cch = divmod(ci, 4)
        xfm = np.ascontiguousarray(x2_full[b, cch * L:(cch + 1) * L, :].T)
        in_maps.append({"x_in": xfm, **w})
    res = run_bass_kernel_spmd(nc, in_maps, core_ids=list(range(NCORE)))
    out = np.empty((2, SEQ, D), np.float32)
    for ci in range(NCORE):
        b, cch = divmod(ci, 4)
        out[b, cch * L:(cch + 1) * L, :] = np.asarray(res.results[ci]["x_out"]).T
    return out


L0H = 513
YW = 2049


def _xpos_tables(t0):
    n = SEQ
    dh = 64
    half = np.arange(0, dh, 2, dtype=np.float64)
    inv_freq = 1.0 / 10000.0 ** (half / dh)
    t = np.arange(t0 - 512, t0 + 2048, dtype=np.float64)
    freqs = t[:, None] * inv_freq[None, :]
    base = (half + 0.4 * dh) / (1.4 * dh)
    power = (t - n // 2) / float(512 // 2)
    scale = base[None, :] ** power[:, None]
    cos = np.cos(freqs)
    sin = np.sin(freqs)
    qc = np.concatenate([cos * scale, cos * scale], 1)
    qs = np.concatenate([-sin * scale, sin * scale], 1)
    kc = np.concatenate([cos / scale, cos / scale], 1)
    ks = np.concatenate([-sin / scale, sin / scale], 1)
    return np.stack([qc, qs, kc, ks]).astype(np.float32)


def _l0f_masks(first_chunk):
    m = np.zeros((12, 128, 512), np.float32)
    for t in range(8):
        r = np.arange(128)[:, None] + 128 * t
        c = np.arange(512)[None, :]
        bad = (c > r) | (c < r - 512)
        m[t][bad] = -1e28
    for t in range(4):
        m[8 + t] = -1e28 if first_chunk else m[t]
    return m


def prep_l0_ffn(ins, core_id):
    b, cch = divmod(core_id, 4)
    t0 = cch * L
    d = {}
    d["l0f_tabs"] = _xpos_tables(t0)
    d["l0f_masks"] = _l0f_masks(cch == 0)
    d["l0f_flag"] = np.full((128, 1), 0.0 if cch == 0 else 1.0, np.float32)
    wq = ins["L0_ffn_wqkv"]
    d["l0f_wqkv"] = np.ascontiguousarray(wq)
    d["l0f_wo"] = np.ascontiguousarray(ins["L0_ffn_wo"])
    d["l0f_qs"] = np.ascontiguousarray(ins["L0_ffn_q_scale"])
    d["l0f_ks"] = np.ascontiguousarray(ins["L0_ffn_k_scale"])
    d["l0f_lng"] = np.ascontiguousarray(ins["L0_ffn_ln_g"])
    d["l0f_lnb"] = np.ascontiguousarray(ins["L0_ffn_ln_b"])
    d["l0f_b2"] = np.ascontiguousarray(ins["L0_ffn_b2"])
    prev = (cch + 3) % 4
    d["l0f_proff"] = np.array([[prev * 128 * (8 * 513), prev * 128 * 8]],
                              np.int32)
    w1 = ins["L0_ffn_w1"]
    b1 = ins["L0_ffn_b1"]
    w2 = ins["L0_ffn_w2"]
    w1i = np.zeros((2, 1024, 44 * 128), np.float32)
    b1i = np.zeros((2, 44 * 128), np.float32)
    w2p = np.zeros((2, 22 * 128, 1024), np.float32)
    for ll in range(2):
        a = w1[ll][:, :2730]
        g = w1[ll][:, 2730:]
        ba, bg = b1[ll][:2730], b1[ll][2730:]
        for j in range(22):
            sl = slice(128 * j, min(128 * (j + 1), 2730))
            n = sl.stop - sl.start
            w1i[ll][:, 256 * j:256 * j + n] = a[:, sl]
            w1i[ll][:, 256 * j + 128:256 * j + 128 + n] = g[:, sl]
            b1i[ll][256 * j:256 * j + n] = ba[sl]
            b1i[ll][256 * j + 128:256 * j + 128 + n] = bg[sl]
            w2p[ll][128 * j:128 * j + n] = w2[ll][sl]
    d["l0f_w1i"] = w1i
    d["l0f_b1i"] = b1i
    d["l0f_w2p"] = w2p
    return d

YBS = 2561


def emit_l0_ffn(p, co, dr, psa, x_in, x_out):
    nc = p.nc
    o_d = nc.dram_tensor("l0f_od", [512, 2048], BF16)
    ybuf_d = nc.dram_tensor("l0f_ybufd", [1024, YW], F32)
    agA_in = nc.dram_tensor("l0f_agAin", [128, 8 * 513], BF16)
    agA_out = nc.dram_tensor("l0f_agAout", [512, 8 * 513], BF16)
    agB_in = nc.dram_tensor("l0f_agBin", [128, 8], BF16)
    agB_out = nc.dram_tensor("l0f_agBout", [512, 8], BF16)

    ybc = p.sbuf("l0f_ybc", [128, 8 * YBS], BF16)
    ystg = p.sbuf("l0f_ystg", [128, 1024])
    pool = p.sbuf("l0f_pool", [128, 2048])
    pool2 = p.sbuf("l0f_pool2", [128, 6144], BF16)
    tmsc = p.sbuf("l0f_tmsc", [128, 1536], BF16)
    shA = p.sbuf("l0f_shA", [128, 11264], BF16)
    kfm = p.sbuf("l0f_kfm", [128, 4 * 2560], BF16)
    v65 = p.sbuf("l0f_v65", [128, 160 * 65], BF16)
    wqk = p.sbuf("l0f_wqk", [128, 8 * 1536], BF16)
    wstr = p.sbuf("l0f_wstr", [128, 4096], BF16)
    row = p.sbuf("l0f_row", [1, 512])
    rowt = p.sbuf("l0f_rowt", [1, 512])
    qsb = p.sbuf("l0f_qsb", [128, 128])
    nrm = p.sbuf("l0f_nrm", [128, 16])
    negbl = p.sbuf("l0f_negbl", [128, 1])
    b1c = p.sbuf("l0f_b1c", [128, 44])
    colz = p.sbuf("l0f_colz", [128, 24])
    flagc = p.sbuf("l0f_flagc", [128, 1])
    hstg = p.sbuf("l0f_hstg", [128, 8], BF16)
    hstf = p.sbuf("l0f_hstf", [128, 8])
    offs = p.sbuf("l0f_offs", [1, 2], mybir.dt.int32)
    qfm = shA[:, 0:8192]
    h2q = shA[:, 0:11264]
    och = shA[:, 0:8192]
    den = pool[0:8, 0:2048]
    rrow = p.sbuf("l0f_rrow", [1, 2048])
    ostg = tmsc[0:64, 0:512]
    ptb = tmsc[:, 512:1536]

    p._l0f_rb = 0
    p._l0f_yst = [0, 0]
    p.get_sem("y0"), p.get_sem("y1")
    p._l0f_geg = [0, 0]
    p._l0f_dfl = 0
    p._l0f_xfree = [0, 0]
    p._l0f_tabfree = 0
    p.dma("gpsimd", flagc[:, :], dr["l0f_flag"].ap(), sem="pre")
    p.dma("gpsimd", offs[:, :], dr["l0f_proff"].ap(), sem="pre")
    p.step("gpsimd", lambda g: g.memset(v65[:, :], 1.0), incs=["gs"])
    pre_ok = [("pre", p.at("pre")), ("gs", p.at("gs"))]

    def ycol(c, t):
        return c * YBS + t + 513

    s1 = psa[0:1, 0:512]
    s2 = psa[0:1, 512:1024]
    mub = psa[:, 1024:1536]
    rsb = psa[:, 1536:2048]
    xsl = [pool[:, 0:512], pool[:, 512:1024]]
    sq2 = [pool[:, 1024:1536], pool[:, 1536:2048]]

    def ln_stats_q(rhs_fn, wait_extra, shifted=False):
        for c in range(8):
            ap, w = rhs_fn(c)
            sl = c % 2
            p.step("scalar",
                   lambda s, sl=sl, ap=ap: s.activation(sq2[sl], ap, AF.Square),
                   waits=w + [("ts", p._l0f_sqfree[sl]), ("vs", p.at("vs"))],
                   incs=["ss"])
            p.step("tensor",
                   lambda t, c=c, ap=ap: t.matmul(
                       s1[:, :], co["onescf"][:, :], ap,
                       start=(c == 0), stop=(c == 7)),
                   waits=w + [co["ready"]] +
                         ([("vs", p.at("vs"))] if c == 0 else []))
            p.step("tensor",
                   lambda t, sl=sl, c=c: t.matmul(
                       s2[:, :], co["onescf"][:, :], sq2[sl],
                       start=(c == 0), stop=(c == 7)),
                   waits=["ss"], incs=["ts"])
            p._l0f_sqfree[sl] = p.at("ts")
        t_stats = p.at("ts")
        p.step("vector", lambda v: v.tensor_scalar_mul(row[:, :], s1[0:1, :],
                                                       1.0 / D),
               waits=[("ts", t_stats)], incs=["vs"])
        p.step("vector", lambda v: v.tensor_mul(rowt[:, :], row[:, :], row[:, :]),
               waits=["vs"], incs=["vs"])
        p.step("vector", lambda v: v.scalar_tensor_tensor(
                   rowt[:, :], s2[0:1, :], 1.0 / D, rowt[:, :],
                   ALU.mult, ALU.subtract),
               waits=["vs"], incs=["vs"])
        p.step("vector", lambda v: v.tensor_scalar_add(rowt[:, :], rowt[:, :],
                                                       EPS),
               waits=["vs"], incs=["vs"])
        v_var = p.at("vs")
        p.step("scalar", lambda s: s.activation(rowt[:, :], rowt[:, :], AF.Sqrt),
               waits=[("vs", v_var)], incs=["ss"])
        p.step("vector", lambda v: v.reciprocal(rowt[:, :], rowt[:, :]),
               waits=["ss"], incs=["vs"])
        v_rs = p.at("vs")
        p.step("tensor", lambda t: t.matmul(mub[:, :], co["ones1f"][:, :],
                                            row[0:1, :], start=True, stop=True),
               waits=[("vs", v_var)])
        p.step("tensor", lambda t: t.matmul(rsb[:, :], co["ones1f"][:, :],
                                            rowt[0:1, :], start=True, stop=True),
               waits=[("vs", v_rs)], incs=["ts"])
        return p.at("ts")

    p._l0f_sqfree = [0, 0]
    for lq in range(4):
        o = lq * 512

        def rhs_ln0(c, o=o):
            sl = c % 2
            ld = p.dma("sync", xsl[sl],
                       bass.AP(x_in, c * 128 * L + o, [[L, 128], [1, 512]]),
                       waits=[("ts", p._l0f_sqfree[sl]),
                              ("vs", p.at("vs"))], sem="sa%d" % sl)
            return xsl[sl], [ld]
        t_b = ln_stats_q(rhs_ln0, [])
        for c in range(8):
            sl = c % 2
            ld = p.dma("sync", xsl[sl],
                       bass.AP(x_in, c * 128 * L + o, [[L, 128], [1, 512]]),
                       waits=[("ts", p._l0f_sqfree[sl]), ("vs", p.at("vs")),
                              ("ss", p.at("ss"))], sem="sa%d" % sl)
            p.step("vector",
                   lambda v, sl=sl: v.tensor_sub(sq2[sl], xsl[sl], mub[:, :]),
                   waits=[ld, ("ts", t_b)], incs=["vs"])
            p.step("vector",
                   lambda v, sl=sl: v.tensor_mul(
                       ystg[:, sl * 512:(sl + 1) * 512], sq2[sl], rsb[:, :]),
                   waits=[("y%d" % sl, p._l0f_yst[sl])], incs=["vs"])
            p.step("vector",
                   lambda v, c=c, o=o, sl=sl: v.tensor_copy(
                       ybc[:, ycol(c, o): ycol(c, o) + 512],
                       ystg[:, sl * 512:(sl + 1) * 512]),
                   incs=["vs"])
            p.dma("gpsimd",
                  bass.AP(ybuf_d, c * 128 * YW + 1 + o, [[YW, 128], [1, 512]]),
                  ystg[:, sl * 512:(sl + 1) * 512], waits=["vs"],
                  sem="y%d" % sl)
            p._l0f_yst[sl] = p.at("y%d" % sl)
    y_ready = ("vs", p.at("vs"))

    import os as _os
    if _os.environ.get("PROBE_L0F_STOP") == "ln":
        p._l0f_ylast = [y_ready, ("y0", p.at("y0")), ("y1", p.at("y1"))]
        NLYR = 0
    else:
        NLYR = int(_os.environ.get("PROBE_L0F_LAYERS", "2"))
    for lyr in range(NLYR):
        lw = [("ts", p.at("ts")), ("vs", p.at("vs")), ("ss", p.at("ss"))]
        w_qk = p.dma("gpsimd", wqk[:, :],
                     bass.AP(dr["l0f_wqkv"], lyr * 1024 * 1536,
                             [[1536, 128], [1536 * 128, 8], [1, 1536]]),
                     waits=lw, sem="c%d" % lyr)
        p.dma("gpsimd", b1c[:, :],
              bass.AP(dr["l0f_b1i"], lyr * 5632, [[1, 128], [128, 44]]),
              waits=lw, allow_slow_non_contiguous=True, sem="c%d" % lyr)
        for ii, nm in enumerate(["l0f_lng", "l0f_lnb", "l0f_b2"]):
            p.dma("gpsimd", colz[:, 8 * ii:8 * ii + 8],
                  bass.AP(dr[nm], lyr * 1024, [[1, 128], [128, 8]]),
                  waits=lw, allow_slow_non_contiguous=True, sem="c%d" % lyr)
        for kind in range(4):
            p.dma("gpsimd", bass.AP(pool2, kind * 64,
                                    [[6144, 128], [256, 20], [1, 64]]),
                  bass.AP(dr["l0f_tabs"], kind * 2560 * 64,
                          [[64, 128], [128 * 64, 20], [1, 64]]),
                  waits=lw, sem="c%d" % lyr)
        p.dma("gpsimd", row[0:1, 0:64],
              bass.AP(dr["l0f_qs"], lyr * 64, [[128, 1], [1, 64]]),
              waits=lw, sem="c%d" % lyr)
        p.dma("gpsimd", row[0:1, 64:128],
              bass.AP(dr["l0f_ks"], lyr * 64, [[128, 1], [1, 64]]),
              waits=lw, sem="c%d" % lyr)
        wconst = ("c%d" % lyr, p.at("c%d" % lyr))
        w_qk = wconst
        p.step("tensor", lambda t: t.matmul(psa[:, 0:64], co["ones1f"][:, :],
                                            row[0:1, 0:64],
                                            start=True, stop=True),
               waits=[wconst, ("vs", p.at("vs"))])
        p.step("tensor", lambda t: t.matmul(psa[:, 64:128], co["ones1f"][:, :],
                                            row[0:1, 64:128],
                                            start=True, stop=True),
               incs=["ts"])
        p.step("vector", lambda v: v.tensor_copy(qsb[:, :], psa[:, 0:128]),
               waits=["ts"], incs=["vs"])
        qsb_ok = p.at("vs")
        for kind in range(4):
            base = (0 if kind < 2 else 64)
            if kind % 2 == 0:
                in1 = bass.AP(qsb, base, [[128, 128], [0, 20], [1, 64]])
            else:
                in1 = bass.AP(qsb, base + 32, [[128, 128], [0, 20], [-32, 2],
                                               [1, 32]])
            tab = bass.AP(pool2, kind * 64, [[6144, 128], [256, 20], [1, 64]])
            p.step("vector",
                   lambda v, tab=tab, in1=in1: v.tensor_tensor(tab, tab, in1,
                                                               ALU.mult),
                   waits=[wconst, ("vs", qsb_ok)], incs=["vs"])
        tab_ok = ("vs", p.at("vs"))
        p.step("vector", lambda v: v.reduce_max(rowt[0:1, 0:1],
                                                row[0:1, 0:64],
                                                AX.X, apply_absolute_value=True),
               waits=[wconst], incs=["vs"])
        p.step("vector", lambda v: v.reduce_max(rowt[0:1, 1:2],
                                                row[0:1, 64:128],
                                                AX.X, apply_absolute_value=True),
               incs=["vs"])
        p.step("vector", lambda v: v.tensor_mul(rowt[0:1, 0:1], rowt[0:1, 0:1],
                                                rowt[0:1, 1:2]),
               incs=["vs"])
        p.step("vector", lambda v: v.tensor_scalar_mul(rowt[0:1, 0:1],
                                                       rowt[0:1, 0:1], -8.0),
               incs=["vs"])
        p.step("tensor", lambda t: t.matmul(psa[:, 128:129], co["ones1f"][:, :],
                                            rowt[0:1, 0:1], start=True,
                                            stop=True),
               waits=["vs"], incs=["ts"])
        p.step("vector", lambda v: v.tensor_copy(negbl[:, :], psa[:, 128:129]),
               waits=["ts"], incs=["vs"])
        negb_ok = ("vs", p.at("vs"))
        if _os.environ.get("PROBE_L0F_STOP") == "consts":
            p._l0f_ylast = [negb_ok, ("y0", p.at("y0")), ("y1", p.at("y1"))]
            break

        src_ready = y_ready if lyr == 0 else p._l0f_ymid
        p.dma("sync", agA_in.ap(),
              bass.AP(ybc, ycol(0, 2048 - 513),
                      [[8 * YBS, 128], [YBS, 8], [1, 513]]),
              waits=[src_ready])
        p.step("gpsimd",
               lambda g: g.collective_compute(
                   "AllGather", ALU.bypass,
                   replica_groups=[[0, 1, 2, 3], [4, 5, 6, 7]],
                   ins=[agA_in.ap().opt()], outs=[agA_out.ap().opt()]),
               waits=["dsy", "dgp", "sa0", "sa1"], incs=["cc"], chain=False)
        agv = ("cc", p.at("cc"))
        p.step("sync", lambda sy: None, waits=[agv, pre_ok[0]], chain=False)
        p.step("sync",
               lambda sy: sy.dma_start(
                   out=bass.AP(ybc, ycol(0, -513),
                               [[8 * YBS, 128], [YBS, 8], [1, 513]]),
                   in_=bass.AP(agA_out, sy.value_load(offs[0:1, 0:1]),
                               [[8 * 513, 128], [513, 8], [1, 513]])),
               incs=[("dsy", 16)], chain=False)
        hl = ("dsy", p.at("dsy"))
        p.step("vector",
               lambda v: v.tensor_scalar(
                   bass.AP(ybc, ycol(0, -513), [[8 * YBS, 128], [YBS, 8],
                                                [1, 513]]),
                   bass.AP(ybc, ycol(0, -513), [[8 * YBS, 128], [YBS, 8],
                                                [1, 513]]),
                   flagc[:, 0:1], None, ALU.mult),
               waits=[hl, pre_ok[0]], incs=["vs"])
        halo_ok = ("vs", p.at("vs"))
        if _os.environ.get("PROBE_L0F_STOP") == "halo":
            p._l0f_ylast = [halo_ok, ("y0", p.at("y0")), ("y1", p.at("y1"))]
            break

        qkvp = psa[:, 0:1536]
        trp = [psa[:, 1536:1664], psa[:, 2048:2176]]
        trot = pool[:, 0:1536]
        ev_free = p.at("vs")
        tr_free = [p.at("vs"), p.at("vs")]
        for lt in range(20):
            t0 = -512 + 128 * lt
            obs = [1, 2] if lt < 4 else [0, 1, 2]
            p.step("tensor",
                   lambda t, lt=lt, t0=t0, obs=obs: [
                       t.matmul(qkvp[:, ob * 512:(ob + 1) * 512],
                                ybc[:, ycol(c, t0) - (1 if c >= 4 else 0):
                                    ycol(c, t0) - (1 if c >= 4 else 0) + 128],
                                wqk[:, c * 1536 + ob * 512:
                                    c * 1536 + (ob + 1) * 512],
                                start=(c == 0), stop=(c == 7))
                       for ob in obs for c in range(8)],
                   waits=[w_qk, halo_ok, ("vs", ev_free)] +
                         ([] if lyr == 0 else []),
                   incs=["ts"])
            t_qkv = p.at("ts")
            p.step("vector",
                   lambda v, obs=obs: v.tensor_copy(
                       tmsc[:, obs[0] * 512:1536], qkvp[:, obs[0] * 512:1536]),
                   waits=[("ts", t_qkv)], incs=["vs"])
            ev_free = p.at("vs")
            tm_ok = p.at("vs")
            for h in range(8):
                p.step("vector",
                       lambda v, lt=lt, h=h: v.tensor_copy(
                           v65[:, (lt * 8 + h) * 65:(lt * 8 + h) * 65 + 64],
                           tmsc[:, 1024 + h * 64:1024 + (h + 1) * 64]),
                       waits=[("vs", tm_ok)] + (pre_ok if lt == 0 else []),
                       incs=(["vs"] if h == 7 else []))
            if _os.environ.get("PROBE_L0F_QKV") == "mm":
                continue
            parts = ([("q", 0)] if lt >= 4 else []) + [("k", 512)]
            for nm2, off2 in parts:
                nb = 0 if nm2 == "q" else 8
                tks = 0 if nm2 == "q" else 2
                p.step("vector",
                       lambda v, off2=off2: v.tensor_mul(
                           trot[:, 0:512], tmsc[:, off2:off2 + 512],
                           tmsc[:, off2:off2 + 512]),
                       waits=[("vs", tm_ok)], incs=["vs"])
                p.step("vector",
                       lambda v, nb=nb: v.tensor_reduce(
                           nrm[:, nb:nb + 8],
                           trot[:, 0:512].rearrange("p (h d) -> p h d", d=64),
                           AX.X, ALU.add),
                       incs=["vs"])
                p.step("scalar",
                       lambda s, nb=nb: s.activation(nrm[:, nb:nb + 8],
                                                     nrm[:, nb:nb + 8], AF.Sqrt),
                       waits=["vs"], incs=["ss"])
                p.step("vector",
                       lambda v, nb=nb: v.tensor_scalar_max(
                           nrm[:, nb:nb + 8], nrm[:, nb:nb + 8], 1e-12),
                       waits=["ss"], incs=["vs"])
                p.step("vector",
                       lambda v, nb=nb: v.reciprocal(nrm[:, nb:nb + 8],
                                                     nrm[:, nb:nb + 8]),
                       incs=["vs"])
                ctab = bass.AP(pool2, lt * 256 + tks * 64,
                               [[6144, 128], [0, 8], [1, 64]])
                stab = bass.AP(pool2, lt * 256 + (tks + 1) * 64,
                               [[6144, 128], [0, 8], [1, 64]])
                p.step("vector",
                       lambda v, off2=off2, ctab=ctab: v.tensor_tensor(
                           trot[:, 0:512], tmsc[:, off2:off2 + 512], ctab,
                           ALU.mult),
                       waits=[tab_ok], incs=["vs"])
                p.step("vector",
                       lambda v, off2=off2, stab=stab: v.tensor_tensor(
                           trot[:, 512:1024],
                           bass.AP(tmsc, off2 + 32,
                                   [[1536, 128], [64, 8], [-32, 2], [1, 32]]),
                           stab, ALU.mult),
                       incs=["vs"])
                p.step("vector",
                       lambda v: v.tensor_add(trot[:, 512:1024], trot[:, 0:512],
                                              trot[:, 512:1024]),
                       incs=["vs"])
                p.step("vector",
                       lambda v, nb=nb: v.tensor_tensor(
                           trot[:, 1024:1536], trot[:, 512:1024],
                           bass.AP(nrm, nb, [[16, 128], [1, 8], [0, 64]]),
                           ALU.mult),
                       incs=["vs"])
                v_rot = p.at("vs")
                if _os.environ.get("PROBE_L0F_QKV") == "rot":
                    continue
                for hp in range(4):
                    sl = hp % 2
                    p.step("tensor",
                           lambda t, hp=hp, sl=sl: t.transpose(
                               trp[sl][:, :], trot[:, 1024 + hp * 128:
                                                   1024 + (hp + 1) * 128],
                               co["identf"][:, :]),
                           waits=[("vs", v_rot), ("vs", tr_free[sl]),
                                  co["ready2"]],
                           incs=["ts"])
                    t_tr = p.at("ts")
                    if nm2 == "q":
                        dst = qfm[:, hp * 2048 + t0: hp * 2048 + t0 + 128]
                    else:
                        dst = kfm[:, hp * 2560 + lt * 128:
                                  hp * 2560 + (lt + 1) * 128]
                    p.step("vector",
                           lambda v, dst=dst, sl=sl: v.tensor_copy(dst,
                                                                   trp[sl][:, :]),
                           waits=[("ts", t_tr)], incs=["vs"])
                    tr_free[sl] = p.at("vs")
        qkv_done = ("vs", p.at("vs"))

        if _os.environ.get("PROBE_L0F_STOP") == "qkv":
            p._l0f_ylast = [("vs", p.at("vs")), ("y0", p.at("y0")),
                            ("y1", p.at("y1"))]
            break
        p.dma("gpsimd", bass.AP(pool2, 0, [[6144, 128], [512, 12], [1, 512]]),
              bass.AP(dr["l0f_masks"], 0, [[512, 128], [128 * 512, 12], [1, 512]]),
              waits=[qkv_done], sem="mk")
        msk_ok = ("mk", p.at("mk"))

        sslot = [psa[:, 2048:2560], psa[:, 2560:3072]]
        oslot = [psa[0:65, 3072:3584], psa[0:65, 3584:4096]]
        s_free = [p.at("ss"), p.at("ss")]
        pt_free = [p.at("ts"), p.at("ts")]
        o_free = [p.at("ss"), p.at("ss")]
        for h in range(8):
            hp = h // 2
            hb = 64 * (h % 2)
            for w in range(4):
                osl = oslot[(h * 4 + w) % 2]
                for kvt in range(8):
                    g = 4 * w + kvt
                    ssl = sslot[kvt % 2]
                    mtile = (8 + kvt) if (w == 0 and kvt < 4) else kvt
                    p.step("tensor",
                           lambda t, hp=hp, hb=hb, g=g, w=w, ssl=ssl: t.matmul(
                               ssl[:, :],
                               kfm[hb:hb + 64, hp * 2560 + g * 128:
                                   hp * 2560 + (g + 1) * 128],
                               qfm[hb:hb + 64, hp * 2048 + w * 512:
                                   hp * 2048 + (w + 1) * 512],
                               start=True, stop=True),
                           waits=[qkv_done, ("ss", s_free[kvt % 2])],
                           incs=["ts"])
                    t_s = p.at("ts")
                    p.step("vector",
                           lambda v, ssl=ssl, mtile=mtile: v.tensor_tensor(
                               ssl[:, :], ssl[:, :],
                               pool2[:, mtile * 512:(mtile + 1) * 512],
                               ALU.add),
                           waits=[("ts", t_s), msk_ok], incs=["vs"])
                    v_m = p.at("vs")
                    p.step("scalar",
                           lambda s, ssl=ssl, kvt=kvt: s.activation(
                               ptb[:, (kvt % 2) * 512:(kvt % 2) * 512 + 512],
                               ssl[:, :], AF.Exp, bias=negbl[:, 0:1], scale=8.0),
                           waits=[("vs", v_m), negb_ok,
                                  ("ts", pt_free[kvt % 2])],
                           incs=["ss"])
                    s_free[kvt % 2] = p.at("ss")
                    s_e = p.at("ss")
                    p.step("tensor",
                           lambda t, g=g, h=h, kvt=kvt, osl=osl: t.matmul(
                               osl[:, :],
                               v65[:, (g * 8 + h) * 65:(g * 8 + h) * 65 + 65],
                               ptb[:, (kvt % 2) * 512:(kvt % 2) * 512 + 512],
                               start=(kvt == 0), stop=(kvt == 7)),
                           waits=[("ss", s_e)] +
                                 ([("ss", o_free[(h * 4 + w) % 2])]
                                  if kvt == 0 else []),
                           incs=["ts"])
                    pt_free[kvt % 2] = p.at("ts")
                t_o = p.at("ts")
                p.step("scalar",
                       lambda s, osl=osl: s.copy(ostg, osl[0:64, :]),
                       waits=[("ts", t_o), ("dsy", p.at("dsy")),
                              ("r1", p.at("r1"))], incs=["ss"])
                p.step("scalar",
                       lambda s, osl=osl: s.copy(row[:, :], osl[64:65, :]),
                       incs=["ss"])
                o_free[(h * 4 + w) % 2] = p.at("ss")
                p.dma("sync",
                      bass.AP(o_d, (128 * hp + hb) * 2048 + w * 512,
                              [[2048, 64], [1, 512]]),
                      ostg, waits=["ss"])
                p.dma("sync",
                      bass.AP(pool, h * 2048 + w * 512, [[2048, 1], [1, 512]]),
                      row[:, :], waits=["ss"], sem="r1")
        o_all = ("dsy", p.at("dsy"))
        den_all = ("r1", p.at("r1"))

        if _os.environ.get("PROBE_L0F_STOP") == "attn":
            p._l0f_ylast = [o_all, den_all, ("y0", p.at("y0")),
                            ("y1", p.at("y1"))]
            break
        p.step("vector", lambda v: v.reciprocal(den[:, :], den[:, :]),
               waits=[den_all], incs=["vs"])
        rec_ok = ("vs", p.at("vs"))
        w_wo = p.dma("gpsimd", wstr[:, :],
                     bass.AP(dr["l0f_wo"], lyr * 512 * 1024,
                             [[1024, 128], [1024 * 128, 4], [1, 1024]]),
                     waits=[("ts", p.at("ts"))], sem="wo")
        ochl = p.dma("sync", och[:, :],
                     bass.AP(o_d, 0, [[2048, 128], [128 * 2048, 4], [1, 2048]]),
                     waits=[o_all], sem="sa0")
        for h in range(8):
            hp = h // 2
            hb = 64 * (h % 2)
            p.dma("sync", rrow[:, :],
                  bass.AP(pool, h * 2048, [[2048, 1], [1, 2048]]),
                  waits=[rec_ok, ("ts", p._l0f_rb)], sem="r1")
            r_ld = ("r1", p.at("r1"))
            for w in range(4):
                rb = psa[:, 1024 + (w % 2) * 512:1024 + (w % 2) * 512 + 512]
                p.step("tensor",
                       lambda t, w=w, rb=rb: t.matmul(
                           rb, co["ones1f"][:, :],
                           rrow[0:1, w * 512:(w + 1) * 512],
                           start=True, stop=True),
                       waits=[r_ld, ("vs", p.at("vs"))], incs=["ts"])
                t_rb = p.at("ts")
                p._l0f_rb = t_rb
                p.step("vector",
                       lambda v, hp=hp, hb=hb, w=w, rb=rb: v.tensor_mul(
                           och[hb:hb + 64, hp * 2048 + w * 512:
                               hp * 2048 + (w + 1) * 512],
                           och[hb:hb + 64, hp * 2048 + w * 512:
                               hp * 2048 + (w + 1) * 512],
                           rb[hb:hb + 64, :]),
                       waits=[("ts", t_rb), ochl], incs=["vs"])
        on_ok = ("vs", p.at("vs"))
        dslot = [psa[:, 0:512], psa[:, 512:1024]]
        d_free = [p.at("vs"), p.at("vs")]
        for cx in range(8):
            for lq2 in range(4):
                sl = (cx * 4 + lq2) % 2
                p.step("tensor",
                       lambda t, cx=cx, lq2=lq2, sl=sl: [
                           t.matmul(dslot[sl][:, :],
                                    wstr[:, hdc * 1024 + cx * 128:
                                         hdc * 1024 + (cx + 1) * 128],
                                    och[:, hdc * 2048 + lq2 * 512:
                                        hdc * 2048 + (lq2 + 1) * 512],
                                    start=(hdc == 0), stop=(hdc == 3))
                           for hdc in range(4)],
                       waits=[w_wo, on_ok, ("vs", d_free[sl])], incs=["ts"])
                t_d = p.at("ts")
                p.step("vector",
                       lambda v, sl=sl: v.tensor_copy(
                           ystg[:, sl * 512:(sl + 1) * 512], dslot[sl][:, :]),
                       waits=[("ts", t_d), ("y%d" % sl, p._l0f_yst[sl])],
                       incs=["vs"])
                d_free[sl] = p.at("vs")
                p.dma("gpsimd",
                      bass.AP(ybuf_d, cx * 128 * YW + 1 + lq2 * 512,
                              [[YW, 128], [1, 512]]),
                      ystg[:, sl * 512:(sl + 1) * 512], waits=["vs"],
                      sem="y%d" % sl, accum_op=ALU.add)
                p._l0f_yst[sl] = p.at("y%d" % sl)
        yacc = [("y0", p.at("y0")), ("y1", p.at("y1"))]
        for c in range(8):
            p.dma("gpsimd", ybc[:, ycol(c, 0):ycol(c, 0) + 2048],
                  bass.AP(ybuf_d, c * 128 * YW + 1, [[YW, 128], [1, 2048]]),
                  waits=yacc + [("vs", p.at("vs"))], sem="r0")
        ymid_ok = ("r0", p.at("r0"))

        p.dma("sync", agB_in.ap(),
              bass.AP(ybc, ycol(0, 2047), [[8 * YBS, 128], [YBS, 8]]),
              waits=[ymid_ok], allow_slow_non_contiguous=True)
        p.step("gpsimd",
               lambda g: g.collective_compute(
                   "AllGather", ALU.bypass,
                   replica_groups=[[0, 1, 2, 3], [4, 5, 6, 7]],
                   ins=[agB_in.ap().opt()], outs=[agB_out.ap().opt()]),
               waits=["dsy", "dgp", "sa0", "sa1", "r0", "r1"], incs=["cc"],
               chain=False)
        agbv = ("cc", p.at("cc"))
        p.step("sync",
               lambda sy: sy.dma_start(
                   out=hstg[:, :],
                   in_=bass.AP(agB_out, sy.value_load(offs[0:1, 1:2]),
                               [[8, 128], [1, 8]])),
               waits=[agbv], incs=[("dsy", 16)], chain=False)
        hb_ld = ("dsy", p.at("dsy"))
        p.step("vector",
               lambda v: v.tensor_scalar(hstf[:, :], hstg[:, :], flagc[:, 0:1],
                                         None, ALU.mult),
               waits=[hb_ld, pre_ok[0]], incs=["vs"])
        p.dma("sync", bass.AP(ybuf_d, 0, [[YW, 128], [128 * YW, 8]]),
              bass.AP(hstf, 0, [[8, 128], [1, 8]]), waits=["vs"],
              allow_slow_non_contiguous=True)
        ymid_halo = ("dsy", p.at("dsy"))

        for lq2 in range(4):
            o2 = lq2 * 512

            def rhs_iln(c, o2=o2):
                off = 1 if c < 4 else 0
                sl = c % 2
                ld = p.dma("sync", xsl[sl],
                           bass.AP(ybuf_d, c * 128 * YW + off + o2,
                                   [[YW, 128], [1, 512]]),
                           waits=[ymid_halo, ("ts", p._l0f_sqfree[sl]),
                                  ("vs", p.at("vs"))], sem="sa%d" % sl)
                return xsl[sl], [ld]
            t_b2 = ln_stats_q(rhs_iln, [])
            for c in range(8):
                off = 1 if c < 4 else 0
                sl = c % 2
                ld = p.dma("sync", xsl[sl],
                           bass.AP(ybuf_d, c * 128 * YW + off + o2,
                                   [[YW, 128], [1, 512]]),
                           waits=[("ts", p._l0f_sqfree[sl]), ("vs", p.at("vs")),
                                  ("ss", p.at("ss"))], sem="sa%d" % sl)
                p.step("vector",
                       lambda v, sl=sl: v.tensor_sub(sq2[sl], xsl[sl],
                                                     mub[:, :]),
                       waits=[ld, ("ts", t_b2)], incs=["vs"])
                p.step("vector",
                       lambda v, sl=sl: v.tensor_mul(sq2[sl], sq2[sl],
                                                     rsb[:, :]),
                       incs=["vs"])
                p.step("vector",
                       lambda v, c=c, o2=o2, sl=sl: v.tensor_scalar(
                           ybc[:, ycol(c, o2):ycol(c, o2) + 512], sq2[sl],
                           colz[:, c:c + 1], colz[:, 8 + c:9 + c],
                           ALU.mult, ALU.add),
                       waits=[wconst], incs=["vs"])
        h_ok = ("vs", p.at("vs"))

        w1_free = [p.at("ts"), p.at("ts")]
        w2_free = [p.at("ts"), p.at("ts")]
        for lq2 in range(4):
            o2 = lq2 * 512
            ap_slot = [psa[:, 0:512], psa[:, 512:1024]]
            gp_slot = [psa[:, 1024:1536], psa[:, 1536:2048]]
            for j in range(22):
                sl = j % 2
                wld = p.dma("gpsimd", wstr[:, sl * 2048:(sl + 1) * 2048],
                            bass.AP(dr["l0f_w1i"], lyr * 1024 * 5632 + j * 256,
                                    [[5632, 128], [5632 * 128, 8], [1, 256]]),
                            waits=[("ts", w1_free[sl])], sem="w%d" % sl)
                p.step("tensor",
                       lambda t, j=j, sl=sl, o2=o2: [
                           t.matmul((ap_slot if half == 0 else gp_slot)[sl][:, :],
                                    wstr[:, sl * 2048 + ci * 256 + half * 128:
                                         sl * 2048 + ci * 256 + half * 128 + 128],
                                    ybc[:, ycol(ci, o2):ycol(ci, o2) + 512],
                                    start=(ci == 0), stop=(ci == 7))
                           for half in range(2) for ci in range(8)],
                       waits=[wld, h_ok, ("vs", p._l0f_geg[sl])],
                       incs=["ts"])
                t_w1 = p.at("ts")
                w1_free[sl] = t_w1
                p.step("scalar",
                       lambda s, j=j, sl=sl: s.activation(
                           tmsc[:, sl * 512:(sl + 1) * 512], gp_slot[sl][:, :],
                           AF.Gelu, bias=b1c[:, 2 * j + 1:2 * j + 2]),
                       waits=[("ts", t_w1), wconst], incs=["ss"])
                p.step("vector",
                       lambda v, j=j, sl=sl: v.scalar_tensor_tensor(
                           h2q[:, j * 512:(j + 1) * 512], ap_slot[sl][:, :],
                           b1c[:, 2 * j:2 * j + 1],
                           tmsc[:, sl * 512:(sl + 1) * 512],
                           ALU.add, ALU.mult),
                       waits=["ss"], incs=["vs"])
                p._l0f_geg[sl] = p.at("vs")
            h2_ok = p.at("vs")
            t_p1 = p.at("ts")
            w2_free = [max(w2_free[0], t_p1), max(w2_free[1], t_p1)]
            dsl2 = [psa[:, 2048 + k * 512:2048 + (k + 1) * 512] for k in range(4)]
            for cg in range(2):
                for j in range(22):
                    sl = j % 2
                    wld = p.dma("gpsimd", wstr[:, 2048 + sl * 512 + 0:
                                               2048 + sl * 512 + 512]
                                if False else
                                bass.AP(wstr, 4096 - (2 - sl) * 512,
                                        [[4096, 128], [1, 512]]),
                                bass.AP(dr["l0f_w2p"],
                                        lyr * 2816 * 1024 + j * 128 * 1024 +
                                        cg * 512,
                                        [[1024, 128], [1, 512]]),
                                waits=[("ts", w2_free[sl])], sem="k%d" % sl)
                    p.step("tensor",
                           lambda t, j=j, sl=sl, cg=cg: [
                               t.matmul(dsl2[ci][:, :],
                                        bass.AP(wstr,
                                                4096 - (2 - sl) * 512 + ci * 128,
                                                [[4096, 128], [1, 128]]),
                                        h2q[:, j * 512:(j + 1) * 512],
                                        start=(j == 0), stop=(j == 21))
                               for ci in range(4)],
                           waits=[wld, ("vs", h2_ok)] +
                                 ([("vs", p._l0f_dfl)] if j == 0 else []),
                           incs=["ts"])
                    w2_free[sl] = p.at("ts")
                t_w2 = p.at("ts")
                for ci in range(4):
                    cxg = cg * 4 + ci
                    sl = ci % 2
                    p.step("vector",
                           lambda v, ci=ci, cxg=cxg, sl=sl: v.tensor_scalar(
                               ystg[:, sl * 512:(sl + 1) * 512], dsl2[ci][:, :],
                               colz[:, 16 + cxg:17 + cxg], None, ALU.add),
                           waits=[("ts", t_w2), wconst,
                                  ("y%d" % sl, p._l0f_yst[sl])],
                           incs=["vs"])
                    p.dma("gpsimd",
                          bass.AP(ybuf_d, cxg * 128 * YW + 1 + o2,
                                  [[YW, 128], [1, 512]]),
                          ystg[:, sl * 512:(sl + 1) * 512], waits=["vs"],
                          sem="y%d" % sl, accum_op=ALU.add)
                    p._l0f_yst[sl] = p.at("y%d" % sl)
                p._l0f_dfl = p.at("vs")
            t_p2 = p.at("ts")
            w1_free = [max(w1_free[0], t_p2), max(w1_free[1], t_p2)]
        p._l0f_tabfree = p.at("vs")
        yacc = [("y0", p.at("y0")), ("y1", p.at("y1"))]
        if lyr == 0:
            for c in range(8):
                p.dma("gpsimd", ybc[:, ycol(c, 0):ycol(c, 0) + 2048],
                      bass.AP(ybuf_d, c * 128 * YW + 1, [[YW, 128], [1, 2048]]),
                      waits=yacc + [("vs", p.at("vs"))], sem="r0")
            p._l0f_ymid = ("r0", p.at("r0"))
        else:
            p._l0f_ymid = yacc[0]
        p._l0f_ylast = yacc

    yfin = p._l0f_ylast
    for c in range(8):
        for lq in range(4):
            sl = (c * 4 + lq) % 2
            o = lq * 512
            ld = p.dma("sync", xsl[sl],
                       bass.AP(x_in, c * 128 * L + o, [[L, 128], [1, 512]]),
                       waits=list(yfin) + [("vs", p._l0f_xfree[sl])],
                       sem="sa%d" % sl)
            ld2 = p.dma("gpsimd", ystg[:, sl * 512:(sl + 1) * 512],
                        bass.AP(ybuf_d, c * 128 * YW + 1 + o,
                                [[YW, 128], [1, 512]]),
                        waits=list(yfin) + [("vs", p._l0f_xfree[sl])],
                        sem="y%d" % sl)
            p.step("vector",
                   lambda v, sl=sl: v.tensor_add(
                       sq2[sl], xsl[sl], ystg[:, sl * 512:(sl + 1) * 512]),
                   waits=[ld, ld2, ("st%d" % sl, p.at("st%d" % sl))],
                   incs=["vs"])
            p._l0f_xfree[sl] = p.at("vs")
            p.dma("sync", bass.AP(x_out, c * 128 * L + o, [[L, 128], [1, 512]]),
                  sq2[sl], waits=["vs"], sem="st%d" % sl)
    return ("st0", p.at("st0")), ("st1", p.at("st1"))


def build_l0_ffn_standalone():
    es = ExitStack()
    nc = bass.Bass("TRN2", target_bir_lowering=False)
    x_in = nc.dram_tensor("x_in", [D, L], F32, kind="ExternalInput")
    ident = nc.dram_tensor("c_ident", [128, 128], F32, kind="ExternalInput")
    dr = {}
    for nme, shape, dt in [
            ("l0f_tabs", [4, 2560, 64], F32), ("l0f_masks", [12, 128, 512], F32),
            ("l0f_flag", [128, 1], F32), ("l0f_proff", [1, 2], mybir.dt.int32),
            ("l0f_wqkv", [2, 1024, 1536], F32), ("l0f_wo", [2, 512, 1024], F32),
            ("l0f_qs", [2, 64], F32), ("l0f_ks", [2, 64], F32),
            ("l0f_lng", [2, 1024], F32), ("l0f_lnb", [2, 1024], F32),
            ("l0f_b2", [2, 1024], F32), ("l0f_w1i", [2, 1024, 5632], F32),
            ("l0f_b1i", [2, 5632], F32), ("l0f_w2p", [2, 2816, 1024], F32)]:
        dr[nme] = nc.dram_tensor(nme, shape, dt, kind="ExternalInput")
    x_out = nc.dram_tensor("x_out", [D, L], F32, kind="ExternalOutput")
    with es:
        p = Prog(nc, es)
        co = setup_consts(p, ident)
        psa = p.psum("psa", [128, 4096])
        m1, m2 = emit_l0_ffn(p, co, dr, psa, x_in, x_out)
        p.step("sync", lambda sy: None, waits=[m1, m2], chain=False)
        p.build()
    return nc


def run_l0_ffn_standalone(x1_full, ins):
    nc = build_l0_ffn_standalone()
    in_maps = []
    id_ = np.eye(128, dtype=np.float32)
    for ci in range(NCORE):
        b, cch = divmod(ci, 4)
        w = prep_l0_ffn(ins, ci)
        xfm = np.ascontiguousarray(x1_full[b, cch * L:(cch + 1) * L, :].T)
        in_maps.append({"x_in": xfm, "c_ident": id_, **w})
    res = run_bass_kernel_spmd(nc, in_maps, core_ids=list(range(NCORE)))
    out = np.empty((2, SEQ, D), np.float32)
    for ci in range(NCORE):
        b, cch = divmod(ci, 4)
        out[b, cch * L:(cch + 1) * L, :] = np.asarray(res.results[ci]["x_out"]).T
    return out


NKH = 8
TOT = 8224


def prep_l0_attn(ins, core_id):
    b, cch = divmod(core_id, 4)
    d = {}
    win = ins["L0_attn_W_in"]
    wmem = np.zeros((1024, 1152), np.float32)
    wmem[:, :1056] = win[:, :1056]
    d["l0a_wmem"] = wmem
    wq = win[:, 1056:].reshape(1024, 4, 32, 8, 2)
    d["l0a_wq"] = np.ascontiguousarray(
        wq.transpose(0, 4, 1, 2, 3).reshape(1024, 2048))
    ck = ins["L0_attn_conv_k"][:, 0, :]
    ckp = np.zeros((4, 1152), np.float32)
    ckp[:, :1056] = ck
    ckc = np.zeros((128, 36), np.float32)
    for ci in range(9):
        for s in range(4):
            ckc[:, ci * 4 + s] = ckp[s, ci * 128:(ci + 1) * 128]
    d["l0a_ckc"] = ckc
    th = ins["L0_attn_theta"][0, 0]
    assert np.allclose(th[..., :4], -th[..., 7:3:-1], atol=1e-6), \
        "theta m-antisymmetry violated"
    d["l0a_throw"] = np.ascontiguousarray(th[..., 4:8].reshape(1, 4096))
    d["l0a_slopes"] = np.concatenate([ins["L0_attn_decay_slopes"],
                                      ins["L0_attn_anchor_slopes"]])[None, :]
    d["l0a_score"] = ins["L0_attn_score_scale"][None, :]
    t0 = cch * L
    pos = np.arange(t0, t0 + L, dtype=np.float64)
    dist = np.maximum(float(SEQ - 1) - pos, 0.0)
    d["l0a_ttab"] = np.stack([dist, pos], 1).astype(np.float32)
    c1 = ins["L0_attn_norm_scale"][:32] * np.diagonal(ins["L0_attn_W_re"])
    c2 = ins["L0_attn_norm_scale"][32:] * np.diagonal(ins["L0_attn_W_im"])
    d["l0a_c1"] = np.ascontiguousarray(c1[None, :])
    d["l0a_c2"] = np.ascontiguousarray(c2[None, :])
    d["l0a_wout"] = np.ascontiguousarray(ins["L0_attn_W_out"])
    tri = np.tril(np.ones((128, 128), np.float32)).T
    d["l0a_tri"] = tri
    w4 = np.zeros((4, 1), np.float32)
    w4[:cch, 0] = 1.0
    d["l0a_w4"] = w4
    d["l0a_flag"] = np.full((128, 1), 0.0 if cch == 0 else 1.0, np.float32)
    return d


def emit_l0_attn(p, co, dr, psa, x_in, x_out):
    nc = p.nc
    z_d = nc.dram_tensor("l0a_zd", [1152, 2176], F32)
    q_d = nc.dram_tensor("l0a_qd", [L, 2048], BF16)
    kv_d = nc.dram_tensor("l0a_kvd", [L, 1152], F32)
    qf_d = nc.dram_tensor("l0a_qfd", [4, L, 512], BF16)
    a_d = nc.dram_tensor("l0a_ad", [2, 8, L, 512], F32)
    o_d2 = nc.dram_tensor("l0a_od", [2, L, 1024], F32)
    agin = nc.dram_tensor("l0a_agin", [1, TOT], F32)
    agout = nc.dram_tensor("l0a_agout", [4, TOT], F32)

    bigb = p.sbuf("l0a_bigb", [128, 17408], BF16)
    bigf = p.sbuf("l0a_bigf", [128, 11264])
    wpool = p.sbuf("l0a_wpool", [128, 17408], BF16)
    thb = p.sbuf("l0a_thb", [128, 4096])
    pw = p.sbuf("l0a_pw", [128, 16 * 32])
    inv = p.sbuf("l0a_inv", [128, 16 * 32])
    stb = p.sbuf("l0a_stb", [128, 2048], BF16)
    stat = p.sbuf("l0a_stat", [128, 8])
    crow = p.sbuf("l0a_crow", [1, 1024])
    evr = p.sbuf("l0a_evr", [1, 1024])
    rows = p.sbuf("l0a_rows", [1, 4096 + 128])
    sprow = p.sbuf("l0a_sp", [1, 32])
    spb = p.sbuf("l0a_spb", [128, 32])
    scb = p.sbuf("l0a_scb", [128, 32])
    c12 = p.sbuf("l0a_c12", [128, 64])
    ttab = p.sbuf("l0a_ttabs", [128, 16 * 2])
    w4c = p.sbuf("l0a_w4c", [4, 1])
    flagc = p.sbuf("l0a_flagc", [128, 1])
    trif = p.sbuf("l0a_trif", [128, 128])
    npi2 = p.sbuf("l0a_npi2", [128, 1])
    xc = bigb[:, 0:17408]
    lns = bigf[:, 0:2048]
    lnt = bigf[:, 2048:3072]
    xltf = bigf[:, 3072:5120]
    stg = bigf[:, 5120:7168]
    wm = wpool[:, 0:9216]
    wst = wpool[:, 9216:17408]
    zcs = bigf[:, 0:4352]
    kvf = bigf[:, 4352:6400]
    kvf2 = bigf[:, 6400:8448]
    kvs = bigf[:, 0:256]
    tt = bigf[:, 256:2304]
    sst = bigf[:, 2304:4352]
    ab = bigf[:, 0:2048]
    ttB = bigf[:, 2048:3584]
    orow = bigf[:, 3584:4096]
    c4s = bigf[0:4, 4096:5120]
    qf = bigb[:, 0:8192]
    oslab = bigf[:, 0:4096]
    ytm = bigf[:, 4096:5120]
    xo = bigf[:, 5120:7168]
    xfs = bigf[:, 7168:9216]
    lns2 = bigf[:, 9216:11264]
    yfm = bigb[:, 8192:10240]
    wout = wpool[:, 0:8192]

    p.dma("gpsimd", trif[:, :], dr["l0a_tri"].ap(), sem="pre")
    p.dma("gpsimd", w4c[:, :], dr["l0a_w4"].ap(), sem="pre")
    p.dma("gpsimd", flagc[:, :], dr["l0a_flag"].ap(), sem="pre")
    p.dma("gpsimd", rows[0:1, 0:4096], dr["l0a_throw"].ap(), sem="pre")
    p.dma("gpsimd", rows[0:1, 4096:4128], dr["l0a_slopes"].ap(), sem="pre")
    p.dma("gpsimd", rows[0:1, 4128:4160], dr["l0a_score"].ap(), sem="pre")
    p.dma("gpsimd", rows[0:1, 4160:4192], dr["l0a_c1"].ap(), sem="pre")
    p.dma("gpsimd", rows[0:1, 4192:4224], dr["l0a_c2"].ap(), sem="pre")
    p.dma("gpsimd", ttab[:, :],
          bass.AP(dr["l0a_ttab"], 0, [[2, 128], [128 * 2, 16], [1, 2]]),
          sem="pre")
    p.step("gpsimd", lambda g: g.memset(npi2[:, :], -np.pi / 2), incs=["gs"])
    pre_ok = [("pre", p.at("pre")), ("gs", p.at("gs"))]
    for cc in range(8):
        p.step("tensor",
               lambda t, cc=cc: t.matmul(psa[:, (cc % 4) * 512:(cc % 4) * 512 + 512],
                                         co["ones1f"][:, :],
                                         rows[0:1, cc * 512:(cc + 1) * 512],
                                         start=True, stop=True),
               waits=pre_ok + [co["ready"], ("vs", p.at("vs"))], incs=["ts"])
        p.step("vector",
               lambda v, cc=cc: v.tensor_copy(thb[:, cc * 512:(cc + 1) * 512],
                                              psa[:, (cc % 4) * 512:(cc % 4) * 512 + 512]),
               waits=["ts"], incs=["vs"])
    p.step("scalar", lambda s: s.activation(sprow[:, :], rows[0:1, 4096:4128],
                                            AF.Abs),
           waits=pre_ok, incs=["ss"])
    p.step("scalar", lambda s: s.activation(sprow[:, :], sprow[:, :], AF.Exp,
                                            scale=-1.0),
           incs=["ss"])
    p.step("vector", lambda v: v.tensor_scalar_add(sprow[:, :], sprow[:, :], 1.0),
           waits=["ss"], incs=["vs"])
    p.step("scalar", lambda s: s.activation(sprow[:, :], sprow[:, :], AF.Ln),
           waits=["vs"], incs=["ss"])
    p.step("vector", lambda v: v.tensor_scalar_max(evr[0:1, 0:32],
                                                   rows[0:1, 4096:4128], 0.0),
           waits=pre_ok + ["ss"], incs=["vs"])
    p.step("vector", lambda v: v.tensor_add(sprow[:, :], sprow[:, :],
                                            evr[0:1, 0:32]),
           waits=["ss"], incs=["vs"])
    p.step("vector", lambda v: v.tensor_scalar_mul(sprow[:, :], sprow[:, :],
                                                   -1.0),
           incs=["vs"])
    v_sp = p.at("vs")
    p.step("tensor", lambda t: t.matmul(psa[:, 0:32], co["ones1f"][:, :],
                                        sprow[0:1, :], start=True, stop=True),
           waits=[("vs", v_sp), ("vs", p.at("vs"))], incs=["ts"])
    p.step("vector", lambda v: v.tensor_copy(spb[:, :], psa[:, 0:32]),
           waits=["ts"], incs=["vs"])
    p.step("tensor", lambda t: t.matmul(psa[:, 32:64], co["ones1f"][:, :],
                                        rows[0:1, 4128:4160], start=True,
                                        stop=True),
           waits=pre_ok + [("vs", p.at("vs"))], incs=["ts"])
    p.step("vector", lambda v: v.tensor_copy(scb[:, :], psa[:, 32:64]),
           waits=["ts"], incs=["vs"])
    p.step("tensor", lambda t: t.matmul(psa[:, 64:128], co["ones1f"][:, :],
                                        rows[0:1, 4160:4224], start=True,
                                        stop=True),
           waits=pre_ok + [("vs", p.at("vs"))], incs=["ts"])
    p.step("vector", lambda v: v.tensor_copy(c12[:, :], psa[:, 64:128]),
           waits=["ts"], incs=["vs"])
    con_ok = ("vs", p.at("vs"))

    trpL = [psa[:, 2560:2688], psa[:, 3072:3200]]
    tr_free = [p.at("vs"), p.at("vs")]
    lns_free = [p.at("vs"), p.at("vs")]
    xlt_free = [p.at("ts"), p.at("ts")]
    rstd_ap = bass.AP(stat, 6, [[8, 128], [0, 1024]])
    for ht in range(17):
        sl = ht % 2
        if ht == 0:
            src = dr["l0a_x0h"].ap()
        else:
            src = bass.AP(x_in, (ht - 1) * 128 * 1024, [[1024, 128], [1, 1024]])
        xv = lns[:, sl * 1024:(sl + 1) * 1024]
        lw = p.dma("sync", xv, src, waits=[("vs", lns_free[sl])],
                   sem="sa" + str(sl))
        p.step("vector", lambda v, xv=xv: v.tensor_reduce(stat[:, 0:1], xv,
                                                          AX.X, ALU.add),
               waits=[lw], incs=["vs"])
        p.step("scalar", lambda s, xv=xv: s.activation(lnt[:, :], xv, AF.Square),
               waits=[lw, "vs"], incs=["ss"])
        p.step("vector", lambda v: v.tensor_reduce(stat[:, 1:2], lnt[:, :],
                                                   AX.X, ALU.add),
               waits=["ss"], incs=["vs"])
        p.step("vector", lambda v: v.tensor_scalar_mul(stat[:, 2:3],
                                                       stat[:, 0:1], 1.0 / D),
               incs=["vs"])
        p.step("vector", lambda v: v.tensor_mul(stat[:, 3:4], stat[:, 2:3],
                                                stat[:, 2:3]),
               incs=["vs"])
        p.step("vector", lambda v: v.tensor_scalar(stat[:, 4:5], stat[:, 1:2],
                                                   1.0 / D, EPS, ALU.mult,
                                                   ALU.add),
               incs=["vs"])
        p.step("vector", lambda v: v.tensor_sub(stat[:, 4:5], stat[:, 4:5],
                                                stat[:, 3:4]),
               incs=["vs"])
        p.step("scalar", lambda s: s.activation(stat[:, 5:6], stat[:, 4:5],
                                                AF.Sqrt),
               waits=["vs"], incs=["ss"])
        p.step("vector", lambda v: v.reciprocal(stat[:, 6:7], stat[:, 5:6]),
               waits=["ss"], incs=["vs"])
        xlf = xltf[:, sl * 1024:(sl + 1) * 1024]
        p.step("vector",
               lambda v, xv=xv, xlf=xlf: v.scalar_tensor_tensor(
                   xlf, xv, stat[:, 2:3], rstd_ap, ALU.subtract, ALU.mult),
               waits=[("ts", xlt_free[sl])], incs=["vs"])
        lns_free[sl] = p.at("vs")
        v_x = p.at("vs")
        for cc in range(8):
            tsl = cc % 2
            p.step("tensor",
                   lambda t, cc=cc, tsl=tsl, xlf=xlf: t.transpose(
                       trpL[tsl][:, :], xlf[:, cc * 128:(cc + 1) * 128],
                       co["identf"][:, :]),
                   waits=[("vs", v_x), ("vs", tr_free[tsl]), co["ready2"]],
                   incs=["ts"])
            t_tr = p.at("ts")
            p.step("vector",
                   lambda v, cc=cc, ht=ht, tsl=tsl: v.tensor_copy(
                       xc[:, cc * 2176 + ht * 128:cc * 2176 + (ht + 1) * 128],
                       trpL[tsl][:, :]),
                   waits=[("ts", t_tr)], incs=["vs"])
            tr_free[tsl] = p.at("vs")
        xlt_free[sl] = p.at("ts")
        if ht == 0:
            halo_ap = bass.AP(bigb, 0, [[17408, 128], [2176, 8], [1, 128]])
            p.step("vector",
                   lambda v: v.tensor_scalar(halo_ap, halo_ap, flagc[:, 0:1],
                                             None, ALU.mult),
                   waits=[pre_ok[0]], incs=["vs"])
    xc_ok = ("vs", p.at("vs"))
    if _os.environ.get("PROBE_L0A_STOP") == "ln":
        p._l0a_last = [xc_ok]
        return

    wm_ok = p.dma("gpsimd", wm,
                  bass.AP(dr["l0a_wmem"], 0,
                          [[1152, 128], [128 * 1152, 8], [1, 1152]]),
                  sem="c0")
    zslots = [psa[:, 0:512], psa[:, 512:1024]]
    zfree = [p.at("vs"), p.at("vs")]
    zst_dma = [None, None]
    blocks = [(0, 512), (512, 512), (1024, 512), (1536, 512), (2048, 128)]
    ii = 0
    for ci in range(9):
        for (b0, bn) in blocks:
            sl = ii % 2
            ii += 1
            for kc in range(8):
                w = [xc_ok, wm_ok] if kc == 0 else []
                if kc == 0:
                    w.append(("vs", zfree[sl]))
                p.step("tensor",
                       lambda t, kc=kc, ci=ci, b0=b0, bn=bn, sl=sl: t.matmul(
                           zslots[sl][:, 0:bn],
                           wm[:, kc * 1152 + ci * 128:kc * 1152 + (ci + 1) * 128],
                           xc[:, kc * 2176 + b0:kc * 2176 + b0 + bn],
                           start=(kc == 0), stop=(kc == 7)),
                       waits=w, incs=(["ts"] if kc == 7 else []))
            t_z = p.at("ts")
            zs = "st" + str(sl)
            p.step("vector",
                   lambda v, sl=sl, bn=bn: v.tensor_copy(
                       stg[:, sl * 512:sl * 512 + bn], zslots[sl][:, 0:bn]),
                   waits=[("ts", t_z)] +
                         ([zst_dma[sl]] if zst_dma[sl] else []),
                   incs=["vs"])
            zfree[sl] = p.at("vs")
            zst_dma[sl] = p.dma(
                "sync", bass.AP(z_d, ci * 128 * 2176 + b0, [[2176, 128], [1, bn]]),
                stg[:, sl * 512:sl * 512 + bn], waits=["vs"], sem=zs)
    z_done = [zst_dma[0], zst_dma[1]]

    qslots = [psa[:, 1024:1536], psa[:, 1536:2048]]
    qfree = [p.at("vs"), p.at("vs")]
    wq_used = [p.at("ts"), p.at("ts")]
    fold_dma = {}
    for ob in range(4):
        wsl = ob % 2
        wq_ok = p.dma("gpsimd", wst[:, wsl * 4096:(wsl + 1) * 4096],
                      bass.AP(dr["l0a_wq"], ob * 512,
                              [[2048, 128], [128 * 2048, 8], [1, 512]]),
                      waits=[("ts", wq_used[wsl])], sem="w" + str(wsl))
        for lt in range(16):
            qsl = lt % 2
            for cc in range(8):
                w = [xc_ok, wq_ok] if cc == 0 else []
                if cc == 0:
                    w.append(("vs", qfree[qsl]))
                p.step("tensor",
                       lambda t, cc=cc, lt=lt, wsl=wsl, qsl=qsl: t.matmul(
                           qslots[qsl][:, :],
                           xc[:, cc * 2176 + 128 + lt * 128:
                              cc * 2176 + 128 + (lt + 1) * 128],
                           wst[:, wsl * 4096 + cc * 512:wsl * 4096 + (cc + 1) * 512],
                           start=(cc == 0), stop=(cc == 7)),
                       waits=w, incs=(["ts"] if cc == 7 else []))
            wq_used[wsl] = p.at("ts")
            t_q = p.at("ts")
            ssl = 2 + qsl
            p.step("vector",
                   lambda v, ssl=ssl, qsl=qsl: v.tensor_copy(
                       stg[:, ssl * 512:(ssl + 1) * 512], qslots[qsl][:, :]),
                   waits=[("ts", t_q)], incs=["vs"])
            qfree[qsl] = p.at("vs")
            base = 5120 + ssl * 512
            inP0 = bass.AP(bigf, base + 4, [[11264, 128], [8, 64], [1, 4]])
            inP1 = bass.AP(bigf, base + 3, [[11264, 128], [8, 64], [-1, 4]])
            for fi, op in ((0, ALU.add), (1, ALU.subtract)):
                fs = (lt % 2) * 2 + fi
                fsem = "qf" + str(fs)
                w = [(fsem, p.at(fsem))] if fsem in p.cnt else []
                p.step("vector",
                       lambda v, fs=fs, op=op, inP0=inP0, inP1=inP1:
                       v.tensor_tensor(stb[:, fs * 256:(fs + 1) * 256],
                                       inP0, inP1, op),
                       waits=w, incs=["vs"])
                f = fi + (2 if ob >= 2 else 0)
                fold_dma[(f, lt, ob % 2)] = p.dma(
                    "sync",
                    bass.AP(qf_d, (f * L + lt * 128) * 512 + (ob % 2) * 256,
                            [[512, 128], [1, 256]]),
                    stb[:, fs * 256:(fs + 1) * 256], waits=["vs"], sem=fsem)
    qfold_done = [("qf%d" % i, p.at("qf%d" % i)) for i in range(4)]
    if _os.environ.get("PROBE_L0A_STOP") == "proj":
        p._l0a_last = z_done + qfold_done
        return

    cks = p.sbuf("l0a_cks", [128, 36])
    ck_ok = p.dma("gpsimd", cks[:, :], dr["l0a_ckc"].ap(), sem="c1")
    zcs_free = [p.at("vs"), p.at("vs")]
    kvf_free = p.at("ts")
    kvd_dma = None
    for ci in range(9):
        sl = ci % 2
        zl = p.dma("sync", zcs[:, sl * 2176:(sl + 1) * 2176],
                   bass.AP(z_d, ci * 128 * 2176, [[2176, 128], [1, 2176]]),
                   waits=z_done + [("vs", zcs_free[sl])], sem="y" + str(sl))
        zb = sl * 2176
        p.step("vector",
               lambda v, zb=zb, ci=ci: v.tensor_scalar(
                   kvf[:, :], zcs[:, zb + 125:zb + 125 + 2048],
                   cks[:, ci * 4:ci * 4 + 1], None, ALU.mult),
               waits=[zl, ck_ok, ("ts", kvf_free)], incs=["vs"])
        for s in range(1, 4):
            p.step("vector",
                   lambda v, zb=zb, ci=ci, s=s: v.scalar_tensor_tensor(
                       kvf[:, :], zcs[:, zb + 125 + s:zb + 125 + s + 2048],
                       cks[:, ci * 4 + s:ci * 4 + s + 1], kvf[:, :],
                       ALU.mult, ALU.add),
                   incs=["vs"])
        zcs_free[sl] = p.at("vs")
        v_kv = p.at("vs")
        for ltb in range(16):
            tsl = ltb % 2
            p.step("tensor",
                   lambda t, ltb=ltb, tsl=tsl: t.transpose(
                       trpL[tsl][:, :], kvf[:, ltb * 128:(ltb + 1) * 128],
                       co["identf"][:, :]),
                   waits=[("vs", v_kv), ("vs", tr_free[tsl]), co["ready2"]],
                   incs=["ts"])
            t_tr = p.at("ts")
            w = [("ts", t_tr)] + ([kvd_dma] if kvd_dma else [])
            p.step("vector",
                   lambda v, ltb=ltb, tsl=tsl: v.tensor_copy(
                       kvf2[:, ltb * 128:(ltb + 1) * 128], trpL[tsl][:, :]),
                   waits=w, incs=["vs"])
            tr_free[tsl] = p.at("vs")
        kvf_free = p.at("ts")
        kvd_dma = p.dma("sync",
                        bass.AP(kv_d, ci * 128,
                                [[1152, 128], [128 * 1152, 16], [1, 128]]),
                        kvf2[:, :], waits=["vs"], sem="r0")
    conv_done = kvd_dma
    trp_clear = [("vs", tr_free[0]), ("vs", tr_free[1])]
    if _os.environ.get("PROBE_L0A_STOP") == "conv":
        p._l0a_last = [conv_done]
        return
    p2_vs = p.at("vs")

    kvs_free = [p.at("vs"), p.at("vs")]
    t1 = tt[:, 0:32]
    t2 = tt[:, 32:64]
    tg_dma = [None, None]
    for lt in range(16):
        sl = lt % 2
        kd = p.dma("sync", kvs[:, sl * 128:sl * 128 + 32],
                   bass.AP(kv_d, lt * 128 * 1152 + 1024, [[1152, 128], [1, 32]]),
                   waits=[conv_done, ("vs", kvs_free[sl])], sem="k" + str(sl))
        p.step("vector",
               lambda v, sl=sl: v.tensor_tensor(
                   t1, kvs[:, sl * 128:sl * 128 + 32], scb[:, :], ALU.mult),
               waits=[kd, con_ok, "ss"], incs=["vs"])
        kvs_free[sl] = p.at("vs")
        p.step("vector",
               lambda v: v.tensor_scalar(t1, t1, 20.0, -20.0, ALU.min, ALU.max),
               incs=["vs"])
        p.step("vector",
               lambda v, lt=lt: v.tensor_tensor(
                   t2[:, 0:28],
                   bass.AP(ttab, lt * 2, [[32, 128], [0, 28]]),
                   spb[:, 0:28], ALU.mult),
               incs=["vs"])
        p.step("vector",
               lambda v, lt=lt: v.tensor_tensor(
                   t2[:, 28:32],
                   bass.AP(ttab, lt * 2 + 1, [[32, 128], [0, 4]]),
                   spb[:, 28:32], ALU.mult),
               incs=["vs"])
        p.step("vector", lambda v: v.tensor_add(t1, t1, t2), incs=["vs"])
        p.step("scalar",
               lambda s, lt=lt: s.activation(pw[:, lt * 32:(lt + 1) * 32],
                                             t1, AF.Exp),
               waits=["vs"], incs=["ss"])
        w = [("ss", p.at("ss"))]
        if lt == 0:
            w += trp_clear + [("vs", p2_vs)]
        p.step("tensor",
               lambda t, lt=lt: t.matmul(psa[0:1, 3072:3104], co["onescf"][:, :],
                                         pw[:, lt * 32:(lt + 1) * 32],
                                         start=(lt == 0), stop=(lt == 15)),
               waits=w, incs=["ts"])
    p.step("vector", lambda v: v.tensor_copy(evr[0:1, 0:32], psa[0:1, 3072:3104]),
           waits=["ts"], incs=["vs"])
    tg_dma[0] = p.dma("sync", bass.AP(agin, 8192, [[32, 1], [1, 32]]),
                      evr[0:1, 0:32], waits=["vs"], sem="tg0")
    if _os.environ.get("PROBE_L0A_STOP") == "pw":
        p._l0a_last = [tg_dma[0]]
        return

    sst_ts = [p.at("ts")] * 4
    c2n_mark = p.at("ss")
    ev_mark = p.at("vs")
    for cp in range(8):
        for lt in range(16):
            sl = lt % 2
            kd = p.dma("sync", kvs[:, sl * 128:(sl + 1) * 128],
                       bass.AP(kv_d, lt * 128 * 1152 + cp * 128,
                               [[1152, 128], [1, 128]]),
                       waits=[conv_done, ("vs", kvs_free[sl])], sem="k" + str(sl))
            kv_ap = bass.AP(bigf, sl * 128, [[11264, 128], [1, 128], [0, 4]])
            p.step("vector",
                   lambda v, cp=cp, kv_ap=kv_ap: v.tensor_tensor(
                       tt[:, 0:512], kv_ap, thb[:, cp * 512:(cp + 1) * 512],
                       ALU.mult),
                   waits=[kd, con_ok, ("ss", c2n_mark)], incs=["vs"])
            kvs_free[sl] = p.at("vs")
            p.step("scalar",
                   lambda s: s.activation(tt[:, 512:1024], tt[:, 0:512],
                                          AF.Sin, scale=0.5),
                   waits=["vs"], incs=["ss"])
            p.step("scalar",
                   lambda s: s.activation(tt[:, 1024:1536], tt[:, 0:512],
                                          AF.Abs, scale=0.5),
                   incs=["ss"])
            p.step("scalar",
                   lambda s: s.activation(tt[:, 0:512], tt[:, 1024:1536],
                                          AF.Sin, bias=npi2[:, 0:1]),
                   waits=["gs"], incs=["ss"])
            c2n_mark = p.at("ss")
            p.step("vector",
                   lambda v: v.tensor_mul(tt[:, 1536:2048], tt[:, 512:1024],
                                          tt[:, 512:1024]),
                   waits=["ss"], incs=["vs"])
            p.step("vector",
                   lambda v: v.tensor_scalar(tt[:, 1536:2048], tt[:, 1536:2048],
                                             -2.0, 1.0, ALU.mult, ALU.add),
                   incs=["vs"])
            pw_ap = bass.AP(pw, lt * 32 + 4 * cp, [[512, 128], [1, 4], [0, 128]])
            srsem = "sr" + str(sl)
            sisem = "si" + str(sl)
            p.step("vector",
                   lambda v, sl=sl, pw_ap=pw_ap: v.tensor_tensor(
                       sst[:, sl * 512:(sl + 1) * 512], tt[:, 1536:2048],
                       pw_ap, ALU.mult),
                   waits=[(srsem, p.at(srsem)), ("ts", sst_ts[sl])],
                   incs=["vs"])
            w = [("vs", p.at("vs"))]
            if lt == 0:
                w.append(("vs", ev_mark))
            p.step("tensor",
                   lambda t, sl=sl, lt=lt: t.matmul(
                       psa[0:1, 0:512], co["onescf"][:, :],
                       sst[:, sl * 512:(sl + 1) * 512],
                       start=(lt == 0), stop=(lt == 15)),
                   waits=w, incs=["ts"])
            p.dma("sync",
                  bass.AP(a_d, (cp * 2048 + lt * 128) * 512,
                          [[512, 128], [1, 512]]),
                  sst[:, sl * 512:(sl + 1) * 512], waits=["vs"], sem=srsem)
            sst_ts[sl] = p.at("ts")
            p.step("vector",
                   lambda v: v.scalar_tensor_tensor(
                       tt[:, 1536:2048], tt[:, 512:1024], -2.0, tt[:, 0:512],
                       ALU.mult, ALU.mult),
                   incs=["vs"])
            p.step("vector",
                   lambda v, sl=sl, pw_ap=pw_ap: v.tensor_tensor(
                       sst[:, 1024 + sl * 512:1024 + (sl + 1) * 512],
                       tt[:, 1536:2048], pw_ap, ALU.mult),
                   waits=[(sisem, p.at(sisem)), ("ts", sst_ts[2 + sl])],
                   incs=["vs"])
            p.step("tensor",
                   lambda t, sl=sl, lt=lt: t.matmul(
                       psa[0:1, 512:1024], co["onescf"][:, :],
                       sst[:, 1024 + sl * 512:1024 + (sl + 1) * 512],
                       start=(lt == 0), stop=(lt == 15)),
                   waits=["vs"], incs=["ts"])
            p.dma("sync",
                  bass.AP(a_d, ((8 + cp) * 2048 + lt * 128) * 512,
                          [[512, 128], [1, 512]]),
                  sst[:, 1024 + sl * 512:1024 + (sl + 1) * 512],
                  waits=["vs"], sem=sisem)
            sst_ts[2 + sl] = p.at("ts")
        t_tot = p.at("ts")
        p.step("vector",
               lambda v: v.tensor_copy(evr[0:1, 0:512], psa[0:1, 0:512]),
               waits=[("ts", t_tot), ("tg0", p.at("tg0"))], incs=["vs"])
        tg_dma[0] = p.dma("sync",
                          bass.AP(agin, cp * 512, [[512, 1], [1, 512]]),
                          evr[0:1, 0:512], waits=["vs"], sem="tg0")
        p.step("vector",
               lambda v: v.tensor_copy(evr[0:1, 512:1024], psa[0:1, 512:1024]),
               waits=[("ts", t_tot), ("tg1", p.at("tg1"))], incs=["vs"])
        tg_dma[1] = p.dma("sync",
                          bass.AP(agin, 4096 + cp * 512, [[512, 1], [1, 512]]),
                          evr[0:1, 512:1024], waits=["vs"], sem="tg1")
        ev_mark = p.at("vs")

    p.step("gpsimd",
           lambda g: g.collective_compute(
               "AllGather", ALU.bypass,
               replica_groups=[[0, 1, 2, 3], [4, 5, 6, 7]],
               ins=[agin.ap().opt()], outs=[agout.ap().opt()]),
           waits=[("tg0", p.at("tg0")), ("tg1", p.at("tg1"))],
           incs=["cc"], chain=False)
    agv = ("cc", p.at("cc"))
    stash_done = [("sr0", p.at("sr0")), ("sr1", p.at("sr1")),
                  ("si0", p.at("si0")), ("si1", p.at("si1"))]
    if _os.environ.get("PROBE_L0A_STOP") == "ag":
        p._l0a_last = [agv] + stash_done
        return

    cum_re = [psa[:, 1024:1536], psa[:, 1536:2048]]
    cum_im = [psa[:, 2048:2560], psa[:, 2560:3072]]
    ph6_vs = p.at("vs")
    invfree = [ph6_vs, ph6_vs]
    prodfree_re = [ph6_vs, ph6_vs]
    prodfree_im = [ph6_vs, ph6_vs]
    c4d = p.dma("sync", c4s[:, 0:32], bass.AP(agout, 8192, [[TOT, 4], [1, 32]]),
                waits=[agv] + stash_done, sem="c40")
    p.step("tensor",
           lambda t: t.matmul(psa[0:1, 0:32], w4c[:, :], c4s[:, 0:32],
                              start=True, stop=True),
           waits=[c4d, pre_ok[0]], incs=["ts"])
    p.step("vector", lambda v: v.tensor_copy(crow[0:1, 0:32], psa[0:1, 0:32]),
           waits=["ts"], incs=["vs"])
    v_add = p.at("vs")
    for lt in range(16):
        sl = lt % 2
        p.step("tensor",
               lambda t, sl=sl, lt=lt: t.matmul(
                   cum_re[sl][:, 0:32], trif[:, :], pw[:, lt * 32:(lt + 1) * 32],
                   start=True, stop=False),
               waits=[("vs", invfree[sl]), ("vs", v_add)])
        p.step("tensor",
               lambda t, sl=sl: t.matmul(cum_re[sl][:, 0:32],
                                         co["ones1f"][:, :], crow[0:1, 0:32],
                                         start=False, stop=True),
               incs=["ts"])
        t_cum = p.at("ts")
        p.step("tensor",
               lambda t, lt=lt: t.matmul(psa[0:1, 0:32], co["onescf"][:, :],
                                         pw[:, lt * 32:(lt + 1) * 32],
                                         start=True, stop=True),
               waits=[("vs", v_add)], incs=["ts"])
        t_ct = p.at("ts")
        p.step("vector",
               lambda v: v.tensor_add(crow[0:1, 0:32], crow[0:1, 0:32],
                                      psa[0:1, 0:32]),
               waits=[("ts", t_ct)], incs=["vs"])
        v_add = p.at("vs")
        p.step("vector",
               lambda v, sl=sl, lt=lt: v.tensor_scalar_max(
                   inv[:, lt * 32:(lt + 1) * 32], cum_re[sl][:, 0:32], 1e-4),
               waits=[("ts", t_cum)], incs=["vs"])
        p.step("vector",
               lambda v, lt=lt: v.reciprocal(inv[:, lt * 32:(lt + 1) * 32],
                                             inv[:, lt * 32:(lt + 1) * 32]),
               incs=["vs"])
        invfree[sl] = p.at("vs")
    cb_mark = p.at("vs")
    if _os.environ.get("PROBE_L0A_STOP") == "den":
        p._l0a_last = [("vs", cb_mark)]
        return

    ab_used = [p.at("ts")] * 4
    qprod = p.at("vs")
    for kq in range(4):
        qw = [agv, ("vs", qprod)] + (qfold_done if kq == 0 else [])
        for f in range(4):
            p.dma("sync", qf[:, f * 2048:(f + 1) * 2048],
                  bass.AP(qf_d, f * L * 512 + kq * 128,
                          [[512, 128], [128 * 512, 16], [1, 128]]),
                  waits=qw, sem="mk")
        qf_ok = ("mk", p.at("mk"))
        for cpl in range(2):
            cp = kq * 2 + cpl
            c4r = p.dma("sync", c4s[:, 0:512],
                        bass.AP(agout, cp * 512, [[TOT, 4], [1, 512]]),
                        waits=[agv], sem="c40")
            c4i = p.dma("sync", c4s[:, 512:1024],
                        bass.AP(agout, 4096 + cp * 512, [[TOT, 4], [1, 512]]),
                        waits=[agv], sem="c41")
            p.step("tensor",
                   lambda t: t.matmul(psa[0:1, 0:512], w4c[:, :],
                                      c4s[:, 0:512], start=True, stop=True),
                   waits=[c4r, ("vs", cb_mark), pre_ok[0]], incs=["ts"])
            p.step("tensor",
                   lambda t: t.matmul(psa[0:1, 512:1024], w4c[:, :],
                                      c4s[:, 512:1024], start=True, stop=True),
                   waits=[c4i], incs=["ts"])
            t_c0 = p.at("ts")
            p.step("vector",
                   lambda v: v.tensor_copy(crow[0:1, 0:512], psa[0:1, 0:512]),
                   waits=[("ts", t_c0)], incs=["vs"])
            p.step("vector",
                   lambda v: v.tensor_copy(crow[0:1, 512:1024],
                                           psa[0:1, 512:1024]),
                   incs=["vs"])
            v_add = p.at("vs")
            for lt in range(16):
                sl = lt % 2
                srs, sis = "sr" + str(sl), "si" + str(sl)
                abw = [agv] if (kq == 0 and cpl == 0 and lt < 2) else []
                ar = p.dma("sync", ab[:, sl * 512:(sl + 1) * 512],
                           bass.AP(a_d, (cp * 2048 + lt * 128) * 512,
                                   [[512, 128], [1, 512]]),
                           waits=abw + [(srs, p.at(srs)), ("ts", ab_used[sl])],
                           sem=srs)
                ai = p.dma("sync", ab[:, 1024 + sl * 512:1024 + (sl + 1) * 512],
                           bass.AP(a_d, ((8 + cp) * 2048 + lt * 128) * 512,
                                   [[512, 128], [1, 512]]),
                           waits=abw + [(sis, p.at(sis)),
                                        ("ts", ab_used[2 + sl])],
                           sem=sis)
                p.step("tensor",
                       lambda t, sl=sl: t.matmul(
                           cum_re[sl][:, :], trif[:, :],
                           ab[:, sl * 512:(sl + 1) * 512],
                           start=True, stop=False),
                       waits=[ar, ("vs", prodfree_re[sl]), ("vs", v_add)])
                p.step("tensor",
                       lambda t, sl=sl: t.matmul(
                           cum_re[sl][:, :], co["ones1f"][:, :],
                           crow[0:1, 0:512], start=False, stop=True),
                       incs=["ts"])
                p.step("tensor",
                       lambda t, sl=sl: t.matmul(
                           cum_im[sl][:, :], trif[:, :],
                           ab[:, 1024 + sl * 512:1024 + (sl + 1) * 512],
                           start=True, stop=False),
                       waits=[ai, ("vs", prodfree_im[sl])])
                p.step("tensor",
                       lambda t, sl=sl: t.matmul(
                           cum_im[sl][:, :], co["ones1f"][:, :],
                           crow[0:1, 512:1024], start=False, stop=True),
                       incs=["ts"])
                p.step("tensor",
                       lambda t, sl=sl: t.matmul(
                           psa[0:1, 0:512], co["onescf"][:, :],
                           ab[:, sl * 512:(sl + 1) * 512],
                           start=True, stop=True),
                       waits=[("vs", v_add)], incs=["ts"])
                p.step("tensor",
                       lambda t, sl=sl: t.matmul(
                           psa[0:1, 512:1024], co["onescf"][:, :],
                           ab[:, 1024 + sl * 512:1024 + (sl + 1) * 512],
                           start=True, stop=True),
                       incs=["ts"])
                t_cyc = p.at("ts")
                ab_used[sl] = t_cyc
                ab_used[2 + sl] = t_cyc
                p.step("vector",
                       lambda v: v.tensor_add(crow[0:1, 0:512],
                                              crow[0:1, 0:512],
                                              psa[0:1, 0:512]),
                       waits=[("ts", t_cyc)], incs=["vs"])
                p.step("vector",
                       lambda v: v.tensor_add(crow[0:1, 512:1024],
                                              crow[0:1, 512:1024],
                                              psa[0:1, 512:1024]),
                       incs=["vs"])
                v_add = p.at("vs")
                cb_mark = v_add
                aps = [bass.AP(bigb, f * 2048 + lt * 128,
                               [[17408, 128], [0, 4], [1, 128]])
                       for f in range(4)]
                stw = (stash_done if (kq == 0 and cpl == 0 and lt == 0) else [])
                p.step("vector",
                       lambda v, sl=sl, a0=aps[0]: v.tensor_tensor(
                           ttB[:, 0:512], cum_re[sl][:, :], a0, ALU.mult),
                       waits=stw + [("ts", t_cyc), qf_ok], incs=["vs"])
                p.step("vector",
                       lambda v, sl=sl, a3=aps[3]: v.tensor_tensor(
                           ttB[:, 512:1024], cum_im[sl][:, :], a3, ALU.mult),
                       incs=["vs"])
                p.step("vector",
                       lambda v: v.tensor_add(ttB[:, 1024:1536], ttB[:, 0:512],
                                              ttB[:, 512:1024]),
                       incs=["vs"])
                osl = (lt % 2) * 2
                osem, osem2 = "o" + str(osl), "o" + str(osl + 1)
                p.step("vector",
                       lambda v, osl=osl: v.tensor_reduce(
                           orow[:, osl * 128:(osl + 1) * 128],
                           ttB[:, 1024:1536].rearrange("p (a b) -> p a b", b=4),
                           AX.X, ALU.add),
                       waits=stw + [(osem, p.at(osem))], incs=["vs"])
                p.dma("sync",
                      bass.AP(o_d2, lt * 128 * 1024 + cp * 128,
                              [[1024, 128], [1, 128]]),
                      orow[:, osl * 128:(osl + 1) * 128], waits=["vs"],
                      sem=osem)
                p.step("vector",
                       lambda v, sl=sl, a1=aps[1]: v.tensor_tensor(
                           ttB[:, 0:512], cum_im[sl][:, :], a1, ALU.mult),
                       incs=["vs"])
                prodfree_im[sl] = p.at("vs")
                p.step("vector",
                       lambda v, sl=sl, a2=aps[2]: v.tensor_tensor(
                           ttB[:, 512:1024], cum_re[sl][:, :], a2, ALU.mult),
                       incs=["vs"])
                prodfree_re[sl] = p.at("vs")
                p.step("vector",
                       lambda v: v.tensor_sub(ttB[:, 1024:1536], ttB[:, 0:512],
                                              ttB[:, 512:1024]),
                       incs=["vs"])
                p.step("vector",
                       lambda v, osl=osl: v.tensor_reduce(
                           orow[:, (osl + 1) * 128:(osl + 2) * 128],
                           ttB[:, 1024:1536].rearrange("p (a b) -> p a b", b=4),
                           AX.X, ALU.add),
                       waits=[(osem2, p.at(osem2))], incs=["vs"])
                p.dma("sync",
                      bass.AP(o_d2, (L + lt * 128) * 1024 + cp * 128,
                              [[1024, 128], [1, 128]]),
                      orow[:, (osl + 1) * 128:(osl + 2) * 128], waits=["vs"],
                      sem=osem2)
        qprod = p.at("vs")

    if _os.environ.get("PROBE_L0A_STOP") == "passb":
        p._l0a_last = [("o%d" % i, p.at("o%d" % i)) for i in range(4)]
        return
    pb_ts = ("ts", p.at("ts"))
    pb_vs = ("vs", p.at("vs"))
    ob_done = [("o%d" % i, p.at("o%d" % i)) for i in range(4)]
    wov = p.dma("gpsimd", wout,
                bass.AP(dr["l0a_wout"], 0,
                        [[1024, 128], [128 * 1024, 8], [1, 1024]]),
                waits=[agv], sem="wo")
    c1_ap = bass.AP(c12, 0, [[64, 128], [0, 32], [1, 32]])
    c2_ap = bass.AP(c12, 32, [[64, 128], [0, 32], [1, 32]])
    os_ts = [pb_ts[1]] * 2
    os_vs = [pb_vs[1]] * 2
    x0_free = [pb_vs[1]] * 2
    dfree = [pb_vs[1]] * 4
    xo_used = [pb_ts[1]] * 2
    for lt in range(16):
        sl = lt % 2
        ysem, xsem = "y" + str(sl), "sa" + str(sl)
        ow = (ob_done + [pb_ts, pb_vs]) if lt < 2 else []
        ol = p.dma("sync", oslab[:, sl * 2048:(sl + 1) * 2048],
                   bass.AP(o_d2, lt * 128 * 1024,
                           [[1024, 128], [L * 1024, 2], [1, 1024]]),
                   waits=ow + [(ysem, p.at(ysem)), ("ts", os_ts[sl]),
                               ("vs", os_vs[sl])],
                   sem=ysem)
        xl = p.dma("sync", lns2[:, sl * 1024:(sl + 1) * 1024],
                   bass.AP(x_in, lt * 128 * 1024, [[1024, 128], [1, 1024]]),
                   waits=[(xsem, p.at(xsem)), ("vs", x0_free[sl])], sem=xsem)
        orh = oslab[:, sl * 2048:sl * 2048 + 1024]
        oih = oslab[:, sl * 2048 + 1024:sl * 2048 + 2048]
        inv_ap = bass.AP(inv, lt * 32, [[512, 128], [1, 32], [0, 32]])
        p.step("vector",
               lambda v, orh=orh, inv_ap=inv_ap: v.tensor_tensor(
                   ytm[:, :], orh, inv_ap, ALU.mult),
               waits=[ol, "ss"] + ([pb_ts] if lt == 0 else []), incs=["vs"])
        p.step("vector",
               lambda v: v.tensor_tensor(ytm[:, :], ytm[:, :], c1_ap, ALU.mult),
               incs=["vs"])
        p.step("vector",
               lambda v, oih=oih, inv_ap=inv_ap: v.tensor_tensor(
                   oih, oih, inv_ap, ALU.mult),
               incs=["vs"])
        p.step("vector",
               lambda v, oih=oih: v.tensor_tensor(oih, oih, c2_ap, ALU.mult),
               incs=["vs"])
        p.step("vector",
               lambda v, oih=oih: v.tensor_add(ytm[:, :], ytm[:, :], oih),
               incs=["vs"])
        p.step("scalar",
               lambda s, orh=orh: s.activation(orh, ytm[:, :], AF.Silu),
               waits=["vs"], incs=["ss"])
        s_y = p.at("ss")
        for ci in range(8):
            tsl = ci % 2
            p.step("tensor",
                   lambda t, tsl=tsl, orh=orh, ci=ci: t.transpose(
                       trpL[tsl][:, :], orh[:, ci * 128:(ci + 1) * 128],
                       co["identf"][:, :]),
                   waits=[("ss", s_y), ("vs", tr_free[tsl]), co["ready2"]],
                   incs=["ts"])
            t_tr = p.at("ts")
            p.step("vector",
                   lambda v, sl=sl, ci=ci, tsl=tsl: v.tensor_copy(
                       yfm[:, sl * 1024 + ci * 128:sl * 1024 + (ci + 1) * 128],
                       trpL[tsl][:, :]),
                   waits=[("ts", t_tr)], incs=["vs"])
            tr_free[tsl] = p.at("vs")
        os_ts[sl] = p.at("ts")
        yv = p.at("vs")
        if _os.environ.get("PROBE_L0A_ASM") == "ytr":
            os_vs[sl] = p.at("vs")
            continue
        for ob in range(2):
            dslot = psa[:, (sl * 2 + ob) * 512:(sl * 2 + ob + 1) * 512]
            for ci in range(8):
                w = [wov, ("vs", yv), ("vs", dfree[sl * 2 + ob])] \
                    if ci == 0 else []
                p.step("tensor",
                       lambda t, sl=sl, ob=ob, ci=ci, dslot=dslot: t.matmul(
                           dslot,
                           yfm[:, sl * 1024 + ci * 128:sl * 1024 + (ci + 1) * 128],
                           wout[:, ci * 1024 + ob * 512:ci * 1024 + (ob + 1) * 512],
                           start=(ci == 0), stop=(ci == 7)),
                       waits=w, incs=(["ts"] if ci == 7 else []))
            t_d = p.at("ts")
            p.step("vector",
                   lambda v, sl=sl, ob=ob, dslot=dslot: v.tensor_add(
                       xo[:, sl * 1024 + ob * 512:sl * 1024 + (ob + 1) * 512],
                       dslot,
                       lns2[:, sl * 1024 + ob * 512:sl * 1024 + (ob + 1) * 512]),
                   waits=[("ts", t_d), xl, ("ts", xo_used[sl])], incs=["vs"])
            dfree[sl * 2 + ob] = p.at("vs")
        x0_free[sl] = p.at("vs")
        os_vs[sl] = p.at("vs")
        if _os.environ.get("PROBE_L0A_ASM") == "wout":
            continue
        v_x1 = p.at("vs")
        stsem = "st" + str(sl)
        for ci in range(8):
            tsl = ci % 2
            p.step("tensor",
                   lambda t, sl=sl, ci=ci, tsl=tsl: t.transpose(
                       trpL[tsl][:, :],
                       xo[:, sl * 1024 + ci * 128:sl * 1024 + (ci + 1) * 128],
                       co["identf"][:, :]),
                   waits=[("vs", v_x1), ("vs", tr_free[tsl]), co["ready2"]],
                   incs=["ts"])
            t_tr = p.at("ts")
            w = [("ts", t_tr)]
            if ci == 0:
                w.append((stsem, p.at(stsem)))
            p.step("vector",
                   lambda v, sl=sl, ci=ci, tsl=tsl: v.tensor_copy(
                       xfs[:, sl * 1024 + ci * 128:sl * 1024 + (ci + 1) * 128],
                       trpL[tsl][:, :]),
                   waits=w, incs=["vs"])
            tr_free[tsl] = p.at("vs")
        xo_used[sl] = p.at("ts")
        if _os.environ.get("PROBE_L0A_ASM") != "xtr":
            for ci in range(8):
                p.dma("sync",
                      bass.AP(x_out, ci * 128 * L + lt * 128,
                              [[L, 128], [1, 128]]),
                      xfs[:, sl * 1024 + ci * 128:sl * 1024 + (ci + 1) * 128],
                      waits=["vs"], sem=stsem)
    if _os.environ.get("PROBE_L0A_ASM") in ("ytr", "wout", "xtr"):
        p._l0a_last = [("vs", p.at("vs")), ("ts", p.at("ts"))]
    else:
        p._l0a_last = [("st0", p.at("st0")), ("st1", p.at("st1"))]


L0A_INPUTS = [
    ("l0a_x0h", [128, 1024]), ("l0a_wmem", [1024, 1152]),
    ("l0a_wq", [1024, 2048]), ("l0a_ckc", [128, 36]),
    ("l0a_throw", [1, 4096]), ("l0a_slopes", [1, 32]),
    ("l0a_score", [1, 32]), ("l0a_ttab", [2048, 2]),
    ("l0a_c1", [1, 32]), ("l0a_c2", [1, 32]),
    ("l0a_wout", [1024, 1024]), ("l0a_tri", [128, 128]),
    ("l0a_w4", [4, 1]), ("l0a_flag", [128, 1]),
]


def build_l0_attn_standalone():
    nc = bass.Bass("TRN2", target_bir_lowering=False)
    with ExitStack() as es:
        p = Prog(nc, es)
        x_in = nc.dram_tensor("x_in", [L, 1024], F32, kind="ExternalInput")
        x_out = nc.dram_tensor("x_out", [1024, L], F32, kind="ExternalOutput")
        ident = nc.dram_tensor("c_ident", [128, 128], F32, kind="ExternalInput")
        dr = {nm: nc.dram_tensor(nm, sh, F32, kind="ExternalInput")
              for nm, sh in L0A_INPUTS}
        psa = p.psum("psa", [128, 4096])
        co = setup_consts(p, ident)
        emit_l0_attn(p, co, dr, psa, x_in, x_out)
        p.step("sync", lambda sy: None, waits=p._l0a_last, chain=False)
        p.build()
    return nc


def run_l0_attn_standalone(x0_full, ins):
    nc = build_l0_attn_standalone()
    id_ = np.eye(128, dtype=np.float32)
    in_maps = []
    for ci in range(NCORE):
        b, cch = divmod(ci, 4)
        w = prep_l0_attn(ins, ci)
        m = {k: w[k] for k in w}
        m["c_ident"] = id_
        m["x_in"] = np.ascontiguousarray(x0_full[b, cch * L:(cch + 1) * L, :])
        if cch == 0:
            m["l0a_x0h"] = np.zeros((128, 1024), np.float32)
        else:
            m["l0a_x0h"] = np.ascontiguousarray(
                x0_full[b, cch * L - 128:cch * L, :])
        in_maps.append(m)
    res = run_bass_kernel_spmd(nc, in_maps, core_ids=list(range(NCORE)))
    out = np.empty((NB, SEQ, D), np.float32)
    for ci in range(NCORE):
        b, cch = divmod(ci, 4)
        out[b, cch * L:(cch + 1) * L, :] = np.asarray(
            res.results[ci]["x_out"], np.float32).T
    return out


# version 03
# speedup vs baseline: 1.1432x; 1.1432x over previous
import os as _os
import numpy as np
from contextlib import ExitStack, contextmanager

import concourse.bass as bass
import concourse.mybir as mybir
from concourse.bass_utils import run_bass_kernel_spmd

F32 = mybir.dt.float32
BF16 = mybir.dt.bfloat16
AF = mybir.ActivationFunctionType
ALU = mybir.AluOpType
AX = mybir.AxisListType

NCORE = 8
NB = 2
SEQ = 8192
D = 1024
L = SEQ // 4
EPS = 1e-5

ENGINES = ("tensor", "vector", "scalar", "gpsimd", "sync")
DSEM = {"sync": "dsy", "scalar": "dsc", "gpsimd": "dgp"}
ALL_SEMS = ("ts", "vs", "ss", "gs", "sy", "dsy", "dsc", "dgp", "cc")


class Prog:
    def __init__(self, nc, es):
        self.nc = nc
        self.es = es
        self.sem = {}
        self.cnt = {}
        for name in ALL_SEMS:
            self.sem[name] = es.enter_context(nc.semaphore(name))
            self.cnt[name] = 0
        self.streams = {e: [] for e in ENGINES}

    def get_sem(self, name):
        if name not in self.sem:
            self.sem[name] = self.es.enter_context(self.nc.semaphore(name))
            self.cnt[name] = 0
        return self.sem[name]

    def at(self, name):
        if name not in self.cnt:
            self.get_sem(name)
        return self.cnt[name]

    CHAINED = {"vector": "vs", "scalar": "ss", "gpsimd": "gs"}

    def step(self, eng, fn, waits=(), incs=(), chain=True):
        wl = []
        cs = self.CHAINED.get(eng) if chain else None
        if cs is not None:
            wl.append((cs, self.cnt[cs]))
            incs = list(incs) + [cs]
        for w in waits:
            if isinstance(w, str):
                wl.append((w, self.at(w)))
            else:
                assert w[1] <= self.at(w[0]), \
                    f"forward wait {w[0]}>={w[1]} planned={self.at(w[0])}"
                wl.append((w[0], w[1]))
        merged = {}
        for inc in incs:
            if isinstance(inc, str):
                inc = (inc, 1)
            merged[inc[0]] = merged.get(inc[0], 0) + inc[1]
        assert len(merged) <= 1, f"one sem update per instruction: {merged}"
        il = []
        for name, k in merged.items():
            self.cnt[name] += k
            il.append((self.sem[name], k))
        sems = self.sem

        def run(e, fn=fn, wl=wl, il=il):
            for name, val in wl:
                if val > 0:
                    e.wait_ge(sems[name], val)
            ins = fn(e)
            if il:
                if isinstance(ins, (list, tuple)):
                    ins = ins[-1]
                for h, k in il:
                    ins.then_inc(h, k)
        self.streams[eng].append(run)

    def dma(self, queue, out, in_, waits=(), extra_incs=(), sem=None, **kw):
        dname = sem if sem is not None else DSEM[queue]
        if sem is not None:
            self.get_sem(sem)
        self.step(queue, lambda e: e.dma_start(out=out, in_=in_, **kw),
                  waits=waits, incs=[(dname, 16)] + list(extra_incs),
                  chain=False)
        return (dname, self.cnt[dname])

    def build(self):
        with self.nc.Block() as blk:
            for ename in ENGINES:
                fns = self.streams[ename]
                if not fns:
                    continue
                getattr(blk, ename)(lambda e, fns=fns: [f(e) for f in fns])
        self.streams = {e: [] for e in ENGINES}

    def sbuf(self, name, shape, dtype=F32):
        return self.es.enter_context(self.nc.sbuf_tensor(name, shape, dtype))

    def psum(self, name, shape, dtype=F32):
        return self.es.enter_context(self.nc.psum_tensor(name, shape, dtype))

    @contextmanager
    def scope(self):
        old = self.es
        with ExitStack() as es2:
            self.es = es2
            try:
                yield
            finally:
                self.es = old



def setup_consts(p, ident_dram=None):
    c = {}
    c["ones1b"] = p.sbuf("c_ones1b", [1, 128], BF16)
    c["onescb"] = p.sbuf("c_onescb", [128, 1], BF16)
    c["ones1f"] = p.sbuf("c_ones1f", [1, 128], F32)
    c["onescf"] = p.sbuf("c_onescf", [128, 1], F32)
    p.step("gpsimd", lambda g: g.memset(c["ones1b"][:, :], 1.0))
    p.step("gpsimd", lambda g: g.memset(c["onescb"][:, :], 1.0))
    p.step("gpsimd", lambda g: g.memset(c["ones1f"][:, :], 1.0))
    p.step("gpsimd", lambda g: g.memset(c["onescf"][:, :], 1.0), incs=["gs"])
    if ident_dram is not None:
        c["identf"] = p.sbuf("c_identf", [128, 128], F32)
        p.dma("gpsimd", c["identf"][:, :], ident_dram.ap())
    c["ready"] = ("gs", p.at("gs"))
    if ident_dram is not None:
        c["ready2"] = ("dgp", p.at("dgp"))
    return c


def load_stream_fm(p, xs, x_dram):
    for cch in range(8):
        p.dma("sync", xs[:, cch * L:(cch + 1) * L],
              bass.AP(x_dram, cch * 128 * L, [[L, 128], [1, L]]))
    return ("dsy", p.at("dsy"))


def store_stream_fm(p, x_dram, xs, waits):
    for cch in range(8):
        p.dma("sync", bass.AP(x_dram, cch * 128 * L, [[L, 128], [1, L]]),
              xs[:, cch * L:(cch + 1) * L], waits=waits)
    return ("dsy", p.at("dsy"))


def make_ln_scratch(p):
    return {"sq": p.sbuf("ln_sq", [128, 1024]), "tmp": p.sbuf("ln_tmp", [128, 512]),
            "row": p.sbuf("ln_row", [1, 512]), "rowt": p.sbuf("ln_rowt", [1, 512])}


def emit_ln_fm(p, co, xs, xc, tag, xs_ready, psa, lsc):
    sq = lsc["sq"]
    tmp = lsc["tmp"]
    s1 = psa[0:1, 0:512]
    s2 = psa[0:1, 512:1024]
    mub = psa[:, 1024:1536]
    rsb = psa[:, 1536:2048]
    row = lsc["row"]
    rowt = lsc["rowt"]
    NQ = L // 512
    slot_done = [0, 0]
    for lq in range(NQ):
        o = lq * 512
        for c in range(8):
            sl = (c % 2) * 512
            p.step("scalar",
                   lambda s, sl=sl, c=c, o=o: s.activation(
                       sq[:, sl:sl + 512], xs[:, c * L + o: c * L + o + 512],
                       AF.Square),
                   waits=list(xs_ready) + [("ts", slot_done[c % 2])],
                   incs=["ss"])
            p.step("tensor",
                   lambda t, c=c, o=o: t.matmul(
                       s1[:, :], co["onescf"][:, :],
                       xs[:, c * L + o: c * L + o + 512],
                       start=(c == 0), stop=(c == 7)),
                   waits=list(xs_ready) + [co["ready"]] +
                         ([("vs", p.at("vs"))] if c == 0 else []))
            p.step("tensor",
                   lambda t, sl=sl, c=c: t.matmul(
                       s2[:, :], co["onescf"][:, :], sq[:, sl:sl + 512],
                       start=(c == 0), stop=(c == 7)),
                   waits=["ss"], incs=["ts"])
            slot_done[c % 2] = p.at("ts")
        t_stats = p.at("ts")
        p.step("vector", lambda v: v.tensor_scalar_mul(row[:, :], s1[0:1, :],
                                                       1.0 / D),
               waits=[("ts", t_stats)], incs=["vs"])
        p.step("vector", lambda v: v.tensor_mul(rowt[:, :], row[:, :], row[:, :]),
               waits=["vs"], incs=["vs"])
        p.step("vector", lambda v: v.scalar_tensor_tensor(
                   rowt[:, :], s2[0:1, :], 1.0 / D, rowt[:, :],
                   ALU.mult, ALU.subtract),
               waits=["vs"], incs=["vs"])
        p.step("vector", lambda v: v.tensor_scalar_add(rowt[:, :], rowt[:, :],
                                                       EPS),
               waits=["vs"], incs=["vs"])
        v_var = p.at("vs")
        p.step("scalar", lambda s: s.activation(rowt[:, :], rowt[:, :], AF.Sqrt),
               waits=[("vs", v_var)], incs=["ss"])
        p.step("vector", lambda v: v.reciprocal(rowt[:, :], rowt[:, :]),
               waits=["ss"], incs=["vs"])
        v_rs = p.at("vs")
        p.step("tensor", lambda t: t.matmul(mub[:, :], co["ones1f"][:, :],
                                            row[0:1, :], start=True, stop=True),
               waits=[("vs", v_var)])
        p.step("tensor", lambda t: t.matmul(rsb[:, :], co["ones1f"][:, :],
                                            rowt[0:1, :], start=True, stop=True),
               waits=[("vs", v_rs)], incs=["ts"])
        t_b = p.at("ts")
        for c in range(8):
            p.step("vector",
                   lambda v, c=c, o=o: v.tensor_sub(
                       tmp[:, :], xs[:, c * L + o: c * L + o + 512], mub[:, :]),
                   waits=[("ts", t_b)] + ([("vs", p.at("vs"))] if c == 0 else []),
                   incs=[])
            p.step("vector",
                   lambda v, c=c, o=o: v.tensor_mul(
                       xc[:, c * L + o: c * L + o + 512], tmp[:, :], rsb[:, :]),
                   incs=(["vs"] if c == 7 else []))
    return ("vs", p.at("vs"))




def prep_l1_ffn(ins):
    rm = ins["L1_ffn_rail_maker"]
    rup = ins["L1_ffn_rail_up_proj"]
    bd8 = np.zeros((4, 8, 128, 128), np.float32)
    for h in range(4):
        for g in range(8):
            for r in range(8):
                bd8[h, g, 16 * r:16 * r + 16, 16 * r:16 * r + 16] = rm[h, 8 * g + r]
    bd2 = np.zeros((2, 4 * 16, 32, 128), np.float32)
    for h in range(4):
        for pp in range(32):
            t = np.zeros((32, 128), np.float32)
            for rr in range(2):
                t[16 * rr:16 * rr + 16, 64 * rr:64 * rr + 64] = rup[h, 2 * pp + rr].T
            bd2[pp % 2, h * 16 + pp // 2] = t
    return {"l1f_bd8": np.ascontiguousarray(bd8),
            "l1f_bd2": np.ascontiguousarray(bd2),
            "l1f_wcT": np.ascontiguousarray(ins["L1_ffn_w_cproj"].T)}


def emit_l1_ffn(p, co, xs, xc, dr, xc_ready, psa):
    bd8 = p.sbuf("l1f_bd8s", [128, 32 * 128], BF16)
    bd2 = p.sbuf("l1f_bd2s", [64, 64 * 128], BF16)
    ysa = p.sbuf("l1f_ysa", [64, 8 * 1024], BF16)
    ysbb = p.sbuf("l1f_ysb", [64, 8 * 1024], BF16)
    hsb = p.sbuf("l1f_hsb", [128, 16 * 1024], BF16)
    wct = p.sbuf("l1f_wct", [128, 2 * 512], BF16)
    p.dma("gpsimd", bd8[:, :],
          bass.AP(dr["l1f_bd8"], 0, [[128, 128], [128 * 128, 32], [1, 128]]))
    p.dma("gpsimd", bass.AP(bd2, 0, [[64 * 128, 32], [128, 64], [1, 128]]),
          bass.AP(dr["l1f_bd2"], 0, [[128, 32], [32 * 128, 64], [1, 128]]))
    p.dma("gpsimd",
          bass.AP(bd2, 32 * 64 * 128, [[64 * 128, 32], [128, 64], [1, 128]]),
          bass.AP(dr["l1f_bd2"], 64 * 32 * 128,
                  [[128, 32], [32 * 128, 64], [1, 128]]))
    w_bd8 = w_bd2a = w_bd2b = ("dgp", p.at("dgp"))

    yslots = [psa[:, 0:1024], psa[:, 1024:2048]]
    zslots = [psa[:, 2048:3072], psa[:, 3072:4096]]
    wct_free = [0, 0]
    y_done = [0, 0]
    y_done_s = [0, 0]
    z_done = [0, 0]
    c_flush = 0
    for half in range(2):
        lo = half * 1024
        for h in range(4):
            for g in range(8):
                yp = yslots[g % 2]
                p.step("tensor",
                       lambda t, h=h, g=g, lo=lo, yp=yp: [
                           t.matmul(yp[:, q * 512:(q + 1) * 512],
                                    bd8[:, (h * 8 + g) * 128:(h * 8 + g + 1) * 128],
                                    xc[:, g * L + lo + q * 512:
                                       g * L + lo + (q + 1) * 512],
                                    start=True, stop=True)
                           for q in range(2)],
                       waits=[w_bd8] + list(xc_ready) +
                             [("vs", max(y_done[g % 2], c_flush)),
                              ("ss", y_done_s[g % 2])],
                       incs=["ts"])
                t_y = p.at("ts")
                p.step("scalar",
                       lambda s, g=g, yp=yp: s.copy(
                           ysa[:, g * 1024:(g + 1) * 1024], yp[0:64, :]),
                       waits=[("ts", t_y)], incs=["ss"])
                y_done_s[g % 2] = p.at("ss")
                p.step("vector",
                       lambda v, g=g, yp=yp: v.tensor_copy(
                           ysbb[:, g * 1024:(g + 1) * 1024], yp[64:128, :]),
                       waits=[("ts", t_y)], incs=["vs"])
                y_done[g % 2] = p.at("vs")
            v_yall = p.at("vs")
            s_yall = p.at("ss")
            for sub in range(2):
                for j in range(16):
                    pp = 16 * sub + j
                    zp = zslots[j % 2]
                    g = pp // 4
                    s4 = pp % 4
                    ysrc = ysa if s4 < 2 else ysbb
                    rb = 32 * (s4 % 2)
                    p.step("tensor",
                           lambda t, h=h, pp=pp, g=g, rb=rb, zp=zp, ysrc=ysrc: [
                               t.matmul(zp[:, q * 512:(q + 1) * 512],
                                        bd2[32 * (pp % 2):32 * (pp % 2) + 32,
                                            (h * 16 + pp // 2) * 128:
                                            (h * 16 + pp // 2 + 1) * 128],
                                        ysrc[rb:rb + 32,
                                             g * 1024 + q * 512:
                                             g * 1024 + (q + 1) * 512],
                                        start=True, stop=True)
                               for q in range(2)],
                           waits=[w_bd2a, w_bd2b, ("vs", max(v_yall, c_flush)),
                                  ("ss", max(s_yall, z_done[j % 2]))],
                           incs=["ts"])
                    t_z = p.at("ts")
                    p.step("scalar",
                           lambda s, j=j, zp=zp: s.activation(
                               hsb[:, j * 1024:(j + 1) * 1024], zp[:, :], AF.Gelu),
                           waits=[("ts", t_z)], incs=["ss"])
                    z_done[j % 2] = p.at("ss")
                s_hall = p.at("ss")
                for mg in range(2):
                    for kc in range(16):
                        slot = kc % 2
                        kglob = h * 32 + sub * 16 + kc
                        wld = p.dma("gpsimd",
                                    wct[:, slot * 512:(slot + 1) * 512],
                                    bass.AP(dr["l1f_wcT"],
                                            kglob * 128 * 1024 + mg * 512,
                                            [[1024, 128], [1, 512]]),
                                    waits=[("ts", wct_free[slot])],
                                    sem="w%d" % slot)
                        for lq in range(2):
                            p.step("tensor",
                                   lambda t, kc=kc, lq=lq, slot=slot: [
                                       t.matmul(psa[:, (mi * 2 + lq) * 512:
                                                    (mi * 2 + lq) * 512 + 512],
                                                wct[:, slot * 512 + mi * 128:
                                                    slot * 512 + (mi + 1) * 128],
                                                hsb[:, kc * 1024 + lq * 512:
                                                    kc * 1024 + lq * 512 + 512],
                                                start=(kc == 0), stop=(kc == 15))
                                       for mi in range(4)],
                                   waits=[wld, ("ss", s_hall)] +
                                         ([("vs", p.at("vs"))] if kc == 0 else []),
                                   incs=(["ts"] if lq == 1 else []))
                        wct_free[slot] = p.at("ts")
                    t_c = p.at("ts")
                    for mi in range(4):
                        for lq in range(2):
                            m = mg * 4 + mi
                            p.step("vector",
                                   lambda v, m=m, mi=mi, lq=lq, lo=lo: v.tensor_add(
                                       xs[:, m * L + lo + lq * 512:
                                          m * L + lo + lq * 512 + 512],
                                       xs[:, m * L + lo + lq * 512:
                                          m * L + lo + lq * 512 + 512],
                                       psa[:, (mi * 2 + lq) * 512:
                                           (mi * 2 + lq) * 512 + 512]),
                                   waits=[("ts", t_c)],
                                   incs=(["vs"] if (mi == 3 and lq == 1) else []))
                    c_flush = p.at("vs")
    return ("vs", p.at("vs"))



def build_l1_ffn_standalone():
    es = ExitStack()
    nc = bass.Bass("TRN2", target_bir_lowering=False)
    x_in = nc.dram_tensor("x_in", [D, L], F32, kind="ExternalInput")
    bd8 = nc.dram_tensor("l1f_bd8", [4, 8, 128, 128], F32, kind="ExternalInput")
    bd2 = nc.dram_tensor("l1f_bd2", [2, 64, 32, 128], F32, kind="ExternalInput")
    wcT = nc.dram_tensor("l1f_wcT", [16384, 1024], F32, kind="ExternalInput")
    x_out = nc.dram_tensor("x_out", [D, L], F32, kind="ExternalOutput")
    dr = {"l1f_bd8": bd8, "l1f_bd2": bd2, "l1f_wcT": wcT}
    with es:
        p = Prog(nc, es)
        co = setup_consts(p)
        xs = p.sbuf("xs", [128, 8 * L])
        xc = p.sbuf("xc", [128, 8 * L], BF16)
        psa = p.psum("psa", [128, 4096])
        lsc = make_ln_scratch(p)
        ld = load_stream_fm(p, xs, x_in)
        lnr = emit_ln_fm(p, co, xs, xc, "ln3", [ld], psa, lsc)
        fr = emit_l1_ffn(p, co, xs, xc, dr, [lnr], psa)
        st = store_stream_fm(p, x_out, xs, [fr])
        p.step("sync", lambda sy: None, waits=[st], chain=False)
        p.build()
    return nc


def run_l1_ffn_standalone(x3_full, ins):
    w = prep_l1_ffn(ins)
    nc = build_l1_ffn_standalone()
    in_maps = []
    for ci in range(NCORE):
        b, cch = divmod(ci, 4)
        xfm = np.ascontiguousarray(x3_full[b, cch * L:(cch + 1) * L, :].T)
        in_maps.append({"x_in": xfm, **w})
    res = run_bass_kernel_spmd(nc, in_maps, core_ids=list(range(NCORE)))
    out = np.empty((2, SEQ, D), np.float32)
    for ci in range(NCORE):
        b, cch = divmod(ci, 4)
        out[b, cch * L:(cch + 1) * L, :] = np.asarray(res.results[ci]["x_out"]).T
    return out


def kernel_multi(**inputs):
    ins = {k: np.asarray(v) for k, v in inputs.items()}
    x0 = np.asarray(ins["L0_attn_x"], np.float32)
    id_ = np.eye(128, dtype=np.float32)
    cores = list(range(NCORE))

    nc = build_l0_attn_standalone()
    in_maps = []
    for ci in cores:
        b, cch = divmod(ci, 4)
        m = dict(prep_l0_attn(ins, ci))
        m["c_ident"] = id_
        m["x_in"] = np.ascontiguousarray(x0[b, cch * L:(cch + 1) * L, :])
        m["l0a_x0h"] = (np.zeros((128, 1024), np.float32) if cch == 0 else
                        np.ascontiguousarray(x0[b, cch * L - 128:cch * L, :]))
        in_maps.append(m)
    res = run_bass_kernel_spmd(nc, in_maps, core_ids=cores)
    xs_ = [np.ascontiguousarray(np.asarray(res.results[ci]["x_out"],
                                           np.float32)) for ci in cores]

    nc = build_l0_ffn_standalone()
    in_maps = []
    for ci in cores:
        m = dict(prep_l0_ffn(ins, ci))
        m["c_ident"] = id_
        m["x_in"] = xs_[ci]
        in_maps.append(m)
    res = run_bass_kernel_spmd(nc, in_maps, core_ids=cores)
    xs_ = [np.ascontiguousarray(np.asarray(res.results[ci]["x_out"],
                                           np.float32)) for ci in cores]

    w = prep_l1_attn(ins)
    nc = build_l1_attn_standalone()
    in_maps = [{"x_in": xs_[ci], **w} for ci in cores]
    res = run_bass_kernel_spmd(nc, in_maps, core_ids=cores)
    xs_ = [np.ascontiguousarray(np.asarray(res.results[ci]["x_out"],
                                           np.float32)) for ci in cores]

    w = prep_l1_ffn(ins)
    nc = build_l1_ffn_standalone()
    in_maps = [{"x_in": xs_[ci], **w} for ci in cores]
    res = run_bass_kernel_spmd(nc, in_maps, core_ids=cores)

    out = np.empty((NB, SEQ, D), np.float32)
    for ci in cores:
        b, cch = divmod(ci, 4)
        out[b, cch * L:(cch + 1) * L, :] = np.asarray(
            res.results[ci]["x_out"], np.float32).T
    return out


AGR = 2049


def prep_l1_attn(ins):
    e2 = np.zeros((8 * 128, 16), np.float32)
    for c in range(8):
        for hh in range(2):
            h = 2 * c + hh
            e2[c * 128 + hh * 64: c * 128 + hh * 64 + 64, h] = 1.0
    return {"l1a_wqT": np.ascontiguousarray(ins["L1_attn_Wq"].T),
            "l1a_wkT": np.ascontiguousarray(ins["L1_attn_Wk"].T),
            "l1a_wvT": np.ascontiguousarray(ins["L1_attn_Wv"].T),
            "l1a_woT": np.ascontiguousarray(ins["L1_attn_Wout"].T),
            "l1a_bout": np.ascontiguousarray(ins["L1_attn_bout"]),
            "l1a_e2": e2}


def emit_l1_attn(p, co, xs, xc, dr, xc_ready, psa):
    nc = p.nc
    agin = nc.dram_tensor("l1a_agin", [AGR, 2048], BF16)
    agout = nc.dram_tensor("l1a_agout", [4 * AGR, 2048], BF16)
    o_d = nc.dram_tensor("l1a_od", [1024, 2048], BF16)

    wbuf = p.sbuf("l1a_wbuf", [128, 8 * 1024], BF16)
    stg = p.sbuf("l1a_stg", [128, 2 * 1024], BF16)
    q_fm = p.sbuf("l1a_qfm", [128, 8 * 2048], BF16)
    sqf = p.sbuf("l1a_sqf", [128, 2048])
    sqs = sqf[:, 0:1024]
    e2s = p.sbuf("l1a_e2s", [128, 8 * 16])
    qn = p.sbuf("l1a_qn", [16, 2048], BF16)
    km1 = p.sbuf("l1a_km1", [16, 1])
    kmb = p.sbuf("l1a_kmb", [16, 128], BF16)
    kmg = p.sbuf("l1a_kmg", [16, 4])
    bline = p.sbuf("l1a_bline", [16, 1])
    nbrow = sqf[0:1, 0:16]
    negb = p.sbuf("l1a_negb", [128, 16])
    kh2 = p.sbuf("l1a_kh2", [128, 8192], BF16)
    v65 = p.sbuf("l1a_v65", [128, 64 * 65], BF16)
    sh1 = p.sbuf("l1a_sh1", [128, 2048], BF16)
    pt = sh1[:, 0:2048]
    dsg = sqf[0:1, 0:2048]
    dst = p.sbuf("l1a_dst", [16, 2048])
    recrow = sqf[0:1, 0:2048]
    och = sh1[:, 0:2048]
    boutc = e2s[:, 0:8]

    p.dma("gpsimd", e2s[:, :],
          bass.AP(dr["l1a_e2"], 0, [[16, 128], [128 * 16, 8], [1, 16]]))
    w_e2 = ("dgp", p.at("dgp"))

    pslot = [psa[:, 0:1024], psa[:, 1024:2048]]
    sqp = psa[0:16, 2048:4096]
    p._stg_free = {}

    def proj_qk(which):
        wT = "l1a_wqT" if which == "q" else "l1a_wkT"
        ev_done = [p.at("ts"), p.at("ts")]
        sq_done = [0, 0]
        for hd in range(8):
            sw = hd % 2
            wld = p.dma("gpsimd", wbuf[:, sw * 1024:(sw + 1) * 1024],
                        bass.AP(dr[wT], hd * 128,
                                [[1024, 128], [1024 * 128, 8], [1, 128]]),
                        waits=[("ts", ev_done[sw])], sem="w%d" % sw)
            for lh in range(2):
                sp = lh
                p.step("tensor",
                       lambda t, hd=hd, lh=lh, sw=sw, sp=sp: [
                           t.matmul(pslot[sp][:, q * 512:(q + 1) * 512],
                                    wbuf[:, sw * 1024 + cc * 128:
                                         sw * 1024 + (cc + 1) * 128],
                                    xc[:, cc * L + lh * 1024 + q * 512:
                                       cc * L + lh * 1024 + (q + 1) * 512],
                                    start=(cc == 0), stop=(cc == 7))
                           for q in range(2) for cc in range(8)],
                       waits=[wld] + list(xc_ready) +
                             [("vs", p.at("vs")), ("ss", p.at("ss"))],
                       incs=["ts"])
                t_mm = p.at("ts")
                if which == "q":
                    p.step("vector",
                           lambda v, hd=hd, lh=lh, sp=sp: v.tensor_copy(
                               q_fm[:, hd * 2048 + lh * 1024:
                                    hd * 2048 + (lh + 1) * 1024],
                               pslot[sp][:, :]),
                           waits=[("ts", t_mm)])
                else:
                    p.step("vector",
                           lambda v, sp=sp: v.tensor_copy(
                               stg[:, sp * 1024:(sp + 1) * 1024],
                               pslot[sp][:, :]),
                           waits=[("ts", t_mm), ("ss", sq_done[sp]),
                                  ("sa%d" % sp, p._stg_free.get(sp, 0))])
                    p.dma("sync",
                          bass.AP(agin, hd * 128 * 2048 + lh * 1024,
                                  [[2048, 128], [1, 1024]]),
                          stg[:, sp * 1024:(sp + 1) * 1024],
                          waits=["vs"], sem="sa%d" % sp)
                    p._stg_free[sp] = p.at("sa%d" % sp)
                src_ap = (q_fm[:, hd * 2048 + lh * 1024:
                               hd * 2048 + (lh + 1) * 1024]
                          if which == "q" else stg[:, sp * 1024:(sp + 1) * 1024])
                p.step("scalar",
                       lambda s, src_ap=src_ap: s.activation(sqs, src_ap,
                                                             AF.Square),
                       waits=["vs"], incs=["ss"])
                sq_done[sp] = p.at("ss")
                p.step("tensor",
                       lambda t, hd=hd, lh=lh: [
                           t.matmul(sqp[:, lh * 1024 + q * 512:
                                        lh * 1024 + (q + 1) * 512],
                                    e2s[:, hd * 16:(hd + 1) * 16],
                                    sqs[:, q * 512:(q + 1) * 512],
                                    start=(hd == 0), stop=(hd == 7))
                           for q in range(2)],
                       waits=["ss", w_e2], incs=["ts"])
                ev_done[sw] = p.at("ts")
        return p.at("ts")

    proj_qk("k")
    t_ksq = p.at("ts")
    p.step("scalar", lambda s: s.activation(dst[:, :], sqp[:, :], AF.Sqrt),
           waits=[("ts", t_ksq)], incs=["ss"])
    p.step("vector", lambda v: v.reduce_max(km1[:, :], dst[:, :], AX.X),
           waits=["ss"], incs=["vs"])
    p.step("vector", lambda v: v.tensor_scalar_mul(km1[:, :], km1[:, :], 1.01),
           incs=["vs"])
    p.step("gpsimd", lambda g: g.memset(kmb[:, :], 0.0), incs=["gs"])
    p.step("vector", lambda v: v.tensor_copy(kmb[:, 0:1], km1[:, :]),
           waits=["gs"], incs=["vs"])
    p.dma("sync", bass.AP(agin, 2048 * 2048, [[2048, 1], [1, 2048]]),
          bass.AP(kmb, 0, [[128, 16], [1, 128]]), waits=["vs"])
    w_vwt = p.dma("gpsimd", wbuf[:, :],
                  bass.AP(dr["l1a_wvT"], 0, [[1024, 128], [1024 * 128, 8],
                                             [1, 1024]]),
                  waits=[("ts", p.at("ts"))])
    vv_done = [p.at("vs"), p.at("vs")]
    for lt in range(16):
        sp = lt % 2
        p.step("tensor",
               lambda t, lt=lt, sp=sp: [
                   t.matmul(pslot[sp][:, hb * 512:(hb + 1) * 512],
                            xc[:, cc * L + lt * 128: cc * L + (lt + 1) * 128],
                            wbuf[:, cc * 1024 + hb * 512:
                                 cc * 1024 + (hb + 1) * 512],
                            start=(cc == 0), stop=(cc == 7))
                   for hb in range(2) for cc in range(8)],
               waits=[w_vwt] + list(xc_ready) +
                     [("vs", vv_done[sp] if lt >= 2 else p.at("vs"))],
               incs=["ts"])
        t_mm = p.at("ts")
        p.step("vector",
               lambda v, sp=sp: v.tensor_copy(stg[:, sp * 1024:(sp + 1) * 1024],
                                              pslot[sp][:, :]),
               waits=[("ts", t_mm), ("sa%d" % sp, p._stg_free.get(sp, 0))],
               incs=["vs"])
        vv_done[sp] = p.at("vs")
        p.dma("sync",
              bass.AP(agin, 1024 * 2048 + lt * 128 * 1024,
                      [[1024, 128], [1, 1024]]),
              stg[:, sp * 1024:(sp + 1) * 1024], waits=["vs"], sem="sa%d" % sp)
        p._stg_free[sp] = p.at("sa%d" % sp)
    q_end = proj_qk("q")
    p.step("scalar", lambda s: s.activation(qn[:, :], sqp[:, :], AF.Sqrt),
           waits=[("ts", q_end)], incs=["ss"])
    p.step("gpsimd",
           lambda g: g.collective_compute(
               "AllGather", ALU.bypass,
               replica_groups=[[0, 1, 2, 3], [4, 5, 6, 7]],
               ins=[agin.ap().opt()], outs=[agout.ap().opt()]),
           waits=["dsy", "dgp", "dsc", "sa0", "sa1"], incs=["cc"],
           chain=False)
    agv = ("cc", p.at("cc"))
    p.dma("gpsimd", kmg[:, :],
          bass.AP(agout, 2048 * 2048, [[128, 16], [AGR * 2048, 4]]),
          waits=[agv], allow_slow_non_contiguous=True, sem="r0")
    k_gl0 = ("r0", p.at("r0"))
    p.step("vector", lambda v: v.reduce_max(km1[:, :], kmg[:, :], AX.X),
           waits=[k_gl0], incs=["vs"])
    p.step("vector", lambda v: v.reduce_max(bline[:, :], qn[:, :], AX.X),
           waits=["ss"], incs=["vs"])
    p.step("vector", lambda v: v.tensor_mul(bline[:, :], bline[:, :], km1[:, :]),
           incs=["vs"])
    p.step("vector", lambda v: v.tensor_scalar_mul(bline[:, :], bline[:, :],
                                                   -0.125),
           incs=["vs"])
    p.dma("sync", nbrow, bass.AP(bline, 0, [[1, 16], [1, 1]]),
          waits=["vs"], allow_slow_non_contiguous=True)
    nb_ld = ("dsy", p.at("dsy"))
    p.step("tensor", lambda t: t.matmul(psa[:, 0:16], co["ones1f"][:, :],
                                        nbrow[0:1, :], start=True, stop=True),
           waits=[nb_ld, ("ss", p.at("ss")), ("vs", p.at("vs"))], incs=["ts"])
    p.step("vector", lambda v: v.tensor_copy(negb[:, :], psa[:, 0:16]),
           waits=["ts"], incs=["vs"])
    negb_ok = ("vs", p.at("vs"))
    p.step("gpsimd", lambda g: g.memset(v65[:, :], 1.0), incs=["gs"])
    v65_ones = ("gs", p.at("gs"))

    ost = psa[0:65, 2048:4096]
    p._l1a_exp = {}
    slab_free = [0, 0]
    vslab_free = 0
    pt_free = [0, 0]
    sS_done = p.at("ss")
    o_evac = p.at("ss")
    for h in range(16):
        par = h % 2
        vsl = v65
        k_ld = p.dma("gpsimd",
                     bass.AP(kh2, 64 * par * 8192,
                             [[8192, 64], [2048, 4], [1, 2048]]),
                     bass.AP(agout, h * 64 * 2048,
                             [[2048, 64], [AGR * 2048, 4], [1, 2048]]),
                     waits=[agv, ("ts", slab_free[par])], sem="k%d" % par)
        for vr in range(4):
            v_ld = p.dma("gpsimd",
                         bass.AP(vsl, vr * 16 * 65,
                                 [[64 * 65, 128], [65, 16], [1, 64]]),
                         bass.AP(agout, (1024 + vr * AGR) * 2048 + h * 64,
                                 [[1024, 128], [128 * 1024, 16], [1, 64]]),
                         waits=[agv, v65_ones, ("ts", vslab_free)],
                         sem="x0")
        pend = None
        for tt in range(64):
            for half in range(2):
                i = tt * 2 + half
                ps = i % 2
                p.step("tensor",
                       lambda t, h=h, tt=tt, par=par, ps=ps, half=half: [
                           t.matmul(pslot[ps][:, q * 512:(q + 1) * 512],
                                    kh2[64 * par:64 * par + 64,
                                        tt * 128:(tt + 1) * 128],
                                    q_fm[64 * (h % 2):64 * (h % 2) + 64,
                                         (h // 2) * 2048 + half * 1024 + q * 512:
                                         (h // 2) * 2048 + half * 1024 +
                                         (q + 1) * 512],
                                    start=True, stop=True)
                           for q in range(2)],
                       waits=[k_ld, ("ss", p._l1a_exp.get((i - 2) % 128, sS_done)),
                              ("vs", negb_ok[1])],
                       incs=["ts"])
                t_s = p.at("ts")
                if pend is not None:
                    p.step("tensor", pend[0], waits=pend[1], incs=["ts"])
                    pt_free[pend[2]] = p.at("ts")
                p.step("scalar",
                       lambda s, h=h, ps=ps: s.activation(
                           pt[:, ps * 1024:(ps + 1) * 1024],
                           pslot[ps][:, :], AF.Exp,
                           bias=negb[:, h:h + 1], scale=0.125),
                       waits=[("ts", t_s), ("ts", pt_free[ps])], incs=["ss"])
                p._l1a_exp[i % 128] = p.at("ss")
                s_exp = p.at("ss")
                avfn = (lambda t, tt=tt, half=half, ps=ps, vsl=vsl: [
                    t.matmul(ost[:, half * 1024 + q * 512:
                                 half * 1024 + (q + 1) * 512],
                             vsl[:, tt * 65:tt * 65 + 65],
                             pt[:, ps * 1024 + q * 512:
                                ps * 1024 + (q + 1) * 512],
                             start=(tt == 0), stop=(tt == 63))
                    for q in range(2)])
                avw = [v_ld, ("ss", s_exp)] + \
                    ([("ss", o_evac)] if tt == 0 and half == 0 else [])
                pend = (avfn, avw, ps)
        p.step("tensor", pend[0], waits=pend[1], incs=["ts"])
        pt_free[pend[2]] = p.at("ts")
        slab_free[par] = p.at("ts")
        vslab_free = p.at("ts")
        t_o = p.at("ts")
        p.step("scalar",
               lambda s: s.copy(stg[0:64, :].rearrange("p (a b) -> p a b", b=1024)
                                if False else stg[0:64, 0:2048], ost[0:64, :]),
               waits=[("ts", t_o), ("dsy", p.at("dsy"))], incs=["ss"])
        p.step("scalar",
               lambda s: s.copy(dsg, ost[64:65, :]),
               incs=["ss"])
        o_evac = p.at("ss")
        p.dma("sync", bass.AP(o_d, h * 64 * 2048, [[2048, 64], [1, 2048]]),
              stg[0:64, 0:2048], waits=["ss"])
        p.dma("sync", bass.AP(dst, h * 2048, [[2048, 1], [1, 2048]]),
              dsg, waits=["ss"])
    den_all = ("dsy", p.at("dsy"))
    p.dma("gpsimd", boutc,
          bass.AP(dr["l1a_bout"], 0, [[1, 128], [128, 8]]),
          waits=[("ts", p.at("ts"))], allow_slow_non_contiguous=True, sem="r0")
    w_bout = ("r0", p.at("r0"))
    p.step("vector", lambda v: v.reciprocal(dst[:, :], dst[:, :]),
           waits=[den_all], incs=["vs"])
    rec_all = ("vs", p.at("vs"))
    o_d2 = nc.dram_tensor("l1a_od2", [1024, 2048], BF16)
    och_free = [p.at("dsy"), p.at("dsy")]
    for ccx in range(8):
        so = 0
        o_ld = p.dma("sync", och[:, so * 2048:(so + 1) * 2048],
                     bass.AP(o_d, ccx * 128 * 2048, [[2048, 128], [1, 2048]]),
                     waits=[den_all, ("dsy", och_free[so])], sem="o%d" % so)
        mul_done = 0
        for hh in range(2):
            hglob = 2 * ccx + hh
            r_ld = p.dma("sync", recrow,
                         bass.AP(dst, hglob * 2048, [[2048, 1], [1, 2048]]),
                         waits=[rec_all, ("vs", mul_done)], sem="r1")
            for lq in range(4):
                rbv = psa[:, 2048 + lq * 512:2048 + lq * 512 + 512]
                p.step("tensor",
                       lambda t, lq=lq, rbv=rbv: t.matmul(
                           rbv, co["ones1f"][:, :],
                           sqf[0:1, lq * 512:(lq + 1) * 512],
                           start=True, stop=True),
                       waits=[r_ld, ("vs", p.at("vs"))], incs=["ts"])
                t_rb = p.at("ts")
                p.step("vector",
                       lambda v, hh=hh, so=so, lq=lq, rbv=rbv: v.tensor_mul(
                           och[64 * hh:64 * hh + 64,
                               so * 2048 + lq * 512:so * 2048 + (lq + 1) * 512],
                           och[64 * hh:64 * hh + 64,
                               so * 2048 + lq * 512:so * 2048 + (lq + 1) * 512],
                           rbv[64 * hh:64 * hh + 64, :]),
                       waits=[("ts", t_rb), o_ld], incs=["vs"])
            mul_done = p.at("vs")
        p.dma("sync", bass.AP(o_d2, ccx * 128 * 2048, [[2048, 128], [1, 2048]]),
              och[:, so * 2048:(so + 1) * 2048], waits=[("vs", mul_done)])
        och_free[so] = p.at("dsy")
    on_all = ("dsy", p.at("dsy"))
    dslot = [psa[:, 0:2048], psa[:, 2048:4096]]
    w_free = [p.at("ts"), p.at("ts")]
    on_free = [p.at("ts"), p.at("ts")]
    d_free = [p.at("vs"), p.at("vs")]
    for ccx in range(8):
        sw = ccx % 2
        wld = p.dma("gpsimd", wbuf[:, sw * 1024:(sw + 1) * 1024],
                    bass.AP(dr["l1a_woT"], ccx * 128,
                            [[1024, 128], [1024 * 128, 8], [1, 128]]),
                    waits=[("ts", w_free[sw])], sem="w%d" % sw)
        for hc in range(8):
            so = 0
            o_ld = p.dma("sync", och[:, so * 2048:(so + 1) * 2048],
                         bass.AP(o_d2, hc * 128 * 2048, [[2048, 128], [1, 2048]]),
                         waits=[on_all, ("ts", on_free[so])], sem="o%d" % so)
            p.step("tensor",
                   lambda t, ccx=ccx, hc=hc, sw=sw, so=so: [
                       t.matmul(dslot[ccx % 2][:, q * 512:(q + 1) * 512],
                                wbuf[:, sw * 1024 + hc * 128:
                                     sw * 1024 + (hc + 1) * 128],
                                och[:, so * 2048 + q * 512:
                                    so * 2048 + (q + 1) * 512],
                                start=(hc == 0), stop=(hc == 7))
                       for q in range(4)],
                   waits=[wld, o_ld] +
                         ([("vs", d_free[ccx % 2])] if hc == 0 else []),
                   incs=["ts"])
            on_free[so] = p.at("ts")
        w_free[sw] = p.at("ts")
        t_d = p.at("ts")
        for lh in range(2):
            p.step("vector",
                   lambda v, ccx=ccx, lh=lh: v.scalar_tensor_tensor(
                       xs[:, ccx * L + lh * 1024: ccx * L + (lh + 1) * 1024],
                       dslot[ccx % 2][:, lh * 1024:(lh + 1) * 1024],
                       boutc[:, ccx:ccx + 1],
                       xs[:, ccx * L + lh * 1024: ccx * L + (lh + 1) * 1024],
                       ALU.add, ALU.add),
                   waits=[("ts", t_d), w_bout], incs=["vs"])
        d_free[ccx % 2] = p.at("vs")
    return ("vs", p.at("vs"))


def build_l1_attn_standalone():
    es = ExitStack()
    nc = bass.Bass("TRN2", target_bir_lowering=False)
    x_in = nc.dram_tensor("x_in", [D, L], F32, kind="ExternalInput")
    dr = {}
    for nme, shape in [("l1a_wqT", [1024, 1024]), ("l1a_wkT", [1024, 1024]),
                       ("l1a_wvT", [1024, 1024]), ("l1a_woT", [1024, 1024]),
                       ("l1a_bout", [1024]), ("l1a_e2", [1024, 16])]:
        dr[nme] = nc.dram_tensor(nme, shape, F32, kind="ExternalInput")
    x_out = nc.dram_tensor("x_out", [D, L], F32, kind="ExternalOutput")
    with es:
        p = Prog(nc, es)
        co = setup_consts(p)
        xs = p.sbuf("xs", [128, 8 * L])
        xc = p.sbuf("xc", [128, 8 * L], BF16)
        psa = p.psum("psa", [128, 4096])
        lsc = make_ln_scratch(p)
        ld = load_stream_fm(p, xs, x_in)
        lnr = emit_ln_fm(p, co, xs, xc, "ln2", [ld], psa, lsc)
        fr = emit_l1_attn(p, co, xs, xc, dr, [lnr], psa)
        st = store_stream_fm(p, x_out, xs, [fr])
        p.step("sync", lambda sy: None, waits=[st], chain=False)
        p.build()
    return nc


def run_l1_attn_standalone(x2_full, ins):
    w = prep_l1_attn(ins)
    nc = build_l1_attn_standalone()
    in_maps = []
    for ci in range(NCORE):
        b, cch = divmod(ci, 4)
        xfm = np.ascontiguousarray(x2_full[b, cch * L:(cch + 1) * L, :].T)
        in_maps.append({"x_in": xfm, **w})
    res = run_bass_kernel_spmd(nc, in_maps, core_ids=list(range(NCORE)))
    out = np.empty((2, SEQ, D), np.float32)
    for ci in range(NCORE):
        b, cch = divmod(ci, 4)
        out[b, cch * L:(cch + 1) * L, :] = np.asarray(res.results[ci]["x_out"]).T
    return out


L0H = 513
YW = 2049


def _xpos_tables(t0):
    n = SEQ
    dh = 64
    half = np.arange(0, dh, 2, dtype=np.float64)
    inv_freq = 1.0 / 10000.0 ** (half / dh)
    t = np.arange(t0 - 512, t0 + 2048, dtype=np.float64)
    freqs = t[:, None] * inv_freq[None, :]
    base = (half + 0.4 * dh) / (1.4 * dh)
    power = (t - n // 2) / float(512 // 2)
    scale = base[None, :] ** power[:, None]
    cos = np.cos(freqs)
    sin = np.sin(freqs)
    qc = np.concatenate([cos * scale, cos * scale], 1)
    qs = np.concatenate([-sin * scale, sin * scale], 1)
    kc = np.concatenate([cos / scale, cos / scale], 1)
    ks = np.concatenate([-sin / scale, sin / scale], 1)
    return np.stack([qc, qs, kc, ks]).astype(np.float32)


def _l0f_masks(first_chunk):
    m = np.zeros((12, 128, 512), np.float32)
    for t in range(8):
        r = np.arange(128)[:, None] + 128 * t
        c = np.arange(512)[None, :]
        bad = (c > r) | (c < r - 512)
        m[t][bad] = -1e28
    for t in range(4):
        m[8 + t] = -1e28 if first_chunk else m[t]
    return m


def prep_l0_ffn(ins, core_id):
    b, cch = divmod(core_id, 4)
    t0 = cch * L
    d = {}
    d["l0f_tabs"] = _xpos_tables(t0)
    d["l0f_masks"] = _l0f_masks(cch == 0)
    d["l0f_flag"] = np.full((128, 1), 0.0 if cch == 0 else 1.0, np.float32)
    wq = ins["L0_ffn_wqkv"]
    d["l0f_wqkv"] = np.ascontiguousarray(wq)
    d["l0f_wo"] = np.ascontiguousarray(ins["L0_ffn_wo"])
    d["l0f_qs"] = np.ascontiguousarray(ins["L0_ffn_q_scale"])
    d["l0f_ks"] = np.ascontiguousarray(ins["L0_ffn_k_scale"])
    d["l0f_lng"] = np.ascontiguousarray(ins["L0_ffn_ln_g"])
    d["l0f_lnb"] = np.ascontiguousarray(ins["L0_ffn_ln_b"])
    d["l0f_b2"] = np.ascontiguousarray(ins["L0_ffn_b2"])
    prev = (cch + 3) % 4
    d["l0f_proff"] = np.array([[prev * 128 * (8 * 513), prev * 128 * 8]],
                              np.int32)
    w1 = ins["L0_ffn_w1"]
    b1 = ins["L0_ffn_b1"]
    w2 = ins["L0_ffn_w2"]
    w1i = np.zeros((2, 1024, 44 * 128), np.float32)
    b1i = np.zeros((2, 44 * 128), np.float32)
    w2p = np.zeros((2, 22 * 128, 1024), np.float32)
    for ll in range(2):
        a = w1[ll][:, :2730]
        g = w1[ll][:, 2730:]
        ba, bg = b1[ll][:2730], b1[ll][2730:]
        for j in range(22):
            sl = slice(128 * j, min(128 * (j + 1), 2730))
            n = sl.stop - sl.start
            w1i[ll][:, 256 * j:256 * j + n] = a[:, sl]
            w1i[ll][:, 256 * j + 128:256 * j + 128 + n] = g[:, sl]
            b1i[ll][256 * j:256 * j + n] = ba[sl]
            b1i[ll][256 * j + 128:256 * j + 128 + n] = bg[sl]
            w2p[ll][128 * j:128 * j + n] = w2[ll][sl]
    d["l0f_w1i"] = w1i
    d["l0f_b1i"] = b1i
    d["l0f_w2p"] = w2p
    return d

YBS = 2561


def emit_l0_ffn(p, co, dr, psa, x_in, x_out):
    nc = p.nc
    o_d = nc.dram_tensor("l0f_od", [512, 2048], BF16)
    ybuf_d = nc.dram_tensor("l0f_ybufd", [1024, YW], F32)
    agA_in = nc.dram_tensor("l0f_agAin", [128, 8 * 513], BF16)
    agA_out = nc.dram_tensor("l0f_agAout", [512, 8 * 513], BF16)
    agB_in = nc.dram_tensor("l0f_agBin", [128, 8], BF16)
    agB_out = nc.dram_tensor("l0f_agBout", [512, 8], BF16)

    ybc = p.sbuf("l0f_ybc", [128, 8 * YBS], BF16)
    ystg = p.sbuf("l0f_ystg", [128, 1024])
    pool = p.sbuf("l0f_pool", [128, 2048])
    pool2 = p.sbuf("l0f_pool2", [128, 6144], BF16)
    tmsc = p.sbuf("l0f_tmsc", [128, 1536], BF16)
    shA = p.sbuf("l0f_shA", [128, 11264], BF16)
    kfm = p.sbuf("l0f_kfm", [128, 4 * 2560], BF16)
    v65 = p.sbuf("l0f_v65", [128, 160 * 65], BF16)
    wqk = p.sbuf("l0f_wqk", [128, 8 * 1536], BF16)
    wstr = p.sbuf("l0f_wstr", [128, 4096], BF16)
    row = p.sbuf("l0f_row", [1, 512])
    rowt = p.sbuf("l0f_rowt", [1, 512])
    qsb = p.sbuf("l0f_qsb", [128, 128])
    nrm = p.sbuf("l0f_nrm", [128, 16])
    negbl = p.sbuf("l0f_negbl", [128, 1])
    b1c = p.sbuf("l0f_b1c", [128, 44])
    colz = p.sbuf("l0f_colz", [128, 24])
    flagc = p.sbuf("l0f_flagc", [128, 1])
    hstg = p.sbuf("l0f_hstg", [128, 8], BF16)
    hstf = p.sbuf("l0f_hstf", [128, 8])
    offs = p.sbuf("l0f_offs", [1, 2], mybir.dt.int32)
    qfm = shA[:, 0:8192]
    h2q = shA[:, 0:11264]
    och = shA[:, 0:8192]
    den = pool[0:8, 0:2048]
    rrow = p.sbuf("l0f_rrow", [1, 2048])
    ostg = tmsc[0:64, 0:512]
    ptb = tmsc[:, 512:1536]

    p._l0f_rb = 0
    p._l0f_yst = [0, 0]
    p.get_sem("y0"), p.get_sem("y1")
    p._l0f_geg = [0, 0]
    p._l0f_dfl = 0
    p._l0f_xfree = [0, 0]
    p._l0f_tabfree = 0
    p.dma("gpsimd", flagc[:, :], dr["l0f_flag"].ap(), sem="pre")
    p.dma("gpsimd", offs[:, :], dr["l0f_proff"].ap(), sem="pre")
    p.step("gpsimd", lambda g: g.memset(v65[:, :], 1.0), incs=["gs"])
    pre_ok = [("pre", p.at("pre")), ("gs", p.at("gs"))]

    def ycol(c, t):
        return c * YBS + t + 513

    s1 = psa[0:1, 0:512]
    s2 = psa[0:1, 512:1024]
    mub = psa[:, 1024:1536]
    rsb = psa[:, 1536:2048]
    xsl = [pool[:, 0:512], pool[:, 512:1024]]
    sq2 = [pool[:, 1024:1536], pool[:, 1536:2048]]

    def ln_stats_q(rhs_fn, wait_extra, shifted=False):
        for c in range(8):
            ap, w = rhs_fn(c)
            sl = c % 2
            p.step("scalar",
                   lambda s, sl=sl, ap=ap: s.activation(sq2[sl], ap, AF.Square),
                   waits=w + [("ts", p._l0f_sqfree[sl]), ("vs", p.at("vs"))],
                   incs=["ss"])
            p.step("tensor",
                   lambda t, c=c, ap=ap: t.matmul(
                       s1[:, :], co["onescf"][:, :], ap,
                       start=(c == 0), stop=(c == 7)),
                   waits=w + [co["ready"]] +
                         ([("vs", p.at("vs"))] if c == 0 else []))
            p.step("tensor",
                   lambda t, sl=sl, c=c: t.matmul(
                       s2[:, :], co["onescf"][:, :], sq2[sl],
                       start=(c == 0), stop=(c == 7)),
                   waits=["ss"], incs=["ts"])
            p._l0f_sqfree[sl] = p.at("ts")
        t_stats = p.at("ts")
        p.step("vector", lambda v: v.tensor_scalar_mul(row[:, :], s1[0:1, :],
                                                       1.0 / D),
               waits=[("ts", t_stats)], incs=["vs"])
        p.step("vector", lambda v: v.tensor_mul(rowt[:, :], row[:, :], row[:, :]),
               waits=["vs"], incs=["vs"])
        p.step("vector", lambda v: v.scalar_tensor_tensor(
                   rowt[:, :], s2[0:1, :], 1.0 / D, rowt[:, :],
                   ALU.mult, ALU.subtract),
               waits=["vs"], incs=["vs"])
        p.step("vector", lambda v: v.tensor_scalar_add(rowt[:, :], rowt[:, :],
                                                       EPS),
               waits=["vs"], incs=["vs"])
        v_var = p.at("vs")
        p.step("scalar", lambda s: s.activation(rowt[:, :], rowt[:, :], AF.Sqrt),
               waits=[("vs", v_var)], incs=["ss"])
        p.step("vector", lambda v: v.reciprocal(rowt[:, :], rowt[:, :]),
               waits=["ss"], incs=["vs"])
        v_rs = p.at("vs")
        p.step("tensor", lambda t: t.matmul(mub[:, :], co["ones1f"][:, :],
                                            row[0:1, :], start=True, stop=True),
               waits=[("vs", v_var)])
        p.step("tensor", lambda t: t.matmul(rsb[:, :], co["ones1f"][:, :],
                                            rowt[0:1, :], start=True, stop=True),
               waits=[("vs", v_rs)], incs=["ts"])
        return p.at("ts")

    p._l0f_sqfree = [0, 0]
    for lq in range(4):
        o = lq * 512

        def rhs_ln0(c, o=o):
            sl = c % 2
            ld = p.dma("sync", xsl[sl],
                       bass.AP(x_in, c * 128 * L + o, [[L, 128], [1, 512]]),
                       waits=[("ts", p._l0f_sqfree[sl]),
                              ("vs", p.at("vs"))], sem="sa%d" % sl)
            return xsl[sl], [ld]
        t_b = ln_stats_q(rhs_ln0, [])
        for c in range(8):
            sl = c % 2
            ld = p.dma("sync", xsl[sl],
                       bass.AP(x_in, c * 128 * L + o, [[L, 128], [1, 512]]),
                       waits=[("ts", p._l0f_sqfree[sl]), ("vs", p.at("vs")),
                              ("ss", p.at("ss"))], sem="sa%d" % sl)
            p.step("vector",
                   lambda v, sl=sl: v.tensor_sub(sq2[sl], xsl[sl], mub[:, :]),
                   waits=[ld, ("ts", t_b)], incs=["vs"])
            p.step("vector",
                   lambda v, sl=sl: v.tensor_mul(
                       ystg[:, sl * 512:(sl + 1) * 512], sq2[sl], rsb[:, :]),
                   waits=[("y%d" % sl, p._l0f_yst[sl])], incs=["vs"])
            p.step("vector",
                   lambda v, c=c, o=o, sl=sl: v.tensor_copy(
                       ybc[:, ycol(c, o): ycol(c, o) + 512],
                       ystg[:, sl * 512:(sl + 1) * 512]),
                   incs=["vs"])
            p.dma("gpsimd",
                  bass.AP(ybuf_d, c * 128 * YW + 1 + o, [[YW, 128], [1, 512]]),
                  ystg[:, sl * 512:(sl + 1) * 512], waits=["vs"],
                  sem="y%d" % sl)
            p._l0f_yst[sl] = p.at("y%d" % sl)
    y_ready = ("vs", p.at("vs"))

    import os as _os
    if _os.environ.get("PROBE_L0F_STOP") == "ln":
        p._l0f_ylast = [y_ready, ("y0", p.at("y0")), ("y1", p.at("y1"))]
        NLYR = 0
    else:
        NLYR = int(_os.environ.get("PROBE_L0F_LAYERS", "2"))
    for lyr in range(NLYR):
        lw = [("ts", p.at("ts")), ("vs", p.at("vs")), ("ss", p.at("ss"))]
        w_qk = p.dma("gpsimd", wqk[:, :],
                     bass.AP(dr["l0f_wqkv"], lyr * 1024 * 1536,
                             [[1536, 128], [1536 * 128, 8], [1, 1536]]),
                     waits=lw, sem="c%d" % lyr)
        p.dma("gpsimd", b1c[:, :],
              bass.AP(dr["l0f_b1i"], lyr * 5632, [[1, 128], [128, 44]]),
              waits=lw, allow_slow_non_contiguous=True, sem="c%d" % lyr)
        for ii, nm in enumerate(["l0f_lng", "l0f_lnb", "l0f_b2"]):
            p.dma("gpsimd", colz[:, 8 * ii:8 * ii + 8],
                  bass.AP(dr[nm], lyr * 1024, [[1, 128], [128, 8]]),
                  waits=lw, allow_slow_non_contiguous=True, sem="c%d" % lyr)
        for kind in range(4):
            p.dma("gpsimd", bass.AP(pool2, kind * 64,
                                    [[6144, 128], [256, 20], [1, 64]]),
                  bass.AP(dr["l0f_tabs"], kind * 2560 * 64,
                          [[64, 128], [128 * 64, 20], [1, 64]]),
                  waits=lw, sem="c%d" % lyr)
        p.dma("gpsimd", row[0:1, 0:64],
              bass.AP(dr["l0f_qs"], lyr * 64, [[128, 1], [1, 64]]),
              waits=lw, sem="c%d" % lyr)
        p.dma("gpsimd", row[0:1, 64:128],
              bass.AP(dr["l0f_ks"], lyr * 64, [[128, 1], [1, 64]]),
              waits=lw, sem="c%d" % lyr)
        wconst = ("c%d" % lyr, p.at("c%d" % lyr))
        w_qk = wconst
        p.step("tensor", lambda t: t.matmul(psa[:, 0:64], co["ones1f"][:, :],
                                            row[0:1, 0:64],
                                            start=True, stop=True),
               waits=[wconst, ("vs", p.at("vs"))])
        p.step("tensor", lambda t: t.matmul(psa[:, 64:128], co["ones1f"][:, :],
                                            row[0:1, 64:128],
                                            start=True, stop=True),
               incs=["ts"])
        p.step("vector", lambda v: v.tensor_copy(qsb[:, :], psa[:, 0:128]),
               waits=["ts"], incs=["vs"])
        qsb_ok = p.at("vs")
        for kind in range(4):
            base = (0 if kind < 2 else 64)
            if kind % 2 == 0:
                in1 = bass.AP(qsb, base, [[128, 128], [0, 20], [1, 64]])
            else:
                in1 = bass.AP(qsb, base + 32, [[128, 128], [0, 20], [-32, 2],
                                               [1, 32]])
            tab = bass.AP(pool2, kind * 64, [[6144, 128], [256, 20], [1, 64]])
            p.step("vector",
                   lambda v, tab=tab, in1=in1: v.tensor_tensor(tab, tab, in1,
                                                               ALU.mult),
                   waits=[wconst, ("vs", qsb_ok)], incs=["vs"])
        tab_ok = ("vs", p.at("vs"))
        p.step("vector", lambda v: v.reduce_max(rowt[0:1, 0:1],
                                                row[0:1, 0:64],
                                                AX.X, apply_absolute_value=True),
               waits=[wconst], incs=["vs"])
        p.step("vector", lambda v: v.reduce_max(rowt[0:1, 1:2],
                                                row[0:1, 64:128],
                                                AX.X, apply_absolute_value=True),
               incs=["vs"])
        p.step("vector", lambda v: v.tensor_mul(rowt[0:1, 0:1], rowt[0:1, 0:1],
                                                rowt[0:1, 1:2]),
               incs=["vs"])
        p.step("vector", lambda v: v.tensor_scalar_mul(rowt[0:1, 0:1],
                                                       rowt[0:1, 0:1], -8.0),
               incs=["vs"])
        p.step("tensor", lambda t: t.matmul(psa[:, 128:129], co["ones1f"][:, :],
                                            rowt[0:1, 0:1], start=True,
                                            stop=True),
               waits=["vs"], incs=["ts"])
        p.step("vector", lambda v: v.tensor_copy(negbl[:, :], psa[:, 128:129]),
               waits=["ts"], incs=["vs"])
        negb_ok = ("vs", p.at("vs"))
        if _os.environ.get("PROBE_L0F_STOP") == "consts":
            p._l0f_ylast = [negb_ok, ("y0", p.at("y0")), ("y1", p.at("y1"))]
            break

        src_ready = y_ready if lyr == 0 else p._l0f_ymid
        p.dma("sync", agA_in.ap(),
              bass.AP(ybc, ycol(0, 2048 - 513),
                      [[8 * YBS, 128], [YBS, 8], [1, 513]]),
              waits=[src_ready])
        p.step("gpsimd",
               lambda g: g.collective_compute(
                   "AllGather", ALU.bypass,
                   replica_groups=[[0, 1, 2, 3], [4, 5, 6, 7]],
                   ins=[agA_in.ap().opt()], outs=[agA_out.ap().opt()]),
               waits=["dsy", "dgp", "sa0", "sa1"], incs=["cc"], chain=False)
        agv = ("cc", p.at("cc"))
        p.step("sync", lambda sy: None, waits=[agv, pre_ok[0]], chain=False)
        p.step("sync",
               lambda sy: sy.dma_start(
                   out=bass.AP(ybc, ycol(0, -513),
                               [[8 * YBS, 128], [YBS, 8], [1, 513]]),
                   in_=bass.AP(agA_out, sy.value_load(offs[0:1, 0:1]),
                               [[8 * 513, 128], [513, 8], [1, 513]])),
               incs=[("dsy", 16)], chain=False)
        hl = ("dsy", p.at("dsy"))
        p.step("vector",
               lambda v: v.tensor_scalar(
                   bass.AP(ybc, ycol(0, -513), [[8 * YBS, 128], [YBS, 8],
                                                [1, 513]]),
                   bass.AP(ybc, ycol(0, -513), [[8 * YBS, 128], [YBS, 8],
                                                [1, 513]]),
                   flagc[:, 0:1], None, ALU.mult),
               waits=[hl, pre_ok[0]], incs=["vs"])
        halo_ok = ("vs", p.at("vs"))
        if _os.environ.get("PROBE_L0F_STOP") == "halo":
            p._l0f_ylast = [halo_ok, ("y0", p.at("y0")), ("y1", p.at("y1"))]
            break

        qkvp = psa[:, 0:1536]
        trp = [psa[:, 1536:1664], psa[:, 2048:2176]]
        trot = pool[:, 0:1536]
        ev_free = p.at("vs")
        tr_free = [p.at("vs"), p.at("vs")]
        for lt in range(20):
            t0 = -512 + 128 * lt
            obs = [1, 2] if lt < 4 else [0, 1, 2]
            p.step("tensor",
                   lambda t, lt=lt, t0=t0, obs=obs: [
                       t.matmul(qkvp[:, ob * 512:(ob + 1) * 512],
                                ybc[:, ycol(c, t0) - (1 if c >= 4 else 0):
                                    ycol(c, t0) - (1 if c >= 4 else 0) + 128],
                                wqk[:, c * 1536 + ob * 512:
                                    c * 1536 + (ob + 1) * 512],
                                start=(c == 0), stop=(c == 7))
                       for ob in obs for c in range(8)],
                   waits=[w_qk, halo_ok, ("vs", ev_free)] +
                         ([] if lyr == 0 else []),
                   incs=["ts"])
            t_qkv = p.at("ts")
            p.step("vector",
                   lambda v, obs=obs: v.tensor_copy(
                       tmsc[:, obs[0] * 512:1536], qkvp[:, obs[0] * 512:1536]),
                   waits=[("ts", t_qkv)], incs=["vs"])
            ev_free = p.at("vs")
            tm_ok = p.at("vs")
            for h in range(8):
                p.step("vector",
                       lambda v, lt=lt, h=h: v.tensor_copy(
                           v65[:, (lt * 8 + h) * 65:(lt * 8 + h) * 65 + 64],
                           tmsc[:, 1024 + h * 64:1024 + (h + 1) * 64]),
                       waits=[("vs", tm_ok)] + (pre_ok if lt == 0 else []),
                       incs=(["vs"] if h == 7 else []))
            if _os.environ.get("PROBE_L0F_QKV") == "mm":
                continue
            parts = ([("q", 0)] if lt >= 4 else []) + [("k", 512)]
            for nm2, off2 in parts:
                nb = 0 if nm2 == "q" else 8
                tks = 0 if nm2 == "q" else 2
                p.step("vector",
                       lambda v, off2=off2: v.tensor_mul(
                           trot[:, 0:512], tmsc[:, off2:off2 + 512],
                           tmsc[:, off2:off2 + 512]),
                       waits=[("vs", tm_ok)], incs=["vs"])
                p.step("vector",
                       lambda v, nb=nb: v.tensor_reduce(
                           nrm[:, nb:nb + 8],
                           trot[:, 0:512].rearrange("p (h d) -> p h d", d=64),
                           AX.X, ALU.add),
                       incs=["vs"])
                p.step("scalar",
                       lambda s, nb=nb: s.activation(nrm[:, nb:nb + 8],
                                                     nrm[:, nb:nb + 8], AF.Sqrt),
                       waits=["vs"], incs=["ss"])
                p.step("vector",
                       lambda v, nb=nb: v.tensor_scalar_max(
                           nrm[:, nb:nb + 8], nrm[:, nb:nb + 8], 1e-12),
                       waits=["ss"], incs=["vs"])
                p.step("vector",
                       lambda v, nb=nb: v.reciprocal(nrm[:, nb:nb + 8],
                                                     nrm[:, nb:nb + 8]),
                       incs=["vs"])
                ctab = bass.AP(pool2, lt * 256 + tks * 64,
                               [[6144, 128], [0, 8], [1, 64]])
                stab = bass.AP(pool2, lt * 256 + (tks + 1) * 64,
                               [[6144, 128], [0, 8], [1, 64]])
                p.step("vector",
                       lambda v, off2=off2, ctab=ctab: v.tensor_tensor(
                           trot[:, 0:512], tmsc[:, off2:off2 + 512], ctab,
                           ALU.mult),
                       waits=[tab_ok], incs=["vs"])
                p.step("vector",
                       lambda v, off2=off2, stab=stab: v.tensor_tensor(
                           trot[:, 512:1024],
                           bass.AP(tmsc, off2 + 32,
                                   [[1536, 128], [64, 8], [-32, 2], [1, 32]]),
                           stab, ALU.mult),
                       incs=["vs"])
                p.step("vector",
                       lambda v: v.tensor_add(trot[:, 512:1024], trot[:, 0:512],
                                              trot[:, 512:1024]),
                       incs=["vs"])
                p.step("vector",
                       lambda v, nb=nb: v.tensor_tensor(
                           trot[:, 1024:1536], trot[:, 512:1024],
                           bass.AP(nrm, nb, [[16, 128], [1, 8], [0, 64]]),
                           ALU.mult),
                       incs=["vs"])
                v_rot = p.at("vs")
                if _os.environ.get("PROBE_L0F_QKV") == "rot":
                    continue
                for hp in range(4):
                    sl = hp % 2
                    p.step("tensor",
                           lambda t, hp=hp, sl=sl: t.transpose(
                               trp[sl][:, :], trot[:, 1024 + hp * 128:
                                                   1024 + (hp + 1) * 128],
                               co["identf"][:, :]),
                           waits=[("vs", v_rot), ("vs", tr_free[sl]),
                                  co["ready2"]],
                           incs=["ts"])
                    t_tr = p.at("ts")
                    if nm2 == "q":
                        dst = qfm[:, hp * 2048 + t0: hp * 2048 + t0 + 128]
                    else:
                        dst = kfm[:, hp * 2560 + lt * 128:
                                  hp * 2560 + (lt + 1) * 128]
                    p.step("vector",
                           lambda v, dst=dst, sl=sl: v.tensor_copy(dst,
                                                                   trp[sl][:, :]),
                           waits=[("ts", t_tr)], incs=["vs"])
                    tr_free[sl] = p.at("vs")
        qkv_done = ("vs", p.at("vs"))

        if _os.environ.get("PROBE_L0F_STOP") == "qkv":
            p._l0f_ylast = [("vs", p.at("vs")), ("y0", p.at("y0")),
                            ("y1", p.at("y1"))]
            break
        p.dma("gpsimd", bass.AP(pool2, 0, [[6144, 128], [512, 12], [1, 512]]),
              bass.AP(dr["l0f_masks"], 0, [[512, 128], [128 * 512, 12], [1, 512]]),
              waits=[qkv_done], sem="mk")
        msk_ok = ("mk", p.at("mk"))

        sslot = [psa[:, 2048:2560], psa[:, 2560:3072]]
        oslot = [psa[0:65, 3072:3584], psa[0:65, 3584:4096]]
        s_free = [p.at("ss"), p.at("ss")]
        pt_free = [p.at("ts"), p.at("ts")]
        o_free = [p.at("ss"), p.at("ss")]
        for h in range(8):
            hp = h // 2
            hb = 64 * (h % 2)
            for w in range(4):
                osl = oslot[(h * 4 + w) % 2]
                for kvt in range(8):
                    g = 4 * w + kvt
                    ssl = sslot[kvt % 2]
                    mtile = (8 + kvt) if (w == 0 and kvt < 4) else kvt
                    p.step("tensor",
                           lambda t, hp=hp, hb=hb, g=g, w=w, ssl=ssl: t.matmul(
                               ssl[:, :],
                               kfm[hb:hb + 64, hp * 2560 + g * 128:
                                   hp * 2560 + (g + 1) * 128],
                               qfm[hb:hb + 64, hp * 2048 + w * 512:
                                   hp * 2048 + (w + 1) * 512],
                               start=True, stop=True),
                           waits=[qkv_done, ("ss", s_free[kvt % 2])],
                           incs=["ts"])
                    t_s = p.at("ts")
                    p.step("vector",
                           lambda v, ssl=ssl, mtile=mtile: v.tensor_tensor(
                               ssl[:, :], ssl[:, :],
                               pool2[:, mtile * 512:(mtile + 1) * 512],
                               ALU.add),
                           waits=[("ts", t_s), msk_ok], incs=["vs"])
                    v_m = p.at("vs")
                    p.step("scalar",
                           lambda s, ssl=ssl, kvt=kvt: s.activation(
                               ptb[:, (kvt % 2) * 512:(kvt % 2) * 512 + 512],
                               ssl[:, :], AF.Exp, bias=negbl[:, 0:1], scale=8.0),
                           waits=[("vs", v_m), negb_ok,
                                  ("ts", pt_free[kvt % 2])],
                           incs=["ss"])
                    s_free[kvt % 2] = p.at("ss")
                    s_e = p.at("ss")
                    p.step("tensor",
                           lambda t, g=g, h=h, kvt=kvt, osl=osl: t.matmul(
                               osl[:, :],
                               v65[:, (g * 8 + h) * 65:(g * 8 + h) * 65 + 65],
                               ptb[:, (kvt % 2) * 512:(kvt % 2) * 512 + 512],
                               start=(kvt == 0), stop=(kvt == 7)),
                           waits=[("ss", s_e)] +
                                 ([("ss", o_free[(h * 4 + w) % 2])]
                                  if kvt == 0 else []),
                           incs=["ts"])
                    pt_free[kvt % 2] = p.at("ts")
                t_o = p.at("ts")
                p.step("scalar",
                       lambda s, osl=osl: s.copy(ostg, osl[0:64, :]),
                       waits=[("ts", t_o), ("dsy", p.at("dsy")),
                              ("r1", p.at("r1"))], incs=["ss"])
                p.step("scalar",
                       lambda s, osl=osl: s.copy(row[:, :], osl[64:65, :]),
                       incs=["ss"])
                o_free[(h * 4 + w) % 2] = p.at("ss")
                p.dma("sync",
                      bass.AP(o_d, (128 * hp + hb) * 2048 + w * 512,
                              [[2048, 64], [1, 512]]),
                      ostg, waits=["ss"])
                p.dma("sync",
                      bass.AP(pool, h * 2048 + w * 512, [[2048, 1], [1, 512]]),
                      row[:, :], waits=["ss"], sem="r1")
        o_all = ("dsy", p.at("dsy"))
        den_all = ("r1", p.at("r1"))

        if _os.environ.get("PROBE_L0F_STOP") == "attn":
            p._l0f_ylast = [o_all, den_all, ("y0", p.at("y0")),
                            ("y1", p.at("y1"))]
            break
        p.step("vector", lambda v: v.reciprocal(den[:, :], den[:, :]),
               waits=[den_all], incs=["vs"])
        rec_ok = ("vs", p.at("vs"))
        w_wo = p.dma("gpsimd", wstr[:, :],
                     bass.AP(dr["l0f_wo"], lyr * 512 * 1024,
                             [[1024, 128], [1024 * 128, 4], [1, 1024]]),
                     waits=[("ts", p.at("ts"))], sem="wo")
        ochl = p.dma("sync", och[:, :],
                     bass.AP(o_d, 0, [[2048, 128], [128 * 2048, 4], [1, 2048]]),
                     waits=[o_all], sem="sa0")
        for h in range(8):
            hp = h // 2
            hb = 64 * (h % 2)
            p.dma("sync", rrow[:, :],
                  bass.AP(pool, h * 2048, [[2048, 1], [1, 2048]]),
                  waits=[rec_ok, ("ts", p._l0f_rb)], sem="r1")
            r_ld = ("r1", p.at("r1"))
            for w in range(4):
                rb = psa[:, 1024 + (w % 2) * 512:1024 + (w % 2) * 512 + 512]
                p.step("tensor",
                       lambda t, w=w, rb=rb: t.matmul(
                           rb, co["ones1f"][:, :],
                           rrow[0:1, w * 512:(w + 1) * 512],
                           start=True, stop=True),
                       waits=[r_ld, ("vs", p.at("vs"))], incs=["ts"])
                t_rb = p.at("ts")
                p._l0f_rb = t_rb
                p.step("vector",
                       lambda v, hp=hp, hb=hb, w=w, rb=rb: v.tensor_mul(
                           och[hb:hb + 64, hp * 2048 + w * 512:
                               hp * 2048 + (w + 1) * 512],
                           och[hb:hb + 64, hp * 2048 + w * 512:
                               hp * 2048 + (w + 1) * 512],
                           rb[hb:hb + 64, :]),
                       waits=[("ts", t_rb), ochl], incs=["vs"])
        on_ok = ("vs", p.at("vs"))
        dslot = [psa[:, 0:512], psa[:, 512:1024]]
        d_free = [p.at("vs"), p.at("vs")]
        for cx in range(8):
            for lq2 in range(4):
                sl = (cx * 4 + lq2) % 2
                p.step("tensor",
                       lambda t, cx=cx, lq2=lq2, sl=sl: [
                           t.matmul(dslot[sl][:, :],
                                    wstr[:, hdc * 1024 + cx * 128:
                                         hdc * 1024 + (cx + 1) * 128],
                                    och[:, hdc * 2048 + lq2 * 512:
                                        hdc * 2048 + (lq2 + 1) * 512],
                                    start=(hdc == 0), stop=(hdc == 3))
                           for hdc in range(4)],
                       waits=[w_wo, on_ok, ("vs", d_free[sl])], incs=["ts"])
                t_d = p.at("ts")
                p.step("vector",
                       lambda v, sl=sl: v.tensor_copy(
                           ystg[:, sl * 512:(sl + 1) * 512], dslot[sl][:, :]),
                       waits=[("ts", t_d), ("y%d" % sl, p._l0f_yst[sl])],
                       incs=["vs"])
                d_free[sl] = p.at("vs")
                p.dma("gpsimd",
                      bass.AP(ybuf_d, cx * 128 * YW + 1 + lq2 * 512,
                              [[YW, 128], [1, 512]]),
                      ystg[:, sl * 512:(sl + 1) * 512], waits=["vs"],
                      sem="y%d" % sl, accum_op=ALU.add)
                p._l0f_yst[sl] = p.at("y%d" % sl)
        yacc = [("y0", p.at("y0")), ("y1", p.at("y1"))]
        for c in range(8):
            p.dma("gpsimd", ybc[:, ycol(c, 0):ycol(c, 0) + 2048],
                  bass.AP(ybuf_d, c * 128 * YW + 1, [[YW, 128], [1, 2048]]),
                  waits=yacc + [("vs", p.at("vs"))], sem="r0")
        ymid_ok = ("r0", p.at("r0"))

        p.dma("sync", agB_in.ap(),
              bass.AP(ybc, ycol(0, 2047), [[8 * YBS, 128], [YBS, 8]]),
              waits=[ymid_ok], allow_slow_non_contiguous=True)
        p.step("gpsimd",
               lambda g: g.collective_compute(
                   "AllGather", ALU.bypass,
                   replica_groups=[[0, 1, 2, 3], [4, 5, 6, 7]],
                   ins=[agB_in.ap().opt()], outs=[agB_out.ap().opt()]),
               waits=["dsy", "dgp", "sa0", "sa1", "r0", "r1"], incs=["cc"],
               chain=False)
        agbv = ("cc", p.at("cc"))
        p.step("sync",
               lambda sy: sy.dma_start(
                   out=hstg[:, :],
                   in_=bass.AP(agB_out, sy.value_load(offs[0:1, 1:2]),
                               [[8, 128], [1, 8]])),
               waits=[agbv], incs=[("dsy", 16)], chain=False)
        hb_ld = ("dsy", p.at("dsy"))
        p.step("vector",
               lambda v: v.tensor_scalar(hstf[:, :], hstg[:, :], flagc[:, 0:1],
                                         None, ALU.mult),
               waits=[hb_ld, pre_ok[0]], incs=["vs"])
        p.dma("sync", bass.AP(ybuf_d, 0, [[YW, 128], [128 * YW, 8]]),
              bass.AP(hstf, 0, [[8, 128], [1, 8]]), waits=["vs"],
              allow_slow_non_contiguous=True)
        ymid_halo = ("dsy", p.at("dsy"))

        for lq2 in range(4):
            o2 = lq2 * 512

            def rhs_iln(c, o2=o2):
                off = 1 if c < 4 else 0
                sl = c % 2
                ld = p.dma("sync", xsl[sl],
                           bass.AP(ybuf_d, c * 128 * YW + off + o2,
                                   [[YW, 128], [1, 512]]),
                           waits=[ymid_halo, ("ts", p._l0f_sqfree[sl]),
                                  ("vs", p.at("vs"))], sem="sa%d" % sl)
                return xsl[sl], [ld]
            t_b2 = ln_stats_q(rhs_iln, [])
            for c in range(8):
                off = 1 if c < 4 else 0
                sl = c % 2
                ld = p.dma("sync", xsl[sl],
                           bass.AP(ybuf_d, c * 128 * YW + off + o2,
                                   [[YW, 128], [1, 512]]),
                           waits=[("ts", p._l0f_sqfree[sl]), ("vs", p.at("vs")),
                                  ("ss", p.at("ss"))], sem="sa%d" % sl)
                p.step("vector",
                       lambda v, sl=sl: v.tensor_sub(sq2[sl], xsl[sl],
                                                     mub[:, :]),
                       waits=[ld, ("ts", t_b2)], incs=["vs"])
                p.step("vector",
                       lambda v, sl=sl: v.tensor_mul(sq2[sl], sq2[sl],
                                                     rsb[:, :]),
                       incs=["vs"])
                p.step("vector",
                       lambda v, c=c, o2=o2, sl=sl: v.tensor_scalar(
                           ybc[:, ycol(c, o2):ycol(c, o2) + 512], sq2[sl],
                           colz[:, c:c + 1], colz[:, 8 + c:9 + c],
                           ALU.mult, ALU.add),
                       waits=[wconst], incs=["vs"])
        h_ok = ("vs", p.at("vs"))

        w1_free = [p.at("ts"), p.at("ts")]
        w2_free = [p.at("ts"), p.at("ts")]
        for lq2 in range(4):
            o2 = lq2 * 512
            ap_slot = [psa[:, 0:512], psa[:, 512:1024]]
            gp_slot = [psa[:, 1024:1536], psa[:, 1536:2048]]
            for j in range(22):
                sl = j % 2
                wld = p.dma("gpsimd", wstr[:, sl * 2048:(sl + 1) * 2048],
                            bass.AP(dr["l0f_w1i"], lyr * 1024 * 5632 + j * 256,
                                    [[5632, 128], [5632 * 128, 8], [1, 256]]),
                            waits=[("ts", w1_free[sl])], sem="w%d" % sl)
                p.step("tensor",
                       lambda t, j=j, sl=sl, o2=o2: [
                           t.matmul((ap_slot if half == 0 else gp_slot)[sl][:, :],
                                    wstr[:, sl * 2048 + ci * 256 + half * 128:
                                         sl * 2048 + ci * 256 + half * 128 + 128],
                                    ybc[:, ycol(ci, o2):ycol(ci, o2) + 512],
                                    start=(ci == 0), stop=(ci == 7))
                           for half in range(2) for ci in range(8)],
                       waits=[wld, h_ok, ("vs", p._l0f_geg[sl])],
                       incs=["ts"])
                t_w1 = p.at("ts")
                w1_free[sl] = t_w1
                p.step("scalar",
                       lambda s, j=j, sl=sl: s.activation(
                           tmsc[:, sl * 512:(sl + 1) * 512], gp_slot[sl][:, :],
                           AF.Gelu, bias=b1c[:, 2 * j + 1:2 * j + 2]),
                       waits=[("ts", t_w1), wconst], incs=["ss"])
                p.step("vector",
                       lambda v, j=j, sl=sl: v.scalar_tensor_tensor(
                           h2q[:, j * 512:(j + 1) * 512], ap_slot[sl][:, :],
                           b1c[:, 2 * j:2 * j + 1],
                           tmsc[:, sl * 512:(sl + 1) * 512],
                           ALU.add, ALU.mult),
                       waits=["ss"], incs=["vs"])
                p._l0f_geg[sl] = p.at("vs")
            h2_ok = p.at("vs")
            t_p1 = p.at("ts")
            w2_free = [max(w2_free[0], t_p1), max(w2_free[1], t_p1)]
            dsl2 = [psa[:, 2048 + k * 512:2048 + (k + 1) * 512] for k in range(4)]
            for cg in range(2):
                for j in range(22):
                    sl = j % 2
                    wld = p.dma("gpsimd", wstr[:, 2048 + sl * 512 + 0:
                                               2048 + sl * 512 + 512]
                                if False else
                                bass.AP(wstr, 4096 - (2 - sl) * 512,
                                        [[4096, 128], [1, 512]]),
                                bass.AP(dr["l0f_w2p"],
                                        lyr * 2816 * 1024 + j * 128 * 1024 +
                                        cg * 512,
                                        [[1024, 128], [1, 512]]),
                                waits=[("ts", w2_free[sl])], sem="k%d" % sl)
                    p.step("tensor",
                           lambda t, j=j, sl=sl, cg=cg: [
                               t.matmul(dsl2[ci][:, :],
                                        bass.AP(wstr,
                                                4096 - (2 - sl) * 512 + ci * 128,
                                                [[4096, 128], [1, 128]]),
                                        h2q[:, j * 512:(j + 1) * 512],
                                        start=(j == 0), stop=(j == 21))
                               for ci in range(4)],
                           waits=[wld, ("vs", h2_ok)] +
                                 ([("vs", p._l0f_dfl)] if j == 0 else []),
                           incs=["ts"])
                    w2_free[sl] = p.at("ts")
                t_w2 = p.at("ts")
                for ci in range(4):
                    cxg = cg * 4 + ci
                    sl = ci % 2
                    p.step("vector",
                           lambda v, ci=ci, cxg=cxg, sl=sl: v.tensor_scalar(
                               ystg[:, sl * 512:(sl + 1) * 512], dsl2[ci][:, :],
                               colz[:, 16 + cxg:17 + cxg], None, ALU.add),
                           waits=[("ts", t_w2), wconst,
                                  ("y%d" % sl, p._l0f_yst[sl])],
                           incs=["vs"])
                    p.dma("gpsimd",
                          bass.AP(ybuf_d, cxg * 128 * YW + 1 + o2,
                                  [[YW, 128], [1, 512]]),
                          ystg[:, sl * 512:(sl + 1) * 512], waits=["vs"],
                          sem="y%d" % sl, accum_op=ALU.add)
                    p._l0f_yst[sl] = p.at("y%d" % sl)
                p._l0f_dfl = p.at("vs")
            t_p2 = p.at("ts")
            w1_free = [max(w1_free[0], t_p2), max(w1_free[1], t_p2)]
        p._l0f_tabfree = p.at("vs")
        yacc = [("y0", p.at("y0")), ("y1", p.at("y1"))]
        if lyr == 0:
            for c in range(8):
                p.dma("gpsimd", ybc[:, ycol(c, 0):ycol(c, 0) + 2048],
                      bass.AP(ybuf_d, c * 128 * YW + 1, [[YW, 128], [1, 2048]]),
                      waits=yacc + [("vs", p.at("vs"))], sem="r0")
            p._l0f_ymid = ("r0", p.at("r0"))
        else:
            p._l0f_ymid = yacc[0]
        p._l0f_ylast = yacc

    yfin = p._l0f_ylast
    for c in range(8):
        for lq in range(4):
            sl = (c * 4 + lq) % 2
            o = lq * 512
            ld = p.dma("sync", xsl[sl],
                       bass.AP(x_in, c * 128 * L + o, [[L, 128], [1, 512]]),
                       waits=list(yfin) + [("vs", p._l0f_xfree[sl])],
                       sem="sa%d" % sl)
            ld2 = p.dma("gpsimd", ystg[:, sl * 512:(sl + 1) * 512],
                        bass.AP(ybuf_d, c * 128 * YW + 1 + o,
                                [[YW, 128], [1, 512]]),
                        waits=list(yfin) + [("vs", p._l0f_xfree[sl])],
                        sem="y%d" % sl)
            p.step("vector",
                   lambda v, sl=sl: v.tensor_add(
                       sq2[sl], xsl[sl], ystg[:, sl * 512:(sl + 1) * 512]),
                   waits=[ld, ld2, ("st%d" % sl, p.at("st%d" % sl))],
                   incs=["vs"])
            p._l0f_xfree[sl] = p.at("vs")
            p.dma("sync", bass.AP(x_out, c * 128 * L + o, [[L, 128], [1, 512]]),
                  sq2[sl], waits=["vs"], sem="st%d" % sl)
    return ("st0", p.at("st0")), ("st1", p.at("st1"))


def build_l0_ffn_standalone():
    es = ExitStack()
    nc = bass.Bass("TRN2", target_bir_lowering=False)
    x_in = nc.dram_tensor("x_in", [D, L], F32, kind="ExternalInput")
    ident = nc.dram_tensor("c_ident", [128, 128], F32, kind="ExternalInput")
    dr = {}
    for nme, shape, dt in [
            ("l0f_tabs", [4, 2560, 64], F32), ("l0f_masks", [12, 128, 512], F32),
            ("l0f_flag", [128, 1], F32), ("l0f_proff", [1, 2], mybir.dt.int32),
            ("l0f_wqkv", [2, 1024, 1536], F32), ("l0f_wo", [2, 512, 1024], F32),
            ("l0f_qs", [2, 64], F32), ("l0f_ks", [2, 64], F32),
            ("l0f_lng", [2, 1024], F32), ("l0f_lnb", [2, 1024], F32),
            ("l0f_b2", [2, 1024], F32), ("l0f_w1i", [2, 1024, 5632], F32),
            ("l0f_b1i", [2, 5632], F32), ("l0f_w2p", [2, 2816, 1024], F32)]:
        dr[nme] = nc.dram_tensor(nme, shape, dt, kind="ExternalInput")
    x_out = nc.dram_tensor("x_out", [D, L], F32, kind="ExternalOutput")
    with es:
        p = Prog(nc, es)
        co = setup_consts(p, ident)
        psa = p.psum("psa", [128, 4096])
        m1, m2 = emit_l0_ffn(p, co, dr, psa, x_in, x_out)
        p.step("sync", lambda sy: None, waits=[m1, m2], chain=False)
        p.build()
    return nc


def run_l0_ffn_standalone(x1_full, ins):
    nc = build_l0_ffn_standalone()
    in_maps = []
    id_ = np.eye(128, dtype=np.float32)
    for ci in range(NCORE):
        b, cch = divmod(ci, 4)
        w = prep_l0_ffn(ins, ci)
        xfm = np.ascontiguousarray(x1_full[b, cch * L:(cch + 1) * L, :].T)
        in_maps.append({"x_in": xfm, "c_ident": id_, **w})
    res = run_bass_kernel_spmd(nc, in_maps, core_ids=list(range(NCORE)))
    out = np.empty((2, SEQ, D), np.float32)
    for ci in range(NCORE):
        b, cch = divmod(ci, 4)
        out[b, cch * L:(cch + 1) * L, :] = np.asarray(res.results[ci]["x_out"]).T
    return out


NKH = 8
TOT = 8224


def prep_l0_attn(ins, core_id):
    b, cch = divmod(core_id, 4)
    d = {}
    win = ins["L0_attn_W_in"]
    wmem = np.zeros((1024, 1152), np.float32)
    wmem[:, :1056] = win[:, :1056]
    d["l0a_wmem"] = wmem
    wq = win[:, 1056:].reshape(1024, 4, 32, 8, 2)
    d["l0a_wq"] = np.ascontiguousarray(
        wq.transpose(0, 4, 1, 2, 3).reshape(1024, 2048))
    ck = ins["L0_attn_conv_k"][:, 0, :]
    ckp = np.zeros((4, 1152), np.float32)
    ckp[:, :1056] = ck
    ckc = np.zeros((128, 36), np.float32)
    for ci in range(9):
        for s in range(4):
            ckc[:, ci * 4 + s] = ckp[s, ci * 128:(ci + 1) * 128]
    d["l0a_ckc"] = ckc
    th = ins["L0_attn_theta"][0, 0]
    assert np.allclose(th[..., :4], -th[..., 7:3:-1], atol=1e-6), \
        "theta m-antisymmetry violated"
    d["l0a_throw"] = np.ascontiguousarray(th[..., 4:8].reshape(1, 4096))
    d["l0a_slopes"] = np.concatenate([ins["L0_attn_decay_slopes"],
                                      ins["L0_attn_anchor_slopes"]])[None, :]
    d["l0a_score"] = ins["L0_attn_score_scale"][None, :]
    t0 = cch * L
    pos = np.arange(t0, t0 + L, dtype=np.float64)
    dist = np.maximum(float(SEQ - 1) - pos, 0.0)
    d["l0a_ttab"] = np.stack([dist, pos], 1).astype(np.float32)
    c1 = ins["L0_attn_norm_scale"][:32] * np.diagonal(ins["L0_attn_W_re"])
    c2 = ins["L0_attn_norm_scale"][32:] * np.diagonal(ins["L0_attn_W_im"])
    d["l0a_c1"] = np.ascontiguousarray(c1[None, :])
    d["l0a_c2"] = np.ascontiguousarray(c2[None, :])
    d["l0a_wout"] = np.ascontiguousarray(ins["L0_attn_W_out"])
    tri = np.tril(np.ones((128, 128), np.float32)).T
    d["l0a_tri"] = tri
    w4 = np.zeros((4, 1), np.float32)
    w4[:cch, 0] = 1.0
    d["l0a_w4"] = w4
    d["l0a_flag"] = np.full((128, 1), 0.0 if cch == 0 else 1.0, np.float32)
    return d


def emit_l0_attn(p, co, dr, psa, x_in, x_out):
    nc = p.nc
    z_d = nc.dram_tensor("l0a_zd", [1152, 2176], F32)
    q_d = nc.dram_tensor("l0a_qd", [L, 2048], BF16)
    kv_d = nc.dram_tensor("l0a_kvd", [L, 1152], F32)
    qf_d = nc.dram_tensor("l0a_qfd", [4, L, 512], BF16)
    a_d = nc.dram_tensor("l0a_ad", [2, 8, L, 512], F32)
    o_d2 = nc.dram_tensor("l0a_od", [2, L, 1024], F32)
    agin = nc.dram_tensor("l0a_agin", [1, TOT], F32)
    agout = nc.dram_tensor("l0a_agout", [4, TOT], F32)

    bigb = p.sbuf("l0a_bigb", [128, 17408], BF16)
    bigf = p.sbuf("l0a_bigf", [128, 11264])
    wpool = p.sbuf("l0a_wpool", [128, 17408], BF16)
    thb = p.sbuf("l0a_thb", [128, 4096])
    pw = p.sbuf("l0a_pw", [128, 16 * 32])
    inv = p.sbuf("l0a_inv", [128, 16 * 32])
    stb = p.sbuf("l0a_stb", [128, 2048], BF16)
    stat = p.sbuf("l0a_stat", [128, 8])
    crow = p.sbuf("l0a_crow", [1, 1024])
    evr = p.sbuf("l0a_evr", [1, 1024])
    rows = p.sbuf("l0a_rows", [1, 4096 + 128])
    sprow = p.sbuf("l0a_sp", [1, 32])
    spb = p.sbuf("l0a_spb", [128, 32])
    scb = p.sbuf("l0a_scb", [128, 32])
    c12 = p.sbuf("l0a_c12", [128, 64])
    ttab = p.sbuf("l0a_ttabs", [128, 16 * 2])
    w4c = p.sbuf("l0a_w4c", [4, 1])
    flagc = p.sbuf("l0a_flagc", [128, 1])
    trif = p.sbuf("l0a_trif", [128, 128])
    npi2 = p.sbuf("l0a_npi2", [128, 1])
    xc = bigb[:, 0:17408]
    lns = bigf[:, 0:2048]
    lnt = bigf[:, 2048:3072]
    xltf = bigf[:, 3072:5120]
    stg = bigf[:, 5120:7168]
    wm = wpool[:, 0:9216]
    wst = wpool[:, 9216:17408]
    zcs = bigf[:, 0:4352]
    kvf = bigf[:, 4352:6400]
    kvf2 = bigf[:, 6400:8448]
    kvs = bigf[:, 0:256]
    tt = bigf[:, 256:2304]
    sst = bigf[:, 2304:4352]
    ab = bigf[:, 0:2048]
    ttB = bigf[:, 2048:3584]
    orow = bigf[:, 3584:4096]
    c4s = bigf[0:4, 4096:5120]
    qf = bigb[:, 0:8192]
    oslab = bigf[:, 0:4096]
    ytm = bigf[:, 4096:5120]
    xo = bigf[:, 5120:7168]
    xfs = bigf[:, 7168:9216]
    lns2 = bigf[:, 9216:11264]
    yfm = bigb[:, 8192:10240]
    wout = wpool[:, 0:8192]

    p.dma("gpsimd", trif[:, :], dr["l0a_tri"].ap(), sem="pre")
    p.dma("gpsimd", w4c[:, :], dr["l0a_w4"].ap(), sem="pre")
    p.dma("gpsimd", flagc[:, :], dr["l0a_flag"].ap(), sem="pre")
    p.dma("gpsimd", rows[0:1, 0:4096], dr["l0a_throw"].ap(), sem="pre")
    p.dma("gpsimd", rows[0:1, 4096:4128], dr["l0a_slopes"].ap(), sem="pre")
    p.dma("gpsimd", rows[0:1, 4128:4160], dr["l0a_score"].ap(), sem="pre")
    p.dma("gpsimd", rows[0:1, 4160:4192], dr["l0a_c1"].ap(), sem="pre")
    p.dma("gpsimd", rows[0:1, 4192:4224], dr["l0a_c2"].ap(), sem="pre")
    p.dma("gpsimd", ttab[:, :],
          bass.AP(dr["l0a_ttab"], 0, [[2, 128], [128 * 2, 16], [1, 2]]),
          sem="pre")
    p.step("gpsimd", lambda g: g.memset(npi2[:, :], -np.pi / 2), incs=["gs"])
    pre_ok = [("pre", p.at("pre")), ("gs", p.at("gs"))]
    for cc in range(8):
        p.step("tensor",
               lambda t, cc=cc: t.matmul(psa[:, (cc % 4) * 512:(cc % 4) * 512 + 512],
                                         co["ones1f"][:, :],
                                         rows[0:1, cc * 512:(cc + 1) * 512],
                                         start=True, stop=True),
               waits=pre_ok + [co["ready"], ("vs", p.at("vs"))], incs=["ts"])
        p.step("vector",
               lambda v, cc=cc: v.tensor_copy(thb[:, cc * 512:(cc + 1) * 512],
                                              psa[:, (cc % 4) * 512:(cc % 4) * 512 + 512]),
               waits=["ts"], incs=["vs"])
    p.step("scalar", lambda s: s.activation(sprow[:, :], rows[0:1, 4096:4128],
                                            AF.Abs),
           waits=pre_ok, incs=["ss"])
    p.step("scalar", lambda s: s.activation(sprow[:, :], sprow[:, :], AF.Exp,
                                            scale=-1.0),
           incs=["ss"])
    p.step("vector", lambda v: v.tensor_scalar_add(sprow[:, :], sprow[:, :], 1.0),
           waits=["ss"], incs=["vs"])
    p.step("scalar", lambda s: s.activation(sprow[:, :], sprow[:, :], AF.Ln),
           waits=["vs"], incs=["ss"])
    p.step("vector", lambda v: v.tensor_scalar_max(evr[0:1, 0:32],
                                                   rows[0:1, 4096:4128], 0.0),
           waits=pre_ok + ["ss"], incs=["vs"])
    p.step("vector", lambda v: v.tensor_add(sprow[:, :], sprow[:, :],
                                            evr[0:1, 0:32]),
           waits=["ss"], incs=["vs"])
    p.step("vector", lambda v: v.tensor_scalar_mul(sprow[:, :], sprow[:, :],
                                                   -1.0),
           incs=["vs"])
    v_sp = p.at("vs")
    p.step("tensor", lambda t: t.matmul(psa[:, 0:32], co["ones1f"][:, :],
                                        sprow[0:1, :], start=True, stop=True),
           waits=[("vs", v_sp), ("vs", p.at("vs"))], incs=["ts"])
    p.step("vector", lambda v: v.tensor_copy(spb[:, :], psa[:, 0:32]),
           waits=["ts"], incs=["vs"])
    p.step("tensor", lambda t: t.matmul(psa[:, 32:64], co["ones1f"][:, :],
                                        rows[0:1, 4128:4160], start=True,
                                        stop=True),
           waits=pre_ok + [("vs", p.at("vs"))], incs=["ts"])
    p.step("vector", lambda v: v.tensor_copy(scb[:, :], psa[:, 32:64]),
           waits=["ts"], incs=["vs"])
    p.step("tensor", lambda t: t.matmul(psa[:, 64:128], co["ones1f"][:, :],
                                        rows[0:1, 4160:4224], start=True,
                                        stop=True),
           waits=pre_ok + [("vs", p.at("vs"))], incs=["ts"])
    p.step("vector", lambda v: v.tensor_copy(c12[:, :], psa[:, 64:128]),
           waits=["ts"], incs=["vs"])
    con_ok = ("vs", p.at("vs"))

    trpL = [psa[:, 2560:2688], psa[:, 3072:3200]]
    tr_free = [p.at("vs"), p.at("vs")]
    lns_free = [p.at("vs"), p.at("vs")]
    xlt_free = [p.at("ts"), p.at("ts")]
    rstd_ap = bass.AP(stat, 6, [[8, 128], [0, 1024]])
    for ht in range(17):
        sl = ht % 2
        if ht == 0:
            src = dr["l0a_x0h"].ap()
        else:
            src = bass.AP(x_in, (ht - 1) * 128 * 1024, [[1024, 128], [1, 1024]])
        xv = lns[:, sl * 1024:(sl + 1) * 1024]
        lw = p.dma("sync", xv, src, waits=[("vs", lns_free[sl])],
                   sem="sa" + str(sl))
        p.step("vector", lambda v, xv=xv: v.tensor_reduce(stat[:, 0:1], xv,
                                                          AX.X, ALU.add),
               waits=[lw], incs=["vs"])
        p.step("scalar", lambda s, xv=xv: s.activation(lnt[:, :], xv, AF.Square),
               waits=[lw, "vs"], incs=["ss"])
        p.step("vector", lambda v: v.tensor_reduce(stat[:, 1:2], lnt[:, :],
                                                   AX.X, ALU.add),
               waits=["ss"], incs=["vs"])
        p.step("vector", lambda v: v.tensor_scalar_mul(stat[:, 2:3],
                                                       stat[:, 0:1], 1.0 / D),
               incs=["vs"])
        p.step("vector", lambda v: v.tensor_mul(stat[:, 3:4], stat[:, 2:3],
                                                stat[:, 2:3]),
               incs=["vs"])
        p.step("vector", lambda v: v.tensor_scalar(stat[:, 4:5], stat[:, 1:2],
                                                   1.0 / D, EPS, ALU.mult,
                                                   ALU.add),
               incs=["vs"])
        p.step("vector", lambda v: v.tensor_sub(stat[:, 4:5], stat[:, 4:5],
                                                stat[:, 3:4]),
               incs=["vs"])
        p.step("scalar", lambda s: s.activation(stat[:, 5:6], stat[:, 4:5],
                                                AF.Sqrt),
               waits=["vs"], incs=["ss"])
        p.step("vector", lambda v: v.reciprocal(stat[:, 6:7], stat[:, 5:6]),
               waits=["ss"], incs=["vs"])
        xlf = xltf[:, sl * 1024:(sl + 1) * 1024]
        p.step("vector",
               lambda v, xv=xv, xlf=xlf: v.scalar_tensor_tensor(
                   xlf, xv, stat[:, 2:3], rstd_ap, ALU.subtract, ALU.mult),
               waits=[("ts", xlt_free[sl])], incs=["vs"])
        lns_free[sl] = p.at("vs")
        v_x = p.at("vs")
        for cc in range(8):
            tsl = cc % 2
            p.step("tensor",
                   lambda t, cc=cc, tsl=tsl, xlf=xlf: t.transpose(
                       trpL[tsl][:, :], xlf[:, cc * 128:(cc + 1) * 128],
                       co["identf"][:, :]),
                   waits=[("vs", v_x), ("vs", tr_free[tsl]), co["ready2"]],
                   incs=["ts"])
            t_tr = p.at("ts")
            p.step("vector",
                   lambda v, cc=cc, ht=ht, tsl=tsl: v.tensor_copy(
                       xc[:, cc * 2176 + ht * 128:cc * 2176 + (ht + 1) * 128],
                       trpL[tsl][:, :]),
                   waits=[("ts", t_tr)], incs=["vs"])
            tr_free[tsl] = p.at("vs")
        xlt_free[sl] = p.at("ts")
        if ht == 0:
            halo_ap = bass.AP(bigb, 0, [[17408, 128], [2176, 8], [1, 128]])
            p.step("vector",
                   lambda v: v.tensor_scalar(halo_ap, halo_ap, flagc[:, 0:1],
                                             None, ALU.mult),
                   waits=[pre_ok[0]], incs=["vs"])
    xc_ok = ("vs", p.at("vs"))
    if _os.environ.get("PROBE_L0A_STOP") == "ln":
        p._l0a_last = [xc_ok]
        return

    wm_ok = p.dma("gpsimd", wm,
                  bass.AP(dr["l0a_wmem"], 0,
                          [[1152, 128], [128 * 1152, 8], [1, 1152]]),
                  sem="ca0")
    zslots = [psa[:, 0:512], psa[:, 512:1024]]
    zfree = [p.at("vs"), p.at("vs")]
    zst_dma = [None, None]
    blocks = [(0, 512), (512, 512), (1024, 512), (1536, 512), (2048, 128)]
    ii = 0
    for ci in range(9):
        for (b0, bn) in blocks:
            sl = ii % 2
            ii += 1
            for kc in range(8):
                w = [xc_ok, wm_ok] if kc == 0 else []
                if kc == 0:
                    w.append(("vs", zfree[sl]))
                p.step("tensor",
                       lambda t, kc=kc, ci=ci, b0=b0, bn=bn, sl=sl: t.matmul(
                           zslots[sl][:, 0:bn],
                           wm[:, kc * 1152 + ci * 128:kc * 1152 + (ci + 1) * 128],
                           xc[:, kc * 2176 + b0:kc * 2176 + b0 + bn],
                           start=(kc == 0), stop=(kc == 7)),
                       waits=w, incs=(["ts"] if kc == 7 else []))
            t_z = p.at("ts")
            zs = "st" + str(sl)
            p.step("vector",
                   lambda v, sl=sl, bn=bn: v.tensor_copy(
                       stg[:, sl * 512:sl * 512 + bn], zslots[sl][:, 0:bn]),
                   waits=[("ts", t_z)] +
                         ([zst_dma[sl]] if zst_dma[sl] else []),
                   incs=["vs"])
            zfree[sl] = p.at("vs")
            zst_dma[sl] = p.dma(
                "sync", bass.AP(z_d, ci * 128 * 2176 + b0, [[2176, 128], [1, bn]]),
                stg[:, sl * 512:sl * 512 + bn], waits=["vs"], sem=zs)
    z_done = [zst_dma[0], zst_dma[1]]

    qslots = [psa[:, 1024:1536], psa[:, 1536:2048]]
    qfree = [p.at("vs"), p.at("vs")]
    wq_used = [p.at("ts"), p.at("ts")]
    fold_dma = {}
    for ob in range(4):
        wsl = ob % 2
        wq_ok = p.dma("gpsimd", wst[:, wsl * 4096:(wsl + 1) * 4096],
                      bass.AP(dr["l0a_wq"], ob * 512,
                              [[2048, 128], [128 * 2048, 8], [1, 512]]),
                      waits=[("ts", wq_used[wsl])], sem="wq" + str(wsl))
        for lt in range(16):
            qsl = lt % 2
            for cc in range(8):
                w = [xc_ok, wq_ok] if cc == 0 else []
                if cc == 0:
                    w.append(("vs", qfree[qsl]))
                p.step("tensor",
                       lambda t, cc=cc, lt=lt, wsl=wsl, qsl=qsl: t.matmul(
                           qslots[qsl][:, :],
                           xc[:, cc * 2176 + 128 + lt * 128:
                              cc * 2176 + 128 + (lt + 1) * 128],
                           wst[:, wsl * 4096 + cc * 512:wsl * 4096 + (cc + 1) * 512],
                           start=(cc == 0), stop=(cc == 7)),
                       waits=w, incs=(["ts"] if cc == 7 else []))
            wq_used[wsl] = p.at("ts")
            t_q = p.at("ts")
            ssl = 2 + qsl
            p.step("vector",
                   lambda v, ssl=ssl, qsl=qsl: v.tensor_copy(
                       stg[:, ssl * 512:(ssl + 1) * 512], qslots[qsl][:, :]),
                   waits=[("ts", t_q)], incs=["vs"])
            qfree[qsl] = p.at("vs")
            base = 5120 + ssl * 512
            inP0 = bass.AP(bigf, base + 4, [[11264, 128], [8, 64], [1, 4]])
            inP1 = bass.AP(bigf, base + 3, [[11264, 128], [8, 64], [-1, 4]])
            for fi, op in ((0, ALU.add), (1, ALU.subtract)):
                fs = (lt % 2) * 2 + fi
                fsem = "qf" + str(fs)
                w = [(fsem, p.at(fsem))] if fsem in p.cnt else []
                p.step("vector",
                       lambda v, fs=fs, op=op, inP0=inP0, inP1=inP1:
                       v.tensor_tensor(stb[:, fs * 256:(fs + 1) * 256],
                                       inP0, inP1, op),
                       waits=w, incs=["vs"])
                f = fi + (2 if ob >= 2 else 0)
                fold_dma[(f, lt, ob % 2)] = p.dma(
                    "sync",
                    bass.AP(qf_d, (f * L + lt * 128) * 512 + (ob % 2) * 256,
                            [[512, 128], [1, 256]]),
                    stb[:, fs * 256:(fs + 1) * 256], waits=["vs"], sem=fsem)
    qfold_done = [("qf%d" % i, p.at("qf%d" % i)) for i in range(4)]
    if _os.environ.get("PROBE_L0A_STOP") == "proj":
        p._l0a_last = z_done + qfold_done
        return

    cks = p.sbuf("l0a_cks", [128, 36])
    ck_ok = p.dma("gpsimd", cks[:, :], dr["l0a_ckc"].ap(), sem="ca1")
    zcs_free = [p.at("vs"), p.at("vs")]
    kvf_free = p.at("ts")
    kvd_dma = None
    for ci in range(9):
        sl = ci % 2
        zl = p.dma("sync", zcs[:, sl * 2176:(sl + 1) * 2176],
                   bass.AP(z_d, ci * 128 * 2176, [[2176, 128], [1, 2176]]),
                   waits=z_done + [("vs", zcs_free[sl])], sem="ya" + str(sl))
        zb = sl * 2176
        p.step("vector",
               lambda v, zb=zb, ci=ci: v.tensor_scalar(
                   kvf[:, :], zcs[:, zb + 125:zb + 125 + 2048],
                   cks[:, ci * 4:ci * 4 + 1], None, ALU.mult),
               waits=[zl, ck_ok, ("ts", kvf_free)], incs=["vs"])
        for s in range(1, 4):
            p.step("vector",
                   lambda v, zb=zb, ci=ci, s=s: v.scalar_tensor_tensor(
                       kvf[:, :], zcs[:, zb + 125 + s:zb + 125 + s + 2048],
                       cks[:, ci * 4 + s:ci * 4 + s + 1], kvf[:, :],
                       ALU.mult, ALU.add),
                   incs=["vs"])
        zcs_free[sl] = p.at("vs")
        v_kv = p.at("vs")
        for ltb in range(16):
            tsl = ltb % 2
            p.step("tensor",
                   lambda t, ltb=ltb, tsl=tsl: t.transpose(
                       trpL[tsl][:, :], kvf[:, ltb * 128:(ltb + 1) * 128],
                       co["identf"][:, :]),
                   waits=[("vs", v_kv), ("vs", tr_free[tsl]), co["ready2"]],
                   incs=["ts"])
            t_tr = p.at("ts")
            w = [("ts", t_tr)] + ([kvd_dma] if kvd_dma else [])
            p.step("vector",
                   lambda v, ltb=ltb, tsl=tsl: v.tensor_copy(
                       kvf2[:, ltb * 128:(ltb + 1) * 128], trpL[tsl][:, :]),
                   waits=w, incs=["vs"])
            tr_free[tsl] = p.at("vs")
        kvf_free = p.at("ts")
        kvd_dma = p.dma("sync",
                        bass.AP(kv_d, ci * 128,
                                [[1152, 128], [128 * 1152, 16], [1, 128]]),
                        kvf2[:, :], waits=["vs"], sem="rv0")
    conv_done = kvd_dma
    trp_clear = [("vs", tr_free[0]), ("vs", tr_free[1])]
    if _os.environ.get("PROBE_L0A_STOP") == "conv":
        p._l0a_last = [conv_done]
        return
    p2_vs = p.at("vs")

    kvs_free = [p.at("vs"), p.at("vs")]
    t1 = tt[:, 0:32]
    t2 = tt[:, 32:64]
    tg_dma = [None, None]
    for lt in range(16):
        sl = lt % 2
        kd = p.dma("sync", kvs[:, sl * 128:sl * 128 + 32],
                   bass.AP(kv_d, lt * 128 * 1152 + 1024, [[1152, 128], [1, 32]]),
                   waits=[conv_done, ("vs", kvs_free[sl])], sem="kv" + str(sl))
        p.step("vector",
               lambda v, sl=sl: v.tensor_tensor(
                   t1, kvs[:, sl * 128:sl * 128 + 32], scb[:, :], ALU.mult),
               waits=[kd, con_ok, "ss"], incs=["vs"])
        kvs_free[sl] = p.at("vs")
        p.step("vector",
               lambda v: v.tensor_scalar(t1, t1, 20.0, -20.0, ALU.min, ALU.max),
               incs=["vs"])
        p.step("vector",
               lambda v, lt=lt: v.tensor_tensor(
                   t2[:, 0:28],
                   bass.AP(ttab, lt * 2, [[32, 128], [0, 28]]),
                   spb[:, 0:28], ALU.mult),
               incs=["vs"])
        p.step("vector",
               lambda v, lt=lt: v.tensor_tensor(
                   t2[:, 28:32],
                   bass.AP(ttab, lt * 2 + 1, [[32, 128], [0, 4]]),
                   spb[:, 28:32], ALU.mult),
               incs=["vs"])
        p.step("vector", lambda v: v.tensor_add(t1, t1, t2), incs=["vs"])
        p.step("scalar",
               lambda s, lt=lt: s.activation(pw[:, lt * 32:(lt + 1) * 32],
                                             t1, AF.Exp),
               waits=["vs"], incs=["ss"])
        w = [("ss", p.at("ss"))]
        if lt == 0:
            w += trp_clear + [("vs", p2_vs)]
        p.step("tensor",
               lambda t, lt=lt: t.matmul(psa[0:1, 3072:3104], co["onescf"][:, :],
                                         pw[:, lt * 32:(lt + 1) * 32],
                                         start=(lt == 0), stop=(lt == 15)),
               waits=w, incs=["ts"])
    p.step("vector", lambda v: v.tensor_copy(evr[0:1, 0:32], psa[0:1, 3072:3104]),
           waits=["ts"], incs=["vs"])
    tg_dma[0] = p.dma("sync", bass.AP(agin, 8192, [[32, 1], [1, 32]]),
                      evr[0:1, 0:32], waits=["vs"], sem="tg0")
    if _os.environ.get("PROBE_L0A_STOP") == "pw":
        p._l0a_last = [tg_dma[0]]
        return

    sst_ts = [p.at("ts")] * 4
    c2n_mark = p.at("ss")
    ev_mark = p.at("vs")
    for cp in range(8):
        for lt in range(16):
            sl = lt % 2
            kd = p.dma("sync", kvs[:, sl * 128:(sl + 1) * 128],
                       bass.AP(kv_d, lt * 128 * 1152 + cp * 128,
                               [[1152, 128], [1, 128]]),
                       waits=[conv_done, ("vs", kvs_free[sl])], sem="kv" + str(sl))
            kv_ap = bass.AP(bigf, sl * 128, [[11264, 128], [1, 128], [0, 4]])
            p.step("vector",
                   lambda v, cp=cp, kv_ap=kv_ap: v.tensor_tensor(
                       tt[:, 0:512], kv_ap, thb[:, cp * 512:(cp + 1) * 512],
                       ALU.mult),
                   waits=[kd, con_ok, ("ss", c2n_mark)], incs=["vs"])
            kvs_free[sl] = p.at("vs")
            p.step("scalar",
                   lambda s: s.activation(tt[:, 512:1024], tt[:, 0:512],
                                          AF.Sin, scale=0.5),
                   waits=["vs"], incs=["ss"])
            p.step("scalar",
                   lambda s: s.activation(tt[:, 1024:1536], tt[:, 0:512],
                                          AF.Abs, scale=0.5),
                   incs=["ss"])
            p.step("scalar",
                   lambda s: s.activation(tt[:, 0:512], tt[:, 1024:1536],
                                          AF.Sin, bias=npi2[:, 0:1]),
                   waits=["gs"], incs=["ss"])
            c2n_mark = p.at("ss")
            p.step("vector",
                   lambda v: v.tensor_mul(tt[:, 1536:2048], tt[:, 512:1024],
                                          tt[:, 512:1024]),
                   waits=["ss"], incs=["vs"])
            p.step("vector",
                   lambda v: v.tensor_scalar(tt[:, 1536:2048], tt[:, 1536:2048],
                                             -2.0, 1.0, ALU.mult, ALU.add),
                   incs=["vs"])
            pw_ap = bass.AP(pw, lt * 32 + 4 * cp, [[512, 128], [1, 4], [0, 128]])
            srsem = "sr" + str(sl)
            sisem = "si" + str(sl)
            p.step("vector",
                   lambda v, sl=sl, pw_ap=pw_ap: v.tensor_tensor(
                       sst[:, sl * 512:(sl + 1) * 512], tt[:, 1536:2048],
                       pw_ap, ALU.mult),
                   waits=[(srsem, p.at(srsem)), ("ts", sst_ts[sl])],
                   incs=["vs"])
            w = [("vs", p.at("vs"))]
            if lt == 0:
                w.append(("vs", ev_mark))
            p.step("tensor",
                   lambda t, sl=sl, lt=lt: t.matmul(
                       psa[0:1, 0:512], co["onescf"][:, :],
                       sst[:, sl * 512:(sl + 1) * 512],
                       start=(lt == 0), stop=(lt == 15)),
                   waits=w, incs=["ts"])
            p.dma("sync",
                  bass.AP(a_d, (cp * 2048 + lt * 128) * 512,
                          [[512, 128], [1, 512]]),
                  sst[:, sl * 512:(sl + 1) * 512], waits=["vs"], sem=srsem)
            sst_ts[sl] = p.at("ts")
            p.step("vector",
                   lambda v: v.scalar_tensor_tensor(
                       tt[:, 1536:2048], tt[:, 512:1024], -2.0, tt[:, 0:512],
                       ALU.mult, ALU.mult),
                   incs=["vs"])
            p.step("vector",
                   lambda v, sl=sl, pw_ap=pw_ap: v.tensor_tensor(
                       sst[:, 1024 + sl * 512:1024 + (sl + 1) * 512],
                       tt[:, 1536:2048], pw_ap, ALU.mult),
                   waits=[(sisem, p.at(sisem)), ("ts", sst_ts[2 + sl])],
                   incs=["vs"])
            p.step("tensor",
                   lambda t, sl=sl, lt=lt: t.matmul(
                       psa[0:1, 512:1024], co["onescf"][:, :],
                       sst[:, 1024 + sl * 512:1024 + (sl + 1) * 512],
                       start=(lt == 0), stop=(lt == 15)),
                   waits=["vs"], incs=["ts"])
            p.dma("sync",
                  bass.AP(a_d, ((8 + cp) * 2048 + lt * 128) * 512,
                          [[512, 128], [1, 512]]),
                  sst[:, 1024 + sl * 512:1024 + (sl + 1) * 512],
                  waits=["vs"], sem=sisem)
            sst_ts[2 + sl] = p.at("ts")
        t_tot = p.at("ts")
        p.step("vector",
               lambda v: v.tensor_copy(evr[0:1, 0:512], psa[0:1, 0:512]),
               waits=[("ts", t_tot), ("tg0", p.at("tg0"))], incs=["vs"])
        tg_dma[0] = p.dma("sync",
                          bass.AP(agin, cp * 512, [[512, 1], [1, 512]]),
                          evr[0:1, 0:512], waits=["vs"], sem="tg0")
        p.step("vector",
               lambda v: v.tensor_copy(evr[0:1, 512:1024], psa[0:1, 512:1024]),
               waits=[("ts", t_tot), ("tg1", p.at("tg1"))], incs=["vs"])
        tg_dma[1] = p.dma("sync",
                          bass.AP(agin, 4096 + cp * 512, [[512, 1], [1, 512]]),
                          evr[0:1, 512:1024], waits=["vs"], sem="tg1")
        ev_mark = p.at("vs")

    p.step("gpsimd",
           lambda g: g.collective_compute(
               "AllGather", ALU.bypass,
               replica_groups=[[0, 1, 2, 3], [4, 5, 6, 7]],
               ins=[agin.ap().opt()], outs=[agout.ap().opt()]),
           waits=[("tg0", p.at("tg0")), ("tg1", p.at("tg1"))],
           incs=["cc"], chain=False)
    agv = ("cc", p.at("cc"))
    stash_done = [("sr0", p.at("sr0")), ("sr1", p.at("sr1")),
                  ("si0", p.at("si0")), ("si1", p.at("si1"))]
    if _os.environ.get("PROBE_L0A_STOP") == "ag":
        p._l0a_last = [agv] + stash_done
        return

    cum_re = [psa[:, 1024:1536], psa[:, 1536:2048]]
    cum_im = [psa[:, 2048:2560], psa[:, 2560:3072]]
    ph6_vs = p.at("vs")
    invfree = [ph6_vs, ph6_vs]
    prodfree_re = [ph6_vs, ph6_vs]
    prodfree_im = [ph6_vs, ph6_vs]
    c4d = p.dma("sync", c4s[:, 0:32], bass.AP(agout, 8192, [[TOT, 4], [1, 32]]),
                waits=[agv] + stash_done, sem="c40")
    p.step("tensor",
           lambda t: t.matmul(psa[0:1, 0:32], w4c[:, :], c4s[:, 0:32],
                              start=True, stop=True),
           waits=[c4d, pre_ok[0]], incs=["ts"])
    p.step("vector", lambda v: v.tensor_copy(crow[0:1, 0:32], psa[0:1, 0:32]),
           waits=["ts"], incs=["vs"])
    v_add = p.at("vs")
    for lt in range(16):
        sl = lt % 2
        p.step("tensor",
               lambda t, sl=sl, lt=lt: t.matmul(
                   cum_re[sl][:, 0:32], trif[:, :], pw[:, lt * 32:(lt + 1) * 32],
                   start=True, stop=False),
               waits=[("vs", invfree[sl]), ("vs", v_add)])
        p.step("tensor",
               lambda t, sl=sl: t.matmul(cum_re[sl][:, 0:32],
                                         co["ones1f"][:, :], crow[0:1, 0:32],
                                         start=False, stop=True),
               incs=["ts"])
        t_cum = p.at("ts")
        p.step("tensor",
               lambda t, lt=lt: t.matmul(psa[0:1, 0:32], co["onescf"][:, :],
                                         pw[:, lt * 32:(lt + 1) * 32],
                                         start=True, stop=True),
               waits=[("vs", v_add)], incs=["ts"])
        t_ct = p.at("ts")
        p.step("vector",
               lambda v: v.tensor_add(crow[0:1, 0:32], crow[0:1, 0:32],
                                      psa[0:1, 0:32]),
               waits=[("ts", t_ct)], incs=["vs"])
        v_add = p.at("vs")
        p.step("vector",
               lambda v, sl=sl, lt=lt: v.tensor_scalar_max(
                   inv[:, lt * 32:(lt + 1) * 32], cum_re[sl][:, 0:32], 1e-4),
               waits=[("ts", t_cum)], incs=["vs"])
        p.step("vector",
               lambda v, lt=lt: v.reciprocal(inv[:, lt * 32:(lt + 1) * 32],
                                             inv[:, lt * 32:(lt + 1) * 32]),
               incs=["vs"])
        invfree[sl] = p.at("vs")
    cb_mark = p.at("vs")
    if _os.environ.get("PROBE_L0A_STOP") == "den":
        p._l0a_last = [("vs", cb_mark)]
        return

    ab_used = [p.at("ts")] * 4
    qprod = p.at("vs")
    for kq in range(4):
        qw = [agv, ("vs", qprod)] + (qfold_done if kq == 0 else [])
        for f in range(4):
            p.dma("sync", qf[:, f * 2048:(f + 1) * 2048],
                  bass.AP(qf_d, f * L * 512 + kq * 128,
                          [[512, 128], [128 * 512, 16], [1, 128]]),
                  waits=qw, sem="mk2")
        qf_ok = ("mk2", p.at("mk2"))
        for cpl in range(2):
            cp = kq * 2 + cpl
            c4r = p.dma("sync", c4s[:, 0:512],
                        bass.AP(agout, cp * 512, [[TOT, 4], [1, 512]]),
                        waits=[agv], sem="c40")
            c4i = p.dma("sync", c4s[:, 512:1024],
                        bass.AP(agout, 4096 + cp * 512, [[TOT, 4], [1, 512]]),
                        waits=[agv], sem="c41")
            p.step("tensor",
                   lambda t: t.matmul(psa[0:1, 0:512], w4c[:, :],
                                      c4s[:, 0:512], start=True, stop=True),
                   waits=[c4r, ("vs", cb_mark), pre_ok[0]], incs=["ts"])
            p.step("tensor",
                   lambda t: t.matmul(psa[0:1, 512:1024], w4c[:, :],
                                      c4s[:, 512:1024], start=True, stop=True),
                   waits=[c4i], incs=["ts"])
            t_c0 = p.at("ts")
            p.step("vector",
                   lambda v: v.tensor_copy(crow[0:1, 0:512], psa[0:1, 0:512]),
                   waits=[("ts", t_c0)], incs=["vs"])
            p.step("vector",
                   lambda v: v.tensor_copy(crow[0:1, 512:1024],
                                           psa[0:1, 512:1024]),
                   incs=["vs"])
            v_add = p.at("vs")
            for lt in range(16):
                sl = lt % 2
                srs, sis = "sr" + str(sl), "si" + str(sl)
                abw = [agv] if (kq == 0 and cpl == 0 and lt < 2) else []
                ar = p.dma("sync", ab[:, sl * 512:(sl + 1) * 512],
                           bass.AP(a_d, (cp * 2048 + lt * 128) * 512,
                                   [[512, 128], [1, 512]]),
                           waits=abw + [(srs, p.at(srs)), ("ts", ab_used[sl])],
                           sem=srs)
                ai = p.dma("sync", ab[:, 1024 + sl * 512:1024 + (sl + 1) * 512],
                           bass.AP(a_d, ((8 + cp) * 2048 + lt * 128) * 512,
                                   [[512, 128], [1, 512]]),
                           waits=abw + [(sis, p.at(sis)),
                                        ("ts", ab_used[2 + sl])],
                           sem=sis)
                p.step("tensor",
                       lambda t, sl=sl: t.matmul(
                           cum_re[sl][:, :], trif[:, :],
                           ab[:, sl * 512:(sl + 1) * 512],
                           start=True, stop=False),
                       waits=[ar, ("vs", prodfree_re[sl]), ("vs", v_add)])
                p.step("tensor",
                       lambda t, sl=sl: t.matmul(
                           cum_re[sl][:, :], co["ones1f"][:, :],
                           crow[0:1, 0:512], start=False, stop=True),
                       incs=["ts"])
                p.step("tensor",
                       lambda t, sl=sl: t.matmul(
                           cum_im[sl][:, :], trif[:, :],
                           ab[:, 1024 + sl * 512:1024 + (sl + 1) * 512],
                           start=True, stop=False),
                       waits=[ai, ("vs", prodfree_im[sl])])
                p.step("tensor",
                       lambda t, sl=sl: t.matmul(
                           cum_im[sl][:, :], co["ones1f"][:, :],
                           crow[0:1, 512:1024], start=False, stop=True),
                       incs=["ts"])
                p.step("tensor",
                       lambda t, sl=sl: t.matmul(
                           psa[0:1, 0:512], co["onescf"][:, :],
                           ab[:, sl * 512:(sl + 1) * 512],
                           start=True, stop=True),
                       waits=[("vs", v_add)], incs=["ts"])
                p.step("tensor",
                       lambda t, sl=sl: t.matmul(
                           psa[0:1, 512:1024], co["onescf"][:, :],
                           ab[:, 1024 + sl * 512:1024 + (sl + 1) * 512],
                           start=True, stop=True),
                       incs=["ts"])
                t_cyc = p.at("ts")
                ab_used[sl] = t_cyc
                ab_used[2 + sl] = t_cyc
                p.step("vector",
                       lambda v: v.tensor_add(crow[0:1, 0:512],
                                              crow[0:1, 0:512],
                                              psa[0:1, 0:512]),
                       waits=[("ts", t_cyc)], incs=["vs"])
                p.step("vector",
                       lambda v: v.tensor_add(crow[0:1, 512:1024],
                                              crow[0:1, 512:1024],
                                              psa[0:1, 512:1024]),
                       incs=["vs"])
                v_add = p.at("vs")
                cb_mark = v_add
                aps = [bass.AP(bigb, f * 2048 + lt * 128,
                               [[17408, 128], [0, 4], [1, 128]])
                       for f in range(4)]
                stw = (stash_done if (kq == 0 and cpl == 0 and lt == 0) else [])
                p.step("vector",
                       lambda v, sl=sl, a0=aps[0]: v.tensor_tensor(
                           ttB[:, 0:512], cum_re[sl][:, :], a0, ALU.mult),
                       waits=stw + [("ts", t_cyc), qf_ok], incs=["vs"])
                p.step("vector",
                       lambda v, sl=sl, a3=aps[3]: v.tensor_tensor(
                           ttB[:, 512:1024], cum_im[sl][:, :], a3, ALU.mult),
                       incs=["vs"])
                p.step("vector",
                       lambda v: v.tensor_add(ttB[:, 1024:1536], ttB[:, 0:512],
                                              ttB[:, 512:1024]),
                       incs=["vs"])
                osl = (lt % 2) * 2
                osem, osem2 = "o" + str(osl), "o" + str(osl + 1)
                p.step("vector",
                       lambda v, osl=osl: v.tensor_reduce(
                           orow[:, osl * 128:(osl + 1) * 128],
                           ttB[:, 1024:1536].rearrange("p (a b) -> p a b", b=4),
                           AX.X, ALU.add),
                       waits=stw + [(osem, p.at(osem))], incs=["vs"])
                p.dma("sync",
                      bass.AP(o_d2, lt * 128 * 1024 + cp * 128,
                              [[1024, 128], [1, 128]]),
                      orow[:, osl * 128:(osl + 1) * 128], waits=["vs"],
                      sem=osem)
                p.step("vector",
                       lambda v, sl=sl, a1=aps[1]: v.tensor_tensor(
                           ttB[:, 0:512], cum_im[sl][:, :], a1, ALU.mult),
                       incs=["vs"])
                prodfree_im[sl] = p.at("vs")
                p.step("vector",
                       lambda v, sl=sl, a2=aps[2]: v.tensor_tensor(
                           ttB[:, 512:1024], cum_re[sl][:, :], a2, ALU.mult),
                       incs=["vs"])
                prodfree_re[sl] = p.at("vs")
                p.step("vector",
                       lambda v: v.tensor_sub(ttB[:, 1024:1536], ttB[:, 0:512],
                                              ttB[:, 512:1024]),
                       incs=["vs"])
                p.step("vector",
                       lambda v, osl=osl: v.tensor_reduce(
                           orow[:, (osl + 1) * 128:(osl + 2) * 128],
                           ttB[:, 1024:1536].rearrange("p (a b) -> p a b", b=4),
                           AX.X, ALU.add),
                       waits=[(osem2, p.at(osem2))], incs=["vs"])
                p.dma("sync",
                      bass.AP(o_d2, (L + lt * 128) * 1024 + cp * 128,
                              [[1024, 128], [1, 128]]),
                      orow[:, (osl + 1) * 128:(osl + 2) * 128], waits=["vs"],
                      sem=osem2)
        qprod = p.at("vs")

    if _os.environ.get("PROBE_L0A_STOP") == "passb":
        p._l0a_last = [("o%d" % i, p.at("o%d" % i)) for i in range(4)]
        return
    pb_ts = ("ts", p.at("ts"))
    pb_vs = ("vs", p.at("vs"))
    ob_done = [("o%d" % i, p.at("o%d" % i)) for i in range(4)]
    wov = p.dma("gpsimd", wout,
                bass.AP(dr["l0a_wout"], 0,
                        [[1024, 128], [128 * 1024, 8], [1, 1024]]),
                waits=[agv], sem="wo")
    c1_ap = bass.AP(c12, 0, [[64, 128], [0, 32], [1, 32]])
    c2_ap = bass.AP(c12, 32, [[64, 128], [0, 32], [1, 32]])
    os_ts = [pb_ts[1]] * 2
    os_vs = [pb_vs[1]] * 2
    x0_free = [pb_vs[1]] * 2
    dfree = [pb_vs[1]] * 4
    xo_used = [pb_ts[1]] * 2
    for lt in range(16):
        sl = lt % 2
        ysem, xsem = "ya" + str(sl), "sa" + str(sl)
        ow = (ob_done + [pb_ts, pb_vs]) if lt < 2 else []
        ol = p.dma("sync", oslab[:, sl * 2048:(sl + 1) * 2048],
                   bass.AP(o_d2, lt * 128 * 1024,
                           [[1024, 128], [L * 1024, 2], [1, 1024]]),
                   waits=ow + [(ysem, p.at(ysem)), ("ts", os_ts[sl]),
                               ("vs", os_vs[sl])],
                   sem=ysem)
        xl = p.dma("sync", lns2[:, sl * 1024:(sl + 1) * 1024],
                   bass.AP(x_in, lt * 128 * 1024, [[1024, 128], [1, 1024]]),
                   waits=[(xsem, p.at(xsem)), ("vs", x0_free[sl])], sem=xsem)
        orh = oslab[:, sl * 2048:sl * 2048 + 1024]
        oih = oslab[:, sl * 2048 + 1024:sl * 2048 + 2048]
        inv_ap = bass.AP(inv, lt * 32, [[512, 128], [1, 32], [0, 32]])
        p.step("vector",
               lambda v, orh=orh, inv_ap=inv_ap: v.tensor_tensor(
                   ytm[:, :], orh, inv_ap, ALU.mult),
               waits=[ol, "ss"] + ([pb_ts] if lt == 0 else []), incs=["vs"])
        p.step("vector",
               lambda v: v.tensor_tensor(ytm[:, :], ytm[:, :], c1_ap, ALU.mult),
               incs=["vs"])
        p.step("vector",
               lambda v, oih=oih, inv_ap=inv_ap: v.tensor_tensor(
                   oih, oih, inv_ap, ALU.mult),
               incs=["vs"])
        p.step("vector",
               lambda v, oih=oih: v.tensor_tensor(oih, oih, c2_ap, ALU.mult),
               incs=["vs"])
        p.step("vector",
               lambda v, oih=oih: v.tensor_add(ytm[:, :], ytm[:, :], oih),
               incs=["vs"])
        p.step("scalar",
               lambda s, orh=orh: s.activation(orh, ytm[:, :], AF.Silu),
               waits=["vs"], incs=["ss"])
        s_y = p.at("ss")
        for ci in range(8):
            tsl = ci % 2
            p.step("tensor",
                   lambda t, tsl=tsl, orh=orh, ci=ci: t.transpose(
                       trpL[tsl][:, :], orh[:, ci * 128:(ci + 1) * 128],
                       co["identf"][:, :]),
                   waits=[("ss", s_y), ("vs", tr_free[tsl]), co["ready2"]],
                   incs=["ts"])
            t_tr = p.at("ts")
            p.step("vector",
                   lambda v, sl=sl, ci=ci, tsl=tsl: v.tensor_copy(
                       yfm[:, sl * 1024 + ci * 128:sl * 1024 + (ci + 1) * 128],
                       trpL[tsl][:, :]),
                   waits=[("ts", t_tr)], incs=["vs"])
            tr_free[tsl] = p.at("vs")
        os_ts[sl] = p.at("ts")
        yv = p.at("vs")
        if _os.environ.get("PROBE_L0A_ASM") == "ytr":
            os_vs[sl] = p.at("vs")
            continue
        for ob in range(2):
            dslot = psa[:, (sl * 2 + ob) * 512:(sl * 2 + ob + 1) * 512]
            for ci in range(8):
                w = [wov, ("vs", yv), ("vs", dfree[sl * 2 + ob])] \
                    if ci == 0 else []
                p.step("tensor",
                       lambda t, sl=sl, ob=ob, ci=ci, dslot=dslot: t.matmul(
                           dslot,
                           yfm[:, sl * 1024 + ci * 128:sl * 1024 + (ci + 1) * 128],
                           wout[:, ci * 1024 + ob * 512:ci * 1024 + (ob + 1) * 512],
                           start=(ci == 0), stop=(ci == 7)),
                       waits=w, incs=(["ts"] if ci == 7 else []))
            t_d = p.at("ts")
            p.step("vector",
                   lambda v, sl=sl, ob=ob, dslot=dslot: v.tensor_add(
                       xo[:, sl * 1024 + ob * 512:sl * 1024 + (ob + 1) * 512],
                       dslot,
                       lns2[:, sl * 1024 + ob * 512:sl * 1024 + (ob + 1) * 512]),
                   waits=[("ts", t_d), xl, ("ts", xo_used[sl])], incs=["vs"])
            dfree[sl * 2 + ob] = p.at("vs")
        x0_free[sl] = p.at("vs")
        os_vs[sl] = p.at("vs")
        if _os.environ.get("PROBE_L0A_ASM") == "wout":
            continue
        v_x1 = p.at("vs")
        stsem = "st" + str(sl)
        for ci in range(8):
            tsl = ci % 2
            p.step("tensor",
                   lambda t, sl=sl, ci=ci, tsl=tsl: t.transpose(
                       trpL[tsl][:, :],
                       xo[:, sl * 1024 + ci * 128:sl * 1024 + (ci + 1) * 128],
                       co["identf"][:, :]),
                   waits=[("vs", v_x1), ("vs", tr_free[tsl]), co["ready2"]],
                   incs=["ts"])
            t_tr = p.at("ts")
            w = [("ts", t_tr)]
            if ci == 0:
                w.append((stsem, p.at(stsem)))
            p.step("vector",
                   lambda v, sl=sl, ci=ci, tsl=tsl: v.tensor_copy(
                       xfs[:, sl * 1024 + ci * 128:sl * 1024 + (ci + 1) * 128],
                       trpL[tsl][:, :]),
                   waits=w, incs=["vs"])
            tr_free[tsl] = p.at("vs")
        xo_used[sl] = p.at("ts")
        if _os.environ.get("PROBE_L0A_ASM") != "xtr":
            for ci in range(8):
                p.dma("sync",
                      bass.AP(x_out, ci * 128 * L + lt * 128,
                              [[L, 128], [1, 128]]),
                      xfs[:, sl * 1024 + ci * 128:sl * 1024 + (ci + 1) * 128],
                      waits=["vs"], sem=stsem)
    if _os.environ.get("PROBE_L0A_ASM") in ("ytr", "wout", "xtr"):
        p._l0a_last = [("vs", p.at("vs")), ("ts", p.at("ts"))]
    else:
        p._l0a_last = [("st0", p.at("st0")), ("st1", p.at("st1"))]


L0A_INPUTS = [
    ("l0a_x0h", [128, 1024]), ("l0a_wmem", [1024, 1152]),
    ("l0a_wq", [1024, 2048]), ("l0a_ckc", [128, 36]),
    ("l0a_throw", [1, 4096]), ("l0a_slopes", [1, 32]),
    ("l0a_score", [1, 32]), ("l0a_ttab", [2048, 2]),
    ("l0a_c1", [1, 32]), ("l0a_c2", [1, 32]),
    ("l0a_wout", [1024, 1024]), ("l0a_tri", [128, 128]),
    ("l0a_w4", [4, 1]), ("l0a_flag", [128, 1]),
]


def build_l0_attn_standalone():
    nc = bass.Bass("TRN2", target_bir_lowering=False)
    with ExitStack() as es:
        p = Prog(nc, es)
        x_in = nc.dram_tensor("x_in", [L, 1024], F32, kind="ExternalInput")
        x_out = nc.dram_tensor("x_out", [1024, L], F32, kind="ExternalOutput")
        ident = nc.dram_tensor("c_ident", [128, 128], F32, kind="ExternalInput")
        dr = {nm: nc.dram_tensor(nm, sh, F32, kind="ExternalInput")
              for nm, sh in L0A_INPUTS}
        psa = p.psum("psa", [128, 4096])
        co = setup_consts(p, ident)
        emit_l0_attn(p, co, dr, psa, x_in, x_out)
        p.step("sync", lambda sy: None, waits=p._l0a_last, chain=False)
        p.build()
    return nc


def run_l0_attn_standalone(x0_full, ins):
    nc = build_l0_attn_standalone()
    id_ = np.eye(128, dtype=np.float32)
    in_maps = []
    for ci in range(NCORE):
        b, cch = divmod(ci, 4)
        w = prep_l0_attn(ins, ci)
        m = {k: w[k] for k in w}
        m["c_ident"] = id_
        m["x_in"] = np.ascontiguousarray(x0_full[b, cch * L:(cch + 1) * L, :])
        if cch == 0:
            m["l0a_x0h"] = np.zeros((128, 1024), np.float32)
        else:
            m["l0a_x0h"] = np.ascontiguousarray(
                x0_full[b, cch * L - 128:cch * L, :])
        in_maps.append(m)
    res = run_bass_kernel_spmd(nc, in_maps, core_ids=list(range(NCORE)))
    out = np.empty((NB, SEQ, D), np.float32)
    for ci in range(NCORE):
        b, cch = divmod(ci, 4)
        out[b, cch * L:(cch + 1) * L, :] = np.asarray(
            res.results[ci]["x_out"], np.float32).T
    return out



def _barrier(p, marks):
    for eng in ENGINES:
        p.step(eng, lambda e: None, waits=marks, chain=False)


def build_fused():
    es = ExitStack()
    nc = bass.Bass("TRN2", target_bir_lowering=False)
    x_in = nc.dram_tensor("x_in", [L, 1024], F32, kind="ExternalInput")
    ident = nc.dram_tensor("c_ident", [128, 128], F32, kind="ExternalInput")
    x_out = nc.dram_tensor("x_out", [D, L], F32, kind="ExternalOutput")
    dr = {nm: nc.dram_tensor(nm, sh, F32, kind="ExternalInput")
          for nm, sh in L0A_INPUTS}
    for nm, sh, dt in [
            ("l0f_tabs", [4, 2560, 64], F32), ("l0f_masks", [12, 128, 512], F32),
            ("l0f_flag", [128, 1], F32), ("l0f_proff", [1, 2], mybir.dt.int32),
            ("l0f_wqkv", [2, 1024, 1536], F32), ("l0f_wo", [2, 512, 1024], F32),
            ("l0f_qs", [2, 64], F32), ("l0f_ks", [2, 64], F32),
            ("l0f_lng", [2, 1024], F32), ("l0f_lnb", [2, 1024], F32),
            ("l0f_b2", [2, 1024], F32), ("l0f_w1i", [2, 1024, 5632], F32),
            ("l0f_b1i", [2, 5632], F32), ("l0f_w2p", [2, 2816, 1024], F32),
            ("l1a_wqT", [1024, 1024], F32), ("l1a_wkT", [1024, 1024], F32),
            ("l1a_wvT", [1024, 1024], F32), ("l1a_woT", [1024, 1024], F32),
            ("l1a_bout", [1024], F32), ("l1a_e2", [1024, 16], F32),
            ("l1f_bd8", [4, 8, 128, 128], F32), ("l1f_bd2", [2, 64, 32, 128], F32),
            ("l1f_wcT", [16384, 1024], F32)]:
        dr[nm] = nc.dram_tensor(nm, sh, dt, kind="ExternalInput")
    xd1 = nc.dram_tensor("xd1", [D, L], F32)
    xd2 = nc.dram_tensor("xd2", [D, L], F32)
    xd3 = nc.dram_tensor("xd3", [D, L], F32)
    with es:
        p = Prog(nc, es)
        psa = p.psum("psa", [128, 4096])
        co = setup_consts(p)
        lsc = make_ln_scratch(p)
        with p.scope():
            co["identf"] = p.sbuf("c_identf", [128, 128])
            p.dma("gpsimd", co["identf"][:, :], ident.ap())
            co["ready2"] = ("dgp", p.at("dgp"))
            with p.scope():
                emit_l0_attn(p, co, dr, psa, x_in, xd1)
            _barrier(p, p._l0a_last)
            with p.scope():
                m1, m2 = emit_l0_ffn(p, co, dr, psa, xd1, xd2)
        _barrier(p, [m1, m2])
        with p.scope():
            xs = p.sbuf("xs_a", [128, 8 * L])
            xc = p.sbuf("xc_a", [128, 8 * L], BF16)
            ld = load_stream_fm(p, xs, xd2)
            lnr = emit_ln_fm(p, co, xs, xc, "ln2", [ld], psa, lsc)
            fr = emit_l1_attn(p, co, xs, xc, dr, [lnr], psa)
            st = store_stream_fm(p, xd3, xs, [fr])
        _barrier(p, [st])
        with p.scope():
            xs2 = p.sbuf("xs_f", [128, 8 * L])
            xc2 = p.sbuf("xc_f", [128, 8 * L], BF16)
            ld = load_stream_fm(p, xs2, xd3)
            lnr = emit_ln_fm(p, co, xs2, xc2, "ln3", [ld], psa, lsc)
            fr = emit_l1_ffn(p, co, xs2, xc2, dr, [lnr], psa)
            st2 = store_stream_fm(p, x_out, xs2, [fr])
        p.step("sync", lambda sy: None, waits=[st2], chain=False)
        p.build()
    return nc


def kernel_fused(**inputs):
    ins = {k: np.asarray(v) for k, v in inputs.items()}
    x0 = np.asarray(ins["L0_attn_x"], np.float32)
    id_ = np.eye(128, dtype=np.float32)
    cores = list(range(NCORE))
    nc = build_fused()
    wa = prep_l1_attn(ins)
    wf = prep_l1_ffn(ins)
    in_maps = []
    for ci in cores:
        b, cch = divmod(ci, 4)
        m = dict(prep_l0_attn(ins, ci))
        m.update(prep_l0_ffn(ins, ci))
        m.update(wa)
        m.update(wf)
        m["c_ident"] = id_
        m["x_in"] = np.ascontiguousarray(x0[b, cch * L:(cch + 1) * L, :])
        m["l0a_x0h"] = (np.zeros((128, 1024), np.float32) if cch == 0 else
                        np.ascontiguousarray(x0[b, cch * L - 128:cch * L, :]))
        in_maps.append(m)
    res = run_bass_kernel_spmd(nc, in_maps, core_ids=cores)
    out = np.empty((NB, SEQ, D), np.float32)
    for ci in cores:
        b, cch = divmod(ci, 4)
        out[b, cch * L:(cch + 1) * L, :] = np.asarray(
            res.results[ci]["x_out"], np.float32).T
    return out


def kernel(**inputs):
    return kernel_fused(**inputs)


# version 04
# speedup vs baseline: 1.1581x; 1.0130x over previous
import os as _os
import numpy as np
from contextlib import ExitStack, contextmanager

import concourse.bass as bass
import concourse.mybir as mybir
from concourse.bass_utils import run_bass_kernel_spmd

F32 = mybir.dt.float32
BF16 = mybir.dt.bfloat16
AF = mybir.ActivationFunctionType
ALU = mybir.AluOpType
AX = mybir.AxisListType

NCORE = 8
NB = 2
SEQ = 8192
D = 1024
L = SEQ // 4
EPS = 1e-5

ENGINES = ("tensor", "vector", "scalar", "gpsimd", "sync")
DSEM = {"sync": "dsy", "scalar": "dsc", "gpsimd": "dgp"}
ALL_SEMS = ("ts", "vs", "ss", "gs", "sy", "dsy", "dsc", "dgp", "cc")


class Prog:
    def __init__(self, nc, es):
        self.nc = nc
        self.es = es
        self.sem = {}
        self.cnt = {}
        for name in ALL_SEMS:
            self.sem[name] = es.enter_context(nc.semaphore(name))
            self.cnt[name] = 0
        self.streams = {e: [] for e in ENGINES}

    def get_sem(self, name):
        if name not in self.sem:
            self.sem[name] = self.es.enter_context(self.nc.semaphore(name))
            self.cnt[name] = 0
        return self.sem[name]

    def at(self, name):
        if name not in self.cnt:
            self.get_sem(name)
        return self.cnt[name]

    CHAINED = {"vector": "vs", "scalar": "ss", "gpsimd": "gs"}

    def step(self, eng, fn, waits=(), incs=(), chain=True):
        wl = []
        cs = self.CHAINED.get(eng) if chain else None
        if cs is not None:
            wl.append((cs, self.cnt[cs]))
            incs = list(incs) + [cs]
        for w in waits:
            if isinstance(w, str):
                wl.append((w, self.at(w)))
            else:
                assert w[1] <= self.at(w[0]), \
                    f"forward wait {w[0]}>={w[1]} planned={self.at(w[0])}"
                wl.append((w[0], w[1]))
        merged = {}
        for inc in incs:
            if isinstance(inc, str):
                inc = (inc, 1)
            merged[inc[0]] = merged.get(inc[0], 0) + inc[1]
        assert len(merged) <= 1, f"one sem update per instruction: {merged}"
        il = []
        for name, k in merged.items():
            self.cnt[name] += k
            il.append((self.sem[name], k))
        sems = self.sem

        def run(e, fn=fn, wl=wl, il=il):
            for name, val in wl:
                if val > 0:
                    e.wait_ge(sems[name], val)
            ins = fn(e)
            if il:
                if isinstance(ins, (list, tuple)):
                    ins = ins[-1]
                for h, k in il:
                    ins.then_inc(h, k)
        self.streams[eng].append(run)

    def dma(self, queue, out, in_, waits=(), extra_incs=(), sem=None, **kw):
        dname = sem if sem is not None else DSEM[queue]
        if sem is not None:
            self.get_sem(sem)
        self.step(queue, lambda e: e.dma_start(out=out, in_=in_, **kw),
                  waits=waits, incs=[(dname, 16)] + list(extra_incs),
                  chain=False)
        return (dname, self.cnt[dname])

    def build(self):
        with self.nc.Block() as blk:
            for ename in ENGINES:
                fns = self.streams[ename]
                if not fns:
                    continue
                getattr(blk, ename)(lambda e, fns=fns: [f(e) for f in fns])
        self.streams = {e: [] for e in ENGINES}

    def sbuf(self, name, shape, dtype=F32):
        return self.es.enter_context(self.nc.sbuf_tensor(name, shape, dtype))

    def psum(self, name, shape, dtype=F32):
        return self.es.enter_context(self.nc.psum_tensor(name, shape, dtype))

    @contextmanager
    def scope(self):
        old = self.es
        with ExitStack() as es2:
            self.es = es2
            try:
                yield
            finally:
                self.es = old



def setup_consts(p, ident_dram=None):
    c = {}
    c["ones1b"] = p.sbuf("c_ones1b", [1, 128], BF16)
    c["onescb"] = p.sbuf("c_onescb", [128, 1], BF16)
    c["ones1f"] = p.sbuf("c_ones1f", [1, 128], F32)
    c["onescf"] = p.sbuf("c_onescf", [128, 1], F32)
    p.step("gpsimd", lambda g: g.memset(c["ones1b"][:, :], 1.0))
    p.step("gpsimd", lambda g: g.memset(c["onescb"][:, :], 1.0))
    p.step("gpsimd", lambda g: g.memset(c["ones1f"][:, :], 1.0))
    p.step("gpsimd", lambda g: g.memset(c["onescf"][:, :], 1.0), incs=["gs"])
    if ident_dram is not None:
        c["identf"] = p.sbuf("c_identf", [128, 128], F32)
        p.dma("gpsimd", c["identf"][:, :], ident_dram.ap())
    c["ready"] = ("gs", p.at("gs"))
    if ident_dram is not None:
        c["ready2"] = ("dgp", p.at("dgp"))
    return c


def load_stream_fm(p, xs, x_dram):
    for cch in range(8):
        p.dma("sync", xs[:, cch * L:(cch + 1) * L],
              bass.AP(x_dram, cch * 128 * L, [[L, 128], [1, L]]))
    return ("dsy", p.at("dsy"))


def store_stream_fm(p, x_dram, xs, waits):
    for cch in range(8):
        p.dma("sync", bass.AP(x_dram, cch * 128 * L, [[L, 128], [1, L]]),
              xs[:, cch * L:(cch + 1) * L], waits=waits)
    return ("dsy", p.at("dsy"))


def make_ln_scratch(p):
    return {"sq": p.sbuf("ln_sq", [128, 1024]), "tmp": p.sbuf("ln_tmp", [128, 512]),
            "row": p.sbuf("ln_row", [1, 512]), "rowt": p.sbuf("ln_rowt", [1, 512])}


def emit_ln_fm(p, co, xs, xc, tag, xs_ready, psa, lsc):
    sq = lsc["sq"]
    tmp = lsc["tmp"]
    s1 = psa[0:1, 0:512]
    s2 = psa[0:1, 512:1024]
    mub = psa[:, 1024:1536]
    rsb = psa[:, 1536:2048]
    row = lsc["row"]
    rowt = lsc["rowt"]
    NQ = L // 512
    slot_done = [0, 0]
    for lq in range(NQ):
        o = lq * 512
        for c in range(8):
            sl = (c % 2) * 512
            p.step("scalar",
                   lambda s, sl=sl, c=c, o=o: s.activation(
                       sq[:, sl:sl + 512], xs[:, c * L + o: c * L + o + 512],
                       AF.Square),
                   waits=list(xs_ready) + [("ts", slot_done[c % 2])],
                   incs=["ss"])
            p.step("tensor",
                   lambda t, c=c, o=o: t.matmul(
                       s1[:, :], co["onescf"][:, :],
                       xs[:, c * L + o: c * L + o + 512],
                       start=(c == 0), stop=(c == 7)),
                   waits=list(xs_ready) + [co["ready"]] +
                         ([("vs", p.at("vs"))] if c == 0 else []))
            p.step("tensor",
                   lambda t, sl=sl, c=c: t.matmul(
                       s2[:, :], co["onescf"][:, :], sq[:, sl:sl + 512],
                       start=(c == 0), stop=(c == 7)),
                   waits=["ss"], incs=["ts"])
            slot_done[c % 2] = p.at("ts")
        t_stats = p.at("ts")
        p.step("vector", lambda v: v.tensor_scalar_mul(row[:, :], s1[0:1, :],
                                                       1.0 / D),
               waits=[("ts", t_stats)], incs=["vs"])
        p.step("vector", lambda v: v.tensor_mul(rowt[:, :], row[:, :], row[:, :]),
               waits=["vs"], incs=["vs"])
        p.step("vector", lambda v: v.scalar_tensor_tensor(
                   rowt[:, :], s2[0:1, :], 1.0 / D, rowt[:, :],
                   ALU.mult, ALU.subtract),
               waits=["vs"], incs=["vs"])
        p.step("vector", lambda v: v.tensor_scalar_add(rowt[:, :], rowt[:, :],
                                                       EPS),
               waits=["vs"], incs=["vs"])
        v_var = p.at("vs")
        p.step("scalar", lambda s: s.activation(rowt[:, :], rowt[:, :], AF.Sqrt),
               waits=[("vs", v_var)], incs=["ss"])
        p.step("vector", lambda v: v.reciprocal(rowt[:, :], rowt[:, :]),
               waits=["ss"], incs=["vs"])
        v_rs = p.at("vs")
        p.step("tensor", lambda t: t.matmul(mub[:, :], co["ones1f"][:, :],
                                            row[0:1, :], start=True, stop=True),
               waits=[("vs", v_var)])
        p.step("tensor", lambda t: t.matmul(rsb[:, :], co["ones1f"][:, :],
                                            rowt[0:1, :], start=True, stop=True),
               waits=[("vs", v_rs)], incs=["ts"])
        t_b = p.at("ts")
        for c in range(8):
            p.step("vector",
                   lambda v, c=c, o=o: v.tensor_sub(
                       tmp[:, :], xs[:, c * L + o: c * L + o + 512], mub[:, :]),
                   waits=[("ts", t_b)] + ([("vs", p.at("vs"))] if c == 0 else []),
                   incs=[])
            p.step("vector",
                   lambda v, c=c, o=o: v.tensor_mul(
                       xc[:, c * L + o: c * L + o + 512], tmp[:, :], rsb[:, :]),
                   incs=(["vs"] if c == 7 else []))
    return ("vs", p.at("vs"))




def prep_l1_ffn(ins):
    rm = ins["L1_ffn_rail_maker"]
    rup = ins["L1_ffn_rail_up_proj"]
    bd8 = np.zeros((4, 8, 128, 128), np.float32)
    for h in range(4):
        for g in range(8):
            for r in range(8):
                bd8[h, g, 16 * r:16 * r + 16, 16 * r:16 * r + 16] = rm[h, 8 * g + r]
    bd2 = np.zeros((2, 4 * 16, 32, 128), np.float32)
    for h in range(4):
        for pp in range(32):
            t = np.zeros((32, 128), np.float32)
            for rr in range(2):
                t[16 * rr:16 * rr + 16, 64 * rr:64 * rr + 64] = rup[h, 2 * pp + rr].T
            bd2[pp % 2, h * 16 + pp // 2] = t
    return {"l1f_bd8": np.ascontiguousarray(bd8),
            "l1f_bd2": np.ascontiguousarray(bd2),
            "l1f_wcT": np.ascontiguousarray(ins["L1_ffn_w_cproj"].T)}


def emit_l1_ffn(p, co, xs, xc, dr, xc_ready, psa):
    bd8 = p.sbuf("l1f_bd8s", [128, 32 * 128], BF16)
    bd2 = p.sbuf("l1f_bd2s", [64, 64 * 128], BF16)
    ysa = p.sbuf("l1f_ysa", [64, 8 * 1024], BF16)
    ysbb = p.sbuf("l1f_ysb", [64, 8 * 1024], BF16)
    hsb = p.sbuf("l1f_hsb", [128, 16 * 1024], BF16)
    wct = p.sbuf("l1f_wct", [128, 2 * 512], BF16)
    p.dma("gpsimd", bd8[:, :],
          bass.AP(dr["l1f_bd8"], 0, [[128, 128], [128 * 128, 32], [1, 128]]))
    p.dma("gpsimd", bass.AP(bd2, 0, [[64 * 128, 32], [128, 64], [1, 128]]),
          bass.AP(dr["l1f_bd2"], 0, [[128, 32], [32 * 128, 64], [1, 128]]))
    p.dma("gpsimd",
          bass.AP(bd2, 32 * 64 * 128, [[64 * 128, 32], [128, 64], [1, 128]]),
          bass.AP(dr["l1f_bd2"], 64 * 32 * 128,
                  [[128, 32], [32 * 128, 64], [1, 128]]))
    w_bd8 = w_bd2a = w_bd2b = ("dgp", p.at("dgp"))

    yslots = [psa[:, 0:1024], psa[:, 1024:2048]]
    zslots = [psa[:, 2048:3072], psa[:, 3072:4096]]
    wct_free = [0, 0]
    y_done = [0, 0]
    y_done_s = [0, 0]
    z_done = [0, 0]
    c_flush = 0
    for half in range(2):
        lo = half * 1024
        for h in range(4):
            for g in range(8):
                yp = yslots[g % 2]
                p.step("tensor",
                       lambda t, h=h, g=g, lo=lo, yp=yp: [
                           t.matmul(yp[:, q * 512:(q + 1) * 512],
                                    bd8[:, (h * 8 + g) * 128:(h * 8 + g + 1) * 128],
                                    xc[:, g * L + lo + q * 512:
                                       g * L + lo + (q + 1) * 512],
                                    start=True, stop=True)
                           for q in range(2)],
                       waits=[w_bd8] + list(xc_ready) +
                             [("vs", max(y_done[g % 2], c_flush)),
                              ("ss", y_done_s[g % 2])],
                       incs=["ts"])
                t_y = p.at("ts")
                p.step("scalar",
                       lambda s, g=g, yp=yp: s.copy(
                           ysa[:, g * 1024:(g + 1) * 1024], yp[0:64, :]),
                       waits=[("ts", t_y)], incs=["ss"])
                y_done_s[g % 2] = p.at("ss")
                p.step("vector",
                       lambda v, g=g, yp=yp: v.tensor_copy(
                           ysbb[:, g * 1024:(g + 1) * 1024], yp[64:128, :]),
                       waits=[("ts", t_y)], incs=["vs"])
                y_done[g % 2] = p.at("vs")
            v_yall = p.at("vs")
            s_yall = p.at("ss")
            for sub in range(2):
                for j in range(16):
                    pp = 16 * sub + j
                    zp = zslots[j % 2]
                    g = pp // 4
                    s4 = pp % 4
                    ysrc = ysa if s4 < 2 else ysbb
                    rb = 32 * (s4 % 2)
                    p.step("tensor",
                           lambda t, h=h, pp=pp, g=g, rb=rb, zp=zp, ysrc=ysrc: [
                               t.matmul(zp[:, q * 512:(q + 1) * 512],
                                        bd2[32 * (pp % 2):32 * (pp % 2) + 32,
                                            (h * 16 + pp // 2) * 128:
                                            (h * 16 + pp // 2 + 1) * 128],
                                        ysrc[rb:rb + 32,
                                             g * 1024 + q * 512:
                                             g * 1024 + (q + 1) * 512],
                                        start=True, stop=True)
                               for q in range(2)],
                           waits=[w_bd2a, w_bd2b, ("vs", max(v_yall, c_flush)),
                                  ("ss", max(s_yall, z_done[j % 2]))],
                           incs=["ts"])
                    t_z = p.at("ts")
                    p.step("scalar",
                           lambda s, j=j, zp=zp: s.activation(
                               hsb[:, j * 1024:(j + 1) * 1024], zp[:, :], AF.Gelu),
                           waits=[("ts", t_z)], incs=["ss"])
                    z_done[j % 2] = p.at("ss")
                s_hall = p.at("ss")
                for mg in range(2):
                    for kc in range(16):
                        slot = kc % 2
                        kglob = h * 32 + sub * 16 + kc
                        wld = p.dma("gpsimd",
                                    wct[:, slot * 512:(slot + 1) * 512],
                                    bass.AP(dr["l1f_wcT"],
                                            kglob * 128 * 1024 + mg * 512,
                                            [[1024, 128], [1, 512]]),
                                    waits=[("ts", wct_free[slot])],
                                    sem="w%d" % slot)
                        for lq in range(2):
                            p.step("tensor",
                                   lambda t, kc=kc, lq=lq, slot=slot: [
                                       t.matmul(psa[:, (mi * 2 + lq) * 512:
                                                    (mi * 2 + lq) * 512 + 512],
                                                wct[:, slot * 512 + mi * 128:
                                                    slot * 512 + (mi + 1) * 128],
                                                hsb[:, kc * 1024 + lq * 512:
                                                    kc * 1024 + lq * 512 + 512],
                                                start=(kc == 0), stop=(kc == 15))
                                       for mi in range(4)],
                                   waits=[wld, ("ss", s_hall)] +
                                         ([("vs", p.at("vs"))] if kc == 0 else []),
                                   incs=(["ts"] if lq == 1 else []))
                        wct_free[slot] = p.at("ts")
                    t_c = p.at("ts")
                    for mi in range(4):
                        for lq in range(2):
                            m = mg * 4 + mi
                            p.step("vector",
                                   lambda v, m=m, mi=mi, lq=lq, lo=lo: v.tensor_add(
                                       xs[:, m * L + lo + lq * 512:
                                          m * L + lo + lq * 512 + 512],
                                       xs[:, m * L + lo + lq * 512:
                                          m * L + lo + lq * 512 + 512],
                                       psa[:, (mi * 2 + lq) * 512:
                                           (mi * 2 + lq) * 512 + 512]),
                                   waits=[("ts", t_c)],
                                   incs=(["vs"] if (mi == 3 and lq == 1) else []))
                    c_flush = p.at("vs")
    return ("vs", p.at("vs"))



def build_l1_ffn_standalone():
    es = ExitStack()
    nc = bass.Bass("TRN2", target_bir_lowering=False)
    x_in = nc.dram_tensor("x_in", [D, L], F32, kind="ExternalInput")
    bd8 = nc.dram_tensor("l1f_bd8", [4, 8, 128, 128], F32, kind="ExternalInput")
    bd2 = nc.dram_tensor("l1f_bd2", [2, 64, 32, 128], F32, kind="ExternalInput")
    wcT = nc.dram_tensor("l1f_wcT", [16384, 1024], F32, kind="ExternalInput")
    x_out = nc.dram_tensor("x_out", [D, L], F32, kind="ExternalOutput")
    dr = {"l1f_bd8": bd8, "l1f_bd2": bd2, "l1f_wcT": wcT}
    with es:
        p = Prog(nc, es)
        co = setup_consts(p)
        xs = p.sbuf("xs", [128, 8 * L])
        xc = p.sbuf("xc", [128, 8 * L], BF16)
        psa = p.psum("psa", [128, 4096])
        lsc = make_ln_scratch(p)
        ld = load_stream_fm(p, xs, x_in)
        lnr = emit_ln_fm(p, co, xs, xc, "ln3", [ld], psa, lsc)
        fr = emit_l1_ffn(p, co, xs, xc, dr, [lnr], psa)
        st = store_stream_fm(p, x_out, xs, [fr])
        p.step("sync", lambda sy: None, waits=[st], chain=False)
        p.build()
    return nc


def run_l1_ffn_standalone(x3_full, ins):
    w = prep_l1_ffn(ins)
    nc = build_l1_ffn_standalone()
    in_maps = []
    for ci in range(NCORE):
        b, cch = divmod(ci, 4)
        xfm = np.ascontiguousarray(x3_full[b, cch * L:(cch + 1) * L, :].T)
        in_maps.append({"x_in": xfm, **w})
    res = run_bass_kernel_spmd(nc, in_maps, core_ids=list(range(NCORE)))
    out = np.empty((2, SEQ, D), np.float32)
    for ci in range(NCORE):
        b, cch = divmod(ci, 4)
        out[b, cch * L:(cch + 1) * L, :] = np.asarray(res.results[ci]["x_out"]).T
    return out


def kernel_multi(**inputs):
    ins = {k: np.asarray(v) for k, v in inputs.items()}
    x0 = np.asarray(ins["L0_attn_x"], np.float32)
    id_ = np.eye(128, dtype=np.float32)
    cores = list(range(NCORE))

    nc = build_l0_attn_standalone()
    in_maps = []
    for ci in cores:
        b, cch = divmod(ci, 4)
        m = dict(prep_l0_attn(ins, ci))
        m["c_ident"] = id_
        m["x_in"] = np.ascontiguousarray(x0[b, cch * L:(cch + 1) * L, :])
        m["l0a_x0h"] = (np.zeros((128, 1024), np.float32) if cch == 0 else
                        np.ascontiguousarray(x0[b, cch * L - 128:cch * L, :]))
        in_maps.append(m)
    res = run_bass_kernel_spmd(nc, in_maps, core_ids=cores)
    xs_ = [np.ascontiguousarray(np.asarray(res.results[ci]["x_out"],
                                           np.float32)) for ci in cores]

    nc = build_l0_ffn_standalone()
    in_maps = []
    for ci in cores:
        m = dict(prep_l0_ffn(ins, ci))
        m["c_ident"] = id_
        m["x_in"] = xs_[ci]
        in_maps.append(m)
    res = run_bass_kernel_spmd(nc, in_maps, core_ids=cores)
    xs_ = [np.ascontiguousarray(np.asarray(res.results[ci]["x_out"],
                                           np.float32)) for ci in cores]

    w = prep_l1_attn(ins)
    nc = build_l1_attn_standalone()
    in_maps = [{"x_in": xs_[ci], **w} for ci in cores]
    res = run_bass_kernel_spmd(nc, in_maps, core_ids=cores)
    xs_ = [np.ascontiguousarray(np.asarray(res.results[ci]["x_out"],
                                           np.float32)) for ci in cores]

    w = prep_l1_ffn(ins)
    nc = build_l1_ffn_standalone()
    in_maps = [{"x_in": xs_[ci], **w} for ci in cores]
    res = run_bass_kernel_spmd(nc, in_maps, core_ids=cores)

    out = np.empty((NB, SEQ, D), np.float32)
    for ci in cores:
        b, cch = divmod(ci, 4)
        out[b, cch * L:(cch + 1) * L, :] = np.asarray(
            res.results[ci]["x_out"], np.float32).T
    return out


AGR = 2049


def prep_l1_attn(ins):
    e2 = np.zeros((8 * 128, 16), np.float32)
    for c in range(8):
        for hh in range(2):
            h = 2 * c + hh
            e2[c * 128 + hh * 64: c * 128 + hh * 64 + 64, h] = 1.0
    return {"l1a_wqT": np.ascontiguousarray(ins["L1_attn_Wq"].T),
            "l1a_wkT": np.ascontiguousarray(ins["L1_attn_Wk"].T),
            "l1a_wvT": np.ascontiguousarray(ins["L1_attn_Wv"].T),
            "l1a_woT": np.ascontiguousarray(ins["L1_attn_Wout"].T),
            "l1a_bout": np.ascontiguousarray(ins["L1_attn_bout"]),
            "l1a_e2": e2}


def emit_l1_attn(p, co, xs, xc, dr, xc_ready, psa):
    nc = p.nc
    agin = nc.dram_tensor("l1a_agin", [AGR, 2048], BF16)
    agout = nc.dram_tensor("l1a_agout", [4 * AGR, 2048], BF16)
    o_d = nc.dram_tensor("l1a_od", [1024, 2048], BF16)

    wbuf = p.sbuf("l1a_wbuf", [128, 8 * 1024], BF16)
    stg = p.sbuf("l1a_stg", [128, 2 * 1024], BF16)
    q_fm = p.sbuf("l1a_qfm", [128, 8 * 2048], BF16)
    sqf = p.sbuf("l1a_sqf", [128, 2048])
    sqs = sqf[:, 0:1024]
    e2s = p.sbuf("l1a_e2s", [128, 8 * 16])
    qn = p.sbuf("l1a_qn", [16, 2048], BF16)
    km1 = p.sbuf("l1a_km1", [16, 1])
    kmb = p.sbuf("l1a_kmb", [16, 128], BF16)
    kmg = p.sbuf("l1a_kmg", [16, 4])
    bline = p.sbuf("l1a_bline", [16, 1])
    nbrow = sqf[0:1, 0:16]
    negb = p.sbuf("l1a_negb", [128, 16])
    kh2 = p.sbuf("l1a_kh2", [128, 8192], BF16)
    v65 = p.sbuf("l1a_v65", [128, 64 * 65], BF16)
    sh1 = p.sbuf("l1a_sh1", [128, 2048], BF16)
    pt = sh1[:, 0:2048]
    dsg = sqf[0:1, 0:2048]
    dst = p.sbuf("l1a_dst", [16, 2048])
    recrow = sqf[0:1, 0:2048]
    och = sh1[:, 0:2048]
    boutc = e2s[:, 0:8]

    p.dma("gpsimd", e2s[:, :],
          bass.AP(dr["l1a_e2"], 0, [[16, 128], [128 * 16, 8], [1, 16]]))
    w_e2 = ("dgp", p.at("dgp"))

    pslot = [psa[:, 0:1024], psa[:, 1024:2048]]
    sqp = psa[0:16, 2048:4096]
    p._stg_free = {}

    def proj_qk(which):
        wT = "l1a_wqT" if which == "q" else "l1a_wkT"
        ev_done = [p.at("ts"), p.at("ts")]
        sq_done = [0, 0]
        for hd in range(8):
            sw = hd % 2
            wld = p.dma("gpsimd", wbuf[:, sw * 1024:(sw + 1) * 1024],
                        bass.AP(dr[wT], hd * 128,
                                [[1024, 128], [1024 * 128, 8], [1, 128]]),
                        waits=[("ts", ev_done[sw])], sem="w%d" % sw)
            for lh in range(2):
                sp = lh
                p.step("tensor",
                       lambda t, hd=hd, lh=lh, sw=sw, sp=sp: [
                           t.matmul(pslot[sp][:, q * 512:(q + 1) * 512],
                                    wbuf[:, sw * 1024 + cc * 128:
                                         sw * 1024 + (cc + 1) * 128],
                                    xc[:, cc * L + lh * 1024 + q * 512:
                                       cc * L + lh * 1024 + (q + 1) * 512],
                                    start=(cc == 0), stop=(cc == 7))
                           for q in range(2) for cc in range(8)],
                       waits=[wld] + list(xc_ready) +
                             [("vs", p.at("vs")), ("ss", p.at("ss"))],
                       incs=["ts"])
                t_mm = p.at("ts")
                if which == "q":
                    p.step("vector",
                           lambda v, hd=hd, lh=lh, sp=sp: v.tensor_copy(
                               q_fm[:, hd * 2048 + lh * 1024:
                                    hd * 2048 + (lh + 1) * 1024],
                               pslot[sp][:, :]),
                           waits=[("ts", t_mm)])
                else:
                    p.step("vector",
                           lambda v, sp=sp: v.tensor_copy(
                               stg[:, sp * 1024:(sp + 1) * 1024],
                               pslot[sp][:, :]),
                           waits=[("ts", t_mm), ("ss", sq_done[sp]),
                                  ("sa%d" % sp, p._stg_free.get(sp, 0))])
                    p.dma("sync",
                          bass.AP(agin, hd * 128 * 2048 + lh * 1024,
                                  [[2048, 128], [1, 1024]]),
                          stg[:, sp * 1024:(sp + 1) * 1024],
                          waits=["vs"], sem="sa%d" % sp)
                    p._stg_free[sp] = p.at("sa%d" % sp)
                src_ap = (q_fm[:, hd * 2048 + lh * 1024:
                               hd * 2048 + (lh + 1) * 1024]
                          if which == "q" else stg[:, sp * 1024:(sp + 1) * 1024])
                p.step("scalar",
                       lambda s, src_ap=src_ap: s.activation(sqs, src_ap,
                                                             AF.Square),
                       waits=["vs"], incs=["ss"])
                sq_done[sp] = p.at("ss")
                p.step("tensor",
                       lambda t, hd=hd, lh=lh: [
                           t.matmul(sqp[:, lh * 1024 + q * 512:
                                        lh * 1024 + (q + 1) * 512],
                                    e2s[:, hd * 16:(hd + 1) * 16],
                                    sqs[:, q * 512:(q + 1) * 512],
                                    start=(hd == 0), stop=(hd == 7))
                           for q in range(2)],
                       waits=["ss", w_e2], incs=["ts"])
                ev_done[sw] = p.at("ts")
        return p.at("ts")

    proj_qk("k")
    t_ksq = p.at("ts")
    p.step("scalar", lambda s: s.activation(dst[:, :], sqp[:, :], AF.Sqrt),
           waits=[("ts", t_ksq)], incs=["ss"])
    p.step("vector", lambda v: v.reduce_max(km1[:, :], dst[:, :], AX.X),
           waits=["ss"], incs=["vs"])
    p.step("vector", lambda v: v.tensor_scalar_mul(km1[:, :], km1[:, :], 1.01),
           incs=["vs"])
    p.step("gpsimd", lambda g: g.memset(kmb[:, :], 0.0), incs=["gs"])
    p.step("vector", lambda v: v.tensor_copy(kmb[:, 0:1], km1[:, :]),
           waits=["gs"], incs=["vs"])
    p.dma("sync", bass.AP(agin, 2048 * 2048, [[2048, 1], [1, 2048]]),
          bass.AP(kmb, 0, [[128, 16], [1, 128]]), waits=["vs"])
    w_vwt = p.dma("gpsimd", wbuf[:, :],
                  bass.AP(dr["l1a_wvT"], 0, [[1024, 128], [1024 * 128, 8],
                                             [1, 1024]]),
                  waits=[("ts", p.at("ts"))])
    vv_done = [p.at("vs"), p.at("vs")]
    for lt in range(16):
        sp = lt % 2
        p.step("tensor",
               lambda t, lt=lt, sp=sp: [
                   t.matmul(pslot[sp][:, hb * 512:(hb + 1) * 512],
                            xc[:, cc * L + lt * 128: cc * L + (lt + 1) * 128],
                            wbuf[:, cc * 1024 + hb * 512:
                                 cc * 1024 + (hb + 1) * 512],
                            start=(cc == 0), stop=(cc == 7))
                   for hb in range(2) for cc in range(8)],
               waits=[w_vwt] + list(xc_ready) +
                     [("vs", vv_done[sp] if lt >= 2 else p.at("vs"))],
               incs=["ts"])
        t_mm = p.at("ts")
        p.step("vector",
               lambda v, sp=sp: v.tensor_copy(stg[:, sp * 1024:(sp + 1) * 1024],
                                              pslot[sp][:, :]),
               waits=[("ts", t_mm), ("sa%d" % sp, p._stg_free.get(sp, 0))],
               incs=["vs"])
        vv_done[sp] = p.at("vs")
        p.dma("sync",
              bass.AP(agin, 1024 * 2048 + lt * 128 * 1024,
                      [[1024, 128], [1, 1024]]),
              stg[:, sp * 1024:(sp + 1) * 1024], waits=["vs"], sem="sa%d" % sp)
        p._stg_free[sp] = p.at("sa%d" % sp)
    q_end = proj_qk("q")
    p.step("scalar", lambda s: s.activation(qn[:, :], sqp[:, :], AF.Sqrt),
           waits=[("ts", q_end)], incs=["ss"])
    p.step("gpsimd",
           lambda g: g.collective_compute(
               "AllGather", ALU.bypass,
               replica_groups=[[0, 1, 2, 3], [4, 5, 6, 7]],
               ins=[agin.ap().opt()], outs=[agout.ap().opt()]),
           waits=["dsy", "dgp", "dsc", "sa0", "sa1"], incs=["cc"],
           chain=False)
    agv = ("cc", p.at("cc"))
    p.dma("gpsimd", kmg[:, :],
          bass.AP(agout, 2048 * 2048, [[128, 16], [AGR * 2048, 4]]),
          waits=[agv], allow_slow_non_contiguous=True, sem="r0")
    k_gl0 = ("r0", p.at("r0"))
    p.step("vector", lambda v: v.reduce_max(km1[:, :], kmg[:, :], AX.X),
           waits=[k_gl0], incs=["vs"])
    p.step("vector", lambda v: v.reduce_max(bline[:, :], qn[:, :], AX.X),
           waits=["ss"], incs=["vs"])
    p.step("vector", lambda v: v.tensor_mul(bline[:, :], bline[:, :], km1[:, :]),
           incs=["vs"])
    p.step("vector", lambda v: v.tensor_scalar_mul(bline[:, :], bline[:, :],
                                                   -0.125),
           incs=["vs"])
    p.dma("sync", nbrow, bass.AP(bline, 0, [[1, 16], [1, 1]]),
          waits=["vs"], allow_slow_non_contiguous=True)
    nb_ld = ("dsy", p.at("dsy"))
    p.step("tensor", lambda t: t.matmul(psa[:, 0:16], co["ones1f"][:, :],
                                        nbrow[0:1, :], start=True, stop=True),
           waits=[nb_ld, ("ss", p.at("ss")), ("vs", p.at("vs"))], incs=["ts"])
    p.step("vector", lambda v: v.tensor_copy(negb[:, :], psa[:, 0:16]),
           waits=["ts"], incs=["vs"])
    negb_ok = ("vs", p.at("vs"))
    p.step("gpsimd", lambda g: g.memset(v65[:, :], 1.0), incs=["gs"])
    v65_ones = ("gs", p.at("gs"))

    ost = psa[0:65, 2048:4096]
    p._l1a_exp = {}
    slab_free = [0, 0]
    vslab_free = 0
    pt_free = [0, 0]
    sS_done = p.at("ss")
    o_evac = p.at("ss")
    for h in range(16):
        par = h % 2
        vsl = v65
        k_ld = p.dma("gpsimd",
                     bass.AP(kh2, 64 * par * 8192,
                             [[8192, 64], [2048, 4], [1, 2048]]),
                     bass.AP(agout, h * 64 * 2048,
                             [[2048, 64], [AGR * 2048, 4], [1, 2048]]),
                     waits=[agv, ("ts", slab_free[par])], sem="k%d" % par)
        for vr in range(4):
            v_ld = p.dma("gpsimd",
                         bass.AP(vsl, vr * 16 * 65,
                                 [[64 * 65, 128], [65, 16], [1, 64]]),
                         bass.AP(agout, (1024 + vr * AGR) * 2048 + h * 64,
                                 [[1024, 128], [128 * 1024, 16], [1, 64]]),
                         waits=[agv, v65_ones, ("ts", vslab_free)],
                         sem="x0")
        pend = None
        for tt in range(64):
            for half in range(2):
                i = tt * 2 + half
                ps = i % 2
                p.step("tensor",
                       lambda t, h=h, tt=tt, par=par, ps=ps, half=half: [
                           t.matmul(pslot[ps][:, q * 512:(q + 1) * 512],
                                    kh2[64 * par:64 * par + 64,
                                        tt * 128:(tt + 1) * 128],
                                    q_fm[64 * (h % 2):64 * (h % 2) + 64,
                                         (h // 2) * 2048 + half * 1024 + q * 512:
                                         (h // 2) * 2048 + half * 1024 +
                                         (q + 1) * 512],
                                    start=True, stop=True)
                           for q in range(2)],
                       waits=[k_ld, ("ss", p._l1a_exp.get((i - 2) % 128, sS_done)),
                              ("vs", negb_ok[1])],
                       incs=["ts"])
                t_s = p.at("ts")
                if pend is not None:
                    p.step("tensor", pend[0], waits=pend[1], incs=["ts"])
                    pt_free[pend[2]] = p.at("ts")
                p.step("scalar",
                       lambda s, h=h, ps=ps: s.activation(
                           pt[:, ps * 1024:(ps + 1) * 1024],
                           pslot[ps][:, :], AF.Exp,
                           bias=negb[:, h:h + 1], scale=0.125),
                       waits=[("ts", t_s), ("ts", pt_free[ps])], incs=["ss"])
                p._l1a_exp[i % 128] = p.at("ss")
                s_exp = p.at("ss")
                avfn = (lambda t, tt=tt, half=half, ps=ps, vsl=vsl: [
                    t.matmul(ost[:, half * 1024 + q * 512:
                                 half * 1024 + (q + 1) * 512],
                             vsl[:, tt * 65:tt * 65 + 65],
                             pt[:, ps * 1024 + q * 512:
                                ps * 1024 + (q + 1) * 512],
                             start=(tt == 0), stop=(tt == 63))
                    for q in range(2)])
                avw = [v_ld, ("ss", s_exp)] + \
                    ([("ss", o_evac)] if tt == 0 and half == 0 else [])
                pend = (avfn, avw, ps)
        p.step("tensor", pend[0], waits=pend[1], incs=["ts"])
        pt_free[pend[2]] = p.at("ts")
        slab_free[par] = p.at("ts")
        vslab_free = p.at("ts")
        t_o = p.at("ts")
        p.step("scalar",
               lambda s: s.copy(stg[0:64, :].rearrange("p (a b) -> p a b", b=1024)
                                if False else stg[0:64, 0:2048], ost[0:64, :]),
               waits=[("ts", t_o), ("dsy", p.at("dsy"))], incs=["ss"])
        p.step("scalar",
               lambda s: s.copy(dsg, ost[64:65, :]),
               incs=["ss"])
        o_evac = p.at("ss")
        p.dma("sync", bass.AP(o_d, h * 64 * 2048, [[2048, 64], [1, 2048]]),
              stg[0:64, 0:2048], waits=["ss"])
        p.dma("sync", bass.AP(dst, h * 2048, [[2048, 1], [1, 2048]]),
              dsg, waits=["ss"])
    den_all = ("dsy", p.at("dsy"))
    p.dma("gpsimd", boutc,
          bass.AP(dr["l1a_bout"], 0, [[1, 128], [128, 8]]),
          waits=[("ts", p.at("ts"))], allow_slow_non_contiguous=True, sem="r0")
    w_bout = ("r0", p.at("r0"))
    p.step("vector", lambda v: v.reciprocal(dst[:, :], dst[:, :]),
           waits=[den_all], incs=["vs"])
    rec_all = ("vs", p.at("vs"))
    o_d2 = nc.dram_tensor("l1a_od2", [1024, 2048], BF16)
    och_free = [p.at("dsy"), p.at("dsy")]
    for ccx in range(8):
        so = 0
        o_ld = p.dma("sync", och[:, so * 2048:(so + 1) * 2048],
                     bass.AP(o_d, ccx * 128 * 2048, [[2048, 128], [1, 2048]]),
                     waits=[den_all, ("dsy", och_free[so])], sem="o%d" % so)
        mul_done = 0
        for hh in range(2):
            hglob = 2 * ccx + hh
            r_ld = p.dma("sync", recrow,
                         bass.AP(dst, hglob * 2048, [[2048, 1], [1, 2048]]),
                         waits=[rec_all, ("vs", mul_done)], sem="r1")
            for lq in range(4):
                rbv = psa[:, 2048 + lq * 512:2048 + lq * 512 + 512]
                p.step("tensor",
                       lambda t, lq=lq, rbv=rbv: t.matmul(
                           rbv, co["ones1f"][:, :],
                           sqf[0:1, lq * 512:(lq + 1) * 512],
                           start=True, stop=True),
                       waits=[r_ld, ("vs", p.at("vs"))], incs=["ts"])
                t_rb = p.at("ts")
                p.step("vector",
                       lambda v, hh=hh, so=so, lq=lq, rbv=rbv: v.tensor_mul(
                           och[64 * hh:64 * hh + 64,
                               so * 2048 + lq * 512:so * 2048 + (lq + 1) * 512],
                           och[64 * hh:64 * hh + 64,
                               so * 2048 + lq * 512:so * 2048 + (lq + 1) * 512],
                           rbv[64 * hh:64 * hh + 64, :]),
                       waits=[("ts", t_rb), o_ld], incs=["vs"])
            mul_done = p.at("vs")
        p.dma("sync", bass.AP(o_d2, ccx * 128 * 2048, [[2048, 128], [1, 2048]]),
              och[:, so * 2048:(so + 1) * 2048], waits=[("vs", mul_done)])
        och_free[so] = p.at("dsy")
    on_all = ("dsy", p.at("dsy"))
    dslot = [psa[:, 0:2048], psa[:, 2048:4096]]
    w_free = [p.at("ts"), p.at("ts")]
    on_free = [p.at("ts"), p.at("ts")]
    d_free = [p.at("vs"), p.at("vs")]
    for ccx in range(8):
        sw = ccx % 2
        wld = p.dma("gpsimd", wbuf[:, sw * 1024:(sw + 1) * 1024],
                    bass.AP(dr["l1a_woT"], ccx * 128,
                            [[1024, 128], [1024 * 128, 8], [1, 128]]),
                    waits=[("ts", w_free[sw])], sem="w%d" % sw)
        for hc in range(8):
            so = 0
            o_ld = p.dma("sync", och[:, so * 2048:(so + 1) * 2048],
                         bass.AP(o_d2, hc * 128 * 2048, [[2048, 128], [1, 2048]]),
                         waits=[on_all, ("ts", on_free[so])], sem="o%d" % so)
            p.step("tensor",
                   lambda t, ccx=ccx, hc=hc, sw=sw, so=so: [
                       t.matmul(dslot[ccx % 2][:, q * 512:(q + 1) * 512],
                                wbuf[:, sw * 1024 + hc * 128:
                                     sw * 1024 + (hc + 1) * 128],
                                och[:, so * 2048 + q * 512:
                                    so * 2048 + (q + 1) * 512],
                                start=(hc == 0), stop=(hc == 7))
                       for q in range(4)],
                   waits=[wld, o_ld] +
                         ([("vs", d_free[ccx % 2])] if hc == 0 else []),
                   incs=["ts"])
            on_free[so] = p.at("ts")
        w_free[sw] = p.at("ts")
        t_d = p.at("ts")
        for lh in range(2):
            p.step("vector",
                   lambda v, ccx=ccx, lh=lh: v.scalar_tensor_tensor(
                       xs[:, ccx * L + lh * 1024: ccx * L + (lh + 1) * 1024],
                       dslot[ccx % 2][:, lh * 1024:(lh + 1) * 1024],
                       boutc[:, ccx:ccx + 1],
                       xs[:, ccx * L + lh * 1024: ccx * L + (lh + 1) * 1024],
                       ALU.add, ALU.add),
                   waits=[("ts", t_d), w_bout], incs=["vs"])
        d_free[ccx % 2] = p.at("vs")
    return ("vs", p.at("vs"))


def build_l1_attn_standalone():
    es = ExitStack()
    nc = bass.Bass("TRN2", target_bir_lowering=False)
    x_in = nc.dram_tensor("x_in", [D, L], F32, kind="ExternalInput")
    dr = {}
    for nme, shape in [("l1a_wqT", [1024, 1024]), ("l1a_wkT", [1024, 1024]),
                       ("l1a_wvT", [1024, 1024]), ("l1a_woT", [1024, 1024]),
                       ("l1a_bout", [1024]), ("l1a_e2", [1024, 16])]:
        dr[nme] = nc.dram_tensor(nme, shape, F32, kind="ExternalInput")
    x_out = nc.dram_tensor("x_out", [D, L], F32, kind="ExternalOutput")
    with es:
        p = Prog(nc, es)
        co = setup_consts(p)
        xs = p.sbuf("xs", [128, 8 * L])
        xc = p.sbuf("xc", [128, 8 * L], BF16)
        psa = p.psum("psa", [128, 4096])
        lsc = make_ln_scratch(p)
        ld = load_stream_fm(p, xs, x_in)
        lnr = emit_ln_fm(p, co, xs, xc, "ln2", [ld], psa, lsc)
        fr = emit_l1_attn(p, co, xs, xc, dr, [lnr], psa)
        st = store_stream_fm(p, x_out, xs, [fr])
        p.step("sync", lambda sy: None, waits=[st], chain=False)
        p.build()
    return nc


def run_l1_attn_standalone(x2_full, ins):
    w = prep_l1_attn(ins)
    nc = build_l1_attn_standalone()
    in_maps = []
    for ci in range(NCORE):
        b, cch = divmod(ci, 4)
        xfm = np.ascontiguousarray(x2_full[b, cch * L:(cch + 1) * L, :].T)
        in_maps.append({"x_in": xfm, **w})
    res = run_bass_kernel_spmd(nc, in_maps, core_ids=list(range(NCORE)))
    out = np.empty((2, SEQ, D), np.float32)
    for ci in range(NCORE):
        b, cch = divmod(ci, 4)
        out[b, cch * L:(cch + 1) * L, :] = np.asarray(res.results[ci]["x_out"]).T
    return out


L0H = 513
YW = 2049


def _xpos_tables(t0):
    n = SEQ
    dh = 64
    half = np.arange(0, dh, 2, dtype=np.float64)
    inv_freq = 1.0 / 10000.0 ** (half / dh)
    t = np.arange(t0 - 512, t0 + 2048, dtype=np.float64)
    freqs = t[:, None] * inv_freq[None, :]
    base = (half + 0.4 * dh) / (1.4 * dh)
    power = (t - n // 2) / float(512 // 2)
    scale = base[None, :] ** power[:, None]
    cos = np.cos(freqs)
    sin = np.sin(freqs)
    qc = np.concatenate([cos * scale, cos * scale], 1)
    qs = np.concatenate([-sin * scale, sin * scale], 1)
    kc = np.concatenate([cos / scale, cos / scale], 1)
    ks = np.concatenate([-sin / scale, sin / scale], 1)
    return np.stack([qc, qs, kc, ks]).astype(np.float32)


def _l0f_masks(first_chunk):
    m = np.zeros((12, 128, 512), np.float32)
    for t in range(8):
        r = np.arange(128)[:, None] + 128 * t
        c = np.arange(512)[None, :]
        bad = (c > r) | (c < r - 512)
        m[t][bad] = -1e28
    for t in range(4):
        m[8 + t] = -1e28 if first_chunk else m[t]
    return m


def prep_l0_ffn(ins, core_id):
    b, cch = divmod(core_id, 4)
    t0 = cch * L
    d = {}
    d["l0f_tabs"] = _xpos_tables(t0)
    d["l0f_masks"] = _l0f_masks(cch == 0)
    d["l0f_flag"] = np.full((128, 1), 0.0 if cch == 0 else 1.0, np.float32)
    wq = ins["L0_ffn_wqkv"]
    d["l0f_wqkv"] = np.ascontiguousarray(wq)
    d["l0f_wo"] = np.ascontiguousarray(ins["L0_ffn_wo"])
    d["l0f_qs"] = np.ascontiguousarray(ins["L0_ffn_q_scale"])
    d["l0f_ks"] = np.ascontiguousarray(ins["L0_ffn_k_scale"])
    d["l0f_lng"] = np.ascontiguousarray(ins["L0_ffn_ln_g"])
    d["l0f_lnb"] = np.ascontiguousarray(ins["L0_ffn_ln_b"])
    d["l0f_b2"] = np.ascontiguousarray(ins["L0_ffn_b2"])
    prev = (cch + 3) % 4
    d["l0f_proff"] = np.array([[prev * 128 * (8 * 513), prev * 128 * 8]],
                              np.int32)
    w1 = ins["L0_ffn_w1"]
    b1 = ins["L0_ffn_b1"]
    w2 = ins["L0_ffn_w2"]
    w1i = np.zeros((2, 1024, 44 * 128), np.float32)
    b1i = np.zeros((2, 44 * 128), np.float32)
    w2p = np.zeros((2, 22 * 128, 1024), np.float32)
    for ll in range(2):
        a = w1[ll][:, :2730]
        g = w1[ll][:, 2730:]
        ba, bg = b1[ll][:2730], b1[ll][2730:]
        for j in range(22):
            sl = slice(128 * j, min(128 * (j + 1), 2730))
            n = sl.stop - sl.start
            w1i[ll][:, 256 * j:256 * j + n] = a[:, sl]
            w1i[ll][:, 256 * j + 128:256 * j + 128 + n] = g[:, sl]
            b1i[ll][256 * j:256 * j + n] = ba[sl]
            b1i[ll][256 * j + 128:256 * j + 128 + n] = bg[sl]
            w2p[ll][128 * j:128 * j + n] = w2[ll][sl]
    d["l0f_w1i"] = w1i
    d["l0f_b1i"] = b1i
    d["l0f_w2p"] = w2p
    return d

YBS = 2561


def emit_l0_ffn(p, co, dr, psa, x_in, x_out):
    nc = p.nc
    o_d = nc.dram_tensor("l0f_od", [512, 2048], BF16)
    ybuf_d = nc.dram_tensor("l0f_ybufd", [1024, YW], F32)
    agA_in = nc.dram_tensor("l0f_agAin", [128, 8 * 513], BF16)
    agA_out = nc.dram_tensor("l0f_agAout", [512, 8 * 513], BF16)
    agB_in = nc.dram_tensor("l0f_agBin", [128, 8], BF16)
    agB_out = nc.dram_tensor("l0f_agBout", [512, 8], BF16)

    ybc = p.sbuf("l0f_ybc", [128, 8 * YBS], BF16)
    ystg = p.sbuf("l0f_ystg", [128, 1024])
    pool = p.sbuf("l0f_pool", [128, 2048])
    pool2 = p.sbuf("l0f_pool2", [128, 6144], BF16)
    tmsc = p.sbuf("l0f_tmsc", [128, 1536], BF16)
    shA = p.sbuf("l0f_shA", [128, 11264], BF16)
    kfm = p.sbuf("l0f_kfm", [128, 4 * 2560], BF16)
    v65 = p.sbuf("l0f_v65", [128, 160 * 65], BF16)
    wqk = p.sbuf("l0f_wqk", [128, 8 * 1536], BF16)
    wstr = p.sbuf("l0f_wstr", [128, 4096], BF16)
    row = p.sbuf("l0f_row", [1, 512])
    rowt = p.sbuf("l0f_rowt", [1, 512])
    qsb = p.sbuf("l0f_qsb", [128, 128])
    nrm = p.sbuf("l0f_nrm", [128, 16])
    negbl = p.sbuf("l0f_negbl", [128, 1])
    b1c = p.sbuf("l0f_b1c", [128, 44])
    colz = p.sbuf("l0f_colz", [128, 24])
    flagc = p.sbuf("l0f_flagc", [128, 1])
    hstg = p.sbuf("l0f_hstg", [128, 8], BF16)
    hstf = p.sbuf("l0f_hstf", [128, 8])
    offs = p.sbuf("l0f_offs", [1, 2], mybir.dt.int32)
    qfm = shA[:, 0:8192]
    h2q = shA[:, 0:11264]
    och = shA[:, 0:8192]
    den = pool[0:8, 0:2048]
    rrow = p.sbuf("l0f_rrow", [1, 2048])
    ostg = tmsc[0:64, 0:512]
    ptb = tmsc[:, 512:1536]

    p._l0f_rb = 0
    p._l0f_yst = [0, 0]
    p.get_sem("y0"), p.get_sem("y1")
    p._l0f_geg = [0, 0]
    p._l0f_dfl = 0
    p._l0f_xfree = [0, 0]
    p._l0f_tabfree = 0
    p.dma("gpsimd", flagc[:, :], dr["l0f_flag"].ap(), sem="pre")
    p.dma("gpsimd", offs[:, :], dr["l0f_proff"].ap(), sem="pre")
    p.step("gpsimd", lambda g: g.memset(v65[:, :], 1.0), incs=["gs"])
    pre_ok = [("pre", p.at("pre")), ("gs", p.at("gs"))]

    def ycol(c, t):
        return c * YBS + t + 513

    s1 = psa[0:1, 0:512]
    s2 = psa[0:1, 512:1024]
    mub = psa[:, 1024:1536]
    rsb = psa[:, 1536:2048]
    xsl = [pool[:, 0:512], pool[:, 512:1024]]
    sq2 = [pool[:, 1024:1536], pool[:, 1536:2048]]

    def ln_stats_q(rhs_fn, wait_extra, shifted=False):
        for c in range(8):
            ap, w = rhs_fn(c)
            sl = c % 2
            p.step("scalar",
                   lambda s, sl=sl, ap=ap: s.activation(sq2[sl], ap, AF.Square),
                   waits=w + [("ts", p._l0f_sqfree[sl]), ("vs", p.at("vs"))],
                   incs=["ss"])
            p.step("tensor",
                   lambda t, c=c, ap=ap: t.matmul(
                       s1[:, :], co["onescf"][:, :], ap,
                       start=(c == 0), stop=(c == 7)),
                   waits=w + [co["ready"]] +
                         ([("vs", p.at("vs"))] if c == 0 else []))
            p.step("tensor",
                   lambda t, sl=sl, c=c: t.matmul(
                       s2[:, :], co["onescf"][:, :], sq2[sl],
                       start=(c == 0), stop=(c == 7)),
                   waits=["ss"], incs=["ts"])
            p._l0f_sqfree[sl] = p.at("ts")
        t_stats = p.at("ts")
        p.step("vector", lambda v: v.tensor_scalar_mul(row[:, :], s1[0:1, :],
                                                       1.0 / D),
               waits=[("ts", t_stats)], incs=["vs"])
        p.step("vector", lambda v: v.tensor_mul(rowt[:, :], row[:, :], row[:, :]),
               waits=["vs"], incs=["vs"])
        p.step("vector", lambda v: v.scalar_tensor_tensor(
                   rowt[:, :], s2[0:1, :], 1.0 / D, rowt[:, :],
                   ALU.mult, ALU.subtract),
               waits=["vs"], incs=["vs"])
        p.step("vector", lambda v: v.tensor_scalar_add(rowt[:, :], rowt[:, :],
                                                       EPS),
               waits=["vs"], incs=["vs"])
        v_var = p.at("vs")
        p.step("scalar", lambda s: s.activation(rowt[:, :], rowt[:, :], AF.Sqrt),
               waits=[("vs", v_var)], incs=["ss"])
        p.step("vector", lambda v: v.reciprocal(rowt[:, :], rowt[:, :]),
               waits=["ss"], incs=["vs"])
        v_rs = p.at("vs")
        p.step("tensor", lambda t: t.matmul(mub[:, :], co["ones1f"][:, :],
                                            row[0:1, :], start=True, stop=True),
               waits=[("vs", v_var)])
        p.step("tensor", lambda t: t.matmul(rsb[:, :], co["ones1f"][:, :],
                                            rowt[0:1, :], start=True, stop=True),
               waits=[("vs", v_rs)], incs=["ts"])
        return p.at("ts")

    p._l0f_sqfree = [0, 0]
    for lq in range(4):
        o = lq * 512

        def rhs_ln0(c, o=o):
            sl = c % 2
            ld = p.dma("sync", xsl[sl],
                       bass.AP(x_in, c * 128 * L + o, [[L, 128], [1, 512]]),
                       waits=[("ts", p._l0f_sqfree[sl]),
                              ("vs", p.at("vs"))], sem="sa%d" % sl)
            return xsl[sl], [ld]
        t_b = ln_stats_q(rhs_ln0, [])
        for c in range(8):
            sl = c % 2
            ld = p.dma("sync", xsl[sl],
                       bass.AP(x_in, c * 128 * L + o, [[L, 128], [1, 512]]),
                       waits=[("ts", p._l0f_sqfree[sl]), ("vs", p.at("vs")),
                              ("ss", p.at("ss"))], sem="sa%d" % sl)
            p.step("vector",
                   lambda v, sl=sl: v.tensor_sub(sq2[sl], xsl[sl], mub[:, :]),
                   waits=[ld, ("ts", t_b)], incs=["vs"])
            p.step("vector",
                   lambda v, sl=sl: v.tensor_mul(
                       ystg[:, sl * 512:(sl + 1) * 512], sq2[sl], rsb[:, :]),
                   waits=[("y%d" % sl, p._l0f_yst[sl])], incs=["vs"])
            p.step("vector",
                   lambda v, c=c, o=o, sl=sl: v.tensor_copy(
                       ybc[:, ycol(c, o): ycol(c, o) + 512],
                       ystg[:, sl * 512:(sl + 1) * 512]),
                   incs=["vs"])
            p.dma("gpsimd",
                  bass.AP(ybuf_d, c * 128 * YW + 1 + o, [[YW, 128], [1, 512]]),
                  ystg[:, sl * 512:(sl + 1) * 512], waits=["vs"],
                  sem="y%d" % sl)
            p._l0f_yst[sl] = p.at("y%d" % sl)
    y_ready = ("vs", p.at("vs"))

    import os as _os
    if _os.environ.get("PROBE_L0F_STOP") == "ln":
        p._l0f_ylast = [y_ready, ("y0", p.at("y0")), ("y1", p.at("y1"))]
        NLYR = 0
    else:
        NLYR = int(_os.environ.get("PROBE_L0F_LAYERS", "2"))
    for lyr in range(NLYR):
        lw = [("ts", p.at("ts")), ("vs", p.at("vs")), ("ss", p.at("ss"))]
        w_qk = p.dma("gpsimd", wqk[:, :],
                     bass.AP(dr["l0f_wqkv"], lyr * 1024 * 1536,
                             [[1536, 128], [1536 * 128, 8], [1, 1536]]),
                     waits=lw, sem="c%d" % lyr)
        p.dma("gpsimd", b1c[:, :],
              bass.AP(dr["l0f_b1i"], lyr * 5632, [[1, 128], [128, 44]]),
              waits=lw, allow_slow_non_contiguous=True, sem="c%d" % lyr)
        for ii, nm in enumerate(["l0f_lng", "l0f_lnb", "l0f_b2"]):
            p.dma("gpsimd", colz[:, 8 * ii:8 * ii + 8],
                  bass.AP(dr[nm], lyr * 1024, [[1, 128], [128, 8]]),
                  waits=lw, allow_slow_non_contiguous=True, sem="c%d" % lyr)
        for kind in range(4):
            p.dma("gpsimd", bass.AP(pool2, kind * 64,
                                    [[6144, 128], [256, 20], [1, 64]]),
                  bass.AP(dr["l0f_tabs"], kind * 2560 * 64,
                          [[64, 128], [128 * 64, 20], [1, 64]]),
                  waits=lw, sem="c%d" % lyr)
        p.dma("gpsimd", row[0:1, 0:64],
              bass.AP(dr["l0f_qs"], lyr * 64, [[128, 1], [1, 64]]),
              waits=lw, sem="c%d" % lyr)
        p.dma("gpsimd", row[0:1, 64:128],
              bass.AP(dr["l0f_ks"], lyr * 64, [[128, 1], [1, 64]]),
              waits=lw, sem="c%d" % lyr)
        wconst = ("c%d" % lyr, p.at("c%d" % lyr))
        w_qk = wconst
        p.step("tensor", lambda t: t.matmul(psa[:, 0:64], co["ones1f"][:, :],
                                            row[0:1, 0:64],
                                            start=True, stop=True),
               waits=[wconst, ("vs", p.at("vs"))])
        p.step("tensor", lambda t: t.matmul(psa[:, 64:128], co["ones1f"][:, :],
                                            row[0:1, 64:128],
                                            start=True, stop=True),
               incs=["ts"])
        p.step("vector", lambda v: v.tensor_copy(qsb[:, :], psa[:, 0:128]),
               waits=["ts"], incs=["vs"])
        qsb_ok = p.at("vs")
        for kind in range(4):
            base = (0 if kind < 2 else 64)
            if kind % 2 == 0:
                in1 = bass.AP(qsb, base, [[128, 128], [0, 20], [1, 64]])
            else:
                in1 = bass.AP(qsb, base + 32, [[128, 128], [0, 20], [-32, 2],
                                               [1, 32]])
            tab = bass.AP(pool2, kind * 64, [[6144, 128], [256, 20], [1, 64]])
            p.step("vector",
                   lambda v, tab=tab, in1=in1: v.tensor_tensor(tab, tab, in1,
                                                               ALU.mult),
                   waits=[wconst, ("vs", qsb_ok)], incs=["vs"])
        tab_ok = ("vs", p.at("vs"))
        p.step("vector", lambda v: v.reduce_max(rowt[0:1, 0:1],
                                                row[0:1, 0:64],
                                                AX.X, apply_absolute_value=True),
               waits=[wconst], incs=["vs"])
        p.step("vector", lambda v: v.reduce_max(rowt[0:1, 1:2],
                                                row[0:1, 64:128],
                                                AX.X, apply_absolute_value=True),
               incs=["vs"])
        p.step("vector", lambda v: v.tensor_mul(rowt[0:1, 0:1], rowt[0:1, 0:1],
                                                rowt[0:1, 1:2]),
               incs=["vs"])
        p.step("vector", lambda v: v.tensor_scalar_mul(rowt[0:1, 0:1],
                                                       rowt[0:1, 0:1], -8.0),
               incs=["vs"])
        p.step("tensor", lambda t: t.matmul(psa[:, 128:129], co["ones1f"][:, :],
                                            rowt[0:1, 0:1], start=True,
                                            stop=True),
               waits=["vs"], incs=["ts"])
        p.step("vector", lambda v: v.tensor_copy(negbl[:, :], psa[:, 128:129]),
               waits=["ts"], incs=["vs"])
        negb_ok = ("vs", p.at("vs"))
        if _os.environ.get("PROBE_L0F_STOP") == "consts":
            p._l0f_ylast = [negb_ok, ("y0", p.at("y0")), ("y1", p.at("y1"))]
            break

        src_ready = y_ready if lyr == 0 else p._l0f_ymid
        p.dma("sync", agA_in.ap(),
              bass.AP(ybc, ycol(0, 2048 - 513),
                      [[8 * YBS, 128], [YBS, 8], [1, 513]]),
              waits=[src_ready])
        p.step("gpsimd",
               lambda g: g.collective_compute(
                   "AllGather", ALU.bypass,
                   replica_groups=[[0, 1, 2, 3], [4, 5, 6, 7]],
                   ins=[agA_in.ap().opt()], outs=[agA_out.ap().opt()]),
               waits=["dsy", "dgp", "sa0", "sa1"], incs=["cc"], chain=False)
        agv = ("cc", p.at("cc"))
        p.step("sync", lambda sy: None, waits=[agv, pre_ok[0]], chain=False)
        p.step("sync",
               lambda sy: sy.dma_start(
                   out=bass.AP(ybc, ycol(0, -513),
                               [[8 * YBS, 128], [YBS, 8], [1, 513]]),
                   in_=bass.AP(agA_out, sy.value_load(offs[0:1, 0:1]),
                               [[8 * 513, 128], [513, 8], [1, 513]])),
               incs=[("dsy", 16)], chain=False)
        hl = ("dsy", p.at("dsy"))
        p.step("vector",
               lambda v: v.tensor_scalar(
                   bass.AP(ybc, ycol(0, -513), [[8 * YBS, 128], [YBS, 8],
                                                [1, 513]]),
                   bass.AP(ybc, ycol(0, -513), [[8 * YBS, 128], [YBS, 8],
                                                [1, 513]]),
                   flagc[:, 0:1], None, ALU.mult),
               waits=[hl, pre_ok[0]], incs=["vs"])
        halo_ok = ("vs", p.at("vs"))
        if _os.environ.get("PROBE_L0F_STOP") == "halo":
            p._l0f_ylast = [halo_ok, ("y0", p.at("y0")), ("y1", p.at("y1"))]
            break

        qkvp = psa[:, 0:1536]
        trp = [psa[:, 1536:1664], psa[:, 2048:2176]]
        trot = pool[:, 0:1536]
        ev_free = p.at("vs")
        tr_free = [p.at("vs"), p.at("vs")]
        for lt in range(20):
            t0 = -512 + 128 * lt
            obs = [1, 2] if lt < 4 else [0, 1, 2]
            p.step("tensor",
                   lambda t, lt=lt, t0=t0, obs=obs: [
                       t.matmul(qkvp[:, ob * 512:(ob + 1) * 512],
                                ybc[:, ycol(c, t0) - (1 if c >= 4 else 0):
                                    ycol(c, t0) - (1 if c >= 4 else 0) + 128],
                                wqk[:, c * 1536 + ob * 512:
                                    c * 1536 + (ob + 1) * 512],
                                start=(c == 0), stop=(c == 7))
                       for ob in obs for c in range(8)],
                   waits=[w_qk, halo_ok, ("vs", ev_free)] +
                         ([] if lyr == 0 else []),
                   incs=["ts"])
            t_qkv = p.at("ts")
            p.step("vector",
                   lambda v, obs=obs: v.tensor_copy(
                       tmsc[:, obs[0] * 512:1536], qkvp[:, obs[0] * 512:1536]),
                   waits=[("ts", t_qkv)], incs=["vs"])
            ev_free = p.at("vs")
            tm_ok = p.at("vs")
            for h in range(8):
                p.step("vector",
                       lambda v, lt=lt, h=h: v.tensor_copy(
                           v65[:, (lt * 8 + h) * 65:(lt * 8 + h) * 65 + 64],
                           tmsc[:, 1024 + h * 64:1024 + (h + 1) * 64]),
                       waits=[("vs", tm_ok)] + (pre_ok if lt == 0 else []),
                       incs=(["vs"] if h == 7 else []))
            if _os.environ.get("PROBE_L0F_QKV") == "mm":
                continue
            parts = ([("q", 0)] if lt >= 4 else []) + [("k", 512)]
            for nm2, off2 in parts:
                nb = 0 if nm2 == "q" else 8
                tks = 0 if nm2 == "q" else 2
                p.step("vector",
                       lambda v, off2=off2: v.tensor_mul(
                           trot[:, 0:512], tmsc[:, off2:off2 + 512],
                           tmsc[:, off2:off2 + 512]),
                       waits=[("vs", tm_ok)], incs=["vs"])
                p.step("vector",
                       lambda v, nb=nb: v.tensor_reduce(
                           nrm[:, nb:nb + 8],
                           trot[:, 0:512].rearrange("p (h d) -> p h d", d=64),
                           AX.X, ALU.add),
                       incs=["vs"])
                p.step("scalar",
                       lambda s, nb=nb: s.activation(nrm[:, nb:nb + 8],
                                                     nrm[:, nb:nb + 8], AF.Sqrt),
                       waits=["vs"], incs=["ss"])
                p.step("vector",
                       lambda v, nb=nb: v.tensor_scalar_max(
                           nrm[:, nb:nb + 8], nrm[:, nb:nb + 8], 1e-12),
                       waits=["ss"], incs=["vs"])
                p.step("vector",
                       lambda v, nb=nb: v.reciprocal(nrm[:, nb:nb + 8],
                                                     nrm[:, nb:nb + 8]),
                       incs=["vs"])
                ctab = bass.AP(pool2, lt * 256 + tks * 64,
                               [[6144, 128], [0, 8], [1, 64]])
                stab = bass.AP(pool2, lt * 256 + (tks + 1) * 64,
                               [[6144, 128], [0, 8], [1, 64]])
                p.step("vector",
                       lambda v, off2=off2, ctab=ctab: v.tensor_tensor(
                           trot[:, 0:512], tmsc[:, off2:off2 + 512], ctab,
                           ALU.mult),
                       waits=[tab_ok], incs=["vs"])
                p.step("vector",
                       lambda v, off2=off2, stab=stab: v.tensor_tensor(
                           trot[:, 512:1024],
                           bass.AP(tmsc, off2 + 32,
                                   [[1536, 128], [64, 8], [-32, 2], [1, 32]]),
                           stab, ALU.mult),
                       incs=["vs"])
                p.step("vector",
                       lambda v: v.tensor_add(trot[:, 512:1024], trot[:, 0:512],
                                              trot[:, 512:1024]),
                       incs=["vs"])
                p.step("vector",
                       lambda v, nb=nb: v.tensor_tensor(
                           trot[:, 1024:1536], trot[:, 512:1024],
                           bass.AP(nrm, nb, [[16, 128], [1, 8], [0, 64]]),
                           ALU.mult),
                       incs=["vs"])
                v_rot = p.at("vs")
                if _os.environ.get("PROBE_L0F_QKV") == "rot":
                    continue
                for hp in range(4):
                    sl = hp % 2
                    p.step("tensor",
                           lambda t, hp=hp, sl=sl: t.transpose(
                               trp[sl][:, :], trot[:, 1024 + hp * 128:
                                                   1024 + (hp + 1) * 128],
                               co["identf"][:, :]),
                           waits=[("vs", v_rot), ("vs", tr_free[sl]),
                                  co["ready2"]],
                           incs=["ts"])
                    t_tr = p.at("ts")
                    if nm2 == "q":
                        dst = qfm[:, hp * 2048 + t0: hp * 2048 + t0 + 128]
                    else:
                        dst = kfm[:, hp * 2560 + lt * 128:
                                  hp * 2560 + (lt + 1) * 128]
                    p.step("vector",
                           lambda v, dst=dst, sl=sl: v.tensor_copy(dst,
                                                                   trp[sl][:, :]),
                           waits=[("ts", t_tr)], incs=["vs"])
                    tr_free[sl] = p.at("vs")
        qkv_done = ("vs", p.at("vs"))

        if _os.environ.get("PROBE_L0F_STOP") == "qkv":
            p._l0f_ylast = [("vs", p.at("vs")), ("y0", p.at("y0")),
                            ("y1", p.at("y1"))]
            break
        p.dma("gpsimd", bass.AP(pool2, 0, [[6144, 128], [512, 12], [1, 512]]),
              bass.AP(dr["l0f_masks"], 0, [[512, 128], [128 * 512, 12], [1, 512]]),
              waits=[qkv_done], sem="mk")
        msk_ok = ("mk", p.at("mk"))

        sslot = [psa[:, 2048:2560], psa[:, 2560:3072]]
        oslot = [psa[0:65, 3072:3584], psa[0:65, 3584:4096]]
        s_free = [p.at("ss"), p.at("ss")]
        pt_free = [p.at("ts"), p.at("ts")]
        o_free = [p.at("ss"), p.at("ss")]
        for h in range(8):
            hp = h // 2
            hb = 64 * (h % 2)
            for w in range(4):
                osl = oslot[(h * 4 + w) % 2]
                for kvt in range(8):
                    g = 4 * w + kvt
                    ssl = sslot[kvt % 2]
                    mtile = (8 + kvt) if (w == 0 and kvt < 4) else kvt
                    p.step("tensor",
                           lambda t, hp=hp, hb=hb, g=g, w=w, ssl=ssl: t.matmul(
                               ssl[:, :],
                               kfm[hb:hb + 64, hp * 2560 + g * 128:
                                   hp * 2560 + (g + 1) * 128],
                               qfm[hb:hb + 64, hp * 2048 + w * 512:
                                   hp * 2048 + (w + 1) * 512],
                               start=True, stop=True),
                           waits=[qkv_done, ("ss", s_free[kvt % 2])],
                           incs=["ts"])
                    t_s = p.at("ts")
                    p.step("vector",
                           lambda v, ssl=ssl, mtile=mtile: v.tensor_tensor(
                               ssl[:, :], ssl[:, :],
                               pool2[:, mtile * 512:(mtile + 1) * 512],
                               ALU.add),
                           waits=[("ts", t_s), msk_ok], incs=["vs"])
                    v_m = p.at("vs")
                    p.step("scalar",
                           lambda s, ssl=ssl, kvt=kvt: s.activation(
                               ptb[:, (kvt % 2) * 512:(kvt % 2) * 512 + 512],
                               ssl[:, :], AF.Exp, bias=negbl[:, 0:1], scale=8.0),
                           waits=[("vs", v_m), negb_ok,
                                  ("ts", pt_free[kvt % 2])],
                           incs=["ss"])
                    s_free[kvt % 2] = p.at("ss")
                    s_e = p.at("ss")
                    p.step("tensor",
                           lambda t, g=g, h=h, kvt=kvt, osl=osl: t.matmul(
                               osl[:, :],
                               v65[:, (g * 8 + h) * 65:(g * 8 + h) * 65 + 65],
                               ptb[:, (kvt % 2) * 512:(kvt % 2) * 512 + 512],
                               start=(kvt == 0), stop=(kvt == 7)),
                           waits=[("ss", s_e)] +
                                 ([("ss", o_free[(h * 4 + w) % 2])]
                                  if kvt == 0 else []),
                           incs=["ts"])
                    pt_free[kvt % 2] = p.at("ts")
                t_o = p.at("ts")
                p.step("scalar",
                       lambda s, osl=osl: s.copy(ostg, osl[0:64, :]),
                       waits=[("ts", t_o), ("dsy", p.at("dsy")),
                              ("r1", p.at("r1"))], incs=["ss"])
                p.step("scalar",
                       lambda s, osl=osl: s.copy(row[:, :], osl[64:65, :]),
                       incs=["ss"])
                o_free[(h * 4 + w) % 2] = p.at("ss")
                p.dma("sync",
                      bass.AP(o_d, (128 * hp + hb) * 2048 + w * 512,
                              [[2048, 64], [1, 512]]),
                      ostg, waits=["ss"])
                p.dma("sync",
                      bass.AP(pool, h * 2048 + w * 512, [[2048, 1], [1, 512]]),
                      row[:, :], waits=["ss"], sem="r1")
        o_all = ("dsy", p.at("dsy"))
        den_all = ("r1", p.at("r1"))

        if _os.environ.get("PROBE_L0F_STOP") == "attn":
            p._l0f_ylast = [o_all, den_all, ("y0", p.at("y0")),
                            ("y1", p.at("y1"))]
            break
        p.step("vector", lambda v: v.reciprocal(den[:, :], den[:, :]),
               waits=[den_all], incs=["vs"])
        rec_ok = ("vs", p.at("vs"))
        w_wo = p.dma("gpsimd", wstr[:, :],
                     bass.AP(dr["l0f_wo"], lyr * 512 * 1024,
                             [[1024, 128], [1024 * 128, 4], [1, 1024]]),
                     waits=[("ts", p.at("ts"))], sem="wo")
        ochl = p.dma("sync", och[:, :],
                     bass.AP(o_d, 0, [[2048, 128], [128 * 2048, 4], [1, 2048]]),
                     waits=[o_all], sem="sa0")
        for h in range(8):
            hp = h // 2
            hb = 64 * (h % 2)
            p.dma("sync", rrow[:, :],
                  bass.AP(pool, h * 2048, [[2048, 1], [1, 2048]]),
                  waits=[rec_ok, ("ts", p._l0f_rb)], sem="r1")
            r_ld = ("r1", p.at("r1"))
            for w in range(4):
                rb = psa[:, 1024 + (w % 2) * 512:1024 + (w % 2) * 512 + 512]
                p.step("tensor",
                       lambda t, w=w, rb=rb: t.matmul(
                           rb, co["ones1f"][:, :],
                           rrow[0:1, w * 512:(w + 1) * 512],
                           start=True, stop=True),
                       waits=[r_ld, ("vs", p.at("vs"))], incs=["ts"])
                t_rb = p.at("ts")
                p._l0f_rb = t_rb
                p.step("vector",
                       lambda v, hp=hp, hb=hb, w=w, rb=rb: v.tensor_mul(
                           och[hb:hb + 64, hp * 2048 + w * 512:
                               hp * 2048 + (w + 1) * 512],
                           och[hb:hb + 64, hp * 2048 + w * 512:
                               hp * 2048 + (w + 1) * 512],
                           rb[hb:hb + 64, :]),
                       waits=[("ts", t_rb), ochl], incs=["vs"])
        on_ok = ("vs", p.at("vs"))
        dslot = [psa[:, 0:512], psa[:, 512:1024]]
        d_free = [p.at("vs"), p.at("vs")]
        for cx in range(8):
            for lq2 in range(4):
                sl = (cx * 4 + lq2) % 2
                p.step("tensor",
                       lambda t, cx=cx, lq2=lq2, sl=sl: [
                           t.matmul(dslot[sl][:, :],
                                    wstr[:, hdc * 1024 + cx * 128:
                                         hdc * 1024 + (cx + 1) * 128],
                                    och[:, hdc * 2048 + lq2 * 512:
                                        hdc * 2048 + (lq2 + 1) * 512],
                                    start=(hdc == 0), stop=(hdc == 3))
                           for hdc in range(4)],
                       waits=[w_wo, on_ok, ("vs", d_free[sl])], incs=["ts"])
                t_d = p.at("ts")
                p.step("vector",
                       lambda v, sl=sl: v.tensor_copy(
                           ystg[:, sl * 512:(sl + 1) * 512], dslot[sl][:, :]),
                       waits=[("ts", t_d), ("y%d" % sl, p._l0f_yst[sl])],
                       incs=["vs"])
                d_free[sl] = p.at("vs")
                p.dma("gpsimd",
                      bass.AP(ybuf_d, cx * 128 * YW + 1 + lq2 * 512,
                              [[YW, 128], [1, 512]]),
                      ystg[:, sl * 512:(sl + 1) * 512], waits=["vs"],
                      sem="y%d" % sl, accum_op=ALU.add)
                p._l0f_yst[sl] = p.at("y%d" % sl)
        yacc = [("y0", p.at("y0")), ("y1", p.at("y1"))]
        for c in range(8):
            p.dma("gpsimd", ybc[:, ycol(c, 0):ycol(c, 0) + 2048],
                  bass.AP(ybuf_d, c * 128 * YW + 1, [[YW, 128], [1, 2048]]),
                  waits=yacc + [("vs", p.at("vs"))], sem="r0")
        ymid_ok = ("r0", p.at("r0"))

        p.dma("sync", agB_in.ap(),
              bass.AP(ybc, ycol(0, 2047), [[8 * YBS, 128], [YBS, 8]]),
              waits=[ymid_ok], allow_slow_non_contiguous=True)
        p.step("gpsimd",
               lambda g: g.collective_compute(
                   "AllGather", ALU.bypass,
                   replica_groups=[[0, 1, 2, 3], [4, 5, 6, 7]],
                   ins=[agB_in.ap().opt()], outs=[agB_out.ap().opt()]),
               waits=["dsy", "dgp", "sa0", "sa1", "r0", "r1"], incs=["cc"],
               chain=False)
        agbv = ("cc", p.at("cc"))
        p.step("sync",
               lambda sy: sy.dma_start(
                   out=hstg[:, :],
                   in_=bass.AP(agB_out, sy.value_load(offs[0:1, 1:2]),
                               [[8, 128], [1, 8]])),
               waits=[agbv], incs=[("dsy", 16)], chain=False)
        hb_ld = ("dsy", p.at("dsy"))
        p.step("vector",
               lambda v: v.tensor_scalar(hstf[:, :], hstg[:, :], flagc[:, 0:1],
                                         None, ALU.mult),
               waits=[hb_ld, pre_ok[0]], incs=["vs"])
        p.dma("sync", bass.AP(ybuf_d, 0, [[YW, 128], [128 * YW, 8]]),
              bass.AP(hstf, 0, [[8, 128], [1, 8]]), waits=["vs"],
              allow_slow_non_contiguous=True)
        ymid_halo = ("dsy", p.at("dsy"))

        for lq2 in range(4):
            o2 = lq2 * 512

            def rhs_iln(c, o2=o2):
                off = 1 if c < 4 else 0
                sl = c % 2
                ld = p.dma("sync", xsl[sl],
                           bass.AP(ybuf_d, c * 128 * YW + off + o2,
                                   [[YW, 128], [1, 512]]),
                           waits=[ymid_halo, ("ts", p._l0f_sqfree[sl]),
                                  ("vs", p.at("vs"))], sem="sa%d" % sl)
                return xsl[sl], [ld]
            t_b2 = ln_stats_q(rhs_iln, [])
            for c in range(8):
                off = 1 if c < 4 else 0
                sl = c % 2
                ld = p.dma("sync", xsl[sl],
                           bass.AP(ybuf_d, c * 128 * YW + off + o2,
                                   [[YW, 128], [1, 512]]),
                           waits=[("ts", p._l0f_sqfree[sl]), ("vs", p.at("vs")),
                                  ("ss", p.at("ss"))], sem="sa%d" % sl)
                p.step("vector",
                       lambda v, sl=sl: v.tensor_sub(sq2[sl], xsl[sl],
                                                     mub[:, :]),
                       waits=[ld, ("ts", t_b2)], incs=["vs"])
                p.step("vector",
                       lambda v, sl=sl: v.tensor_mul(sq2[sl], sq2[sl],
                                                     rsb[:, :]),
                       incs=["vs"])
                p.step("vector",
                       lambda v, c=c, o2=o2, sl=sl: v.tensor_scalar(
                           ybc[:, ycol(c, o2):ycol(c, o2) + 512], sq2[sl],
                           colz[:, c:c + 1], colz[:, 8 + c:9 + c],
                           ALU.mult, ALU.add),
                       waits=[wconst], incs=["vs"])
        h_ok = ("vs", p.at("vs"))

        w1_free = [p.at("ts"), p.at("ts")]
        w2_free = [p.at("ts"), p.at("ts")]
        for lq2 in range(4):
            o2 = lq2 * 512
            ap_slot = [psa[:, 0:512], psa[:, 512:1024]]
            gp_slot = [psa[:, 1024:1536], psa[:, 1536:2048]]
            for j in range(22):
                sl = j % 2
                wld = p.dma("gpsimd", wstr[:, sl * 2048:(sl + 1) * 2048],
                            bass.AP(dr["l0f_w1i"], lyr * 1024 * 5632 + j * 256,
                                    [[5632, 128], [5632 * 128, 8], [1, 256]]),
                            waits=[("ts", w1_free[sl])], sem="w%d" % sl)
                p.step("tensor",
                       lambda t, j=j, sl=sl, o2=o2: [
                           t.matmul((ap_slot if half == 0 else gp_slot)[sl][:, :],
                                    wstr[:, sl * 2048 + ci * 256 + half * 128:
                                         sl * 2048 + ci * 256 + half * 128 + 128],
                                    ybc[:, ycol(ci, o2):ycol(ci, o2) + 512],
                                    start=(ci == 0), stop=(ci == 7))
                           for half in range(2) for ci in range(8)],
                       waits=[wld, h_ok, ("vs", p._l0f_geg[sl])],
                       incs=["ts"])
                t_w1 = p.at("ts")
                w1_free[sl] = t_w1
                p.step("scalar",
                       lambda s, j=j, sl=sl: s.activation(
                           tmsc[:, sl * 512:(sl + 1) * 512], gp_slot[sl][:, :],
                           AF.Gelu, bias=b1c[:, 2 * j + 1:2 * j + 2]),
                       waits=[("ts", t_w1), wconst], incs=["ss"])
                p.step("vector",
                       lambda v, j=j, sl=sl: v.scalar_tensor_tensor(
                           h2q[:, j * 512:(j + 1) * 512], ap_slot[sl][:, :],
                           b1c[:, 2 * j:2 * j + 1],
                           tmsc[:, sl * 512:(sl + 1) * 512],
                           ALU.add, ALU.mult),
                       waits=["ss"], incs=["vs"])
                p._l0f_geg[sl] = p.at("vs")
            h2_ok = p.at("vs")
            t_p1 = p.at("ts")
            w2_free = [max(w2_free[0], t_p1), max(w2_free[1], t_p1)]
            dsl2 = [psa[:, 2048 + k * 512:2048 + (k + 1) * 512] for k in range(4)]
            for cg in range(2):
                for j in range(22):
                    sl = j % 2
                    wld = p.dma("gpsimd", wstr[:, 2048 + sl * 512 + 0:
                                               2048 + sl * 512 + 512]
                                if False else
                                bass.AP(wstr, 4096 - (2 - sl) * 512,
                                        [[4096, 128], [1, 512]]),
                                bass.AP(dr["l0f_w2p"],
                                        lyr * 2816 * 1024 + j * 128 * 1024 +
                                        cg * 512,
                                        [[1024, 128], [1, 512]]),
                                waits=[("ts", w2_free[sl])], sem="k%d" % sl)
                    p.step("tensor",
                           lambda t, j=j, sl=sl, cg=cg: [
                               t.matmul(dsl2[ci][:, :],
                                        bass.AP(wstr,
                                                4096 - (2 - sl) * 512 + ci * 128,
                                                [[4096, 128], [1, 128]]),
                                        h2q[:, j * 512:(j + 1) * 512],
                                        start=(j == 0), stop=(j == 21))
                               for ci in range(4)],
                           waits=[wld, ("vs", h2_ok)] +
                                 ([("vs", p._l0f_dfl)] if j == 0 else []),
                           incs=["ts"])
                    w2_free[sl] = p.at("ts")
                t_w2 = p.at("ts")
                for ci in range(4):
                    cxg = cg * 4 + ci
                    sl = ci % 2
                    p.step("vector",
                           lambda v, ci=ci, cxg=cxg, sl=sl: v.tensor_scalar(
                               ystg[:, sl * 512:(sl + 1) * 512], dsl2[ci][:, :],
                               colz[:, 16 + cxg:17 + cxg], None, ALU.add),
                           waits=[("ts", t_w2), wconst,
                                  ("y%d" % sl, p._l0f_yst[sl])],
                           incs=["vs"])
                    p.dma("gpsimd",
                          bass.AP(ybuf_d, cxg * 128 * YW + 1 + o2,
                                  [[YW, 128], [1, 512]]),
                          ystg[:, sl * 512:(sl + 1) * 512], waits=["vs"],
                          sem="y%d" % sl, accum_op=ALU.add)
                    p._l0f_yst[sl] = p.at("y%d" % sl)
                p._l0f_dfl = p.at("vs")
            t_p2 = p.at("ts")
            w1_free = [max(w1_free[0], t_p2), max(w1_free[1], t_p2)]
        p._l0f_tabfree = p.at("vs")
        yacc = [("y0", p.at("y0")), ("y1", p.at("y1"))]
        if lyr == 0:
            for c in range(8):
                p.dma("gpsimd", ybc[:, ycol(c, 0):ycol(c, 0) + 2048],
                      bass.AP(ybuf_d, c * 128 * YW + 1, [[YW, 128], [1, 2048]]),
                      waits=yacc + [("vs", p.at("vs"))], sem="r0")
            p._l0f_ymid = ("r0", p.at("r0"))
        else:
            p._l0f_ymid = yacc[0]
        p._l0f_ylast = yacc

    yfin = p._l0f_ylast
    for c in range(8):
        for lq in range(4):
            sl = (c * 4 + lq) % 2
            o = lq * 512
            ld = p.dma("sync", xsl[sl],
                       bass.AP(x_in, c * 128 * L + o, [[L, 128], [1, 512]]),
                       waits=list(yfin) + [("vs", p._l0f_xfree[sl])],
                       sem="sa%d" % sl)
            ld2 = p.dma("gpsimd", ystg[:, sl * 512:(sl + 1) * 512],
                        bass.AP(ybuf_d, c * 128 * YW + 1 + o,
                                [[YW, 128], [1, 512]]),
                        waits=list(yfin) + [("vs", p._l0f_xfree[sl])],
                        sem="y%d" % sl)
            p.step("vector",
                   lambda v, sl=sl: v.tensor_add(
                       sq2[sl], xsl[sl], ystg[:, sl * 512:(sl + 1) * 512]),
                   waits=[ld, ld2, ("st%d" % sl, p.at("st%d" % sl))],
                   incs=["vs"])
            p._l0f_xfree[sl] = p.at("vs")
            p.dma("sync", bass.AP(x_out, c * 128 * L + o, [[L, 128], [1, 512]]),
                  sq2[sl], waits=["vs"], sem="st%d" % sl)
    return ("st0", p.at("st0")), ("st1", p.at("st1"))


def build_l0_ffn_standalone():
    es = ExitStack()
    nc = bass.Bass("TRN2", target_bir_lowering=False)
    x_in = nc.dram_tensor("x_in", [D, L], F32, kind="ExternalInput")
    ident = nc.dram_tensor("c_ident", [128, 128], F32, kind="ExternalInput")
    dr = {}
    for nme, shape, dt in [
            ("l0f_tabs", [4, 2560, 64], F32), ("l0f_masks", [12, 128, 512], F32),
            ("l0f_flag", [128, 1], F32), ("l0f_proff", [1, 2], mybir.dt.int32),
            ("l0f_wqkv", [2, 1024, 1536], F32), ("l0f_wo", [2, 512, 1024], F32),
            ("l0f_qs", [2, 64], F32), ("l0f_ks", [2, 64], F32),
            ("l0f_lng", [2, 1024], F32), ("l0f_lnb", [2, 1024], F32),
            ("l0f_b2", [2, 1024], F32), ("l0f_w1i", [2, 1024, 5632], F32),
            ("l0f_b1i", [2, 5632], F32), ("l0f_w2p", [2, 2816, 1024], F32)]:
        dr[nme] = nc.dram_tensor(nme, shape, dt, kind="ExternalInput")
    x_out = nc.dram_tensor("x_out", [D, L], F32, kind="ExternalOutput")
    with es:
        p = Prog(nc, es)
        co = setup_consts(p, ident)
        psa = p.psum("psa", [128, 4096])
        m1, m2 = emit_l0_ffn(p, co, dr, psa, x_in, x_out)
        p.step("sync", lambda sy: None, waits=[m1, m2], chain=False)
        p.build()
    return nc


def run_l0_ffn_standalone(x1_full, ins):
    nc = build_l0_ffn_standalone()
    in_maps = []
    id_ = np.eye(128, dtype=np.float32)
    for ci in range(NCORE):
        b, cch = divmod(ci, 4)
        w = prep_l0_ffn(ins, ci)
        xfm = np.ascontiguousarray(x1_full[b, cch * L:(cch + 1) * L, :].T)
        in_maps.append({"x_in": xfm, "c_ident": id_, **w})
    res = run_bass_kernel_spmd(nc, in_maps, core_ids=list(range(NCORE)))
    out = np.empty((2, SEQ, D), np.float32)
    for ci in range(NCORE):
        b, cch = divmod(ci, 4)
        out[b, cch * L:(cch + 1) * L, :] = np.asarray(res.results[ci]["x_out"]).T
    return out


NKH = 8
TOT = 8224


def prep_l0_attn(ins, core_id):
    b, cch = divmod(core_id, 4)
    d = {}
    win = ins["L0_attn_W_in"]
    wmem = np.zeros((1024, 1152), np.float32)
    wmem[:, :1056] = win[:, :1056]
    d["l0a_wmem"] = wmem
    wq = win[:, 1056:].reshape(1024, 4, 32, 8, 2)
    d["l0a_wq"] = np.ascontiguousarray(
        wq.transpose(0, 4, 1, 2, 3).reshape(1024, 2048))
    ck = ins["L0_attn_conv_k"][:, 0, :]
    ckp = np.zeros((4, 1152), np.float32)
    ckp[:, :1056] = ck
    ckc = np.zeros((128, 36), np.float32)
    for ci in range(9):
        for s in range(4):
            ckc[:, ci * 4 + s] = ckp[s, ci * 128:(ci + 1) * 128]
    d["l0a_ckc"] = ckc
    th = ins["L0_attn_theta"][0, 0]
    assert np.allclose(th[..., :4], -th[..., 7:3:-1], atol=1e-6), \
        "theta m-antisymmetry violated"
    d["l0a_throw"] = np.ascontiguousarray(th[..., 4:8].reshape(1, 4096))
    d["l0a_slopes"] = np.concatenate([ins["L0_attn_decay_slopes"],
                                      ins["L0_attn_anchor_slopes"]])[None, :]
    d["l0a_score"] = ins["L0_attn_score_scale"][None, :]
    t0 = cch * L
    pos = np.arange(t0, t0 + L, dtype=np.float64)
    dist = np.maximum(float(SEQ - 1) - pos, 0.0)
    d["l0a_ttab"] = np.stack([dist, pos], 1).astype(np.float32)
    c1 = ins["L0_attn_norm_scale"][:32] * np.diagonal(ins["L0_attn_W_re"])
    c2 = ins["L0_attn_norm_scale"][32:] * np.diagonal(ins["L0_attn_W_im"])
    d["l0a_c1"] = np.ascontiguousarray(c1[None, :])
    d["l0a_c2"] = np.ascontiguousarray(c2[None, :])
    d["l0a_wout"] = np.ascontiguousarray(ins["L0_attn_W_out"])
    tri = np.tril(np.ones((128, 128), np.float32)).T
    d["l0a_tri"] = tri
    w4 = np.zeros((4, 1), np.float32)
    w4[:cch, 0] = 1.0
    d["l0a_w4"] = w4
    d["l0a_flag"] = np.full((128, 1), 0.0 if cch == 0 else 1.0, np.float32)
    return d


def emit_l0_attn(p, co, dr, psa, x_in, x_out):
    nc = p.nc
    z_d = nc.dram_tensor("l0a_zd", [1152, 2176], F32)
    q_d = nc.dram_tensor("l0a_qd", [L, 2048], BF16)
    kv_d = nc.dram_tensor("l0a_kvd", [L, 1152], F32)
    qf_d = nc.dram_tensor("l0a_qfd", [4, L, 512], BF16)
    a_d = nc.dram_tensor("l0a_ad", [2, 8, L, 512], F32)
    o_d2 = nc.dram_tensor("l0a_od", [2, L, 1024], F32)
    agin = nc.dram_tensor("l0a_agin", [1, TOT], F32)
    agout = nc.dram_tensor("l0a_agout", [4, TOT], F32)

    bigb = p.sbuf("l0a_bigb", [128, 17408], BF16)
    bigf = p.sbuf("l0a_bigf", [128, 11264])
    wpool = p.sbuf("l0a_wpool", [128, 17408], BF16)
    thb = p.sbuf("l0a_thb", [128, 4096])
    pw = p.sbuf("l0a_pw", [128, 16 * 32])
    inv = p.sbuf("l0a_inv", [128, 16 * 32])
    stb = p.sbuf("l0a_stb", [128, 2048], BF16)
    stat = p.sbuf("l0a_stat", [128, 8])
    crow = p.sbuf("l0a_crow", [1, 1024])
    evr = p.sbuf("l0a_evr", [1, 1024])
    rows = p.sbuf("l0a_rows", [1, 4096 + 128])
    sprow = p.sbuf("l0a_sp", [1, 32])
    spb = p.sbuf("l0a_spb", [128, 32])
    scb = p.sbuf("l0a_scb", [128, 32])
    c12 = p.sbuf("l0a_c12", [128, 64])
    ttab = p.sbuf("l0a_ttabs", [128, 16 * 2])
    w4c = p.sbuf("l0a_w4c", [4, 1])
    flagc = p.sbuf("l0a_flagc", [128, 1])
    trif = p.sbuf("l0a_trif", [128, 128])
    npi2 = p.sbuf("l0a_npi2", [128, 1])
    xc = bigb[:, 0:17408]
    lns = bigf[:, 0:2048]
    lnt = bigf[:, 2048:3072]
    xltf = bigf[:, 3072:5120]
    stg = bigf[:, 5120:7168]
    wm = wpool[:, 0:9216]
    wst = wpool[:, 9216:17408]
    zcs = bigf[:, 0:4352]
    kvf = bigf[:, 4352:6400]
    kvf2 = bigf[:, 6400:8448]
    kvs = bigf[:, 0:256]
    tt = bigf[:, 256:4352]
    sst = bigf[:, 4352:6400]
    ab = bigf[:, 0:2048]
    ttB = bigf[:, 2048:3584]
    orow = bigf[:, 3584:4096]
    c4s = bigf[0:4, 4096:5120]
    qf = bigb[:, 0:8192]
    oslab = bigf[:, 0:4096]
    ytm = bigf[:, 4096:5120]
    xo = bigf[:, 5120:7168]
    xfs = bigf[:, 7168:9216]
    lns2 = bigf[:, 9216:11264]
    yfm = bigb[:, 8192:10240]
    wout = wpool[:, 0:8192]

    p.dma("gpsimd", trif[:, :], dr["l0a_tri"].ap(), sem="pre")
    p.dma("gpsimd", w4c[:, :], dr["l0a_w4"].ap(), sem="pre")
    p.dma("gpsimd", flagc[:, :], dr["l0a_flag"].ap(), sem="pre")
    p.dma("gpsimd", rows[0:1, 0:4096], dr["l0a_throw"].ap(), sem="pre")
    p.dma("gpsimd", rows[0:1, 4096:4128], dr["l0a_slopes"].ap(), sem="pre")
    p.dma("gpsimd", rows[0:1, 4128:4160], dr["l0a_score"].ap(), sem="pre")
    p.dma("gpsimd", rows[0:1, 4160:4192], dr["l0a_c1"].ap(), sem="pre")
    p.dma("gpsimd", rows[0:1, 4192:4224], dr["l0a_c2"].ap(), sem="pre")
    p.dma("gpsimd", ttab[:, :],
          bass.AP(dr["l0a_ttab"], 0, [[2, 128], [128 * 2, 16], [1, 2]]),
          sem="pre")
    p.step("gpsimd", lambda g: g.memset(npi2[:, :], -np.pi / 2), incs=["gs"])
    pre_ok = [("pre", p.at("pre")), ("gs", p.at("gs"))]
    for cc in range(8):
        p.step("tensor",
               lambda t, cc=cc: t.matmul(psa[:, (cc % 4) * 512:(cc % 4) * 512 + 512],
                                         co["ones1f"][:, :],
                                         rows[0:1, cc * 512:(cc + 1) * 512],
                                         start=True, stop=True),
               waits=pre_ok + [co["ready"], ("vs", p.at("vs"))], incs=["ts"])
        p.step("vector",
               lambda v, cc=cc: v.tensor_copy(thb[:, cc * 512:(cc + 1) * 512],
                                              psa[:, (cc % 4) * 512:(cc % 4) * 512 + 512]),
               waits=["ts"], incs=["vs"])
    p.step("scalar", lambda s: s.activation(sprow[:, :], rows[0:1, 4096:4128],
                                            AF.Abs),
           waits=pre_ok, incs=["ss"])
    p.step("scalar", lambda s: s.activation(sprow[:, :], sprow[:, :], AF.Exp,
                                            scale=-1.0),
           incs=["ss"])
    p.step("vector", lambda v: v.tensor_scalar_add(sprow[:, :], sprow[:, :], 1.0),
           waits=["ss"], incs=["vs"])
    p.step("scalar", lambda s: s.activation(sprow[:, :], sprow[:, :], AF.Ln),
           waits=["vs"], incs=["ss"])
    p.step("vector", lambda v: v.tensor_scalar_max(evr[0:1, 0:32],
                                                   rows[0:1, 4096:4128], 0.0),
           waits=pre_ok + ["ss"], incs=["vs"])
    p.step("vector", lambda v: v.tensor_add(sprow[:, :], sprow[:, :],
                                            evr[0:1, 0:32]),
           waits=["ss"], incs=["vs"])
    p.step("vector", lambda v: v.tensor_scalar_mul(sprow[:, :], sprow[:, :],
                                                   -1.0),
           incs=["vs"])
    v_sp = p.at("vs")
    p.step("tensor", lambda t: t.matmul(psa[:, 0:32], co["ones1f"][:, :],
                                        sprow[0:1, :], start=True, stop=True),
           waits=[("vs", v_sp), ("vs", p.at("vs"))], incs=["ts"])
    p.step("vector", lambda v: v.tensor_copy(spb[:, :], psa[:, 0:32]),
           waits=["ts"], incs=["vs"])
    p.step("tensor", lambda t: t.matmul(psa[:, 32:64], co["ones1f"][:, :],
                                        rows[0:1, 4128:4160], start=True,
                                        stop=True),
           waits=pre_ok + [("vs", p.at("vs"))], incs=["ts"])
    p.step("vector", lambda v: v.tensor_copy(scb[:, :], psa[:, 32:64]),
           waits=["ts"], incs=["vs"])
    p.step("tensor", lambda t: t.matmul(psa[:, 64:128], co["ones1f"][:, :],
                                        rows[0:1, 4160:4224], start=True,
                                        stop=True),
           waits=pre_ok + [("vs", p.at("vs"))], incs=["ts"])
    p.step("vector", lambda v: v.tensor_copy(c12[:, :], psa[:, 64:128]),
           waits=["ts"], incs=["vs"])
    con_ok = ("vs", p.at("vs"))

    trpL = [psa[:, 2560:2688], psa[:, 3072:3200]]
    tr_free = [p.at("vs"), p.at("vs")]
    lns_free = [p.at("vs"), p.at("vs")]
    xlt_free = [p.at("ts"), p.at("ts")]
    rstd_ap = bass.AP(stat, 6, [[8, 128], [0, 1024]])
    for ht in range(17):
        sl = ht % 2
        if ht == 0:
            src = dr["l0a_x0h"].ap()
        else:
            src = bass.AP(x_in, (ht - 1) * 128 * 1024, [[1024, 128], [1, 1024]])
        xv = lns[:, sl * 1024:(sl + 1) * 1024]
        lw = p.dma("sync", xv, src, waits=[("vs", lns_free[sl])],
                   sem="sa" + str(sl))
        p.step("vector", lambda v, xv=xv: v.tensor_reduce(stat[:, 0:1], xv,
                                                          AX.X, ALU.add),
               waits=[lw], incs=["vs"])
        p.step("scalar", lambda s, xv=xv: s.activation(lnt[:, :], xv, AF.Square),
               waits=[lw, "vs"], incs=["ss"])
        p.step("vector", lambda v: v.tensor_reduce(stat[:, 1:2], lnt[:, :],
                                                   AX.X, ALU.add),
               waits=["ss"], incs=["vs"])
        p.step("vector", lambda v: v.tensor_scalar_mul(stat[:, 2:3],
                                                       stat[:, 0:1], 1.0 / D),
               incs=["vs"])
        p.step("vector", lambda v: v.tensor_mul(stat[:, 3:4], stat[:, 2:3],
                                                stat[:, 2:3]),
               incs=["vs"])
        p.step("vector", lambda v: v.tensor_scalar(stat[:, 4:5], stat[:, 1:2],
                                                   1.0 / D, EPS, ALU.mult,
                                                   ALU.add),
               incs=["vs"])
        p.step("vector", lambda v: v.tensor_sub(stat[:, 4:5], stat[:, 4:5],
                                                stat[:, 3:4]),
               incs=["vs"])
        p.step("scalar", lambda s: s.activation(stat[:, 5:6], stat[:, 4:5],
                                                AF.Sqrt),
               waits=["vs"], incs=["ss"])
        p.step("vector", lambda v: v.reciprocal(stat[:, 6:7], stat[:, 5:6]),
               waits=["ss"], incs=["vs"])
        xlf = xltf[:, sl * 1024:(sl + 1) * 1024]
        p.step("vector",
               lambda v, xv=xv, xlf=xlf: v.scalar_tensor_tensor(
                   xlf, xv, stat[:, 2:3], rstd_ap, ALU.subtract, ALU.mult),
               waits=[("ts", xlt_free[sl])], incs=["vs"])
        lns_free[sl] = p.at("vs")
        v_x = p.at("vs")
        for cc in range(8):
            tsl = cc % 2
            p.step("tensor",
                   lambda t, cc=cc, tsl=tsl, xlf=xlf: t.transpose(
                       trpL[tsl][:, :], xlf[:, cc * 128:(cc + 1) * 128],
                       co["identf"][:, :]),
                   waits=[("vs", v_x), ("vs", tr_free[tsl]), co["ready2"]],
                   incs=["ts"])
            t_tr = p.at("ts")
            p.step("vector",
                   lambda v, cc=cc, ht=ht, tsl=tsl: v.tensor_copy(
                       xc[:, cc * 2176 + ht * 128:cc * 2176 + (ht + 1) * 128],
                       trpL[tsl][:, :]),
                   waits=[("ts", t_tr)], incs=["vs"])
            tr_free[tsl] = p.at("vs")
        xlt_free[sl] = p.at("ts")
        if ht == 0:
            halo_ap = bass.AP(bigb, 0, [[17408, 128], [2176, 8], [1, 128]])
            p.step("vector",
                   lambda v: v.tensor_scalar(halo_ap, halo_ap, flagc[:, 0:1],
                                             None, ALU.mult),
                   waits=[pre_ok[0]], incs=["vs"])
    xc_ok = ("vs", p.at("vs"))
    if _os.environ.get("PROBE_L0A_STOP") == "ln":
        p._l0a_last = [xc_ok]
        return

    wm_ok = p.dma("gpsimd", wm,
                  bass.AP(dr["l0a_wmem"], 0,
                          [[1152, 128], [128 * 1152, 8], [1, 1152]]),
                  sem="ca0")
    zslots = [psa[:, 0:512], psa[:, 512:1024]]
    zfree = [p.at("vs"), p.at("vs")]
    zst_dma = [None, None]
    blocks = [(0, 512), (512, 512), (1024, 512), (1536, 512), (2048, 128)]
    ii = 0
    for ci in range(9):
        for (b0, bn) in blocks:
            sl = ii % 2
            ii += 1
            for kc in range(8):
                w = [xc_ok, wm_ok] if kc == 0 else []
                if kc == 0:
                    w.append(("vs", zfree[sl]))
                p.step("tensor",
                       lambda t, kc=kc, ci=ci, b0=b0, bn=bn, sl=sl: t.matmul(
                           zslots[sl][:, 0:bn],
                           wm[:, kc * 1152 + ci * 128:kc * 1152 + (ci + 1) * 128],
                           xc[:, kc * 2176 + b0:kc * 2176 + b0 + bn],
                           start=(kc == 0), stop=(kc == 7)),
                       waits=w, incs=(["ts"] if kc == 7 else []))
            t_z = p.at("ts")
            zs = "st" + str(sl)
            p.step("vector",
                   lambda v, sl=sl, bn=bn: v.tensor_copy(
                       stg[:, sl * 512:sl * 512 + bn], zslots[sl][:, 0:bn]),
                   waits=[("ts", t_z)] +
                         ([zst_dma[sl]] if zst_dma[sl] else []),
                   incs=["vs"])
            zfree[sl] = p.at("vs")
            zst_dma[sl] = p.dma(
                "sync", bass.AP(z_d, ci * 128 * 2176 + b0, [[2176, 128], [1, bn]]),
                stg[:, sl * 512:sl * 512 + bn], waits=["vs"], sem=zs)
    z_done = [zst_dma[0], zst_dma[1]]

    qslots = [psa[:, 1024:1536], psa[:, 1536:2048]]
    qfree = [p.at("vs"), p.at("vs")]
    wq_used = [p.at("ts"), p.at("ts")]
    fold_dma = {}
    for ob in range(4):
        wsl = ob % 2
        wq_ok = p.dma("gpsimd", wst[:, wsl * 4096:(wsl + 1) * 4096],
                      bass.AP(dr["l0a_wq"], ob * 512,
                              [[2048, 128], [128 * 2048, 8], [1, 512]]),
                      waits=[("ts", wq_used[wsl])], sem="wq" + str(wsl))
        for lt in range(16):
            qsl = lt % 2
            for cc in range(8):
                w = [xc_ok, wq_ok] if cc == 0 else []
                if cc == 0:
                    w.append(("vs", qfree[qsl]))
                p.step("tensor",
                       lambda t, cc=cc, lt=lt, wsl=wsl, qsl=qsl: t.matmul(
                           qslots[qsl][:, :],
                           xc[:, cc * 2176 + 128 + lt * 128:
                              cc * 2176 + 128 + (lt + 1) * 128],
                           wst[:, wsl * 4096 + cc * 512:wsl * 4096 + (cc + 1) * 512],
                           start=(cc == 0), stop=(cc == 7)),
                       waits=w, incs=(["ts"] if cc == 7 else []))
            wq_used[wsl] = p.at("ts")
            t_q = p.at("ts")
            ssl = 2 + qsl
            p.step("vector",
                   lambda v, ssl=ssl, qsl=qsl: v.tensor_copy(
                       stg[:, ssl * 512:(ssl + 1) * 512], qslots[qsl][:, :]),
                   waits=[("ts", t_q)], incs=["vs"])
            qfree[qsl] = p.at("vs")
            base = 5120 + ssl * 512
            inP0 = bass.AP(bigf, base + 4, [[11264, 128], [8, 64], [1, 4]])
            inP1 = bass.AP(bigf, base + 3, [[11264, 128], [8, 64], [-1, 4]])
            for fi, op in ((0, ALU.add), (1, ALU.subtract)):
                fs = (lt % 2) * 2 + fi
                fsem = "qf" + str(fs)
                w = [(fsem, p.at(fsem))] if fsem in p.cnt else []
                p.step("vector",
                       lambda v, fs=fs, op=op, inP0=inP0, inP1=inP1:
                       v.tensor_tensor(stb[:, fs * 256:(fs + 1) * 256],
                                       inP0, inP1, op),
                       waits=w, incs=["vs"])
                f = fi + (2 if ob >= 2 else 0)
                fold_dma[(f, lt, ob % 2)] = p.dma(
                    "sync",
                    bass.AP(qf_d, (f * L + lt * 128) * 512 + (ob % 2) * 256,
                            [[512, 128], [1, 256]]),
                    stb[:, fs * 256:(fs + 1) * 256], waits=["vs"], sem=fsem)
    qfold_done = [("qf%d" % i, p.at("qf%d" % i)) for i in range(4)]
    if _os.environ.get("PROBE_L0A_STOP") == "proj":
        p._l0a_last = z_done + qfold_done
        return

    cks = p.sbuf("l0a_cks", [128, 36])
    ck_ok = p.dma("gpsimd", cks[:, :], dr["l0a_ckc"].ap(), sem="ca1")
    zcs_free = [p.at("vs"), p.at("vs")]
    kvf_free = p.at("ts")
    kvd_dma = None
    for ci in range(9):
        sl = ci % 2
        zl = p.dma("sync", zcs[:, sl * 2176:(sl + 1) * 2176],
                   bass.AP(z_d, ci * 128 * 2176, [[2176, 128], [1, 2176]]),
                   waits=z_done + [("vs", zcs_free[sl])], sem="ya" + str(sl))
        zb = sl * 2176
        p.step("vector",
               lambda v, zb=zb, ci=ci: v.tensor_scalar(
                   kvf[:, :], zcs[:, zb + 125:zb + 125 + 2048],
                   cks[:, ci * 4:ci * 4 + 1], None, ALU.mult),
               waits=[zl, ck_ok, ("ts", kvf_free)], incs=["vs"])
        for s in range(1, 4):
            p.step("vector",
                   lambda v, zb=zb, ci=ci, s=s: v.scalar_tensor_tensor(
                       kvf[:, :], zcs[:, zb + 125 + s:zb + 125 + s + 2048],
                       cks[:, ci * 4 + s:ci * 4 + s + 1], kvf[:, :],
                       ALU.mult, ALU.add),
                   incs=["vs"])
        zcs_free[sl] = p.at("vs")
        v_kv = p.at("vs")
        for ltb in range(16):
            tsl = ltb % 2
            p.step("tensor",
                   lambda t, ltb=ltb, tsl=tsl: t.transpose(
                       trpL[tsl][:, :], kvf[:, ltb * 128:(ltb + 1) * 128],
                       co["identf"][:, :]),
                   waits=[("vs", v_kv), ("vs", tr_free[tsl]), co["ready2"]],
                   incs=["ts"])
            t_tr = p.at("ts")
            w = [("ts", t_tr)] + ([kvd_dma] if kvd_dma else [])
            p.step("vector",
                   lambda v, ltb=ltb, tsl=tsl: v.tensor_copy(
                       kvf2[:, ltb * 128:(ltb + 1) * 128], trpL[tsl][:, :]),
                   waits=w, incs=["vs"])
            tr_free[tsl] = p.at("vs")
        kvf_free = p.at("ts")
        kvd_dma = p.dma("sync",
                        bass.AP(kv_d, ci * 128,
                                [[1152, 128], [128 * 1152, 16], [1, 128]]),
                        kvf2[:, :], waits=["vs"], sem="rv0")
    conv_done = kvd_dma
    trp_clear = [("vs", tr_free[0]), ("vs", tr_free[1])]
    if _os.environ.get("PROBE_L0A_STOP") == "conv":
        p._l0a_last = [conv_done]
        return
    p2_vs = p.at("vs")

    kvs_free = [p.at("vs"), p.at("vs")]
    t1 = tt[:, 0:32]
    t2 = tt[:, 32:64]
    tg_dma = [None, None]
    for lt in range(16):
        sl = lt % 2
        kd = p.dma("sync", kvs[:, sl * 128:sl * 128 + 32],
                   bass.AP(kv_d, lt * 128 * 1152 + 1024, [[1152, 128], [1, 32]]),
                   waits=[conv_done, ("vs", kvs_free[sl])], sem="kv" + str(sl))
        p.step("vector",
               lambda v, sl=sl: v.tensor_tensor(
                   t1, kvs[:, sl * 128:sl * 128 + 32], scb[:, :], ALU.mult),
               waits=[kd, con_ok, "ss"], incs=["vs"])
        kvs_free[sl] = p.at("vs")
        p.step("vector",
               lambda v: v.tensor_scalar(t1, t1, 20.0, -20.0, ALU.min, ALU.max),
               incs=["vs"])
        p.step("vector",
               lambda v, lt=lt: v.tensor_tensor(
                   t2[:, 0:28],
                   bass.AP(ttab, lt * 2, [[32, 128], [0, 28]]),
                   spb[:, 0:28], ALU.mult),
               incs=["vs"])
        p.step("vector",
               lambda v, lt=lt: v.tensor_tensor(
                   t2[:, 28:32],
                   bass.AP(ttab, lt * 2 + 1, [[32, 128], [0, 4]]),
                   spb[:, 28:32], ALU.mult),
               incs=["vs"])
        p.step("vector", lambda v: v.tensor_add(t1, t1, t2), incs=["vs"])
        p.step("scalar",
               lambda s, lt=lt: s.activation(pw[:, lt * 32:(lt + 1) * 32],
                                             t1, AF.Exp),
               waits=["vs"], incs=["ss"])
        w = [("ss", p.at("ss"))]
        if lt == 0:
            w += trp_clear + [("vs", p2_vs)]
        p.step("tensor",
               lambda t, lt=lt: t.matmul(psa[0:1, 3072:3104], co["onescf"][:, :],
                                         pw[:, lt * 32:(lt + 1) * 32],
                                         start=(lt == 0), stop=(lt == 15)),
               waits=w, incs=["ts"])
    p.step("vector", lambda v: v.tensor_copy(evr[0:1, 0:32], psa[0:1, 3072:3104]),
           waits=["ts"], incs=["vs"])
    tg_dma[0] = p.dma("sync", bass.AP(agin, 8192, [[32, 1], [1, 32]]),
                      evr[0:1, 0:32], waits=["vs"], sem="tg0")
    if _os.environ.get("PROBE_L0A_STOP") == "pw":
        p._l0a_last = [tg_dma[0]]
        return

    sst_ts = [p.at("ts")] * 4
    c2n_marks = [p.at("ss"), p.at("ss")]
    ev_mark = p.at("vs")
    pend = None

    def _pa_tail(st):
        (pi, tb, sl, pw_ap, lt, cp) = st
        srsem, sisem = "sr" + str(sl), "si" + str(sl)
        p.step("vector",
               lambda v, tb=tb: v.tensor_mul(
                   tt[:, tb + 1536:tb + 2048], tt[:, tb + 512:tb + 1024],
                   tt[:, tb + 512:tb + 1024]),
               waits=[("ss", c2n_marks[pi])], incs=["vs"])
        p.step("vector",
               lambda v, tb=tb: v.tensor_scalar(
                   tt[:, tb + 1536:tb + 2048], tt[:, tb + 1536:tb + 2048],
                   -2.0, 1.0, ALU.mult, ALU.add),
               incs=["vs"])
        p.step("vector",
               lambda v, sl=sl, tb=tb, pw_ap=pw_ap: v.tensor_tensor(
                   sst[:, sl * 512:(sl + 1) * 512], tt[:, tb + 1536:tb + 2048],
                   pw_ap, ALU.mult),
               waits=[(srsem, p.at(srsem)), ("ts", sst_ts[sl])], incs=["vs"])
        w = [("vs", p.at("vs"))]
        if lt == 0:
            w.append(("vs", ev_mark))
        p.step("tensor",
               lambda t, sl=sl, lt=lt: t.matmul(
                   psa[0:1, 0:512], co["onescf"][:, :],
                   sst[:, sl * 512:(sl + 1) * 512],
                   start=(lt == 0), stop=(lt == 15)),
               waits=w, incs=["ts"])
        p.dma("sync", bass.AP(a_d, (cp * 2048 + lt * 128) * 512,
                              [[512, 128], [1, 512]]),
              sst[:, sl * 512:(sl + 1) * 512], waits=["vs"], sem=srsem)
        sst_ts[sl] = p.at("ts")
        p.step("vector",
               lambda v, tb=tb: v.scalar_tensor_tensor(
                   tt[:, tb + 1536:tb + 2048], tt[:, tb + 512:tb + 1024], -2.0,
                   tt[:, tb + 0:tb + 512], ALU.mult, ALU.mult),
               incs=["vs"])
        p.step("vector",
               lambda v, sl=sl, tb=tb, pw_ap=pw_ap: v.tensor_tensor(
                   sst[:, 1024 + sl * 512:1024 + (sl + 1) * 512],
                   tt[:, tb + 1536:tb + 2048], pw_ap, ALU.mult),
               waits=[(sisem, p.at(sisem)), ("ts", sst_ts[2 + sl])],
               incs=["vs"])
        p.step("tensor",
               lambda t, sl=sl, lt=lt: t.matmul(
                   psa[0:1, 512:1024], co["onescf"][:, :],
                   sst[:, 1024 + sl * 512:1024 + (sl + 1) * 512],
                   start=(lt == 0), stop=(lt == 15)),
               waits=["vs"], incs=["ts"])
        p.dma("sync", bass.AP(a_d, ((8 + cp) * 2048 + lt * 128) * 512,
                              [[512, 128], [1, 512]]),
              sst[:, 1024 + sl * 512:1024 + (sl + 1) * 512],
              waits=["vs"], sem=sisem)
        sst_ts[2 + sl] = p.at("ts")

    for cp in range(8):
        for lt in range(16):
            it = cp * 16 + lt
            pi = it % 2
            tb = pi * 2048
            sl = lt % 2
            kd = p.dma("sync", kvs[:, sl * 128:(sl + 1) * 128],
                       bass.AP(kv_d, lt * 128 * 1152 + cp * 128,
                               [[1152, 128], [1, 128]]),
                       waits=[conv_done, ("vs", kvs_free[sl])],
                       sem="kv" + str(sl))
            kv_ap = bass.AP(bigf, sl * 128, [[11264, 128], [1, 128], [0, 4]])
            p.step("vector",
                   lambda v, cp=cp, tb=tb, kv_ap=kv_ap: v.tensor_tensor(
                       tt[:, tb + 0:tb + 512], kv_ap,
                       thb[:, cp * 512:(cp + 1) * 512], ALU.mult),
                   waits=[kd, con_ok, ("ss", c2n_marks[pi])], incs=["vs"])
            kvs_free[sl] = p.at("vs")
            p.step("scalar",
                   lambda s, tb=tb: s.activation(tt[:, tb + 512:tb + 1024],
                                                 tt[:, tb + 0:tb + 512],
                                                 AF.Sin, scale=0.5),
                   waits=["vs"], incs=["ss"])
            p.step("scalar",
                   lambda s, tb=tb: s.activation(tt[:, tb + 1024:tb + 1536],
                                                 tt[:, tb + 0:tb + 512],
                                                 AF.Abs, scale=0.5),
                   incs=["ss"])
            p.step("scalar",
                   lambda s, tb=tb: s.activation(tt[:, tb + 0:tb + 512],
                                                 tt[:, tb + 1024:tb + 1536],
                                                 AF.Sin, bias=npi2[:, 0:1]),
                   waits=["gs"], incs=["ss"])
            c2n_marks[pi] = p.at("ss")
            pw_ap = bass.AP(pw, lt * 32 + 4 * cp,
                            [[512, 128], [1, 4], [0, 128]])
            if pend is not None:
                _pa_tail(pend)
            pend = (pi, tb, sl, pw_ap, lt, cp)
        _pa_tail(pend)
        pend = None
        t_tot = p.at("ts")
        p.step("vector",
               lambda v: v.tensor_copy(evr[0:1, 0:512], psa[0:1, 0:512]),
               waits=[("ts", t_tot), ("tg0", p.at("tg0"))], incs=["vs"])
        tg_dma[0] = p.dma("sync",
                          bass.AP(agin, cp * 512, [[512, 1], [1, 512]]),
                          evr[0:1, 0:512], waits=["vs"], sem="tg0")
        p.step("vector",
               lambda v: v.tensor_copy(evr[0:1, 512:1024], psa[0:1, 512:1024]),
               waits=[("ts", t_tot), ("tg1", p.at("tg1"))], incs=["vs"])
        tg_dma[1] = p.dma("sync",
                          bass.AP(agin, 4096 + cp * 512, [[512, 1], [1, 512]]),
                          evr[0:1, 512:1024], waits=["vs"], sem="tg1")
        ev_mark = p.at("vs")

    p.step("gpsimd",
           lambda g: g.collective_compute(
               "AllGather", ALU.bypass,
               replica_groups=[[0, 1, 2, 3], [4, 5, 6, 7]],
               ins=[agin.ap().opt()], outs=[agout.ap().opt()]),
           waits=[("tg0", p.at("tg0")), ("tg1", p.at("tg1"))],
           incs=["cc"], chain=False)
    agv = ("cc", p.at("cc"))
    stash_done = [("sr0", p.at("sr0")), ("sr1", p.at("sr1")),
                  ("si0", p.at("si0")), ("si1", p.at("si1"))]
    if _os.environ.get("PROBE_L0A_STOP") == "ag":
        p._l0a_last = [agv] + stash_done
        return

    cum_re = [psa[:, 1024:1536], psa[:, 1536:2048]]
    cum_im = [psa[:, 2048:2560], psa[:, 2560:3072]]
    ph6_vs = p.at("vs")
    invfree = [ph6_vs, ph6_vs]
    prodfree_re = [ph6_vs, ph6_vs]
    prodfree_im = [ph6_vs, ph6_vs]
    c4d = p.dma("sync", c4s[:, 0:32], bass.AP(agout, 8192, [[TOT, 4], [1, 32]]),
                waits=[agv] + stash_done, sem="c40")
    p.step("tensor",
           lambda t: t.matmul(psa[0:1, 0:32], w4c[:, :], c4s[:, 0:32],
                              start=True, stop=True),
           waits=[c4d, pre_ok[0]], incs=["ts"])
    p.step("vector", lambda v: v.tensor_copy(crow[0:1, 0:32], psa[0:1, 0:32]),
           waits=["ts"], incs=["vs"])
    v_add = p.at("vs")
    for lt in range(16):
        sl = lt % 2
        p.step("tensor",
               lambda t, sl=sl, lt=lt: t.matmul(
                   cum_re[sl][:, 0:32], trif[:, :], pw[:, lt * 32:(lt + 1) * 32],
                   start=True, stop=False),
               waits=[("vs", invfree[sl]), ("vs", v_add)])
        p.step("tensor",
               lambda t, sl=sl: t.matmul(cum_re[sl][:, 0:32],
                                         co["ones1f"][:, :], crow[0:1, 0:32],
                                         start=False, stop=True),
               incs=["ts"])
        t_cum = p.at("ts")
        p.step("tensor",
               lambda t, lt=lt: t.matmul(psa[0:1, 0:32], co["onescf"][:, :],
                                         pw[:, lt * 32:(lt + 1) * 32],
                                         start=True, stop=True),
               waits=[("vs", v_add)], incs=["ts"])
        t_ct = p.at("ts")
        p.step("vector",
               lambda v: v.tensor_add(crow[0:1, 0:32], crow[0:1, 0:32],
                                      psa[0:1, 0:32]),
               waits=[("ts", t_ct)], incs=["vs"])
        v_add = p.at("vs")
        p.step("vector",
               lambda v, sl=sl, lt=lt: v.tensor_scalar_max(
                   inv[:, lt * 32:(lt + 1) * 32], cum_re[sl][:, 0:32], 1e-4),
               waits=[("ts", t_cum)], incs=["vs"])
        p.step("vector",
               lambda v, lt=lt: v.reciprocal(inv[:, lt * 32:(lt + 1) * 32],
                                             inv[:, lt * 32:(lt + 1) * 32]),
               incs=["vs"])
        invfree[sl] = p.at("vs")
    cb_mark = p.at("vs")
    if _os.environ.get("PROBE_L0A_STOP") == "den":
        p._l0a_last = [("vs", cb_mark)]
        return

    ab_used = [p.at("ts")] * 4
    qprod = p.at("vs")
    for kq in range(4):
        qw = [agv, ("vs", qprod)] + (qfold_done if kq == 0 else [])
        for f in range(4):
            p.dma("sync", qf[:, f * 2048:(f + 1) * 2048],
                  bass.AP(qf_d, f * L * 512 + kq * 128,
                          [[512, 128], [128 * 512, 16], [1, 128]]),
                  waits=qw, sem="mk2")
        qf_ok = ("mk2", p.at("mk2"))
        for cpl in range(2):
            cp = kq * 2 + cpl
            c4r = p.dma("sync", c4s[:, 0:512],
                        bass.AP(agout, cp * 512, [[TOT, 4], [1, 512]]),
                        waits=[agv], sem="c40")
            c4i = p.dma("sync", c4s[:, 512:1024],
                        bass.AP(agout, 4096 + cp * 512, [[TOT, 4], [1, 512]]),
                        waits=[agv], sem="c41")
            p.step("tensor",
                   lambda t: t.matmul(psa[0:1, 0:512], w4c[:, :],
                                      c4s[:, 0:512], start=True, stop=True),
                   waits=[c4r, ("vs", cb_mark), pre_ok[0]], incs=["ts"])
            p.step("tensor",
                   lambda t: t.matmul(psa[0:1, 512:1024], w4c[:, :],
                                      c4s[:, 512:1024], start=True, stop=True),
                   waits=[c4i], incs=["ts"])
            t_c0 = p.at("ts")
            p.step("vector",
                   lambda v: v.tensor_copy(crow[0:1, 0:512], psa[0:1, 0:512]),
                   waits=[("ts", t_c0)], incs=["vs"])
            p.step("vector",
                   lambda v: v.tensor_copy(crow[0:1, 512:1024],
                                           psa[0:1, 512:1024]),
                   incs=["vs"])
            v_add = p.at("vs")
            for lt in range(16):
                sl = lt % 2
                srs, sis = "sr" + str(sl), "si" + str(sl)
                abw = [agv] if (kq == 0 and cpl == 0 and lt < 2) else []
                ar = p.dma("sync", ab[:, sl * 512:(sl + 1) * 512],
                           bass.AP(a_d, (cp * 2048 + lt * 128) * 512,
                                   [[512, 128], [1, 512]]),
                           waits=abw + [(srs, p.at(srs)), ("ts", ab_used[sl])],
                           sem=srs)
                ai = p.dma("sync", ab[:, 1024 + sl * 512:1024 + (sl + 1) * 512],
                           bass.AP(a_d, ((8 + cp) * 2048 + lt * 128) * 512,
                                   [[512, 128], [1, 512]]),
                           waits=abw + [(sis, p.at(sis)),
                                        ("ts", ab_used[2 + sl])],
                           sem=sis)
                p.step("tensor",
                       lambda t, sl=sl: t.matmul(
                           cum_re[sl][:, :], trif[:, :],
                           ab[:, sl * 512:(sl + 1) * 512],
                           start=True, stop=False),
                       waits=[ar, ("vs", prodfree_re[sl]), ("vs", v_add)])
                p.step("tensor",
                       lambda t, sl=sl: t.matmul(
                           cum_re[sl][:, :], co["ones1f"][:, :],
                           crow[0:1, 0:512], start=False, stop=True),
                       incs=["ts"])
                p.step("tensor",
                       lambda t, sl=sl: t.matmul(
                           cum_im[sl][:, :], trif[:, :],
                           ab[:, 1024 + sl * 512:1024 + (sl + 1) * 512],
                           start=True, stop=False),
                       waits=[ai, ("vs", prodfree_im[sl])])
                p.step("tensor",
                       lambda t, sl=sl: t.matmul(
                           cum_im[sl][:, :], co["ones1f"][:, :],
                           crow[0:1, 512:1024], start=False, stop=True),
                       incs=["ts"])
                p.step("tensor",
                       lambda t, sl=sl: t.matmul(
                           psa[0:1, 0:512], co["onescf"][:, :],
                           ab[:, sl * 512:(sl + 1) * 512],
                           start=True, stop=True),
                       waits=[("vs", v_add)], incs=["ts"])
                p.step("tensor",
                       lambda t, sl=sl: t.matmul(
                           psa[0:1, 512:1024], co["onescf"][:, :],
                           ab[:, 1024 + sl * 512:1024 + (sl + 1) * 512],
                           start=True, stop=True),
                       incs=["ts"])
                t_cyc = p.at("ts")
                ab_used[sl] = t_cyc
                ab_used[2 + sl] = t_cyc
                p.step("vector",
                       lambda v: v.tensor_add(crow[0:1, 0:512],
                                              crow[0:1, 0:512],
                                              psa[0:1, 0:512]),
                       waits=[("ts", t_cyc)], incs=["vs"])
                p.step("vector",
                       lambda v: v.tensor_add(crow[0:1, 512:1024],
                                              crow[0:1, 512:1024],
                                              psa[0:1, 512:1024]),
                       incs=["vs"])
                v_add = p.at("vs")
                cb_mark = v_add
                aps = [bass.AP(bigb, f * 2048 + lt * 128,
                               [[17408, 128], [0, 4], [1, 128]])
                       for f in range(4)]
                stw = (stash_done if (kq == 0 and cpl == 0 and lt == 0) else [])
                p.step("vector",
                       lambda v, sl=sl, a0=aps[0]: v.tensor_tensor(
                           ttB[:, 0:512], cum_re[sl][:, :], a0, ALU.mult),
                       waits=stw + [("ts", t_cyc), qf_ok], incs=["vs"])
                p.step("vector",
                       lambda v, sl=sl, a3=aps[3]: v.tensor_tensor(
                           ttB[:, 512:1024], cum_im[sl][:, :], a3, ALU.mult),
                       incs=["vs"])
                p.step("vector",
                       lambda v: v.tensor_add(ttB[:, 1024:1536], ttB[:, 0:512],
                                              ttB[:, 512:1024]),
                       incs=["vs"])
                osl = (lt % 2) * 2
                osem, osem2 = "o" + str(osl), "o" + str(osl + 1)
                p.step("vector",
                       lambda v, osl=osl: v.tensor_reduce(
                           orow[:, osl * 128:(osl + 1) * 128],
                           ttB[:, 1024:1536].rearrange("p (a b) -> p a b", b=4),
                           AX.X, ALU.add),
                       waits=stw + [(osem, p.at(osem))], incs=["vs"])
                p.dma("sync",
                      bass.AP(o_d2, lt * 128 * 1024 + cp * 128,
                              [[1024, 128], [1, 128]]),
                      orow[:, osl * 128:(osl + 1) * 128], waits=["vs"],
                      sem=osem)
                p.step("vector",
                       lambda v, sl=sl, a1=aps[1]: v.tensor_tensor(
                           ttB[:, 0:512], cum_im[sl][:, :], a1, ALU.mult),
                       incs=["vs"])
                prodfree_im[sl] = p.at("vs")
                p.step("vector",
                       lambda v, sl=sl, a2=aps[2]: v.tensor_tensor(
                           ttB[:, 512:1024], cum_re[sl][:, :], a2, ALU.mult),
                       incs=["vs"])
                prodfree_re[sl] = p.at("vs")
                p.step("vector",
                       lambda v: v.tensor_sub(ttB[:, 1024:1536], ttB[:, 0:512],
                                              ttB[:, 512:1024]),
                       incs=["vs"])
                p.step("vector",
                       lambda v, osl=osl: v.tensor_reduce(
                           orow[:, (osl + 1) * 128:(osl + 2) * 128],
                           ttB[:, 1024:1536].rearrange("p (a b) -> p a b", b=4),
                           AX.X, ALU.add),
                       waits=[(osem2, p.at(osem2))], incs=["vs"])
                p.dma("sync",
                      bass.AP(o_d2, (L + lt * 128) * 1024 + cp * 128,
                              [[1024, 128], [1, 128]]),
                      orow[:, (osl + 1) * 128:(osl + 2) * 128], waits=["vs"],
                      sem=osem2)
        qprod = p.at("vs")

    if _os.environ.get("PROBE_L0A_STOP") == "passb":
        p._l0a_last = [("o%d" % i, p.at("o%d" % i)) for i in range(4)]
        return
    pb_ts = ("ts", p.at("ts"))
    pb_vs = ("vs", p.at("vs"))
    ob_done = [("o%d" % i, p.at("o%d" % i)) for i in range(4)]
    wov = p.dma("gpsimd", wout,
                bass.AP(dr["l0a_wout"], 0,
                        [[1024, 128], [128 * 1024, 8], [1, 1024]]),
                waits=[agv], sem="wo")
    c1_ap = bass.AP(c12, 0, [[64, 128], [0, 32], [1, 32]])
    c2_ap = bass.AP(c12, 32, [[64, 128], [0, 32], [1, 32]])
    os_ts = [pb_ts[1]] * 2
    os_vs = [pb_vs[1]] * 2
    x0_free = [pb_vs[1]] * 2
    dfree = [pb_vs[1]] * 4
    xo_used = [pb_ts[1]] * 2
    for lt in range(16):
        sl = lt % 2
        ysem, xsem = "ya" + str(sl), "sa" + str(sl)
        ow = (ob_done + [pb_ts, pb_vs]) if lt < 2 else []
        ol = p.dma("sync", oslab[:, sl * 2048:(sl + 1) * 2048],
                   bass.AP(o_d2, lt * 128 * 1024,
                           [[1024, 128], [L * 1024, 2], [1, 1024]]),
                   waits=ow + [(ysem, p.at(ysem)), ("ts", os_ts[sl]),
                               ("vs", os_vs[sl])],
                   sem=ysem)
        xl = p.dma("sync", lns2[:, sl * 1024:(sl + 1) * 1024],
                   bass.AP(x_in, lt * 128 * 1024, [[1024, 128], [1, 1024]]),
                   waits=[(xsem, p.at(xsem)), ("vs", x0_free[sl])], sem=xsem)
        orh = oslab[:, sl * 2048:sl * 2048 + 1024]
        oih = oslab[:, sl * 2048 + 1024:sl * 2048 + 2048]
        inv_ap = bass.AP(inv, lt * 32, [[512, 128], [1, 32], [0, 32]])
        p.step("vector",
               lambda v, orh=orh, inv_ap=inv_ap: v.tensor_tensor(
                   ytm[:, :], orh, inv_ap, ALU.mult),
               waits=[ol, "ss"] + ([pb_ts] if lt == 0 else []), incs=["vs"])
        p.step("vector",
               lambda v: v.tensor_tensor(ytm[:, :], ytm[:, :], c1_ap, ALU.mult),
               incs=["vs"])
        p.step("vector",
               lambda v, oih=oih, inv_ap=inv_ap: v.tensor_tensor(
                   oih, oih, inv_ap, ALU.mult),
               incs=["vs"])
        p.step("vector",
               lambda v, oih=oih: v.tensor_tensor(oih, oih, c2_ap, ALU.mult),
               incs=["vs"])
        p.step("vector",
               lambda v, oih=oih: v.tensor_add(ytm[:, :], ytm[:, :], oih),
               incs=["vs"])
        p.step("scalar",
               lambda s, orh=orh: s.activation(orh, ytm[:, :], AF.Silu),
               waits=["vs"], incs=["ss"])
        s_y = p.at("ss")
        for ci in range(8):
            tsl = ci % 2
            p.step("tensor",
                   lambda t, tsl=tsl, orh=orh, ci=ci: t.transpose(
                       trpL[tsl][:, :], orh[:, ci * 128:(ci + 1) * 128],
                       co["identf"][:, :]),
                   waits=[("ss", s_y), ("vs", tr_free[tsl]), co["ready2"]],
                   incs=["ts"])
            t_tr = p.at("ts")
            p.step("vector",
                   lambda v, sl=sl, ci=ci, tsl=tsl: v.tensor_copy(
                       yfm[:, sl * 1024 + ci * 128:sl * 1024 + (ci + 1) * 128],
                       trpL[tsl][:, :]),
                   waits=[("ts", t_tr)], incs=["vs"])
            tr_free[tsl] = p.at("vs")
        os_ts[sl] = p.at("ts")
        yv = p.at("vs")
        if _os.environ.get("PROBE_L0A_ASM") == "ytr":
            os_vs[sl] = p.at("vs")
            continue
        for ob in range(2):
            dslot = psa[:, (sl * 2 + ob) * 512:(sl * 2 + ob + 1) * 512]
            for ci in range(8):
                w = [wov, ("vs", yv), ("vs", dfree[sl * 2 + ob])] \
                    if ci == 0 else []
                p.step("tensor",
                       lambda t, sl=sl, ob=ob, ci=ci, dslot=dslot: t.matmul(
                           dslot,
                           yfm[:, sl * 1024 + ci * 128:sl * 1024 + (ci + 1) * 128],
                           wout[:, ci * 1024 + ob * 512:ci * 1024 + (ob + 1) * 512],
                           start=(ci == 0), stop=(ci == 7)),
                       waits=w, incs=(["ts"] if ci == 7 else []))
            t_d = p.at("ts")
            p.step("vector",
                   lambda v, sl=sl, ob=ob, dslot=dslot: v.tensor_add(
                       xo[:, sl * 1024 + ob * 512:sl * 1024 + (ob + 1) * 512],
                       dslot,
                       lns2[:, sl * 1024 + ob * 512:sl * 1024 + (ob + 1) * 512]),
                   waits=[("ts", t_d), xl, ("ts", xo_used[sl])], incs=["vs"])
            dfree[sl * 2 + ob] = p.at("vs")
        x0_free[sl] = p.at("vs")
        os_vs[sl] = p.at("vs")
        if _os.environ.get("PROBE_L0A_ASM") == "wout":
            continue
        v_x1 = p.at("vs")
        stsem = "st" + str(sl)
        for ci in range(8):
            tsl = ci % 2
            p.step("tensor",
                   lambda t, sl=sl, ci=ci, tsl=tsl: t.transpose(
                       trpL[tsl][:, :],
                       xo[:, sl * 1024 + ci * 128:sl * 1024 + (ci + 1) * 128],
                       co["identf"][:, :]),
                   waits=[("vs", v_x1), ("vs", tr_free[tsl]), co["ready2"]],
                   incs=["ts"])
            t_tr = p.at("ts")
            w = [("ts", t_tr)]
            if ci == 0:
                w.append((stsem, p.at(stsem)))
            p.step("vector",
                   lambda v, sl=sl, ci=ci, tsl=tsl: v.tensor_copy(
                       xfs[:, sl * 1024 + ci * 128:sl * 1024 + (ci + 1) * 128],
                       trpL[tsl][:, :]),
                   waits=w, incs=["vs"])
            tr_free[tsl] = p.at("vs")
        xo_used[sl] = p.at("ts")
        if _os.environ.get("PROBE_L0A_ASM") != "xtr":
            for ci in range(8):
                p.dma("sync",
                      bass.AP(x_out, ci * 128 * L + lt * 128,
                              [[L, 128], [1, 128]]),
                      xfs[:, sl * 1024 + ci * 128:sl * 1024 + (ci + 1) * 128],
                      waits=["vs"], sem=stsem)
    if _os.environ.get("PROBE_L0A_ASM") in ("ytr", "wout", "xtr"):
        p._l0a_last = [("vs", p.at("vs")), ("ts", p.at("ts"))]
    else:
        p._l0a_last = [("st0", p.at("st0")), ("st1", p.at("st1"))]


L0A_INPUTS = [
    ("l0a_x0h", [128, 1024]), ("l0a_wmem", [1024, 1152]),
    ("l0a_wq", [1024, 2048]), ("l0a_ckc", [128, 36]),
    ("l0a_throw", [1, 4096]), ("l0a_slopes", [1, 32]),
    ("l0a_score", [1, 32]), ("l0a_ttab", [2048, 2]),
    ("l0a_c1", [1, 32]), ("l0a_c2", [1, 32]),
    ("l0a_wout", [1024, 1024]), ("l0a_tri", [128, 128]),
    ("l0a_w4", [4, 1]), ("l0a_flag", [128, 1]),
]


def build_l0_attn_standalone():
    nc = bass.Bass("TRN2", target_bir_lowering=False)
    with ExitStack() as es:
        p = Prog(nc, es)
        x_in = nc.dram_tensor("x_in", [L, 1024], F32, kind="ExternalInput")
        x_out = nc.dram_tensor("x_out", [1024, L], F32, kind="ExternalOutput")
        ident = nc.dram_tensor("c_ident", [128, 128], F32, kind="ExternalInput")
        dr = {nm: nc.dram_tensor(nm, sh, F32, kind="ExternalInput")
              for nm, sh in L0A_INPUTS}
        psa = p.psum("psa", [128, 4096])
        co = setup_consts(p, ident)
        emit_l0_attn(p, co, dr, psa, x_in, x_out)
        p.step("sync", lambda sy: None, waits=p._l0a_last, chain=False)
        p.build()
    return nc


def run_l0_attn_standalone(x0_full, ins):
    nc = build_l0_attn_standalone()
    id_ = np.eye(128, dtype=np.float32)
    in_maps = []
    for ci in range(NCORE):
        b, cch = divmod(ci, 4)
        w = prep_l0_attn(ins, ci)
        m = {k: w[k] for k in w}
        m["c_ident"] = id_
        m["x_in"] = np.ascontiguousarray(x0_full[b, cch * L:(cch + 1) * L, :])
        if cch == 0:
            m["l0a_x0h"] = np.zeros((128, 1024), np.float32)
        else:
            m["l0a_x0h"] = np.ascontiguousarray(
                x0_full[b, cch * L - 128:cch * L, :])
        in_maps.append(m)
    res = run_bass_kernel_spmd(nc, in_maps, core_ids=list(range(NCORE)))
    out = np.empty((NB, SEQ, D), np.float32)
    for ci in range(NCORE):
        b, cch = divmod(ci, 4)
        out[b, cch * L:(cch + 1) * L, :] = np.asarray(
            res.results[ci]["x_out"], np.float32).T
    return out



def _barrier(p, marks):
    for eng in ENGINES:
        p.step(eng, lambda e: None, waits=marks, chain=False)


def build_fused():
    es = ExitStack()
    nc = bass.Bass("TRN2", target_bir_lowering=False)
    x_in = nc.dram_tensor("x_in", [L, 1024], F32, kind="ExternalInput")
    ident = nc.dram_tensor("c_ident", [128, 128], F32, kind="ExternalInput")
    x_out = nc.dram_tensor("x_out", [D, L], F32, kind="ExternalOutput")
    dr = {nm: nc.dram_tensor(nm, sh, F32, kind="ExternalInput")
          for nm, sh in L0A_INPUTS}
    for nm, sh, dt in [
            ("l0f_tabs", [4, 2560, 64], F32), ("l0f_masks", [12, 128, 512], F32),
            ("l0f_flag", [128, 1], F32), ("l0f_proff", [1, 2], mybir.dt.int32),
            ("l0f_wqkv", [2, 1024, 1536], F32), ("l0f_wo", [2, 512, 1024], F32),
            ("l0f_qs", [2, 64], F32), ("l0f_ks", [2, 64], F32),
            ("l0f_lng", [2, 1024], F32), ("l0f_lnb", [2, 1024], F32),
            ("l0f_b2", [2, 1024], F32), ("l0f_w1i", [2, 1024, 5632], F32),
            ("l0f_b1i", [2, 5632], F32), ("l0f_w2p", [2, 2816, 1024], F32),
            ("l1a_wqT", [1024, 1024], F32), ("l1a_wkT", [1024, 1024], F32),
            ("l1a_wvT", [1024, 1024], F32), ("l1a_woT", [1024, 1024], F32),
            ("l1a_bout", [1024], F32), ("l1a_e2", [1024, 16], F32),
            ("l1f_bd8", [4, 8, 128, 128], F32), ("l1f_bd2", [2, 64, 32, 128], F32),
            ("l1f_wcT", [16384, 1024], F32)]:
        dr[nm] = nc.dram_tensor(nm, sh, dt, kind="ExternalInput")
    xd1 = nc.dram_tensor("xd1", [D, L], F32)
    xd2 = nc.dram_tensor("xd2", [D, L], F32)
    xd3 = nc.dram_tensor("xd3", [D, L], F32)
    with es:
        p = Prog(nc, es)
        psa = p.psum("psa", [128, 4096])
        co = setup_consts(p)
        lsc = make_ln_scratch(p)
        with p.scope():
            co["identf"] = p.sbuf("c_identf", [128, 128])
            p.dma("gpsimd", co["identf"][:, :], ident.ap())
            co["ready2"] = ("dgp", p.at("dgp"))
            with p.scope():
                emit_l0_attn(p, co, dr, psa, x_in, xd1)
            _barrier(p, p._l0a_last)
            with p.scope():
                m1, m2 = emit_l0_ffn(p, co, dr, psa, xd1, xd2)
        _barrier(p, [m1, m2])
        with p.scope():
            xs = p.sbuf("xs_a", [128, 8 * L])
            xc = p.sbuf("xc_a", [128, 8 * L], BF16)
            ld = load_stream_fm(p, xs, xd2)
            lnr = emit_ln_fm(p, co, xs, xc, "ln2", [ld], psa, lsc)
            fr = emit_l1_attn(p, co, xs, xc, dr, [lnr], psa)
            st = store_stream_fm(p, xd3, xs, [fr])
        _barrier(p, [st])
        with p.scope():
            xs2 = p.sbuf("xs_f", [128, 8 * L])
            xc2 = p.sbuf("xc_f", [128, 8 * L], BF16)
            ld = load_stream_fm(p, xs2, xd3)
            lnr = emit_ln_fm(p, co, xs2, xc2, "ln3", [ld], psa, lsc)
            fr = emit_l1_ffn(p, co, xs2, xc2, dr, [lnr], psa)
            st2 = store_stream_fm(p, x_out, xs2, [fr])
        p.step("sync", lambda sy: None, waits=[st2], chain=False)
        p.build()
    return nc


def kernel_fused(**inputs):
    ins = {k: np.asarray(v) for k, v in inputs.items()}
    x0 = np.asarray(ins["L0_attn_x"], np.float32)
    id_ = np.eye(128, dtype=np.float32)
    cores = list(range(NCORE))
    nc = build_fused()
    wa = prep_l1_attn(ins)
    wf = prep_l1_ffn(ins)
    in_maps = []
    for ci in cores:
        b, cch = divmod(ci, 4)
        m = dict(prep_l0_attn(ins, ci))
        m.update(prep_l0_ffn(ins, ci))
        m.update(wa)
        m.update(wf)
        m["c_ident"] = id_
        m["x_in"] = np.ascontiguousarray(x0[b, cch * L:(cch + 1) * L, :])
        m["l0a_x0h"] = (np.zeros((128, 1024), np.float32) if cch == 0 else
                        np.ascontiguousarray(x0[b, cch * L - 128:cch * L, :]))
        in_maps.append(m)
    res = run_bass_kernel_spmd(nc, in_maps, core_ids=cores)
    out = np.empty((NB, SEQ, D), np.float32)
    for ci in cores:
        b, cch = divmod(ci, 4)
        out[b, cch * L:(cch + 1) * L, :] = np.asarray(
            res.results[ci]["x_out"], np.float32).T
    return out


def kernel(**inputs):
    return kernel_fused(**inputs)
